# version 01
# baseline (speedup 1.0000x reference)
import numpy as np
import ml_dtypes
import concourse.bass as bass
import concourse.mybir as mybir
from concourse.bass import IndirectOffsetOnAxis
from concourse.bass_utils import run_bass_kernel_spmd

F32 = mybir.dt.float32
BF16 = mybir.dt.bfloat16
I32 = mybir.dt.int32
AF = mybir.ActivationFunctionType
ALU = mybir.AluOpType
AX = mybir.AxisListType
BF = ml_dtypes.bfloat16

T = 2048
D = 1024
NCH = 8
TB = 512
NTB = T // TB
H = 16
INV_SQRT96 = float(1.0 / np.sqrt(96.0))
SQRT96 = float(np.sqrt(96.0))

ENGINES = ("tensor", "vector", "scalar", "gpsimd", "sync")

import os
_BLK_LIMIT = int(os.environ.get("K_BLK_LIMIT", "0")) or None
_BLK_COUNT = [0]



def host_constants():
    c = {}
    c["ident_bf"] = np.eye(128, dtype=BF)
    c["ident_f32"] = np.eye(128, dtype=np.float32)
    c["ones_col_bf"] = np.ones((128, 1), dtype=BF)
    c["ones128_bf"] = np.ones((128, 128), dtype=BF)
    c["ones_f32_row"] = np.ones((1, 128), dtype=np.float32)
    c["eps_col"] = np.full((128, 1), 1e-5, dtype=np.float32)
    m = np.zeros((128, 4 * TB), dtype=BF)
    for r in range(4):
        j = np.arange(128)[:, None] + 128 * r
        i = np.arange(TB)[None, :]
        m[:, r * TB:(r + 1) * TB] = (j <= i).astype(BF)
    c["diag_masks"] = m
    t = np.arange(T, dtype=np.float32)
    inv = (1.0 / 10000.0 ** (np.arange(0, 32, 2, dtype=np.float32) / 32)) \
        .astype(np.float32)
    freqs = t[:, None] * inv[None, :]
    emb = np.concatenate([freqs, freqs], -1).astype(np.float32)
    scale = (np.cos(emb) + np.sin(emb)).astype(np.float32)
    c["scaleq"] = np.ascontiguousarray(np.tile(scale.T, (4, 1)).astype(np.float32))
    c["scalek"] = np.ascontiguousarray(scale.T.astype(np.float32))
    eh = np.zeros((128, NCH * 16), dtype=BF)
    for cc in range(NCH):
        for p in range(128):
            eh[p, cc * 16 + 2 * cc + (p >= 64)] = 1.0
    c["ehall"] = eh
    c["ones32x16"] = np.ones((32, 16), dtype=BF)
    selh = np.zeros((32, 16 * 16), dtype=BF)
    for h in range(16):
        selh[:, h * 16 + h] = 1.0
    c["selh"] = selh
    c["iota32"] = np.broadcast_to(np.arange(T, dtype=np.float32),
                                  (32, T)).copy()
    c["base32"] = np.array([0.0, 0.02063896507024765, 0.04340777546167374, 0.06852622330188751, 0.09623675793409348, 0.1268068552017212, 0.1605316549539566, 0.19773666560649872, 0.23878106474876404, 0.2840610444545746, 0.33401361107826233, 0.38912108540534973, 0.4499153792858124, 0.5169833302497864, 0.5909722447395325, 0.6725966334342957, 0.7626441121101379, 0.8619838953018188, 0.9715754389762878, 1.0924760103225708, 1.2258524894714355, 1.3729935884475708, 1.535318374633789, 1.7143945693969727, 1.9119499921798706, 2.129892349243164, 2.3703253269195557, 2.6355690956115723, 2.92818546295166, 3.2509994506835938, 3.6071248054504395, 4.000000953674316], dtype=np.float32).reshape(32, 1)
    ec = np.zeros((32, 128), dtype=np.float32)
    es = np.zeros((32, 128), dtype=np.float32)
    for dd in range(128):
        m_ = (dd % 64) // 2
        ec[m_, dd] = 1.0
        es[m_, dd] = 1.0 if dd % 2 == 1 else -1.0
    c["ec"] = ec
    c["es"] = es
    perm = np.zeros((128, 128), dtype=BF)
    for dd in range(0, 128, 2):
        perm[dd + 1, dd] = 1.0
        perm[dd, dd + 1] = 1.0
    c["perm"] = perm
    return c


CONSTS = host_constants()



class Ctx:
    def __init__(self, nc):
        self.nc = nc
        self.semh = {}
        self.cnt = {}

    def add_sem(self, name):
        self.semh[name] = self.nc.semaphore(name).__enter__()
        self.cnt[name] = 0

    def bump(self, name, n=1):
        self.cnt[name] += n
        return self.cnt[name]


class Blk:
    def __init__(self, ctx):
        self.ctx = ctx
        self.ops = {e: [] for e in ENGINES}

    def add(self, eng, fn, inc=None, n=1):
        if inc is None:
            self.ops[eng].append(fn)
            return None
        h = self.ctx.semh[inc]
        v = self.ctx.bump(inc, n)
        self.ops[eng].append(lambda e, f=fn, h=h, n=n: f(e).then_inc(h, n))
        return v

    def wait(self, eng, sem, val):
        if val is None or val <= 0:
            return
        assert val <= self.ctx.cnt[sem], (sem, val, self.ctx.cnt[sem])
        h = self.ctx.semh[sem]
        self.ops[eng].append(lambda e, h=h, v=val: e.wait_ge(h, v))

    def vf(self, fn):
        v = self.add("vector", fn, inc="ev")
        self.wait("vector", "ev", v)
        return v

    def dma(self, queue, out_ap, in_ap, sem):
        return self.add(queue, lambda e, o=out_ap, i=in_ap: e.dma_start(o, i),
                        inc=sem, n=16)

    def mm(self, out, lhsT, rhs, start, stop, inc=None):
        return self.add(
            "tensor",
            lambda e, o=out, l=lhsT, r=rhs, s=start, p=stop:
            e.matmul(o, l, r, start=s, stop=p, skip_group_check=True),
            inc=inc)

    def tr(self, out, in_, ident, inc=None):
        return self.add(
            "tensor",
            lambda e, o=out, i=in_, d=ident: e.transpose(o, i, d), inc=inc)

    def emit(self):
        _BLK_COUNT[0] += 1
        if _BLK_LIMIT is not None and _BLK_COUNT[0] > _BLK_LIMIT:
            return
        nc = self.ctx.nc
        ops = self.ops
        with nc.Block() as blk:
            for name in ENGINES:
                if not ops[name]:
                    continue

                def body(eng, lst=tuple(ops[name])):
                    for f in lst:
                        f(eng)
                getattr(blk, name)(body)



CONST_DEFS = {
    "ONESC": ("ones_col_bf", [128, 1], BF16),
    "ONES128": ("ones128_bf", [128, 128], BF16),
    "ONESR": ("ones_f32_row", [1, 128], F32),
    "EPSC": ("eps_col", [128, 1], F32),
    "DMASK": ("diag_masks", [128, 4 * TB], BF16),
    "SCQ": ("scaleq", [128, T], F32),
    "SCK": ("scalek", [32, T], F32),
    "EHALL": ("ehall", [128, NCH * 16], BF16),
    "O32X16": ("ones32x16", [32, 16], BF16),
    "SELH": ("selh", [32, 16 * 16], BF16),
    "IDB": ("ident_bf", [128, 128], BF16),
    "IDF": ("ident_f32", [128, 128], F32),
}
CORE_CONSTS = ["ONESC", "ONES128", "ONESR", "EPSC", "IDB", "IDF"]
A_CONSTS = CORE_CONSTS + ["DMASK", "SCQ", "SCK", "EHALL", "O32X16", "SELH"]


def alloc_consts(nc, keys=None):
    sb = nc.alloc_sbuf_tensor
    C = {}
    for k in (keys or list(CONST_DEFS)):
        ion, shape, dt = CONST_DEFS[k]
        C[k] = sb("K_" + k, shape, dt)
    return C


def load_consts(blk, C, cio):
    for sbn in C:
        ion = CONST_DEFS[sbn][0]
        blk.dma("sync", C[sbn][:, :], cio[ion][:, :], "ds")


def emit_ln(ctx, S, C, pb, src, dst, ln_w_col=None, ln_b_col=None,
            dst_fn=None, fp32_cb=None):
    XB, SQ, MU, RSTD = S["XB"], S["SQ"], S["MU"], S["RSTD"]
    TF1, TF2 = S["TF1"], S["TF2"]
    blk = Blk(ctx)
    cast_v = []
    for c in range(NCH):
        v = blk.add("vector", lambda e, c=c: e.tensor_copy(
            XB[:, c * TB:(c + 1) * TB], src[:, c * TB:(c + 1) * TB]), inc="ev")
        cast_v.append(v)
    sq_v = []
    for c in range(NCH):
        blk.wait("scalar", "ev", cast_v[c])
        v = blk.add("scalar", lambda e, c=c: e.activation(
            SQ[:, c * TB:(c + 1) * TB], XB[:, c * TB:(c + 1) * TB],
            AF.Square), inc="ea")
        sq_v.append(v)
    for c in range(NCH):
        blk.wait("tensor", "ev", cast_v[c])
        blk.mm(pb[0][:, :], C["ONES128"][:, :], XB[:, c * TB:(c + 1) * TB],
               start=(c == 0), stop=(c == NCH - 1))
    v = None
    for c in range(NCH):
        blk.wait("tensor", "ea", sq_v[c])
        v = blk.mm(pb[1][:, :], C["ONES128"][:, :], SQ[:, c * TB:(c + 1) * TB],
                   start=(c == 0), stop=(c == NCH - 1),
                   inc=("mm" if c == NCH - 1 else None))
    blk.wait("vector", "mm", v)
    blk.add("vector", lambda e: e.tensor_scalar(
        MU[:, :], pb[0][:, :], 1.0 / D, None, ALU.mult))
    blk.add("vector", lambda e: e.tensor_mul(TF1[:, :], MU[:, :], MU[:, :]))
    v = blk.add("vector", lambda e: e.scalar_tensor_tensor(
        TF2[:, :], pb[1][:, :], 1.0 / D, TF1[:, :], ALU.mult, ALU.subtract),
        inc="ev")
    blk.wait("scalar", "ev", v)
    v = blk.add("scalar", lambda e: e.activation(
        TF1[:, :], TF2[:, :], AF.Sqrt, bias=C["EPSC"][:, 0:1]), inc="ea")
    blk.wait("vector", "ea", v)
    blk.add("vector", lambda e: e.reciprocal(RSTD[:, :], TF1[:, :]))
    out_v = None
    cb_mm = None
    for c in range(NCH):
        dstap = dst_fn(c) if dst_fn is not None else \
            dst[:, c * TB:(c + 1) * TB]
        blk.add("vector", lambda e, c=c: e.tensor_sub(
            TF2[:, :], src[:, c * TB:(c + 1) * TB], MU[:, :]))
        if ln_w_col is None:
            if fp32_cb is None:
                out_v = blk.add("vector", lambda e, c=c, dstap=dstap:
                                e.tensor_mul(dstap, TF2[:, :], RSTD[:, :]),
                                inc=("ev" if c == NCH - 1 else None))
            else:
                if c > 0 and cb_mm is not None:
                    blk.wait("vector", "mm", cb_mm)
                fv = blk.add("vector", lambda e, c=c: e.tensor_mul(
                    TF1[:, :], TF2[:, :], RSTD[:, :]), inc="ev")
                out_v = blk.add("vector", lambda e, c=c, dstap=dstap:
                                e.tensor_copy(dstap, TF1[:, :]),
                                inc=("ev" if c == NCH - 1 else None))
                cb_mm = fp32_cb(blk, c, TF1, fv)
        else:
            blk.add("vector", lambda e, c=c: e.tensor_mul(
                TF2[:, :], TF2[:, :], RSTD[:, :]))
            out_v = blk.add("vector", lambda e, c=c, dstap=dstap:
                            e.tensor_scalar(dstap, TF2[:, :],
                                            ln_w_col[:, c:c + 1],
                                            ln_b_col[:, c:c + 1],
                                            ALU.mult, ALU.add),
                            inc=("ev" if c == NCH - 1 else None))
    blk.emit()
    return out_v


def stream_mm(ctx, blk, specs, ws, n_kc, moving, psum_of, evac,
              stream_engine="gpsimd"):
    nb = len(specs)
    mm_tgt = [None] * nb
    ev_tgt = [None] * nb
    for i in range(nb):
        buf = ws[i % 2]
        if i >= 2:
            blk.wait(stream_engine, "mm", mm_tgt[i - 2])
        ld = blk.dma(stream_engine,
                     buf[:, 0:n_kc * 128].rearrange("p (c n) -> p c n",
                                                    c=n_kc),
                     specs[i], "dg")
        blk.wait("tensor", "dg", ld)
        if i >= 4:
            blk.wait("tensor", ev_tgt[i - 4][0], ev_tgt[i - 4][1])
        pbank = psum_of(i)
        for kc in range(n_kc):
            last = kc == n_kc - 1
            mm_tgt[i] = blk.mm(pbank, buf[:, kc * 128:(kc + 1) * 128],
                               moving(i, kc), start=(kc == 0), stop=last,
                               inc=("mm" if last else None))
        ev_tgt[i] = evac(blk, i, mm_tgt[i])
    return mm_tgt, ev_tgt



def alloc_stage_A(nc):
    sb = nc.alloc_sbuf_tensor
    S = {}
    S["XF"] = sb("A_XF", [128, NCH * TB], F32)
    S["XLN"] = sb("A_XLN", [128, NCH * TB], BF16)
    S["XB"] = sb("A_XB", [128, NCH * TB], BF16)
    S["SQ"] = sb("A_SQ", [128, NCH * TB], BF16)
    S["MU"] = sb("A_MU", [128, TB], F32)
    S["RSTD"] = sb("A_RSTD", [128, TB], F32)
    S["TF1"] = sb("A_TF1", [128, TB], F32)
    S["TF2"] = sb("A_TF2", [128, TB], F32)
    S["TF3"] = None
    S["T16"] = sb("A_T16", [16, 1], F32)
    S["TF3"] = S["MU"]
    S["KC"] = sb("A_KC", [128, NCH * T], BF16)
    S["VT"] = sb("A_VT", [128, 16 * 16 * 80], BF16)
    S["KRA"] = sb("A_KRA", [97, T], BF16)
    S["QC"] = sb("A_QC", [128, NCH * TB], BF16)
    S["QRA"] = sb("A_QRA", [97, H * TB], BF16)
    S["SQR"] = sb("A_SQR", [32, 2 * TB], BF16)
    S["KN2H"] = sb("A_KN2H", [16, T], F32)
    S["KMS16"] = sb("A_KMS16", [16, 1], F32)
    S["QN16"] = sb("A_QN16", [16, TB], F32)
    S["B16"] = sb("A_B16", [16, TB], F32)
    S["RD"] = [sb(f"A_RD{i}", [1, TB], F32) for i in range(2)]
    S["RDB"] = [sb(f"A_RDB{i}", [64, TB], F32) for i in range(2)]
    S["PT"] = [sb(f"A_PT{i}", [128, TB], BF16) for i in range(2)]
    S["PTM"] = [sb(f"A_PTM{i}", [128, TB], BF16) for i in range(2)]
    S["AO"] = S["XB"]
    S["WS"] = [sb(f"A_WS{i}", [128, NCH * 128], BF16) for i in range(2)]
    S["WT"] = None
    S["WM"] = [sb(f"A_WM{i}", [128, D], BF16) for i in range(2)]
    S["WT"] = S["WM"]
    return S


def emit_compose_mats(ctx, S, C, pb, io):
    nc = ctx.nc
    wdq, wuq, wqr = io["L0_attn_W_DQ"], io["L0_attn_W_UQ"], io["L0_attn_W_QR"]
    wdkv, wuk, wuv = io["L0_attn_W_DKV"], io["L0_attn_W_UK"], io["L0_attn_W_UV"]
    wdqt, wdkvt = io["scr_WDQT"], io["scr_WDKVT"]
    m1, m2, m3, m4 = io["scr_M1"], io["scr_M2"], io["scr_M3"], io["scr_M4"]
    WS, WT = S["WS"], S["WT"]

    blk = Blk(ctx)
    st_t = {}
    ev_hist = {}
    for i in range(34):
        buf = WS[i % 2]
        obuf = WT[i % 2]
        if i >= 2:
            blk.wait("gpsimd", "mm", st_t[i - 2][0])
        if i < 32:
            src = wdq[:, i * 128:(i + 1) * 128].rearrange(
                "(c p) n -> p c n", c=NCH)
        else:
            src = wdkv[:, (i - 32) * 128:(i - 31) * 128].rearrange(
                "(c p) n -> p c n", c=NCH)
        ld = blk.dma("gpsimd", buf[:, :].rearrange("p (c n) -> p c n", c=NCH),
                     src, "dg")
        blk.wait("tensor", "dg", ld)
        if i >= 2:
            blk.wait("vector", "ds", st_t[i - 2][2])
        last_mm = None
        vv = None
        for c in range(NCH):
            pbb = pb[4 + c % 2]
            prev = ev_hist.get((i, c - 2)) or ev_hist.get((i - 1, NCH - 2 + c))
            if prev:
                blk.wait("tensor", "ev", prev)
            last_mm = blk.tr(pbb.bitcast(BF16)[:, 0:128],
                             buf[:, c * 128:(c + 1) * 128], C["IDB"][:, :],
                             inc="mm")
            blk.wait("vector", "mm", last_mm)
            vv = blk.add("vector", lambda e, obuf=obuf, c=c, pbb=pbb:
                         e.tensor_copy(obuf[:, c * 128:(c + 1) * 128],
                                       pbb.bitcast(BF16)[:, 0:128]), inc="ev")
            ev_hist[(i, c)] = vv
        blk.wait("sync", "ev", vv)
        if i < 32:
            dst = wdqt[i * 128:(i + 1) * 128, :]
        else:
            dst = wdkvt[(i - 32) * 128:(i - 31) * 128, :]
        sv = blk.dma("sync", dst, obuf[:, :], "ds")
        st_t[i] = (last_mm, vv, sv)
    blk.emit()

    for eb in range(3):
        blk = Blk(ctx)
        blk.wait("gpsimd", "ds", ctx.cnt["ds"])
        mm_last = {}
        for fc in range(32):
            buf, mbuf = WS[fc % 2], S["WM"][fc % 2]
            if fc >= 2:
                blk.wait("gpsimd", "mm", mm_last[fc - 2])
            ld1 = blk.dma("gpsimd",
                          buf[:, :].rearrange("p (c n) -> p c n", c=NCH),
                          wdqt[fc * 128:(fc + 1) * 128, :].rearrange(
                              "p (c n) -> p c n", c=NCH), "dg")
            if eb < 2:
                msrc = wuq[fc * 128:(fc + 1) * 128, eb * 512:(eb + 1) * 512]
            else:
                msrc = wqr[fc * 128:(fc + 1) * 128, :]
            ld2 = blk.dma("gpsimd", mbuf[:, 0:512], msrc, "dg")
            blk.wait("tensor", "dg", max(ld1, ld2))
            for dch in range(NCH):
                mm_last[fc] = blk.mm(
                    pb[dch][:, :], buf[:, dch * 128:(dch + 1) * 128],
                    mbuf[:, 0:512], start=(fc == 0), stop=(fc == 31),
                    inc=("mm" if dch == NCH - 1 else None))
        XB = S["XB"]
        for dch in range(NCH):
            blk.wait("vector", "mm", mm_last[31])
            v = blk.add("vector", lambda e, dch=dch: e.tensor_copy(
                XB[:, dch * 512:(dch + 1) * 512], pb[dch][:, :]), inc="ev")
            blk.wait("sync", "ev", v)
            if eb < 2:
                dst = m1[dch * 128:(dch + 1) * 128, eb * 512:(eb + 1) * 512]
            else:
                dst = m2[dch * 128:(dch + 1) * 128, :]
            blk.dma("sync", dst, XB[:, dch * 512:(dch + 1) * 512], "ds")
        blk.wait("sync", "ds", ctx.cnt["ds"])
        blk.emit()

    blk = Blk(ctx)
    ld1 = blk.dma("gpsimd", WS[0][:, 0:D].rearrange("p (c n) -> p c n", c=NCH),
                  wdkvt[0:128, :].rearrange("p (c n) -> p c n", c=NCH), "dg")
    ld2 = blk.dma("gpsimd", WS[1][:, 0:D].rearrange("p (c n) -> p c n", c=NCH),
                  wdkvt[128:256, :].rearrange("p (c n) -> p c n", c=NCH), "dg")
    XB = S["XB"]
    prev_ev = {}
    st_x = {}
    wm_last = {}
    wm2 = [S["WM"][0][:, 0:1024], S["WM"][1][:, 0:1024]]
    for mi, (wsrc, mdst) in enumerate([(wuk, m3), (wuv, m4)]):
        for eb in range(2):
            pi = mi * 2 + eb
            mb = wm2[pi % 2]
            if pi >= 2:
                blk.wait("gpsimd", "mm", wm_last[pi - 2])
            ld3 = blk.dma("gpsimd",
                          mb.rearrange("p (c n) -> p c n", c=2),
                          wsrc[:, eb * 512:(eb + 1) * 512].rearrange(
                              "(c p) n -> p c n", c=2), "dg")
            blk.wait("tensor", "dg", max(ld1, ld2, ld3))
            for dch in range(NCH):
                b = dch % 4
                if b in prev_ev:
                    blk.wait("tensor", "ev", prev_ev[b])
                v = None
                for k2 in range(2):
                    v = blk.mm(pb[b][:, :],
                               WS[k2][:, dch * 128:(dch + 1) * 128],
                               mb[:, k2 * 512:(k2 + 1) * 512],
                               start=(k2 == 0), stop=(k2 == 1),
                               inc=("mm" if k2 == 1 else None))
                wm_last[pi] = v
                blk.wait("vector", "mm", v)
                if dch in st_x:
                    blk.wait("vector", "ds", st_x[dch])
                v = blk.add("vector", lambda e, dch=dch, b=b: e.tensor_copy(
                    XB[:, dch * 512:(dch + 1) * 512], pb[b][:, :]), inc="ev")
                prev_ev[b] = v
                blk.wait("sync", "ev", v)
                st_x[dch] = blk.dma(
                    "sync",
                    mdst[dch * 128:(dch + 1) * 128, eb * 512:(eb + 1) * 512],
                    XB[:, dch * 512:(dch + 1) * 512], "ds")
    blk.wait("sync", "ds", ctx.cnt["ds"])
    blk.emit()


def emit_stage_A(ctx, S, C, pb, io):
    nc = ctx.nc
    xT = io["xT"]

    emit_compose_mats(ctx, S, C, pb, io)

    blk = Blk(ctx)
    blk.add("gpsimd", lambda e: e.memset(S["KRA"][32:33, :], 1.0))
    blk.add("gpsimd", lambda e: e.memset(S["KRA"][96:97, :], 1.0))
    blk.add("gpsimd", lambda e: e.memset(
        S["VT"][:, :].rearrange("p (a b) -> p a b", b=80)[:, :, 64:65], 1.0))
    blk.dma("gpsimd", S["XF"][:, :].rearrange("p (c n) -> p c n", c=NCH),
            xT[:, 0:TB].rearrange("(c p) n -> p c n", c=NCH), "dg")
    blk.wait("gpsimd", "dg", ctx.cnt["dg"])
    blk.emit()

    for tb in range(NTB):
        _stage_A_tb(ctx, S, C, pb, io, tb)


def _stage_A_tb(ctx, S, C, pb, io, tb):
    xT, x1T = io["xT"], io["x1T"]
    m1, m2, m3, m4 = io["scr_M1"], io["scr_M2"], io["scr_M3"], io["scr_M4"]
    wkr, wo = io["L0_attn_W_KR"], io["L0_attn_W_O"]
    XF, XLN, SQ = S["XF"], S["XLN"], S["SQ"]
    KC, VT, KRA, QC, QRA, SQR = (S["KC"], S["VT"], S["KRA"], S["QC"],
                                 S["QRA"], S["SQR"])

    emit_ln(ctx, S, C, pb, XF, XLN)

    blk = Blk(ctx)
    m3_specs = [m3[:, oc * 128:(oc + 1) * 128].rearrange("(c p) n -> p c n",
                                                         c=NCH)
                for oc in range(NCH)]

    def kc_evac(blk, i, mv):
        blk.wait("vector", "mm", mv)
        return ("ev", blk.add("vector", lambda e, i=i: e.tensor_copy(
            KC[:, i * T + tb * TB:i * T + tb * TB + TB], pb[i % 4][:, :]),
            inc="ev"))

    stream_mm(ctx, blk, m3_specs, S["WS"], NCH,
              moving=lambda i, kc: XLN[:, kc * TB:(kc + 1) * TB],
              psum_of=lambda i: pb[i % 4][:, :], evac=kc_evac)
    ld = blk.dma("gpsimd", S["WM"][0][:, 0:NCH * 32].rearrange(
        "p (c n) -> p c n", c=NCH),
        wkr[:, :].rearrange("(c p) n -> p c n", c=NCH), "dg")
    blk.wait("tensor", "dg", ld)
    for kc in range(NCH):
        v = blk.mm(pb[4][0:32, :], S["WM"][0][:, kc * 32:(kc + 1) * 32],
                   XLN[:, kc * TB:(kc + 1) * TB],
                   start=(kc == 0), stop=(kc == NCH - 1),
                   inc=("mm" if kc == NCH - 1 else None))
    blk.wait("vector", "mm", v)
    blk.add("vector", lambda e: e.tensor_mul(
        KRA[0:32, tb * TB:(tb + 1) * TB], pb[4][0:32, :],
        C["SCK"][:, tb * TB:(tb + 1) * TB]), inc="ev")
    blk.add("vector", lambda e: e.tensor_mul(
        KRA[64:96, tb * TB:(tb + 1) * TB], pb[4][0:32, :],
        C["SCK"][:, tb * TB:(tb + 1) * TB]), inc="ev")
    blk.emit()

    blk = Blk(ctx)
    m4ld = {}
    for kc in range(NCH):
        mb = S["WM"][kc % 2]
        if kc >= 2:
            blk.wait("gpsimd", "mm", m4ld[kc - 2])
        ld = blk.dma("gpsimd", mb[:, 0:D], m4[kc * 128:(kc + 1) * 128, :],
                     "dg")
        blk.wait("tensor", "dg", ld)
        lastmm = None
        for tt in range(4):
            for nv in range(2):
                i = tt * 2 + nv
                last_of_kc = (tt == 3 and nv == 1)
                lastmm = blk.mm(
                    pb[i][:, :],
                    XLN[:, kc * TB + tt * 128:kc * TB + tt * 128 + 128],
                    mb[:, nv * 512:(nv + 1) * 512],
                    start=(kc == 0), stop=(kc == NCH - 1),
                    inc=("mm" if (kc == NCH - 1 or last_of_kc) else None))
                if kc == NCH - 1:
                    kt = tb * 4 + tt
                    blk.wait("vector", "mm", lastmm)
                    dst = VT[:, kt * 1280 + nv * 640:kt * 1280 + (nv + 1) * 640] \
                        .rearrange("p (h n) -> p h n", h=8)[:, :, 0:64]
                    src = pb[i][:, :].rearrange("p (h n) -> p h n", h=8)
                    blk.add("vector",
                            lambda e, dst=dst, src=src:
                            e.tensor_copy(dst, src), inc="ev")
        m4ld[kc] = lastmm
    blk.emit()

    blk = Blk(ctx)
    sqv = []
    for oc in range(NCH):
        v = blk.add("scalar", lambda e, oc=oc: e.activation(
            SQ[:, oc * TB:(oc + 1) * TB],
            KC[:, oc * T + tb * TB:oc * T + tb * TB + TB], AF.Square),
            inc="ea")
        sqv.append(v)
    vkr = blk.add("scalar", lambda e: e.activation(
        SQR[0:32, 0:TB], KRA[0:32, tb * TB:(tb + 1) * TB], AF.Square),
        inc="ea")
    for oc in range(NCH):
        blk.wait("tensor", "ea", sqv[oc])
        blk.mm(pb[7][0:16, :], C["EHALL"][:, oc * 16:(oc + 1) * 16],
               SQ[:, oc * TB:(oc + 1) * TB], start=(oc == 0), stop=False)
    blk.wait("tensor", "ea", vkr)
    v = blk.mm(pb[7][0:16, :], C["O32X16"][:, :], SQR[0:32, 0:TB],
               start=False, stop=True, inc="mm")
    blk.wait("vector", "mm", v)
    v = blk.add("vector", lambda e: e.tensor_copy(
        S["KN2H"][:, tb * TB:(tb + 1) * TB], pb[7][0:16, :]), inc="ev")
    blk.wait("vector", "ev", v)
    v = blk.add("vector", lambda e: e.reduce_max(
        S["T16"][:, :], S["KN2H"][:, 0:(tb + 1) * TB], axis=AX.X), inc="ev")
    blk.wait("scalar", "ev", v)
    blk.add("scalar", lambda e: e.activation(S["KMS16"][:, :], S["T16"][:, :],
                                             AF.Sqrt), inc="ea")
    blk.emit()

    blk = Blk(ctx)
    q_specs = [m1[:, oc * 128:(oc + 1) * 128].rearrange("(c p) n -> p c n",
                                                        c=NCH)
               for oc in range(NCH)] + \
              [m2[:, ocr * 128:(ocr + 1) * 128].rearrange("(c p) n -> p c n",
                                                          c=NCH)
               for ocr in range(4)]

    def q_evac(blk, i, mv):
        blk.wait("vector", "mm", mv)
        if i < NCH:
            return ("ev", blk.add("vector", lambda e, i=i: e.tensor_copy(
                QC[:, i * TB:(i + 1) * TB], pb[i % 4][:, :]), inc="ev"))
        ocr = i - NCH
        last = None
        for j in range(4):
            h = ocr * 4 + j
            blk.add("vector", lambda e, h=h, j=j, i=i: e.tensor_mul(
                QRA[0:32, h * TB:(h + 1) * TB],
                pb[i % 4][32 * j:32 * (j + 1), :],
                C["SCQ"][32 * j:32 * (j + 1), tb * TB:(tb + 1) * TB]))
            last = blk.add("vector", lambda e, h=h, j=j, i=i: e.tensor_mul(
                QRA[64:96, h * TB:(h + 1) * TB],
                pb[i % 4][32 * j:32 * (j + 1), :],
                C["SCQ"][32 * j:32 * (j + 1), tb * TB:(tb + 1) * TB]),
                inc=("ev" if j == 3 else None))
        return ("ev", last)

    stream_mm(ctx, blk, q_specs, S["WS"], NCH,
              moving=lambda i, kc: XLN[:, kc * TB:(kc + 1) * TB],
              psum_of=lambda i: pb[i % 4][:, :], evac=q_evac)
    blk.emit()

    blk = Blk(ctx)
    sqv = []
    for oc in range(NCH):
        v = blk.add("scalar", lambda e, oc=oc: e.activation(
            SQ[:, oc * TB:(oc + 1) * TB], QC[:, oc * TB:(oc + 1) * TB],
            AF.Square), inc="ea")
        sqv.append(v)
    for oc in range(NCH):
        blk.wait("tensor", "ea", sqv[oc])
        blk.mm(pb[4][0:16, :], C["EHALL"][:, oc * 16:(oc + 1) * 16],
               SQ[:, oc * TB:(oc + 1) * TB], start=(oc == 0), stop=False)
    v = None
    qr_mm = {}
    for h in range(H):
        if h >= 2:
            blk.wait("scalar", "mm", qr_mm[h - 2])
        vh = blk.add("scalar", lambda e, h=h: e.activation(
            SQR[0:32, (h % 2) * TB:(h % 2 + 1) * TB],
            QRA[0:32, h * TB:(h + 1) * TB], AF.Square), inc="ea")
        blk.wait("tensor", "ea", vh)
        v = blk.mm(pb[4][0:16, :], C["SELH"][:, h * 16:(h + 1) * 16],
                   SQR[0:32, (h % 2) * TB:(h % 2 + 1) * TB],
                   start=False, stop=(h == H - 1), inc="mm")
        qr_mm[h] = v
    blk.wait("scalar", "mm", v)
    v = blk.add("scalar", lambda e: e.activation(
        S["QN16"][:, :], pb[4][0:16, :], AF.Sqrt), inc="ea")
    blk.wait("vector", "ea", v)
    v = blk.add("vector", lambda e: e.tensor_scalar(
        S["B16"][:, :], S["QN16"][:, :], S["KMS16"][:, 0:1], -SQRT96,
        ALU.mult, ALU.mult), inc="ev")
    blk.wait("gpsimd", "ev", v)
    for h in range(H):
        blk.dma("gpsimd", QRA[32:33, h * TB:(h + 1) * TB],
                S["B16"][h:h + 1, :], "dg")
        blk.dma("gpsimd", QRA[96:97, h * TB:(h + 1) * TB],
                S["B16"][h:h + 1, :], "dg")
    blk.wait("tensor", "dg", ctx.cnt["dg"])
    blk.emit()

    nkt = 4 * (tb + 1)
    blk = Blk(ctx)
    AO = S["AO"]
    ea_t, prod_t, mm_t, pv_t, pv_in = {}, {}, {}, {}, {}
    ao_t, cp_t = {}, {}
    diag_gs = []

    _NOPV = os.environ.get("K_ATT_NOPV", "0") == "1"
    _NOMM2 = os.environ.get("K_ATT_NOMM2", "0") == "1"

    def emit_pv(h, kt, g):
        if _NOPV:
            pv_t[g] = ctx.cnt["mm"]
            return
        sem, tgt = prod_t[g]
        blk.wait("tensor", sem, tgt)
        if kt == 0 and h >= 2:
            blk.wait("tensor", "ev", ao_t[h - 2])
        pv_t[g] = blk.mm(pb[4 + h % 2][0:65, :],
                         VT[:, (kt * 16 + h) * 80:(kt * 16 + h) * 80 + 65],
                         pv_in[g][:, :], start=(kt == 0),
                         stop=(kt == nkt - 1), inc="mm")

    g = -1
    _H = int(os.environ.get("K_ATT_H", "0")) or H
    _EPI = os.environ.get("K_ATT_EPI", "1") == "1"
    for h in range(_H):
        hc, hr = h // 2, (h % 2) * 64
        for kt in range(nkt):
            g += 1
            sb_ = pb[g % 2]
            if g >= 2:
                blk.wait("tensor", "ea", ea_t[g - 2])
            blk.mm(sb_[:, :],
                   KC[hr:hr + 64, hc * T + kt * 128:hc * T + (kt + 1) * 128],
                   QC[hr:hr + 64, hc * TB:(hc + 1) * TB],
                   start=True, stop=_NOMM2)
            if _NOMM2:
                mm_t[g] = blk.add(
                    "tensor", lambda e: e.ldweights(C["IDB"][:, :]), inc="mm")
            else:
                qb_ = (h % 2) * 64
                mm_t[g] = blk.mm(sb_[:, :],
                                 KRA[qb_:qb_ + 33, kt * 128:(kt + 1) * 128],
                                 QRA[qb_:qb_ + 33, h * TB:(h + 1) * TB],
                                 start=False, stop=True, inc="mm")
            diag = kt >= 4 * tb
            pt = S["PT"][g % 2]
            blk.wait("scalar", "mm", mm_t[g])
            if g >= 2:
                blk.wait("scalar", "mm", pv_t[g - 2])
            ea_t[g] = blk.add("scalar", lambda e, sb_=sb_, pt=pt: e.activation(
                pt[:, :], sb_[:, :], AF.Exp, scale=INV_SQRT96), inc="ea")
            if diag:
                nd = len(diag_gs)
                ptm = S["PTM"][nd % 2]
                blk.wait("vector", "ea", ea_t[g])
                if nd >= 2:
                    blk.wait("vector", "mm", pv_t[diag_gs[nd - 2]])
                r = kt - 4 * tb
                vv = blk.add("vector", lambda e, pt=pt, ptm=ptm, r=r:
                             e.tensor_mul(ptm[:, :], pt[:, :],
                                          C["DMASK"][:, r * TB:(r + 1) * TB]),
                             inc="ev")
                prod_t[g] = ("ev", vv)
                pv_in[g] = ptm
                diag_gs.append(g)
            else:
                prod_t[g] = ("ea", ea_t[g])
                pv_in[g] = pt
            if kt >= 1:
                emit_pv(h, kt - 1, g - 1)
        emit_pv(h, nkt - 1, g)
        if not _EPI:
            continue
        ob = pb[4 + h % 2]
        rb = pb[6 + h % 2]
        blk.wait("vector", "mm", pv_t[g])
        rdv = blk.add("vector", lambda e, ob=ob, h=h: e.reciprocal(
            S["RD"][h % 2][:, :], ob[64:65, :]), inc="ev")
        blk.wait("tensor", "ev", rdv)
        if h >= 2:
            blk.wait("tensor", "ev", cp_t[h - 2])
        rbv = blk.mm(rb[0:64, :], C["ONESR"][0:1, 0:64],
                     S["RD"][h % 2][0:1, :], start=True, stop=True, inc="mm")
        blk.wait("vector", "mm", rbv)
        cp_t[h] = blk.add("vector", lambda e, rb=rb, h=h: e.tensor_copy(
            S["RDB"][h % 2][:, :], rb[0:64, :]), inc="ev")
        ao_t[h] = blk.add("vector", lambda e, ob=ob, hc=hc, hr=hr, h=h:
                          e.tensor_mul(AO[hr:hr + 64, hc * TB:(hc + 1) * TB],
                                       ob[0:64, :], S["RDB"][h % 2][:, :]),
                          inc="ev")
    blk.emit()

    blk = Blk(ctx)
    wo_specs = [wo[:, oc * 128:(oc + 1) * 128].rearrange("(c p) n -> p c n",
                                                         c=NCH)
                for oc in range(NCH)]
    st_t = {}

    def wo_evac(blk, i, mv):
        blk.wait("vector", "mm", mv)
        if i >= 2:
            blk.wait("vector", "ds", st_t[i - 2])
        tfo = S["TF1"] if i % 2 == 0 else S["TF3"]
        v = blk.add("vector", lambda e, i=i, tfo=tfo: e.tensor_add(
            tfo[:, :], pb[i % 4][:, :], XF[:, i * TB:(i + 1) * TB]), inc="ev")
        blk.wait("sync", "ev", v)
        st_t[i] = blk.dma(
            "sync", x1T[i * 128:(i + 1) * 128, tb * TB:(tb + 1) * TB],
            tfo[:, :], "ds")
        return ("ev", v)

    stream_mm(ctx, blk, wo_specs, S["WS"], NCH,
              moving=lambda i, kc: AO[:, kc * TB:(kc + 1) * TB],
              psum_of=lambda i: pb[i % 4][:, :], evac=wo_evac)
    blk.emit()

    blk = Blk(ctx)
    if tb + 1 < NTB:
        blk.dma("gpsimd", S["XF"][:, :].rearrange("p (c n) -> p c n", c=NCH),
                xT[:, (tb + 1) * TB:(tb + 2) * TB].rearrange(
                    "(c p) n -> p c n", c=NCH), "dg")
        blk.wait("gpsimd", "dg", ctx.cnt["dg"])
    else:
        blk.wait("sync", "ds", ctx.cnt["ds"])
        blk.wait("gpsimd", "dg", ctx.cnt["dg"])
    blk.emit()



CAP = 2560
BIG = float(2 ** 26)
RG8 = [[0, 1, 2, 3, 4, 5, 6, 7]]


def alloc_stage_B(nc):
    sb = nc.alloc_sbuf_tensor
    S = {}
    A2 = sb("B_A2", [128, 2 * NCH * TB], F32)
    S["XF"] = A2.ap()[:, 0:NCH * TB]
    S["XB"] = A2.ap().bitcast(BF16)[:, 2 * NCH * TB:3 * NCH * TB]
    S["SQ"] = A2.ap().bitcast(BF16)[:, 3 * NCH * TB:4 * NCH * TB]
    S["HT"] = A2.ap().bitcast(BF16)[:, 0:32 * TB]
    S["MU"] = sb("B_MU", [128, TB], F32)
    S["RSTD"] = sb("B_RSTD", [128, TB], F32)
    S["TF1"] = sb("B_TF1", [128, TB], F32)
    S["TF2"] = sb("B_TF2", [128, TB], F32)
    A1 = sb("B_A1", [128, NCH * CAP], BF16)
    S["XLNB"] = A1.ap()[:, 0:NCH * T]
    S["BUF"] = A1.ap()
    S["TOK"] = sb("B_TOK", [128, D], BF16)
    S["RWF"] = sb("B_RWF", [128, NCH * 8], F32)
    S["RBROW"] = sb("B_RBROW", [1, 8], BF16)
    S["LGT"] = sb("B_LGT", [128, 16 * 8], F32)
    S["GT"] = sb("B_GT", [128, 16 * 8], F32)
    S["MK1"] = sb("B_MK1", [128, 16 * 8], F32)
    S["MK2"] = sb("B_MK2", [128, 16 * 8], F32)
    S["MKB"] = sb("B_MKB", [128, 16 * 8], BF16)
    S["T8A"] = sb("B_T8A", [128, 8], F32)
    S["T8B"] = sb("B_T8B", [128, 8], F32)
    S["T8C"] = sb("B_T8C", [128, 8], F32)
    S["T8D"] = sb("B_T8D", [128, 8], F32)
    S["TC1"] = sb("B_TC1", [128, 1], F32)
    S["TC2"] = sb("B_TC2", [128, 1], F32)
    S["MKE"] = sb("B_MKE", [8, T], F32)
    S["CUM"] = sb("B_CUM", [8, T], F32)
    S["GCUM"] = sb("B_GCUM", [8, T], F32)
    S["DSTF"] = sb("B_DSTF", [8, T], F32)
    S["CNT8"] = sb("B_CNT8", [8, 1], F32)
    S["CNT64"] = sb("B_CNT64", [64, 1], F32)
    S["OFFP"] = sb("B_OFFP", [8, 1], F32)
    S["V8A"] = sb("B_V8A", [8, 4], F32)
    S["ROW8"] = sb("B_ROW8", [1, 16], F32)
    S["GOB"] = sb("B_GOB", [128, 8], F32)
    S["DELB"] = sb("B_DELB", [128, 8], F32)
    S["IDF1"] = sb("B_IDF1", [128, 20], F32)
    S["IDF2"] = sb("B_IDF2", [128, 20], F32)
    S["SIOTA"] = sb("B_SIOTA", [128, 20], F32)
    S["IDXG"] = sb("B_IDXG", [128, 20], I32)
    S["IDXL1"] = sb("B_IDXL1", [128, 16], I32)
    S["IDXL2"] = sb("B_IDXL2", [128, 16], I32)
    S["IDX1"] = sb("B_IDX1", [128, 16], I32)
    S["IDX2"] = sb("B_IDX2", [128, 16], I32)
    S["W1TOK"] = sb("B_W1TOK", [128, 16], F32)
    S["W2TOK"] = sb("B_W2TOK", [128, 16], F32)
    S["YF"] = sb("B_YF", [128, NCH * TB], BF16)
    S["B1S"] = sb("B_B1S", [128, 32], F32)
    S["B2S"] = sb("B_B2S", [128, 8], F32)
    S["BW"] = [sb(f"B_BW{i}", [128, 32 * 128], BF16) for i in range(2)]
    S["G1"] = sb("B_G1", [128, D], BF16)
    S["G2"] = sb("B_G2", [128, D], BF16)
    S["GS"] = sb("B_GS", [128, D], F32)
    S["XR"] = sb("B_XR", [128, NCH * 128], F32)
    S["OT"] = [sb(f"B_OT{i}", [128, 128], F32) for i in range(2)]
    S["SEL8F"] = sb("B_SEL8F", [64, 8], F32)
    S["SELLO"] = sb("B_SELLO", [64, 8], F32)
    S["SELPF"] = sb("B_SELPF", [64, 8], F32)
    S["TRI8S"] = sb("B_TRI8S", [8, 8], F32)
    S["SH8"] = sb("B_SH8", [8, 8], F32)
    S["R4096"] = sb("B_R4096", [8, 1], F32)
    S["RBM1"] = sb("B_RBM1", [128, 8], F32)
    return S


def emit_stage_B(ctx, S, C, pb, io):
    nc = ctx.nc
    x1T, x2T = io["x1T"], io["x2T"]
    rw, rb = io["L0_ffn_router_w"], io["L0_ffn_router_b"]
    w1e, w2e, b1e, b2e = io["w1_e"], io["w2_e"], io["b1_e"], io["b2_e"]
    myxln, mycnt = io["scr_myxln"], io["scr_mycnt"]
    cnt64d = io["scr_cnt64"]
    sendb, sball = io["scr_sendb"], io["scr_sball"]
    yd, yall = io["scr_yd"], io["scr_yall"]

    blk = Blk(ctx)
    blk.dma("sync", S["RWF"][:, :].rearrange("p (c n) -> p c n", c=NCH),
            rw[:, :].rearrange("(c p) n -> p c n", c=NCH), "ds")
    blk.dma("sync", S["SEL8F"][:, :], io["sel8f"][:, :], "ds")
    blk.dma("sync", S["SELLO"][:, :], io["sello"][:, :], "ds")
    blk.dma("sync", S["SELPF"][:, :], io["selp"][:, :], "ds")
    blk.dma("sync", S["TRI8S"][:, :], io["tri8s"][:, :], "ds")
    blk.dma("sync", S["SH8"][:, :], io["sh8"][:, :], "ds")
    blk.dma("sync", S["R4096"][:, :], io["r4096"][:, :], "ds")
    blk.dma("sync", S["RBM1"][:, :], io["rbasem1"][:, :], "ds")
    blk.dma("sync", S["SIOTA"][:, :], io["siota"][:, :], "ds")
    blk.dma("gpsimd", S["RBROW"][:, :], rb[None, :], "dg")
    blk.dma("gpsimd", S["XR"][0:32, 0:128],
            b1e[:].rearrange("(a b) -> a b", b=128), "dg")
    blk.dma("gpsimd", S["XR"][0:8, 128:256],
            b2e[:].rearrange("(a b) -> a b", b=128), "dg")
    blk.wait("tensor", "dg", ctx.cnt["dg"])
    blk.wait("tensor", "ds", ctx.cnt["ds"])
    blk.tr(pb[6][:, 0:32], S["XR"][0:32, 0:128], C["IDF"][0:32, 0:32])
    v2 = blk.tr(pb[7][:, 0:8], S["XR"][0:8, 128:256], C["IDF"][0:8, 0:8],
                inc="mm")
    blk.wait("vector", "mm", v2)
    blk.add("vector", lambda e: e.tensor_copy(S["B1S"][:, :], pb[6][:, 0:32]))
    blk.add("vector", lambda e: e.tensor_copy(S["B2S"][:, :], pb[7][:, 0:8]))
    blk.emit()

    for tb in range(NTB):
        blk0 = Blk(ctx)
        blk0.dma("gpsimd", S["XF"][:, :].rearrange("p (c n) -> p c n", c=NCH),
                 x1T[:, tb * TB:(tb + 1) * TB].rearrange("(c p) n -> p c n",
                                                         c=NCH), "dg")
        blk0.wait("gpsimd", "dg", ctx.cnt["dg"])
        blk0.emit()

        def logits_cb(blk, c, tf1, fv, tb=tb):
            blk.wait("tensor", "ev", fv)
            v = None
            for tt in range(4):
                v = blk.mm(pb[2 + tt][:, 0:8],
                           tf1[:, tt * 128:(tt + 1) * 128],
                           S["RWF"][:, c * 8:(c + 1) * 8],
                           start=(c == 0), stop=False,
                           inc=("mm" if tt == 3 else None))
            return v

        emit_ln(ctx, S, C, pb, S["XF"], None,
                dst_fn=lambda c, tb=tb: S["XLNB"][:, c * T + tb * TB:
                                                  c * T + (tb + 1) * TB],
                fp32_cb=logits_cb)

        blk = Blk(ctx)
        for tt in range(4):
            v = blk.mm(pb[2 + tt][:, 0:8], C["ONES128"][0:1, :],
                       S["RBROW"][0:1, :], start=False, stop=True, inc="mm")
            blk.wait("vector", "mm", v)
            blk.add("vector", lambda e, tt=tt, tb=tb: e.tensor_copy(
                S["LGT"][:, (tb * 4 + tt) * 8:(tb * 4 + tt + 1) * 8],
                pb[2 + tt][:, 0:8]), inc="ev")
        blk.emit()

        blk = Blk(ctx)
        pr_ev = {}
        for tt in range(4):
            for c in range(NCH):
                bk = c % 2
                if pr_ev.get(bk):
                    blk.wait("tensor", "ev", pr_ev[bk])
                v = blk.tr(pb[bk].bitcast(BF16)[:, 0:128],
                           S["XLNB"][:, c * T + tb * TB + tt * 128:
                                     c * T + tb * TB + (tt + 1) * 128],
                           C["IDB"][:, :], inc="mm")
                blk.wait("vector", "mm", v)
                pr_ev[bk] = blk.add(
                    "vector", lambda e, c=c, bk=bk: e.tensor_copy(
                        S["TOK"][:, c * 128:(c + 1) * 128],
                        pb[bk].bitcast(BF16)[:, 0:128]), inc="ev")
            blk.wait("sync", "ev", ctx.cnt["ev"])
            sv = blk.dma("sync",
                         myxln[tb * TB + tt * 128:tb * TB + (tt + 1) * 128, :],
                         S["TOK"][:, :], "ds")
            blk.wait("vector", "ds", sv)
        blk.emit()

        blk = Blk(ctx)
        for tt in range(4):
            j = tb * 4 + tt
            LG = S["LGT"][:, j * 8:(j + 1) * 8]
            blk.vf(lambda e, LG=LG: e.reduce_max(
                S["TC1"][:, :], LG, axis=AX.X))
            blk.vf(lambda e, LG=LG, j=j: e.tensor_scalar(
                S["MK1"][:, j * 8:(j + 1) * 8], LG, S["TC1"][:, 0:1], None,
                ALU.is_equal))
            blk.vf(lambda e, LG=LG, j=j: e.scalar_tensor_tensor(
                S["T8A"][:, :], S["MK1"][:, j * 8:(j + 1) * 8], -1e30, LG,
                ALU.mult, ALU.add))
            blk.vf(lambda e: e.reduce_max(
                S["TC2"][:, :], S["T8A"][:, :], axis=AX.X))
            blk.vf(lambda e, j=j: e.tensor_scalar(
                S["MK2"][:, j * 8:(j + 1) * 8], S["T8A"][:, :],
                S["TC2"][:, 0:1], None, ALU.is_equal))
            blk.vf(lambda e, j=j: e.tensor_add(
                S["MKB"][:, j * 8:(j + 1) * 8],
                S["MK1"][:, j * 8:(j + 1) * 8],
                S["MK2"][:, j * 8:(j + 1) * 8]))
            vv = blk.vf(lambda e: e.tensor_scalar(
                S["TC2"][:, :], S["TC1"][:, :], -1.0, None, ALU.mult))
            blk.wait("scalar", "ev", vv)
            av = blk.add("scalar", lambda e, LG=LG: e.activation(
                S["T8B"][:, :], LG, AF.Exp, bias=S["TC2"][:, 0:1]), inc="ea")
            blk.wait("vector", "ea", av)
            blk.vf(lambda e: e.reduce_sum(
                S["TC1"][:, :], S["T8B"][:, :], axis=AX.X))
            blk.vf(lambda e: e.reciprocal(
                S["TC2"][:, 0:1], S["TC1"][:, 0:1]))
            blk.vf(lambda e, j=j: e.tensor_scalar(
                S["GT"][:, j * 8:(j + 1) * 8], S["T8B"][:, :],
                S["TC2"][:, 0:1], None, ALU.mult))
        blk.emit()

    blk = Blk(ctx)
    pr = {}
    for j in range(16):
        bk = j % 2
        if pr.get(bk):
            blk.wait("tensor", "ev", pr[bk])
        v = blk.tr(pb[bk].bitcast(BF16)[0:8, 0:128],
                   S["MKB"][:, j * 8:(j + 1) * 8], C["IDB"][:, :], inc="mm")
        blk.wait("vector", "mm", v)
        pr[bk] = blk.add("vector", lambda e, j=j, bk=bk: e.tensor_copy(
            S["MKE"][0:8, j * 128:(j + 1) * 128],
            pb[bk].bitcast(BF16)[0:8, 0:128]), inc="ev")
    blk.wait("vector", "ev", ctx.cnt["ev"])
    v = blk.add("vector", lambda e: e.tensor_tensor_scan(
        S["CUM"][:, :], S["MKE"][:, :], S["MKE"][:, :], 0.0,
        ALU.add, ALU.bypass), inc="ev")
    blk.wait("vector", "ev", v)
    v2 = blk.add("vector", lambda e: e.tensor_copy(
        S["CNT8"][:, :], S["CUM"][:, T - 1:T]), inc="ev")
    blk.wait("sync", "ev", ctx.cnt["ev"])
    s2 = blk.dma("sync", mycnt[:, :], S["CNT8"][:, :], "ds")
    blk.wait("gpsimd", "ds", s2)
    blk.add("gpsimd", lambda e: e.collective_compute(
        "AllGather", ALU.bypass, replica_groups=RG8,
        ins=[mycnt.ap().opt()], outs=[cnt64d.ap().opt()]), inc="cc")
    blk.wait("gpsimd", "cc", ctx.cnt["cc"])
    l2 = blk.dma("gpsimd", S["CNT64"][:, :], cnt64d[:, :], "dg")
    blk.wait("gpsimd", "dg", l2)
    blk.emit()

    blk = Blk(ctx)
    vv = blk.mm(pb[5][0:8, 0:1], S["TRI8S"][:, :], S["CNT8"][:, :],
                start=True, stop=True, inc="mm")
    blk.wait("vector", "mm", vv)
    blk.vf(lambda e: e.tensor_copy(S["V8A"][0:8, 0:1], pb[5][0:8, 0:1]))
    dst_v = blk.vf(lambda e: e.tensor_scalar(
        S["DSTF"][:, :], S["CUM"][:, :], S["V8A"][0:8, 0:1], -1.0,
        ALU.add, ALU.add))
    blk.wait("tensor", "ev", dst_v)
    sc_prev = None
    cp_ev = {}
    for j in range(16):
        bk = j % 2
        if cp_ev.get(bk):
            blk.wait("tensor", "ev", cp_ev[bk])
        v = blk.tr(pb[bk][:, 0:8], S["DSTF"][0:8, j * 128:(j + 1) * 128],
                   C["IDF"][0:8, 0:8], inc="mm")
        blk.wait("vector", "mm", v)
        cp_ev[bk] = blk.vf(lambda e, bk=bk: e.tensor_copy(
            S["T8D"][:, :], pb[bk][:, 0:8]))
        blk.vf(lambda e, j=j: e.tensor_mul(
            S["T8C"][:, :], S["MK1"][:, j * 8:(j + 1) * 8], S["T8D"][:, :]))
        blk.vf(lambda e: e.reduce_sum(
            S["TC1"][:, :], S["T8C"][:, :], axis=AX.X))
        blk.vf(lambda e, j=j: e.tensor_copy(
            S["IDXL1"][:, j:j + 1], S["TC1"][:, :]))
        blk.vf(lambda e, j=j: e.tensor_mul(
            S["T8C"][:, :], S["MK2"][:, j * 8:(j + 1) * 8], S["T8D"][:, :]))
        blk.vf(lambda e: e.reduce_sum(
            S["TC1"][:, :], S["T8C"][:, :], axis=AX.X))
        v = blk.vf(lambda e, j=j: e.tensor_copy(
            S["IDXL2"][:, j:j + 1], S["TC1"][:, :]))
        blk.wait("gpsimd", "ev", v)
        if sc_prev is not None:
            blk.wait("gpsimd", "dg", sc_prev)
        lv = blk.dma("gpsimd", S["TOK"][:, :],
                     myxln[j * 128:(j + 1) * 128, :], "dg")
        blk.wait("gpsimd", "dg", lv)
        blk.add("gpsimd", lambda e, j=j: e.indirect_dma_start(
            sendb[0:128, :],
            IndirectOffsetOnAxis(S["IDXL1"][:, j:j + 1], 0),
            S["TOK"][:, :], None), inc="dg", n=16)
        sc_prev = blk.add("gpsimd", lambda e, j=j: e.indirect_dma_start(
            sendb[0:128, :],
            IndirectOffsetOnAxis(S["IDXL2"][:, j:j + 1], 0),
            S["TOK"][:, :], None), inc="dg", n=16)
    blk.wait("gpsimd", "dg", ctx.cnt["dg"])
    blk.add("gpsimd", lambda e: e.collective_compute(
        "AllGather", ALU.bypass, replica_groups=RG8,
        ins=[sendb.ap().opt()], outs=[sball.ap().opt()]), inc="cc")
    blk.wait("gpsimd", "cc", ctx.cnt["cc"])
    v = blk.mm(pb[4][0:8, 0:1], S["SELPF"][:, :], S["CNT64"][:, :],
               start=True, stop=True, inc="mm")
    blk.wait("vector", "mm", v)
    blk.vf(lambda e: e.tensor_copy(S["OFFP"][:, :], pb[4][0:8, 0:1]))
    blk.vf(lambda e: e.tensor_scalar(
        S["GCUM"][:, :], S["CUM"][:, :], S["OFFP"][0:8, 0:1], None, ALU.add))
    blk.emit()

    blk = Blk(ctx)
    v = blk.mm(pb[0][0:8, 0:1], S["SEL8F"][:, :], S["CNT64"][:, :],
               start=True, stop=True)
    v = blk.mm(pb[1][0:8, 0:1], S["SELLO"][:, :], S["CNT64"][:, :],
               start=True, stop=True, inc="mm")
    blk.wait("vector", "mm", v)
    blk.vf(lambda e: e.tensor_copy(S["V8A"][0:8, 0:1],
                                   pb[0][0:8, 0:1]))
    vv = blk.vf(lambda e: e.tensor_copy(
        S["V8A"][0:8, 1:2], pb[1][0:8, 0:1]))
    blk.wait("tensor", "ev", vv)
    v = blk.mm(pb[2][0:8, 0:1], S["TRI8S"][:, :], S["V8A"][0:8, 0:1],
               start=True, stop=True, inc="mm")
    blk.wait("vector", "mm", v)
    blk.vf(lambda e: e.tensor_copy(S["V8A"][0:8, 2:3], pb[2][0:8, 0:1]))
    blk.vf(lambda e: e.tensor_sub(
        S["V8A"][0:8, 3:4], S["V8A"][0:8, 1:2], S["V8A"][0:8, 2:3]))
    vv = blk.vf(lambda e: e.tensor_add(
        S["V8A"][0:8, 3:4], S["V8A"][0:8, 3:4], S["R4096"][0:8, 0:1]))
    blk.wait("tensor", "ev", vv)
    v = blk.mm(pb[3][0:8, 0:1], S["SH8"][:, :], S["V8A"][0:8, 3:4],
               start=True, stop=True, inc="mm")
    blk.wait("vector", "mm", v)
    vv = blk.vf(lambda e: e.tensor_sub(
        S["V8A"][0:8, 1:2], S["V8A"][0:8, 3:4], pb[3][0:8, 0:1]))
    blk.wait("tensor", "ev", vv)
    v = blk.tr(pb[4][0:1, 0:8], S["V8A"][0:8, 2:3], C["IDF"][0:8, 0:8])
    v = blk.tr(pb[5][0:1, 0:8], S["V8A"][0:8, 1:2], C["IDF"][0:8, 0:8],
               inc="mm")
    blk.wait("vector", "mm", v)
    blk.vf(lambda e: e.tensor_copy(S["ROW8"][0:1, 0:8], pb[4][0:1, 0:8]))
    vv = blk.vf(lambda e: e.tensor_copy(
        S["ROW8"][0:1, 8:16], pb[5][0:1, 0:8]))
    blk.wait("tensor", "ev", vv)
    v = blk.mm(pb[6][:, 0:8], C["ONESR"][0:1, :], S["ROW8"][0:1, 0:8],
               start=True, stop=True)
    v = blk.mm(pb[7][:, 0:8], C["ONESR"][0:1, :], S["ROW8"][0:1, 8:16],
               start=True, stop=True, inc="mm")
    blk.wait("vector", "mm", v)
    blk.vf(lambda e: e.tensor_copy(S["GOB"][:, :], pb[6][:, 0:8]))
    blk.vf(lambda e: e.tensor_copy(S["DELB"][:, :], pb[7][:, 0:8]))
    blk.vf(lambda e: e.tensor_copy(S["IDF1"][:, :], S["SIOTA"][:, :]))
    cur, nxt = "IDF1", "IDF2"
    for r in range(8):
        blk.vf(lambda e, r=r: e.tensor_scalar(
            S["GS"][:, 0:20], S["SIOTA"][:, :], S["GOB"][:, r:r + 1], None,
            ALU.is_ge))
        blk.vf(lambda e, r=r, cur=cur, nxt=nxt:
               e.scalar_tensor_tensor(
                   S[nxt][:, :], S["GS"][:, 0:20], S["DELB"][:, r:r + 1],
                   S[cur][:, :], ALU.mult, ALU.add))
        cur, nxt = nxt, cur
    blk.vf(lambda e, cur=cur, nxt=nxt: e.tensor_scalar_min(
        S[nxt][:, :], S[cur][:, :], 32767.0))
    v = blk.vf(lambda e, nxt=nxt: e.tensor_copy(
        S["IDXG"][:, :], S[nxt][:, :]))
    blk.wait("gpsimd", "ev", v)
    pr = {}
    g_prev = None
    for tt in range(20):
        if g_prev is not None:
            blk.wait("gpsimd", "mm", g_prev)
        lv = blk.add("gpsimd", lambda e, tt=tt: e.indirect_dma_start(
            S["TOK"][:, :], None, sball[0:128, :],
            IndirectOffsetOnAxis(S["IDXG"][:, tt:tt + 1], 0)),
            inc="dg", n=16)
        blk.wait("tensor", "dg", lv)
        for c in range(NCH):
            bk = c % 2
            if pr.get(bk):
                blk.wait("tensor", "ev", pr[bk])
            g_prev = blk.tr(pb[bk].bitcast(BF16)[:, 0:128],
                            S["TOK"][:, c * 128:(c + 1) * 128],
                            C["IDB"][:, :], inc="mm")
            blk.wait("vector", "mm", g_prev)
            pr[bk] = blk.add("vector", lambda e, c=c, tt=tt, bk=bk:
                             e.tensor_copy(
                                 S["BUF"][:, c * CAP + tt * 128:
                                          c * CAP + (tt + 1) * 128],
                                 pb[bk].bitcast(BF16)[:, 0:128]), inc="ev")
    blk.emit()

    for tb in range(5):
        blk = Blk(ctx)
        w1_specs = [w1e[:, hc * 128:(hc + 1) * 128].rearrange(
            "(c p) n -> p c n", c=NCH) for hc in range(32)]

        def h_evac(blk, i, mv, tb=tb):
            blk.wait("scalar", "mm", mv)
            return ("ea", blk.add("scalar", lambda e, i=i: e.activation(
                S["HT"][:, i * TB:(i + 1) * TB], pb[i % 4][:, :], AF.Gelu,
                bias=S["B1S"][:, i:i + 1]), inc="ea"))

        stream_mm(ctx, blk, w1_specs, S["BW"], NCH,
                  moving=lambda i, kc, tb=tb:
                  S["BUF"][:, kc * CAP + tb * TB:kc * CAP + (tb + 1) * TB],
                  psum_of=lambda i: pb[i % 4][:, :], evac=h_evac)
        blk.emit()
        blk = Blk(ctx)
        w2_specs = [w2e[:, oc * 128:(oc + 1) * 128].rearrange(
            "(c p) n -> p c n", c=32) for oc in range(NCH)]

        def y_evac(blk, i, mv, tb=tb):
            blk.wait("scalar", "mm", mv)
            return ("ea", blk.add("scalar", lambda e, i=i: e.activation(
                S["YF"][:, i * TB:(i + 1) * TB], pb[i % 4][:, :], AF.Identity,
                bias=S["B2S"][:, i:i + 1]), inc="ea"))

        stream_mm(ctx, blk, w2_specs, S["BW"], 32,
                  moving=lambda i, kc: S["HT"][:, kc * TB:(kc + 1) * TB],
                  psum_of=lambda i: pb[i % 4][:, :], evac=y_evac)
        blk.emit()
        blk = Blk(ctx)
        blk.wait("tensor", "ea", ctx.cnt["ea"])
        pr = {}
        for tt in range(4):
            for c in range(NCH):
                bk = c % 2
                if pr.get(bk):
                    blk.wait("tensor", "ev", pr[bk])
                v = blk.tr(pb[bk].bitcast(BF16)[:, 0:128],
                           S["YF"][:, c * TB + tt * 128:
                                   c * TB + (tt + 1) * 128],
                           C["IDB"][:, :], inc="mm")
                blk.wait("vector", "mm", v)
                pr[bk] = blk.add("vector", lambda e, c=c, bk=bk:
                                 e.tensor_copy(
                                     S["TOK"][:, c * 128:(c + 1) * 128],
                                     pb[bk].bitcast(BF16)[:, 0:128]),
                                 inc="ev")
            blk.wait("sync", "ev", ctx.cnt["ev"])
            sv = blk.dma("sync",
                         yd[tb * TB + tt * 128:tb * TB + (tt + 1) * 128, :],
                         S["TOK"][:, :], "ds")
            blk.wait("vector", "ds", sv)
        blk.emit()

    blk = Blk(ctx)
    blk.wait("gpsimd", "ds", ctx.cnt["ds"])
    blk.add("gpsimd", lambda e: e.collective_compute(
        "AllGather", ALU.bypass, replica_groups=RG8,
        ins=[yd.ap().opt()], outs=[yall.ap().opt()]), inc="cc")
    blk.wait("gpsimd", "cc", ctx.cnt["cc"])
    blk.emit()

    for j in range(16):
        blk = Blk(ctx)
        v = blk.tr(pb[0][:, 0:8], S["GCUM"][0:8, j * 128:(j + 1) * 128],
                   C["IDF"][0:8, 0:8], inc="mm")
        blk.wait("vector", "mm", v)
        blk.vf(lambda e: e.tensor_copy(S["T8A"][:, :], pb[0][:, 0:8]))
        blk.vf(lambda e: e.tensor_scalar(
            S["T8B"][:, :], S["T8A"][:, :], float(CAP), None, ALU.is_le))
        blk.vf(lambda e: e.tensor_add(
            S["T8D"][:, :], S["T8A"][:, :], S["RBM1"][:, :]))
        blk.vf(lambda e: e.tensor_copy(S["T8A"][:, :], S["T8D"][:, :]))
        for k, (MKk, IDXk, WTOKk) in enumerate(
                [(S["MK1"], S["IDX1"], S["W1TOK"]),
                 (S["MK2"], S["IDX2"], S["W2TOK"])]):
            blk.vf(lambda e, MKk=MKk, j=j: e.tensor_mul(
                S["T8C"][:, :], MKk[:, j * 8:(j + 1) * 8], S["T8B"][:, :]))
            blk.vf(lambda e: e.tensor_mul(
                S["T8D"][:, :], S["T8C"][:, :], S["T8A"][:, :]))
            blk.vf(lambda e: e.reduce_sum(
                S["TC1"][:, :], S["T8D"][:, :], axis=AX.X))
            blk.vf(lambda e, IDXk=IDXk, j=j: e.tensor_copy(
                IDXk[:, j:j + 1], S["TC1"][:, :]))
            blk.vf(lambda e, j=j: e.tensor_mul(
                S["T8D"][:, :], S["T8C"][:, :],
                S["GT"][:, j * 8:(j + 1) * 8]))
            blk.vf(lambda e, WTOKk=WTOKk, j=j: e.reduce_sum(
                WTOKk[:, j:j + 1], S["T8D"][:, :], axis=AX.X))
        blk.wait("gpsimd", "ev", ctx.cnt["ev"])
        blk.add("gpsimd", lambda e, j=j: e.indirect_dma_start(
            S["G1"][:, :], None, yall[0:128, :],
            IndirectOffsetOnAxis(S["IDX1"][:, j:j + 1], 0)), inc="dg", n=16)
        blk.add("gpsimd", lambda e, j=j: e.indirect_dma_start(
            S["G2"][:, :], None, yall[0:128, :],
            IndirectOffsetOnAxis(S["IDX2"][:, j:j + 1], 0)), inc="dg", n=16)
        lv = blk.dma("gpsimd", S["XR"][:, :].rearrange("p (c n) -> p c n",
                                                       c=NCH),
                     x1T[:, j * 128:(j + 1) * 128].rearrange(
                         "(c p) n -> p c n", c=NCH), "dg")
        blk.wait("vector", "dg", ctx.cnt["dg"])
        blk.add("vector", lambda e, j=j: e.tensor_scalar(
            S["GS"][:, :], S["G1"][:, :], S["W1TOK"][:, j:j + 1], None,
            ALU.mult))
        v = blk.add("vector", lambda e, j=j: e.scalar_tensor_tensor(
            S["GS"][:, :], S["G2"][:, :], S["W2TOK"][:, j:j + 1],
            S["GS"][:, :], ALU.mult, ALU.add), inc="ev")
        blk.wait("tensor", "ev", v)
        st_prev = {}
        ev_prev = {}
        for c in range(NCH):
            bk = 2 + c % 2
            if ev_prev.get(bk):
                blk.wait("tensor", "ev", ev_prev[bk])
            v = blk.tr(pb[bk][:, 0:128],
                       S["GS"][:, c * 128:(c + 1) * 128], C["IDF"][:, :],
                       inc="mm")
            blk.wait("vector", "mm", v)
            ot = S["OT"][c % 2]
            if st_prev.get(c % 2):
                blk.wait("vector", "ds", st_prev[c % 2])
            vv = blk.add("vector", lambda e, c=c, bk=bk, ot=ot: e.tensor_add(
                ot[:, :], pb[bk][:, 0:128],
                S["XR"][:, c * 128:(c + 1) * 128]), inc="ev")
            ev_prev[bk] = vv
            blk.wait("sync", "ev", vv)
            st_prev[c % 2] = blk.dma(
                "sync", x2T[c * 128:(c + 1) * 128, j * 128:(j + 1) * 128],
                ot[:, :], "ds")
        blk.emit()

    if "dbg" in io:
        blk = Blk(ctx)
        for nm, hd in io["dbg"].items():
            if nm == "BUFS":
                blk.dma("sync", hd[:, :], S["BUF"][:, 0:NCH * CAP], "ds")
            else:
                blk.dma("sync", hd[:, :], S[nm][:, :] if not isinstance(
                    S[nm], list) else S[nm][0][:, :], "ds")
        blk.wait("sync", "ds", ctx.cnt["ds"])
        blk.emit()

    blk = Blk(ctx)
    blk.wait("sync", "ds", ctx.cnt["ds"])
    blk.wait("gpsimd", "dg", ctx.cnt["dg"])
    blk.emit()



CW_C1, CW_C2, CW_C3 = 6.28125, 0.0019378662, -2.5590314e-06
INV2PI = 0.15915494
MAGIC = 12582912.0
TAY_A = float(0.125 / np.sqrt(2.0))
TAY_B = float(1.0 / np.sqrt(2.0))
PIO2 = float(np.float32(np.pi / 2))
TWO_PI = float(np.float32(2 * np.pi))
PI_F = float(np.float32(np.pi))


def alloc_stage_C(nc):
    sb = nc.alloc_sbuf_tensor
    S = {}
    A2 = sb("C_A2", [128, 2 * NCH * TB], F32)
    S["XF"] = A2.ap()[:, 0:NCH * TB]
    S["LN1F"] = A2.ap()[:, NCH * TB:2 * NCH * TB]
    S["XB"] = sb("C_XB", [128, NCH * TB], BF16)
    S["SQ"] = sb("C_SQ", [128, NCH * TB], BF16)
    S["MU"] = sb("C_MU", [128, TB], F32)
    S["RSTD"] = sb("C_RSTD", [128, TB], F32)
    S["TF1"] = sb("C_TF1", [128, TB], F32)
    S["TF2"] = sb("C_TF2", [128, TB], F32)
    S["TF3"] = None
    S["XLN2"] = sb("C_XLN2", [128, NCH * TB], BF16)
    S["TF3"] = S["MU"]
    QB = sb("C_QB", [128, NCH * T], BF16)
    KB = sb("C_KB", [128, NCH * T], BF16)
    S["QB"] = QB.ap()
    S["KB"] = KB.ap()
    S["VT"] = sb("C_VT", [128, 16 * 16 * 80], BF16)
    S["AO"] = S["XB"]
    S["CT"] = sb("C_CT", [128, T], BF16)
    S["ST"] = sb("C_ST", [128, T], BF16)
    qf = QB.ap().bitcast(F32)
    kf = KB.ap().bitcast(F32)
    S["A32A"] = qf[0:32, 0:T]
    S["A32B"] = qf[0:32, T:2 * T]
    S["A32C"] = qf[0:32, 2 * T:3 * T]
    S["SINT"] = qf[0:32, 3 * T:4 * T]
    S["COST"] = kf[0:32, 0:T]
    S["IOTA32"] = kf[0:32, T:2 * T]
    S["TH"] = sb("C_TH", [32, 4], F32)
    S["TAYB"] = sb("C_TAYB", [128, 1], F32)
    S["LNW"] = sb("C_LNW", [128, NCH], F32)
    S["LNB"] = sb("C_LNB", [128, NCH], F32)
    S["VSH"] = sb("C_VSH", [65, 16], F32)
    S["RD"] = [sb(f"C_RD{i}", [1, TB], F32) for i in range(2)]
    S["RDB"] = [sb(f"C_RDB{i}", [64, TB], F32) for i in range(2)]
    S["NUM"] = [sb(f"C_NUM{i}", [64, TB], BF16) for i in range(2)]
    S["PT"] = [sb(f"C_PT{i}", [128, TB], BF16) for i in range(2)]
    S["WS"] = [sb(f"C_WS{i}", [128, NCH * 128], BF16) for i in range(2)]
    S["WM"] = S["WS"]
    S["BASE32"] = sb("C_BASE32", [32, 1], F32)
    S["ECX"] = sb("C_ECX", [32, 128], F32)
    S["ESX"] = sb("C_ESX", [32, 128], F32)
    S["PERM"] = sb("C_PERM", [128, 128], BF16)
    return S


def emit_stage_C(ctx, S, C, pb, io):
    nc = ctx.nc
    x2T, x3T = io["x2T"], io["x3T"]
    wq, wkv, wo = io["L1_attn_w_q"], io["L1_attn_w_kv"], io["L1_attn_w_o"]
    lnw, lnb, theta = io["L1_attn_ln_w"], io["L1_attn_ln_b"], \
        io["L1_attn_theta"]

    blk = Blk(ctx)
    blk.dma("sync", S["IOTA32"][:, :], io["c_iota32"][:, :], "ds")
    blk.dma("sync", S["BASE32"][:, :], io["c_base32"][:, :], "ds")
    blk.dma("sync", S["ECX"][:, :], io["c_ec"][:, :], "ds")
    blk.dma("sync", S["ESX"][:, :], io["c_es"][:, :], "ds")
    blk.dma("sync", S["PERM"][:, :], io["c_perm"][:, :], "ds")
    blk.dma("gpsimd", S["TH"][0:1, 0:1], theta[None, None], "dg")
    blk.add("gpsimd", lambda e: e.memset(S["TAYB"][:, :], TAY_B))
    blk.dma("gpsimd", S["XF"][0:8, 0:128],
            lnw[:].rearrange("(a b) -> a b", b=128), "dg")
    blk.dma("gpsimd", S["XF"][0:8, 128:256],
            lnb[:].rearrange("(a b) -> a b", b=128), "dg")
    blk.wait("tensor", "dg", ctx.cnt["dg"])
    blk.wait("tensor", "ds", ctx.cnt["ds"])
    blk.tr(pb[0][:, 0:8], S["XF"][0:8, 0:128], C["IDF"][0:8, 0:8])
    v = blk.tr(pb[1][:, 0:8], S["XF"][0:8, 128:256], C["IDF"][0:8, 0:8],
               inc="mm")
    blk.wait("vector", "mm", v)
    blk.vf(lambda e: e.tensor_copy(S["LNW"][:, :], pb[0][:, 0:8]))
    blk.vf(lambda e: e.tensor_copy(S["LNB"][:, :], pb[1][:, 0:8]))
    v = blk.mm(pb[2][0:32, 0:1], C["ONESR"][0:1, 0:32], S["TH"][0:1, 0:1],
               start=True, stop=True, inc="mm")
    blk.wait("vector", "mm", v)
    blk.vf(lambda e: e.tensor_scalar(
        S["TH"][0:32, 1:2], pb[2][0:32, 0:1], 1.0 / 220.0, None, ALU.mult))
    blk.vf(lambda e: e.scalar_tensor_tensor(
        S["TH"][0:32, 3:4], S["TH"][0:32, 1:2], -220.0, pb[2][0:32, 0:1],
        ALU.mult, ALU.add))
    blk.vf(lambda e: e.scalar_tensor_tensor(
        S["TH"][0:32, 1:2], S["TH"][0:32, 3:4], 1.0 / 220.0,
        S["TH"][0:32, 1:2], ALU.mult, ALU.add))
    blk.vf(lambda e: e.tensor_mul(
        S["TH"][0:32, 2:3], S["BASE32"][:, :], S["TH"][0:32, 1:2]))
    blk.vf(lambda e: e.tensor_scalar(
        S["A32A"][:, :], S["IOTA32"][:, :], S["TH"][0:32, 2:3], None,
        ALU.mult))
    for (dst, shift) in ((S["SINT"], 0.0), (S["COST"], 0.25)):
        blk.vf(lambda e, shift=shift: e.tensor_scalar(
            S["A32B"][:, :], S["A32A"][:, :], INV2PI, shift + MAGIC,
            ALU.mult, ALU.add))
        blk.vf(lambda e: e.tensor_scalar(
            S["A32B"][:, :], S["A32B"][:, :], -MAGIC, None, ALU.add))
        blk.vf(lambda e: e.scalar_tensor_tensor(
            S["A32C"][:, :], S["A32B"][:, :], -CW_C1, S["A32A"][:, :],
            ALU.mult, ALU.add))
        blk.vf(lambda e: e.scalar_tensor_tensor(
            S["A32C"][:, :], S["A32B"][:, :], -CW_C2, S["A32C"][:, :],
            ALU.mult, ALU.add))
        blk.vf(lambda e: e.scalar_tensor_tensor(
            S["A32C"][:, :], S["A32B"][:, :], -CW_C3, S["A32C"][:, :],
            ALU.mult, ALU.add))
        if shift:
            blk.vf(lambda e: e.tensor_scalar(
                S["A32C"][:, :], S["A32C"][:, :], PIO2, None, ALU.add))
        blk.vf(lambda e: e.tensor_scalar(
            S["A32B"][:, :], S["A32C"][:, :], PI_F, None, ALU.is_gt))
        blk.vf(lambda e: e.scalar_tensor_tensor(
            S["A32C"][:, :], S["A32B"][:, :], -TWO_PI, S["A32C"][:, :],
            ALU.mult, ALU.add))
        blk.vf(lambda e: e.tensor_scalar(
            S["A32B"][:, :], S["A32C"][:, :], -PI_F, None, ALU.is_lt))
        v = blk.vf(lambda e: e.scalar_tensor_tensor(
            S["A32C"][:, :], S["A32B"][:, :], TWO_PI, S["A32C"][:, :],
            ALU.mult, ALU.add))
        blk.wait("scalar", "ev", v)
        blk.add("scalar", lambda e, dst=dst: e.activation(
            dst[:, :], S["A32C"][:, :], AF.Sin), inc="ea")
    blk.wait("tensor", "ea", ctx.cnt["ea"])
    ex_ev = {}
    for half, (tab, exp_, dst) in enumerate(
            [(S["COST"], S["ECX"], S["CT"]), (S["SINT"], S["ESX"], S["ST"])]):
        for sl in range(4):
            b = (half * 4 + sl) % 4
            if ex_ev.get(b):
                blk.wait("tensor", "ev", ex_ev[b])
            v = blk.mm(pb[b][:, :], exp_[:, :],
                       tab[:, sl * TB:(sl + 1) * TB], start=True, stop=True,
                       inc="mm")
            blk.wait("vector", "mm", v)
            ex_ev[b] = blk.add("vector", lambda e, dst=dst, sl=sl, b=b:
                               e.tensor_copy(dst[:, sl * TB:(sl + 1) * TB],
                                             pb[b][:, :]), inc="ev")
    blk.emit()

    for tb in range(NTB):
        blk0 = Blk(ctx)
        blk0.dma("gpsimd", S["XF"][:, :].rearrange("p (c n) -> p c n", c=NCH),
                 x2T[:, tb * TB:(tb + 1) * TB].rearrange(
                     "(c p) n -> p c n", c=NCH), "dg")
        blk0.wait("gpsimd", "dg", ctx.cnt["dg"])
        blk0.emit()
        emit_ln(ctx, S, C, pb, S["XF"], None,
                dst_fn=lambda c: S["LN1F"][:, c * TB:(c + 1) * TB])
        emit_ln(ctx, S, C, pb, S["LN1F"], S["XLN2"],
                ln_w_col=S["LNW"], ln_b_col=S["LNB"])

        blk = Blk(ctx)
        q_specs = [wq[:, oc * 128:(oc + 1) * 128].rearrange(
            "(c p) n -> p c n", c=NCH) for oc in range(NCH)] + \
            [wkv[:, oc * 128:(oc + 1) * 128].rearrange(
                "(c p) n -> p c n", c=NCH) for oc in range(NCH)]

        def qk_evac(blk, i, mv, tb=tb):
            blk.wait("vector", "mm", mv)
            dst = S["QB"] if i < NCH else S["KB"]
            oc = i % NCH
            return ("ev", blk.add("vector", lambda e, dst=dst, oc=oc:
                                  e.tensor_copy(
                                      dst[:, oc * T + tb * TB:
                                          oc * T + tb * TB + TB],
                                      pb[i % 4][:, :]), inc="ev"))

        stream_mm(ctx, blk, q_specs, S["WS"], NCH,
                  moving=lambda i, kc: S["XLN2"][:, kc * TB:(kc + 1) * TB],
                  psum_of=lambda i: pb[i % 4][:, :], evac=qk_evac)
        blk.emit()

        blk = Blk(ctx)
        m4ld = {}
        for kc in range(NCH):
            mb = S["WM"][kc % 2]
            if kc >= 2:
                blk.wait("gpsimd", "mm", m4ld[kc - 2])
            ld = blk.dma("gpsimd", mb[:, 0:D],
                         wkv[kc * 128:(kc + 1) * 128, D:2 * D], "dg")
            blk.wait("tensor", "dg", ld)
            lastmm = None
            for tt in range(4):
                for nv in range(2):
                    i = tt * 2 + nv
                    last_of_kc = (tt == 3 and nv == 1)
                    lastmm = blk.mm(
                        pb[i][:, :],
                        S["XLN2"][:, kc * TB + tt * 128:
                                  kc * TB + tt * 128 + 128],
                        mb[:, nv * 512:(nv + 1) * 512],
                        start=(kc == 0), stop=(kc == NCH - 1),
                        inc=("mm" if (kc == NCH - 1 or last_of_kc)
                             else None))
                    if kc == NCH - 1:
                        kt = tb * 4 + tt
                        blk.wait("vector", "mm", lastmm)
                        dst = S["VT"][:, kt * 1280 + nv * 640:
                                      kt * 1280 + (nv + 1) * 640] \
                            .rearrange("p (h n) -> p h n", h=8)[:, :, 0:64]
                        srcp = pb[i][:, :].rearrange("p (h n) -> p h n", h=8)
                        blk.add("vector", lambda e, dst=dst, srcp=srcp:
                                e.tensor_copy(dst, srcp), inc="ev")
            m4ld[kc] = lastmm
        blk.emit()

    blk = Blk(ctx)
    blk.add("gpsimd", lambda e: e.memset(
        S["VT"][:, :].rearrange("p (a b) -> p a b", b=80)[:, :, 64:65], 1.0))
    gv = blk.add("gpsimd", lambda e: e.engine_nop(), inc="eg")
    blk.wait("tensor", "eg", gv)
    for h in range(H):
        for kt in range(16):
            blk.mm(pb[h % 2][0:65, h // 2:h // 2 + 1],
                   S["VT"][:, (kt * 16 + h) * 80:(kt * 16 + h) * 80 + 65],
                   C["ONESC"][:, 0:1],
                   start=(kt == 0), stop=(kt == 15),
                   inc=("mm" if kt == 15 else None))
    blk.wait("vector", "mm", ctx.cnt["mm"])
    blk.vf(lambda e: e.tensor_scalar(
        S["VSH"][:, 0:8], pb[0][0:65, 0:8], 0.5, None, ALU.mult))
    blk.vf(lambda e: e.tensor_scalar(
        S["VSH"][:, 8:16], pb[1][0:65, 0:8], 0.5, None, ALU.mult))
    blk.emit()

    for X in (S["QB"], S["KB"]):
        blk = Blk(ctx)
        pr = {}
        for c in range(NCH):
            for sl in range(4):
                i = c * 4 + sl
                bk = i % 2
                if pr.get(bk):
                    blk.wait("tensor", "ev", pr[bk])
                v = blk.mm(pb[bk][:, :], S["PERM"][:, :],
                           X[:, c * T + sl * TB:c * T + (sl + 1) * TB],
                           start=True, stop=True, inc="mm")
                blk.wait("vector", "mm", v)
                blk.add("vector", lambda e, X=X, c=c, sl=sl: e.tensor_mul(
                    S["TF1"][:, :],
                    X[:, c * T + sl * TB:c * T + (sl + 1) * TB],
                    S["CT"][:, sl * TB:(sl + 1) * TB]))
                blk.add("vector", lambda e, bk=bk, sl=sl: e.tensor_mul(
                    S["TF2"][:, :], pb[bk][:, :],
                    S["ST"][:, sl * TB:(sl + 1) * TB]))
                pr[bk] = blk.add("vector", lambda e, X=X, c=c, sl=sl:
                                 e.tensor_add(
                                     X[:, c * T + sl * TB:
                                       c * T + (sl + 1) * TB],
                                     S["TF1"][:, :], S["TF2"][:, :]),
                                 inc="ev")
        blk.emit()

    for tb in range(NTB):
        blk0 = Blk(ctx)
        blk0.dma("gpsimd", S["XF"][:, :].rearrange("p (c n) -> p c n", c=NCH),
                 x2T[:, tb * TB:(tb + 1) * TB].rearrange(
                     "(c p) n -> p c n", c=NCH), "dg")
        blk0.wait("gpsimd", "dg", ctx.cnt["dg"])
        blk0.emit()

        blk = Blk(ctx)
        AO = S["AO"]
        ea_t, mm_t, pv_t = {}, {}, {}
        ao_t, cp_t = {}, {}
        g = -1
        nkt = 16

        def emit_pv(h, kt, g):
            blk.wait("tensor", "ea", ea_t[g])
            if kt == 0 and h >= 2:
                blk.wait("tensor", "ev", ao_t[h - 2])
            pv_t[g] = blk.mm(pb[4 + h % 2][0:65, :],
                             S["VT"][:, (kt * 16 + h) * 80:
                                     (kt * 16 + h) * 80 + 65],
                             S["PT"][g % 2][:, :], start=(kt == 0),
                             stop=(kt == nkt - 1), inc="mm")

        for h in range(H):
            hc, hr = h // 2, (h % 2) * 64
            for kt in range(nkt):
                g += 1
                sb_ = pb[g % 2]
                if g >= 2:
                    blk.wait("tensor", "ea", ea_t[g - 2])
                mm_t[g] = blk.mm(
                    sb_[:, :],
                    S["KB"][hr:hr + 64, hc * T + kt * 128:
                            hc * T + (kt + 1) * 128],
                    S["QB"][hr:hr + 64, hc * T + tb * TB:
                            hc * T + (tb + 1) * TB],
                    start=True, stop=True, inc="mm")
                pt = S["PT"][g % 2]
                blk.wait("scalar", "mm", mm_t[g])
                if g >= 2:
                    blk.wait("scalar", "mm", pv_t[g - 2])
                ea_t[g] = blk.add("scalar", lambda e, sb_=sb_, pt=pt:
                                  e.activation(pt[:, :], sb_[:, :], AF.Square,
                                               bias=S["TAYB"][:, 0:1],
                                               scale=TAY_A), inc="ea")
                if kt >= 1:
                    emit_pv(h, kt - 1, g - 1)
            emit_pv(h, nkt - 1, g)
            ob = pb[4 + h % 2]
            rb = pb[6 + h % 2]
            blk.wait("scalar", "mm", pv_t[g])
            if h >= 2:
                blk.wait("scalar", "ev", ao_t[h - 2])
            vcol = (h % 2) * 8 + h // 2
            nv = blk.add("scalar", lambda e, ob=ob, h=h, vcol=vcol:
                         e.activation(S["NUM"][h % 2][:, :], ob[0:64, :],
                                      AF.Identity,
                                      bias=S["VSH"][0:64, vcol:vcol + 1]),
                         inc="ea")
            blk.wait("vector", "mm", pv_t[g])
            rdv = blk.vf(lambda e, ob=ob, h=h: e.tensor_scalar(
                S["RD"][h % 2][:, :], ob[64:65, :], 1024.0, None, ALU.add))
            rdv = blk.vf(lambda e, h=h: e.reciprocal(
                S["RD"][h % 2][:, :], S["RD"][h % 2][:, :]))
            blk.wait("tensor", "ev", rdv)
            if h >= 2:
                blk.wait("tensor", "ev", cp_t[h - 2])
            rbv = blk.mm(rb[0:64, :], C["ONESR"][0:1, 0:64],
                         S["RD"][h % 2][0:1, :], start=True, stop=True,
                         inc="mm")
            blk.wait("vector", "mm", rbv)
            cp_t[h] = blk.add("vector", lambda e, rb=rb, h=h: e.tensor_copy(
                S["RDB"][h % 2][:, :], rb[0:64, :]), inc="ev")
            blk.wait("vector", "ea", nv)
            ao_t[h] = blk.add("vector", lambda e, h=h, hc=hc, hr=hr:
                              e.tensor_mul(
                                  AO[hr:hr + 64, hc * TB:(hc + 1) * TB],
                                  S["NUM"][h % 2][:, :],
                                  S["RDB"][h % 2][:, :]), inc="ev")
        blk.emit()

        blk = Blk(ctx)
        wo_specs = [wo[:, oc * 128:(oc + 1) * 128].rearrange(
            "(c p) n -> p c n", c=NCH) for oc in range(NCH)]
        st_t = {}

        def wo_evac(blk, i, mv, tb=tb):
            blk.wait("vector", "mm", mv)
            if i >= 2:
                blk.wait("vector", "ds", st_t[i - 2])
            tfo = S["TF1"] if i % 2 == 0 else S["TF3"]
            v = blk.add("vector", lambda e, i=i, tfo=tfo: e.tensor_add(
                tfo[:, :], pb[i % 4][:, :], S["XF"][:, i * TB:(i + 1) * TB]),
                inc="ev")
            blk.wait("sync", "ev", v)
            st_t[i] = blk.dma(
                "sync", x3T[i * 128:(i + 1) * 128, tb * TB:(tb + 1) * TB],
                tfo[:, :], "ds")
            return ("ev", v)

        stream_mm(ctx, blk, wo_specs, S["WS"], NCH,
                  moving=lambda i, kc: AO[:, kc * TB:(kc + 1) * TB],
                  psum_of=lambda i: pb[i % 4][:, :], evac=wo_evac)
        blk.emit()

    if "dbgC" in io:
        blk = Blk(ctx)
        for nm, hd in io["dbgC"].items():
            blk.dma("sync", hd[:, :], S[nm][:, :] if nm != "AO"
                    else S["AO"][:, :], "ds")
        blk.emit()

    blk = Blk(ctx)
    blk.wait("sync", "ds", ctx.cnt["ds"])
    blk.emit()


C_SHAPES = {"L1_attn_ln_w": [1024], "L1_attn_ln_b": [1024],
            "L1_attn_w_q": [1024, 1024], "L1_attn_w_kv": [1024, 2048],
            "L1_attn_w_o": [1024, 1024], "L1_attn_theta": [1]}


def declare_stage_C_io(nc, io):
    for name, shape in C_SHAPES.items():
        io[name] = nc.dram_tensor(name, shape, F32, kind="ExternalInput")
    if os.environ.get("K_DEBUG_OUT", "0") == "1":
        io["dbgC"] = {}
        for nm, shape, dt in [("CT", [128, T], BF16), ("ST", [128, T], BF16),
                              ("QB", [128, NCH * T], BF16),
                              ("KB", [128, NCH * T], BF16),
                              ("VSH", [65, 16], F32),
                              ("VT", [128, 16 * 16 * 80], BF16),
                              ("XLN2", [128, NCH * TB], BF16),
                              ("AO", [128, NCH * TB], BF16)]:
            io["dbgC"][nm] = nc.dram_tensor("dbgC_" + nm, shape, dt,
                                            kind="ExternalOutput")


def stage_C_in_map(x2_batch, inputs):
    m = {"x2T": np.ascontiguousarray(x2_batch.T.astype(np.float32))}
    for k in C_SHAPES:
        if k == "L1_attn_theta":
            m[k] = np.asarray(inputs[k], np.float32).reshape(1)
        else:
            m[k] = np.asarray(inputs[k], np.float32)
    for k, v in CONSTS.items():
        m["c_" + k] = v
    return m


def build_program_C():
    nc = bass.Bass("TRN2", target_bir_lowering=False)
    io = {}
    io["x2T"] = nc.dram_tensor("x2T", [D, T], F32, kind="ExternalInput")
    io["x3T"] = nc.dram_tensor("x3T", [D, T], F32, kind="ExternalOutput")
    declare_stage_C_io(nc, io)
    for k, v in CONSTS.items():
        io["c_" + k] = nc.dram_tensor("c_" + k, list(v.shape),
                                      BF16 if v.dtype == BF else F32,
                                      kind="ExternalInput")
    ctx = Ctx(nc)
    for s in ("dg", "ds", "mm", "ev", "ea", "eg"):
        ctx.add_sem(s)
    C = alloc_consts(nc, CORE_CONSTS)
    S = alloc_stage_C(nc)
    pb = [nc.alloc_psum_tensor(f"pb{i}", [128, TB], F32) for i in range(8)]
    blk = Blk(ctx)
    load_consts(blk, C, {k: io["c_" + k] for k in CONSTS})
    blk.wait("sync", "ds", ctx.cnt["ds"])
    blk.emit()
    emit_stage_C(ctx, S, C, pb, io)
    return nc



def alloc_stage_D(nc):
    sb = nc.alloc_sbuf_tensor
    S = {}
    A2 = sb("D_A2", [128, 2 * NCH * TB], F32)
    S["XF"] = A2.ap()[:, 0:NCH * TB]
    S["XB"] = A2.ap().bitcast(BF16)[:, 2 * NCH * TB:3 * NCH * TB]
    S["SQ"] = A2.ap().bitcast(BF16)[:, 3 * NCH * TB:4 * NCH * TB]
    S["MU"] = sb("D_MU", [128, TB], F32)
    S["RSTD"] = sb("D_RSTD", [128, TB], F32)
    S["TF1"] = sb("D_TF1", [128, TB], F32)
    S["TF2"] = sb("D_TF2", [128, TB], F32)
    S["XLND"] = sb("D_XLND", [128, NCH * T], BF16)
    S["HALL"] = sb("D_HALL", [128, 16 * 4 * TB], BF16)
    S["SHH"] = sb("D_SHH", [128, NCH * TB], BF16)
    S["TM1"] = sb("D_TM1", [128, TB], BF16)
    S["TM2"] = sb("D_TM2", [128, TB], BF16)
    S["TM3"] = sb("D_TM3", [128, TB], BF16)
    S["GWTF"] = sb("D_GWTF", [128, NCH * 16], F32)
    S["LGT"] = sb("D_LGT", [128, 16 * 16], F32)
    S["GTW"] = sb("D_GTW", [128, 16 * 16], F32)
    S["GTE"] = sb("D_GTE", [16, T], F32)
    S["GTEB"] = sb("D_GTEB", [16, T], BF16)
    S["EBB"] = sb("D_EBB", [128, 16], F32)
    S["T16A"] = sb("D_T16A", [128, 16], F32)
    S["T16B"] = sb("D_T16B", [128, 16], F32)
    S["TC1"] = sb("D_TC1", [128, 1], F32)
    S["TC2"] = sb("D_TC2", [128, 1], F32)
    S["B1SD"] = sb("D_B1SD", [128, 64], F32)
    S["B3SD"] = sb("D_B3SD", [128, 64], F32)
    S["SB1D"] = sb("D_SB1D", [128, 8], F32)
    S["SB3D"] = sb("D_SB3D", [128, 8], F32)
    S["B2SB"] = sb("D_B2SB", [16, D], BF16)
    S["SB2R"] = sb("D_SB2R", [1, D], BF16)
    S["ONER"] = sb("D_ONER", [1, TB], BF16)
    S["GBC"] = sb("D_GBC", [128, TB], F32)
    S["GROW"] = sb("D_GROW", [1, TB], F32)
    S["WS"] = [sb(f"D_WS{i}", [128, NCH * 128], BF16) for i in range(2)]
    S["OT"] = [sb(f"D_OT{i}", [128, TB], F32) for i in range(2)]
    S["STG"] = sb("D_STG", [128, NCH * 128], BF16)
    S["STGF"] = sb("D_STGF", [128, 5 * 128], F32)
    return S


def emit_prep_D(ctx, S, C, pb, io):
    nc = ctx.nc
    w1p, w3p, w2p = io["w1_pair"], io["w3_pair"], io["w2_pair"]
    shm = io["sh_one"]
    myw1t, myw3t, myw2t, mysht = (io["scr_myw1t"], io["scr_myw3t"],
                                  io["scr_myw2t"], io["scr_mysht"])
    allw1t, allw3t, allw2t, allsh = (io["scr_allw1t"], io["scr_allw3t"],
                                     io["scr_allw2t"], io["scr_allsh"])
    blk = Blk(ctx)
    jobs = []
    for le in range(2):
        jobs.append((lambda rb, le=le: w1p[le, :, rb * 128:(rb + 1) * 128]
                     .rearrange("(c p) n -> p c n", c=4),
                     myw1t, le * 1024, 8, 4))
        jobs.append((lambda rb, le=le: w3p[le, :, rb * 128:(rb + 1) * 128]
                     .rearrange("(c p) n -> p c n", c=4),
                     myw3t, le * 1024, 8, 4))
        jobs.append((lambda rb, le=le: w2p[le, :, rb * 128:(rb + 1) * 128]
                     .rearrange("(c p) n -> p c n", c=8),
                     myw2t, le * 512, 4, 8))
    jobs.append((lambda rb: shm[:, rb * 128:(rb + 1) * 128]
                 .rearrange("(c p) n -> p c n", c=8),
                 mysht, 0, 8, 8))
    st_t = {}
    prev_ev = {}
    i = 0
    for (srcf, dstd, dro, nrb, nfc) in jobs:
        for rb in range(nrb):
            buf = S["WS"][i % 2]
            obuf = S["STG"]
            if i >= 2:
                blk.wait("gpsimd", "mm", st_t[i - 2][0])
            ld = blk.dma("gpsimd", buf[:, 0:nfc * 128].rearrange(
                "p (c n) -> p c n", c=nfc), srcf(rb), "dg")
            blk.wait("tensor", "dg", ld)
            if i >= 1:
                blk.wait("vector", "ds", st_t[i - 1][1])
            lastmm = None
            vv = None
            for c in range(nfc):
                pbb = pb[4 + c % 2]
                if prev_ev.get(c % 2):
                    blk.wait("tensor", "ev", prev_ev[c % 2])
                lastmm = blk.tr(pbb.bitcast(BF16)[:, 0:128],
                                buf[:, c * 128:(c + 1) * 128],
                                C["IDB"][:, :], inc="mm")
                blk.wait("vector", "mm", lastmm)
                vv = blk.add("vector", lambda e, obuf=obuf, c=c, pbb=pbb:
                             e.tensor_copy(obuf[:, c * 128:(c + 1) * 128],
                                           pbb.bitcast(BF16)[:, 0:128]),
                             inc="ev")
                prev_ev[c % 2] = vv
            blk.wait("sync", "ev", vv)
            sv = blk.dma("sync",
                         dstd[dro + rb * 128:dro + (rb + 1) * 128,
                              0:nfc * 128],
                         obuf[:, 0:nfc * 128], "ds")
            st_t[i] = (lastmm, sv)
            i += 1
    blk.wait("gpsimd", "ds", ctx.cnt["ds"])
    for (src, dst) in [(myw1t, allw1t), (myw3t, allw3t), (myw2t, allw2t),
                       (mysht, allsh)]:
        blk.add("gpsimd", lambda e, src=src, dst=dst: e.collective_compute(
            "AllGather", ALU.bypass, replica_groups=RG8,
            ins=[src.ap().opt()], outs=[dst.ap().opt()]), inc="cc")
    blk.wait("gpsimd", "cc", ctx.cnt["cc"])
    blk.emit()


def emit_stage_D(ctx, S, C, pb, io):
    nc = ctx.nc
    x3T, outT = io["x3T"], io["outT"]
    allw1t, allw3t, allw2t, allsh = (io["scr_allw1t"], io["scr_allw3t"],
                                     io["scr_allw2t"], io["scr_allsh"])
    gted = io["scr_gted"]

    blk = Blk(ctx)
    gw_ev = {}
    for cb in range(NCH):
        ld = blk.dma("gpsimd", S["STGF"][0:16, 0:128],
                     io["L1_ffn_gate_w"][:, cb * 128:(cb + 1) * 128], "dg")
        blk.wait("tensor", "dg", ld)
        if cb >= 1:
            blk.wait("tensor", "ev", gw_ev[cb - 1])
        v = blk.tr(pb[0][:, 0:16], S["STGF"][0:16, 0:128],
                   C["IDF"][0:16, 0:16], inc="mm")
        blk.wait("vector", "mm", v)
        gw_ev[cb] = blk.add("vector", lambda e, cb=cb: e.tensor_copy(
            S["GWTF"][:, cb * 16:(cb + 1) * 16], pb[0][:, 0:16]), inc="ev")
        blk.wait("gpsimd", "ev", gw_ev[cb])
    ld = blk.dma("gpsimd", S["STGF"][0:64, 0:128],
                 io["L1_ffn_b1"][:, :].rearrange("a (b n) -> (a b) n", b=4),
                 "dg")
    ld = blk.dma("gpsimd", S["STGF"][0:64, 128:256],
                 io["L1_ffn_b3"][:, :].rearrange("a (b n) -> (a b) n", b=4),
                 "dg")
    ld = blk.dma("gpsimd", S["STGF"][0:8, 256:384],
                 io["L1_ffn_sb1"][:].rearrange("(a n) -> a n", n=128), "dg")
    ld = blk.dma("gpsimd", S["STGF"][0:8, 384:512],
                 io["L1_ffn_sb3"][:].rearrange("(a n) -> a n", n=128), "dg")
    ld = blk.dma("gpsimd", S["B2SB"][:, :], io["L1_ffn_b2"][:, :], "dg")
    ld = blk.dma("gpsimd", S["SB2R"][:, :], io["L1_ffn_sb2"][None, :], "dg")
    ld = blk.dma("gpsimd", S["STGF"][0:1, 512:528],
                 io["L1_ffn_expert_bias"][None, :], "dg")
    blk.add("gpsimd", lambda e: e.memset(S["ONER"][:, :], 1.0))
    blk.wait("tensor", "dg", ctx.cnt["dg"])
    v = blk.tr(pb[1][:, 0:64], S["STGF"][0:64, 0:128], C["IDF"][0:64, 0:64])
    v = blk.tr(pb[2][:, 0:64], S["STGF"][0:64, 128:256],
               C["IDF"][0:64, 0:64])
    v = blk.tr(pb[3][:, 0:8], S["STGF"][0:8, 256:384], C["IDF"][0:8, 0:8])
    v = blk.tr(pb[6][:, 0:8], S["STGF"][0:8, 384:512],
               C["IDF"][0:8, 0:8], inc="mm")
    vb = blk.mm(pb[7][:, 0:16], C["ONESR"][0:1, :], S["STGF"][0:1, 512:528],
                start=True, stop=True, inc="mm")
    blk.wait("vector", "mm", vb)
    blk.vf(lambda e: e.tensor_copy(S["B1SD"][:, :], pb[1][:, 0:64]))
    blk.vf(lambda e: e.tensor_copy(S["B3SD"][:, :], pb[2][:, 0:64]))
    blk.vf(lambda e: e.tensor_copy(S["SB1D"][:, :], pb[3][:, 0:8]))
    blk.vf(lambda e: e.tensor_copy(S["SB3D"][:, :], pb[6][:, 0:8]))
    blk.vf(lambda e: e.tensor_copy(S["EBB"][:, :], pb[7][:, 0:16]))
    blk.emit()

    for tb in range(NTB):
        blk0 = Blk(ctx)
        blk0.dma("gpsimd", S["XF"][:, :].rearrange("p (c n) -> p c n", c=NCH),
                 x3T[:, tb * TB:(tb + 1) * TB].rearrange("(c p) n -> p c n",
                                                         c=NCH), "dg")
        blk0.wait("gpsimd", "dg", ctx.cnt["dg"])
        blk0.emit()

        def logits_cb(blk, c, tf1, fv, tb=tb):
            blk.wait("tensor", "ev", fv)
            v = None
            for tt in range(4):
                last = (c == NCH - 1)
                v = blk.mm(pb[2 + tt][:, 0:16],
                           tf1[:, tt * 128:(tt + 1) * 128],
                           S["GWTF"][:, c * 16:(c + 1) * 16],
                           start=(c == 0), stop=last,
                           inc=("mm" if tt == 3 else None))
            return v

        emit_ln(ctx, S, C, pb, S["XF"], None,
                dst_fn=lambda c, tb=tb: S["XLND"][:, c * T + tb * TB:
                                                  c * T + (tb + 1) * TB],
                fp32_cb=logits_cb)

        blk = Blk(ctx)
        blk.wait("vector", "mm", ctx.cnt["mm"])
        for tt in range(4):
            blk.add("vector", lambda e, tt=tt, tb=tb: e.tensor_copy(
                S["LGT"][:, (tb * 4 + tt) * 16:(tb * 4 + tt + 1) * 16],
                pb[2 + tt][:, 0:16]), inc="ev")
        for tt in range(4):
            j = tb * 4 + tt
            LG = S["LGT"][:, j * 16:(j + 1) * 16]
            blk.vf(lambda e, LG=LG: e.tensor_add(
                S["T16A"][:, :], LG, S["EBB"][:, :]))
            blk.vf(lambda e: e.reduce_max(S["TC1"][:, :], S["T16A"][:, :],
                                          axis=AX.X))
            blk.vf(lambda e: e.tensor_scalar(
                S["T16B"][:, :], S["T16A"][:, :], S["TC1"][:, 0:1], None,
                ALU.is_equal))
            blk.vf(lambda e: e.scalar_tensor_tensor(
                S["T16A"][:, :], S["T16B"][:, :], -1e30, S["T16A"][:, :],
                ALU.mult, ALU.add))
            blk.vf(lambda e: e.reduce_max(S["TC1"][:, :], S["T16A"][:, :],
                                          axis=AX.X))
            blk.vf(lambda e: e.tensor_scalar(
                S["T16A"][:, :], S["T16A"][:, :], S["TC1"][:, 0:1], None,
                ALU.is_equal))
            mv = blk.vf(lambda e: e.tensor_add(
                S["T16B"][:, :], S["T16B"][:, :], S["T16A"][:, :]))
            blk.wait("scalar", "ev", mv)
            av = blk.add("scalar", lambda e, LG=LG: e.activation(
                S["T16A"][:, :], LG, AF.Sigmoid), inc="ea")
            blk.wait("vector", "ea", av)
            blk.vf(lambda e: e.tensor_mul(
                S["T16A"][:, :], S["T16A"][:, :], S["T16B"][:, :]))
            blk.vf(lambda e: e.reduce_sum(S["TC1"][:, :], S["T16A"][:, :],
                                          axis=AX.X))
            blk.vf(lambda e: e.tensor_scalar(
                S["TC1"][:, :], S["TC1"][:, :], 1e-10, None, ALU.add))
            blk.vf(lambda e: e.reciprocal(S["TC2"][:, 0:1],
                                          S["TC1"][:, 0:1]))
            blk.vf(lambda e, j=j: e.tensor_scalar(
                S["GTW"][:, j * 16:(j + 1) * 16], S["T16A"][:, :],
                S["TC2"][:, 0:1], None, ALU.mult))
        blk.emit()

    blk = Blk(ctx)
    pr = {}
    for j in range(16):
        bk = j % 2
        if pr.get(bk):
            blk.wait("tensor", "ev", pr[bk])
        v = blk.tr(pb[bk][0:16, 0:128], S["GTW"][:, j * 16:(j + 1) * 16],
                   C["IDF"][:, :], inc="mm")
        blk.wait("vector", "mm", v)
        pr[bk] = blk.add("vector", lambda e, j=j, bk=bk: e.tensor_copy(
            S["GTE"][0:16, j * 128:(j + 1) * 128], pb[bk][0:16, 0:128]),
            inc="ev")
    blk.wait("vector", "ev", ctx.cnt["ev"])
    v = blk.add("vector", lambda e: e.tensor_copy(S["GTEB"][:, :],
                                                  S["GTE"][:, :]), inc="ev")
    blk.wait("sync", "ev", v)
    blk.dma("sync", gted[:, :], S["GTE"][:, :], "ds")
    blk.wait("sync", "ds", ctx.cnt["ds"])
    blk.emit()

    for tb in range(NTB):
        blk0 = Blk(ctx)
        blk0.dma("gpsimd", S["XF"][:, :].rearrange("p (c n) -> p c n", c=NCH),
                 x3T[:, tb * TB:(tb + 1) * TB].rearrange("(c p) n -> p c n",
                                                         c=NCH), "dg")
        blk0.wait("gpsimd", "dg", ctx.cnt["dg"])
        blk0.emit()
        blk = Blk(ctx)
        specs = []
        for e in range(16):
            for oc in range(4):
                specs.append(("w1", e, oc))
                specs.append(("w3", e, oc))
        for oc in range(NCH):
            specs.append(("sw1", 0, oc))
            specs.append(("sw3", 0, oc))

        gb_ev = [None]
        pair_ev = [None]

        def spec_src(i):
            kind, e, oc = specs[i]
            if kind == "w1":
                return allw1t[e * 1024:(e + 1) * 1024,
                              oc * 128:(oc + 1) * 128].rearrange(
                    "(c p) n -> p c n", c=NCH)
            if kind == "w3":
                return allw3t[e * 1024:(e + 1) * 1024,
                              oc * 128:(oc + 1) * 128].rearrange(
                    "(c p) n -> p c n", c=NCH)
            base = 0 if kind == "sw1" else 1024
            return allsh[base:base + 1024,
                         oc * 128:(oc + 1) * 128].rearrange(
                "(c p) n -> p c n", c=NCH)

        def a_evac(blk, i, mv, tb=tb):
            kind, e, oc = specs[i]
            if kind == "w1":
                blk.wait("scalar", "mm", mv)
                if pair_ev[0] is not None:
                    blk.wait("scalar", "ev", pair_ev[0])
                return ("ea", blk.add("scalar", lambda e_, e=e, oc=oc, i=i:
                                      e_.activation(
                                          S["TM1"][:, :], pb[i % 4][:, :],
                                          AF.Silu,
                                          bias=S["B1SD"][:, e * 4 + oc:
                                                         e * 4 + oc + 1]),
                                      inc="ea"))
            if kind == "sw1":
                blk.wait("scalar", "mm", mv)
                if pair_ev[0] is not None:
                    blk.wait("scalar", "ev", pair_ev[0])
                return ("ea", blk.add("scalar", lambda e_, oc=oc, i=i:
                                      e_.activation(
                                          S["TM1"][:, :], pb[i % 4][:, :],
                                          AF.Silu,
                                          bias=S["SB1D"][:, oc:oc + 1]),
                                      inc="ea"))
            if kind == "sw3":
                blk.wait("scalar", "mm", mv)
                v = blk.add("scalar", lambda e_, oc=oc, i=i: e_.activation(
                    S["TM2"][:, :], pb[i % 4][:, :], AF.Identity,
                    bias=S["SB3D"][:, oc:oc + 1]), inc="ea")
                blk.wait("vector", "ea", v)
                rr = blk.add(
                    "vector", lambda e_, oc=oc: e_.tensor_mul(
                        S["SHH"][:, oc * TB:(oc + 1) * TB],
                        S["TM1"][:, :], S["TM2"][:, :]), inc="ev")
                pair_ev[0] = rr
                return ("ev", rr)
            blk.wait("scalar", "mm", mv)
            v = blk.add("scalar", lambda e_, e=e, oc=oc, i=i: e_.activation(
                S["TM2"][:, :], pb[i % 4][:, :], AF.Identity,
                bias=S["B3SD"][:, e * 4 + oc:e * 4 + oc + 1]), inc="ea")
            if oc == 0:
                gl = blk.dma("gpsimd", S["GROW"][:, :],
                             gted[e:e + 1, tb * TB:(tb + 1) * TB], "dg")
                blk.wait("tensor", "dg", gl)
                if gb_ev[0] is not None:
                    blk.wait("tensor", "ev", gb_ev[0])
                gv = blk.mm(pb[6][:, :], C["ONESR"][0:1, :],
                            S["GROW"][0:1, :], start=True, stop=True,
                            inc="mm")
                blk.wait("vector", "mm", gv)
                gb_ev[0] = blk.add("vector", lambda e_: e_.tensor_copy(
                    S["GBC"][:, :], pb[6][:, :]), inc="ev")
            blk.wait("vector", "ea", v)
            blk.add("vector", lambda e_: e_.tensor_mul(
                S["TM3"][:, :], S["TM1"][:, :], S["TM2"][:, :]))
            rr = blk.add(
                "vector", lambda e_, e=e, oc=oc: e_.tensor_mul(
                    S["HALL"][:, (e * 4 + oc) * TB:(e * 4 + oc + 1) * TB],
                    S["TM3"][:, :], S["GBC"][:, :]), inc="ev")
            pair_ev[0] = rr
            return ("ev", rr)

        stream_mm(ctx, blk, [spec_src(i) for i in range(len(specs))],
                  S["WS"], NCH,
                  moving=lambda i, kc, tb=tb: S["XLND"][:, kc * T + tb * TB:
                                                        kc * T + tb * TB + TB],
                  psum_of=lambda i: pb[i % 4][:, :], evac=a_evac)
        blk.emit()

        blk = Blk(ctx)
        st_t = {}
        ld_t = {}
        mm_t = {}
        for dch in range(NCH):
            bank = dch % 2
            for e in range(17):
                i = dch * 17 + e
                buf = S["WS"][i % 2]
                if i >= 2:
                    blk.wait("gpsimd", "mm", mm_t[i - 2])
                if e < 16:
                    srcw = allw2t[e * 512:(e + 1) * 512,
                                  dch * 128:(dch + 1) * 128].rearrange(
                        "(c p) n -> p c n", c=4)
                    nfc = 4
                else:
                    srcw = allsh[2048 + 0:2048 + 1024,
                                 dch * 128:(dch + 1) * 128].rearrange(
                        "(c p) n -> p c n", c=NCH)
                    nfc = NCH
                ld = blk.dma("gpsimd", buf[:, 0:nfc * 128].rearrange(
                    "p (c n) -> p c n", c=nfc), srcw, "dg")
                blk.wait("tensor", "dg", ld)
                if e == 0 and dch >= 2:
                    blk.wait("tensor", "ev", st_t[dch - 2][0])
                for fc in range(nfc):
                    if e < 16:
                        movs = S["HALL"][:, (e * 4 + fc) * TB:
                                         (e * 4 + fc + 1) * TB]
                    else:
                        movs = S["SHH"][:, fc * TB:(fc + 1) * TB]
                    mm_t[i] = blk.mm(pb[bank][:, :],
                                     buf[:, fc * 128:(fc + 1) * 128],
                                     movs, start=(e == 0 and fc == 0),
                                     stop=False,
                                     inc=("mm" if fc == nfc - 1 else None))
            blk.mm(pb[bank][:, :], S["B2SB"][:, dch * 128:(dch + 1) * 128],
                   S["GTEB"][:, tb * TB:(tb + 1) * TB],
                   start=False, stop=False)
            v = blk.mm(pb[bank][:, :],
                       S["SB2R"][0:1, dch * 128:(dch + 1) * 128],
                       S["ONER"][0:1, :], start=False, stop=True, inc="mm")
            blk.wait("vector", "mm", v)
            if dch >= 2:
                blk.wait("vector", "ds", st_t[dch - 2][1])
            ot = S["OT"][dch % 2]
            vv = blk.add("vector", lambda e_, dch=dch, bank=bank, ot=ot:
                         e_.tensor_add(ot[:, :], pb[bank][:, :],
                                       S["XF"][:, dch * TB:(dch + 1) * TB]),
                         inc="ev")
            blk.wait("sync", "ev", vv)
            sv = blk.dma("sync",
                         outT[dch * 128:(dch + 1) * 128,
                              tb * TB:(tb + 1) * TB], ot[:, :], "ds")
            st_t[dch] = (vv, sv)
        blk.emit()

    if "dbgD" in io:
        blk = Blk(ctx)
        for nm, hd in io["dbgD"].items():
            blk.dma("sync", hd[:, :], S[nm][:, :], "ds")
        blk.emit()

    blk = Blk(ctx)
    blk.wait("sync", "ds", ctx.cnt["ds"])
    blk.wait("gpsimd", "dg", ctx.cnt["dg"])
    blk.emit()


D_SHAPES = {"L1_ffn_gate_w": [16, 1024], "L1_ffn_expert_bias": [16],
            "L1_ffn_b1": [16, 512], "L1_ffn_b3": [16, 512],
            "L1_ffn_b2": [16, 1024], "L1_ffn_sb1": [1024],
            "L1_ffn_sb3": [1024], "L1_ffn_sb2": [1024],
            "w1_pair": [2, 512, 1024], "w3_pair": [2, 512, 1024],
            "w2_pair": [2, 1024, 512], "sh_one": [1024, 1024]}


def declare_stage_D_io(nc, io):
    for name, shape in D_SHAPES.items():
        io[name] = nc.dram_tensor(name, shape, F32, kind="ExternalInput")
    if os.environ.get("K_DEBUG_OUT", "0") == "1":
        io["dbgD"] = {}
        for nm, shape, dt in [("GTW", [128, 256], F32),
                              ("LGT", [128, 256], F32),
                              ("GWTF", [128, 128], F32),
                              ("EBB", [128, 16], F32),
                              ("B1SD", [128, 64], F32),
                              ("SB1D", [128, 8], F32),
                              ("SHH", [128, NCH * TB], BF16),
                              ("HALL", [128, 64 * TB], BF16),
                              ("XLND", [128, NCH * T], BF16)]:
            io["dbgD"][nm] = nc.dram_tensor("dbgD_" + nm, shape, dt,
                                            kind="ExternalOutput")
    io["scr_myw1t"] = nc.dram_tensor("scr_myw1t", [2048, 512], BF16)
    io["scr_myw3t"] = nc.dram_tensor("scr_myw3t", [2048, 512], BF16)
    io["scr_myw2t"] = nc.dram_tensor("scr_myw2t", [1024, 1024], BF16)
    io["scr_mysht"] = nc.dram_tensor("scr_mysht", [1024, 1024], BF16)
    io["scr_allw1t"] = nc.dram_tensor("scr_allw1t", [16384, 512], BF16)
    io["scr_allw3t"] = nc.dram_tensor("scr_allw3t", [16384, 512], BF16)
    io["scr_allw2t"] = nc.dram_tensor("scr_allw2t", [8192, 1024], BF16)
    io["scr_allsh"] = nc.dram_tensor("scr_allsh", [8192, 1024], BF16)
    io["scr_gted"] = nc.dram_tensor("scr_gted", [16, T], F32)


def stage_D_in_map(core, x3_batch, inputs):
    m = {"x3T": np.ascontiguousarray(x3_batch.T.astype(np.float32))}
    for k in ("L1_ffn_gate_w", "L1_ffn_expert_bias", "L1_ffn_b1",
              "L1_ffn_b3", "L1_ffn_b2", "L1_ffn_sb1", "L1_ffn_sb3",
              "L1_ffn_sb2"):
        m[k] = np.asarray(inputs[k], np.float32)
    m["w1_pair"] = np.ascontiguousarray(
        np.asarray(inputs["L1_ffn_w1"], np.float32)[2 * core:2 * core + 2])
    m["w3_pair"] = np.ascontiguousarray(
        np.asarray(inputs["L1_ffn_w3"], np.float32)[2 * core:2 * core + 2])
    m["w2_pair"] = np.ascontiguousarray(
        np.asarray(inputs["L1_ffn_w2"], np.float32)[2 * core:2 * core + 2])
    if core == 0:
        sh = np.asarray(inputs["L1_ffn_sw1"], np.float32)
    elif core == 1:
        sh = np.asarray(inputs["L1_ffn_sw3"], np.float32)
    elif core == 2:
        sh = np.asarray(inputs["L1_ffn_sw2"], np.float32)
    else:
        sh = np.zeros((1024, 1024), np.float32)
    m["sh_one"] = np.ascontiguousarray(sh)
    for k, v in CONSTS.items():
        m["c_" + k] = v
    return m


def build_program_D():
    nc = bass.Bass("TRN2", target_bir_lowering=False)
    io = {}
    io["x3T"] = nc.dram_tensor("x3T", [D, T], F32, kind="ExternalInput")
    io["outT"] = nc.dram_tensor("outT", [D, T], F32, kind="ExternalOutput")
    declare_stage_D_io(nc, io)
    for k, v in CONSTS.items():
        io["c_" + k] = nc.dram_tensor("c_" + k, list(v.shape),
                                      BF16 if v.dtype == BF else F32,
                                      kind="ExternalInput")
    ctx = Ctx(nc)
    for s in ("dg", "ds", "mm", "ev", "ea", "cc"):
        ctx.add_sem(s)
    C = alloc_consts(nc, CORE_CONSTS)
    S = alloc_stage_D(nc)
    pb = [nc.alloc_psum_tensor(f"pb{i}", [128, TB], F32) for i in range(8)]
    blk = Blk(ctx)
    load_consts(blk, C, {k: io["c_" + k] for k in CONSTS})
    blk.wait("sync", "ds", ctx.cnt["ds"])
    blk.emit()
    emit_prep_D(ctx, S, C, pb, io)
    emit_stage_D(ctx, S, C, pb, io)
    return nc



A_WEIGHTS = ["L0_attn_W_DQ", "L0_attn_W_UQ", "L0_attn_W_DKV", "L0_attn_W_UK",
             "L0_attn_W_UV", "L0_attn_W_QR", "L0_attn_W_KR", "L0_attn_W_O"]
A_SHAPES = {"L0_attn_W_DQ": [1024, 4096], "L0_attn_W_UQ": [4096, 1024],
            "L0_attn_W_DKV": [1024, 256], "L0_attn_W_UK": [256, 1024],
            "L0_attn_W_UV": [256, 1024], "L0_attn_W_QR": [4096, 512],
            "L0_attn_W_KR": [1024, 32], "L0_attn_W_O": [1024, 1024]}


def build_program_A():
    nc = bass.Bass("TRN2", target_bir_lowering=False)
    io = {}
    io["xT"] = nc.dram_tensor("xT", [D, T], F32, kind="ExternalInput")
    io["x1T"] = nc.dram_tensor("x1T", [D, T], F32, kind="ExternalOutput")
    for name in A_WEIGHTS:
        io[name] = nc.dram_tensor(name, A_SHAPES[name], F32,
                                  kind="ExternalInput")
    for k, v in CONSTS.items():
        io["c_" + k] = nc.dram_tensor("c_" + k, list(v.shape),
                                      BF16 if v.dtype == BF else F32,
                                      kind="ExternalInput")
    io["scr_WDQT"] = nc.dram_tensor("scr_WDQT", [4096, 1024], BF16)
    io["scr_WDKVT"] = nc.dram_tensor("scr_WDKVT", [256, 1024], BF16)
    io["scr_M1"] = nc.dram_tensor("scr_M1", [1024, 1024], BF16)
    io["scr_M2"] = nc.dram_tensor("scr_M2", [1024, 512], BF16)
    io["scr_M3"] = nc.dram_tensor("scr_M3", [1024, 1024], BF16)
    io["scr_M4"] = nc.dram_tensor("scr_M4", [1024, 1024], BF16)

    ctx = Ctx(nc)
    for s in ("dg", "ds", "mm", "ev", "ea"):
        ctx.add_sem(s)
    C = alloc_consts(nc, A_CONSTS)
    S = alloc_stage_A(nc)
    pb = [nc.alloc_psum_tensor(f"pb{i}", [128, TB], F32) for i in range(8)]

    blk = Blk(ctx)
    load_consts(blk, C, {k: io["c_" + k] for k in CONSTS})
    blk.wait("sync", "ds", ctx.cnt["ds"])
    blk.emit()

    emit_stage_A(ctx, S, C, pb, io)
    return nc


def stage_A_in_map(x_batch, inputs):
    m = {"xT": np.ascontiguousarray(x_batch.T)}
    for k in A_WEIGHTS:
        m[k] = np.asarray(inputs[k], dtype=np.float32)
    for k, v in CONSTS.items():
        m["c_" + k] = v
    return m


B_SHAPES = {"L0_ffn_router_w": [1024, 8], "L0_ffn_router_b": [8],
            "w1_e": [1024, 4096], "b1_e": [4096],
            "w2_e": [4096, 1024], "b2_e": [1024],
            "sel8f": [64, 8], "sello": [64, 8], "selp": [64, 8],
            "tri8s": [8, 8], "sh8": [8, 8], "r4096": [8, 1],
            "rbasem1": [128, 8], "siota": [128, 20]}


def declare_stage_B_io(nc, io):
    dbg = os.environ.get("K_DEBUG_OUT", "0") == "1"
    kind = "ExternalOutput" if dbg else "Internal"
    for name, shape in B_SHAPES.items():
        io[name] = nc.dram_tensor(name, shape, F32, kind="ExternalInput")
    io["scr_myxln"] = nc.dram_tensor("scr_myxln", [T, D], BF16, kind=kind)
    io["scr_mycnt"] = nc.dram_tensor("scr_mycnt", [8, 1], F32)
    io["scr_cnt64"] = nc.dram_tensor("scr_cnt64", [64, 1], F32)
    io["scr_sendb"] = nc.dram_tensor("scr_sendb", [2 * T, D], BF16)
    io["scr_sball"] = nc.dram_tensor("scr_sball", [16 * T, D], BF16)
    io["scr_yd"] = nc.dram_tensor("scr_yd", [CAP, D], BF16)
    io["scr_yall"] = nc.dram_tensor("scr_yall", [8 * CAP, D], BF16)
    if dbg:
        io["dbg"] = {}
        for nm, shape, dt in [("CNT64", [64, 1], F32), ("GCUM", [8, T], F32),
                              ("IDXL1", [128, 16], I32),
                              ("IDXL2", [128, 16], I32),
                              ("IDXG", [128, 20], I32),
                              ("IDX1", [128, 16], I32),
                              ("IDX2", [128, 16], I32),
                              ("W1TOK", [128, 16], F32),
                              ("W2TOK", [128, 16], F32),
                              ("GT", [128, 128], F32),
                              ("LGT", [128, 128], F32),
                              ("MK1", [128, 128], F32),
                              ("MK2", [128, 128], F32),
                              ("G1", [128, D], BF16),
                              ("G2", [128, D], BF16),
                              ("BUFS", [128, NCH * CAP], BF16)]:
            io["dbg"][nm] = nc.dram_tensor("dbg_" + nm, shape, dt,
                                           kind="ExternalOutput")


def stage_B_in_map(core, x1_batch, inputs):
    m = {"x1T": np.ascontiguousarray(x1_batch.T.astype(np.float32))}
    m["L0_ffn_router_w"] = np.asarray(inputs["L0_ffn_router_w"], np.float32)
    m["L0_ffn_router_b"] = np.asarray(inputs["L0_ffn_router_b"], np.float32)
    m["w1_e"] = np.ascontiguousarray(np.asarray(inputs["L0_ffn_w1"],
                                                np.float32)[core])
    m["b1_e"] = np.ascontiguousarray(np.asarray(inputs["L0_ffn_b1"],
                                                np.float32)[core])
    m["w2_e"] = np.ascontiguousarray(np.asarray(inputs["L0_ffn_w2"],
                                                np.float32)[core])
    m["b2_e"] = np.ascontiguousarray(np.asarray(inputs["L0_ffn_b2"],
                                                np.float32)[core])
    sel8f = np.zeros((64, 8), dtype=np.float32)
    sello = np.zeros((64, 8), dtype=np.float32)
    selp = np.zeros((64, 8), dtype=np.float32)
    for r in range(8):
        sel8f[8 * r + core, r] = 1.0
        for e in range(8):
            if e < core:
                sello[8 * r + e, r] = 1.0
            if r < core:
                selp[8 * r + e, e] = 1.0
    m["sel8f"] = sel8f
    m["sello"] = sello
    m["selp"] = selp
    m["tri8s"] = np.triu(np.ones((8, 8), np.float32), 1).copy()
    sh8 = np.zeros((8, 8), np.float32)
    for mm_ in range(1, 8):
        sh8[mm_ - 1, mm_] = 1.0
    m["sh8"] = sh8
    m["r4096"] = (np.arange(8, dtype=np.float32) * 4096).reshape(8, 1)
    rb = np.zeros((128, 8), np.float32)
    rb[:, :] = (np.arange(8) * CAP - 1.0)[None, :]
    m["rbasem1"] = rb
    si = np.zeros((128, 20), np.float32)
    si[:, :] = np.arange(128)[:, None] + 128 * np.arange(20)[None, :]
    m["siota"] = si
    for k, v in CONSTS.items():
        m["c_" + k] = v
    return m


def build_program_B():
    nc = bass.Bass("TRN2", target_bir_lowering=False)
    io = {}
    io["x1T"] = nc.dram_tensor("x1T", [D, T], F32, kind="ExternalInput")
    io["x2T"] = nc.dram_tensor("x2T", [D, T], F32, kind="ExternalOutput")
    declare_stage_B_io(nc, io)
    for k, v in CONSTS.items():
        io["c_" + k] = nc.dram_tensor("c_" + k, list(v.shape),
                                      BF16 if v.dtype == BF else F32,
                                      kind="ExternalInput")
    ctx = Ctx(nc)
    for s in ("dg", "ds", "mm", "ev", "ea", "cc"):
        ctx.add_sem(s)
    C = alloc_consts(nc, CORE_CONSTS)
    S = alloc_stage_B(nc)
    pb = [nc.alloc_psum_tensor(f"pb{i}", [128, TB], F32) for i in range(8)]
    blk = Blk(ctx)
    load_consts(blk, C, {k: io["c_" + k] for k in CONSTS})
    blk.wait("sync", "ds", ctx.cnt["ds"])
    blk.emit()
    emit_stage_B(ctx, S, C, pb, io)
    return nc



def kernel(**inputs):
    x = np.asarray(inputs["L0_attn_x"], dtype=np.float32)
    n = 8

    ncA = build_program_A()
    mapsA = [stage_A_in_map(x[c], inputs) for c in range(n)]
    resA = run_bass_kernel_spmd(ncA, mapsA, core_ids=list(range(n)))
    x1 = [np.asarray(resA.results[c]["x1T"]).T for c in range(n)]

    ncB = build_program_B()
    mapsB = [stage_B_in_map(c, x1[c], inputs) for c in range(n)]
    resB = run_bass_kernel_spmd(ncB, mapsB, core_ids=list(range(n)))
    x2 = [np.asarray(resB.results[c]["x2T"]).T for c in range(n)]

    ncC = build_program_C()
    mapsC = [stage_C_in_map(x2[c], inputs) for c in range(n)]
    resC = run_bass_kernel_spmd(ncC, mapsC, core_ids=list(range(n)))
    x3 = [np.asarray(resC.results[c]["x3T"]).T for c in range(n)]

    ncD = build_program_D()
    mapsD = [stage_D_in_map(c, x3[c], inputs) for c in range(n)]
    resD = run_bass_kernel_spmd(ncD, mapsD, core_ids=list(range(n)))
    out = np.stack([np.asarray(resD.results[c]["outT"]).T
                    for c in range(n)], axis=0)
    return out.astype(np.float32)


# version 02
# speedup vs baseline: 1.1128x; 1.1128x over previous
import contextlib
import numpy as np
import ml_dtypes
import concourse.bass as bass
import concourse.mybir as mybir
from concourse.bass import IndirectOffsetOnAxis
from concourse.bass_utils import run_bass_kernel_spmd

F32 = mybir.dt.float32
BF16 = mybir.dt.bfloat16
I32 = mybir.dt.int32
AF = mybir.ActivationFunctionType
ALU = mybir.AluOpType
AX = mybir.AxisListType
BF = ml_dtypes.bfloat16

T = 2048
D = 1024
NCH = 8
TB = 512
NTB = T // TB
H = 16
INV_SQRT96 = float(1.0 / np.sqrt(96.0))
SQRT96 = float(np.sqrt(96.0))

ENGINES = ("tensor", "vector", "scalar", "gpsimd", "sync")

import os
_BLK_LIMIT = int(os.environ.get("K_BLK_LIMIT", "0")) or None
_BLK_COUNT = [0]



def host_constants():
    c = {}
    c["ident_bf"] = np.eye(128, dtype=BF)
    c["ident_f32"] = np.eye(128, dtype=np.float32)
    c["ones_col_bf"] = np.ones((128, 1), dtype=BF)
    c["ones128_bf"] = np.ones((128, 128), dtype=BF)
    c["ones_f32_row"] = np.ones((1, 128), dtype=np.float32)
    c["eps_col"] = np.full((128, 1), 1e-5, dtype=np.float32)
    m = np.zeros((128, 4 * TB), dtype=BF)
    for r in range(4):
        j = np.arange(128)[:, None] + 128 * r
        i = np.arange(TB)[None, :]
        m[:, r * TB:(r + 1) * TB] = (j <= i).astype(BF)
    c["diag_masks"] = m
    t = np.arange(T, dtype=np.float32)
    inv = (1.0 / 10000.0 ** (np.arange(0, 32, 2, dtype=np.float32) / 32)) \
        .astype(np.float32)
    freqs = t[:, None] * inv[None, :]
    emb = np.concatenate([freqs, freqs], -1).astype(np.float32)
    scale = (np.cos(emb) + np.sin(emb)).astype(np.float32)
    c["scaleq"] = np.ascontiguousarray(np.tile(scale.T, (4, 1)).astype(np.float32))
    c["scalek"] = np.ascontiguousarray(scale.T.astype(np.float32))
    eh = np.zeros((128, NCH * 16), dtype=BF)
    for cc in range(NCH):
        for p in range(128):
            eh[p, cc * 16 + 2 * cc + (p >= 64)] = 1.0
    c["ehall"] = eh
    c["ones32x16"] = np.ones((32, 16), dtype=BF)
    selh = np.zeros((32, 16 * 16), dtype=BF)
    for h in range(16):
        selh[:, h * 16 + h] = 1.0
    c["selh"] = selh
    c["iota32"] = np.broadcast_to(np.arange(T, dtype=np.float32),
                                  (32, T)).copy()
    c["base32"] = np.array([0.0, 0.02063896507024765, 0.04340777546167374, 0.06852622330188751, 0.09623675793409348, 0.1268068552017212, 0.1605316549539566, 0.19773666560649872, 0.23878106474876404, 0.2840610444545746, 0.33401361107826233, 0.38912108540534973, 0.4499153792858124, 0.5169833302497864, 0.5909722447395325, 0.6725966334342957, 0.7626441121101379, 0.8619838953018188, 0.9715754389762878, 1.0924760103225708, 1.2258524894714355, 1.3729935884475708, 1.535318374633789, 1.7143945693969727, 1.9119499921798706, 2.129892349243164, 2.3703253269195557, 2.6355690956115723, 2.92818546295166, 3.2509994506835938, 3.6071248054504395, 4.000000953674316], dtype=np.float32).reshape(32, 1)
    ec = np.zeros((32, 128), dtype=np.float32)
    es = np.zeros((32, 128), dtype=np.float32)
    for dd in range(128):
        m_ = (dd % 64) // 2
        ec[m_, dd] = 1.0
        es[m_, dd] = 1.0 if dd % 2 == 1 else -1.0
    c["ec"] = ec
    c["es"] = es
    perm = np.zeros((128, 128), dtype=BF)
    for dd in range(0, 128, 2):
        perm[dd + 1, dd] = 1.0
        perm[dd, dd + 1] = 1.0
    c["perm"] = perm
    return c


CONSTS = host_constants()



class Ctx:
    def __init__(self, nc, prefix=""):
        self.nc = nc
        self.prefix = prefix
        self.semh = {}
        self.cnt = {}

    def add_sem(self, name):
        self.semh[name] = self.nc.semaphore(self.prefix + name).__enter__()
        self.cnt[name] = 0

    def bump(self, name, n=1):
        self.cnt[name] += n
        return self.cnt[name]


class Blk:
    def __init__(self, ctx):
        self.ctx = ctx
        self.ops = {e: [] for e in ENGINES}

    def add(self, eng, fn, inc=None, n=1):
        if inc is None:
            self.ops[eng].append(fn)
            return None
        h = self.ctx.semh[inc]
        v = self.ctx.bump(inc, n)
        self.ops[eng].append(lambda e, f=fn, h=h, n=n: f(e).then_inc(h, n))
        return v

    def wait(self, eng, sem, val):
        if val is None or val <= 0:
            return
        assert val <= self.ctx.cnt[sem], (sem, val, self.ctx.cnt[sem])
        h = self.ctx.semh[sem]
        self.ops[eng].append(lambda e, h=h, v=val: e.wait_ge(h, v))

    def vf(self, fn):
        v = self.add("vector", fn, inc="ev")
        self.wait("vector", "ev", v)
        return v

    def dma(self, queue, out_ap, in_ap, sem):
        return self.add(queue, lambda e, o=out_ap, i=in_ap: e.dma_start(o, i),
                        inc=sem, n=16)

    def mm(self, out, lhsT, rhs, start, stop, inc=None):
        return self.add(
            "tensor",
            lambda e, o=out, l=lhsT, r=rhs, s=start, p=stop:
            e.matmul(o, l, r, start=s, stop=p, skip_group_check=True),
            inc=inc)

    def tr(self, out, in_, ident, inc=None):
        return self.add(
            "tensor",
            lambda e, o=out, i=in_, d=ident: e.transpose(o, i, d), inc=inc)

    def emit(self):
        _BLK_COUNT[0] += 1
        if _BLK_LIMIT is not None and _BLK_COUNT[0] > _BLK_LIMIT:
            return
        nc = self.ctx.nc
        ops = self.ops
        with nc.Block() as blk:
            for name in ENGINES:
                if not ops[name]:
                    continue

                def body(eng, lst=tuple(ops[name])):
                    for f in lst:
                        f(eng)
                getattr(blk, name)(body)



CONST_DEFS = {
    "ONESC": ("ones_col_bf", [128, 1], BF16),
    "ONES128": ("ones128_bf", [128, 128], BF16),
    "ONESR": ("ones_f32_row", [1, 128], F32),
    "EPSC": ("eps_col", [128, 1], F32),
    "DMASK": ("diag_masks", [128, 4 * TB], BF16),
    "SCQ": ("scaleq", [128, T], F32),
    "SCK": ("scalek", [32, T], F32),
    "EHALL": ("ehall", [128, NCH * 16], BF16),
    "O32X16": ("ones32x16", [32, 16], BF16),
    "SELH": ("selh", [32, 16 * 16], BF16),
    "IDB": ("ident_bf", [128, 128], BF16),
    "IDF": ("ident_f32", [128, 128], F32),
}
CORE_CONSTS = ["ONESC", "ONES128", "ONESR", "EPSC", "IDB", "IDF"]
A_CONSTS = CORE_CONSTS + ["DMASK", "SCQ", "SCK", "EHALL", "O32X16", "SELH"]


def alloc_consts(nc, keys=None):
    sb = nc.alloc_sbuf_tensor
    C = {}
    for k in (keys or list(CONST_DEFS)):
        ion, shape, dt = CONST_DEFS[k]
        C[k] = sb("K_" + k, shape, dt)
    return C


def load_consts(blk, C, cio):
    for sbn in C:
        ion = CONST_DEFS[sbn][0]
        blk.dma("sync", C[sbn][:, :], cio[ion][:, :], "ds")


def emit_ln(ctx, S, C, pb, src, dst, ln_w_col=None, ln_b_col=None,
            dst_fn=None, fp32_cb=None):
    XB, SQ, MU, RSTD = S["XB"], S["SQ"], S["MU"], S["RSTD"]
    TF1, TF2 = S["TF1"], S["TF2"]
    blk = Blk(ctx)
    cast_v = []
    for c in range(NCH):
        v = blk.add("vector", lambda e, c=c: e.tensor_copy(
            XB[:, c * TB:(c + 1) * TB], src[:, c * TB:(c + 1) * TB]), inc="ev")
        cast_v.append(v)
    sq_v = []
    for c in range(NCH):
        blk.wait("scalar", "ev", cast_v[c])
        v = blk.add("scalar", lambda e, c=c: e.activation(
            SQ[:, c * TB:(c + 1) * TB], XB[:, c * TB:(c + 1) * TB],
            AF.Square), inc="ea")
        sq_v.append(v)
    for c in range(NCH):
        blk.wait("tensor", "ev", cast_v[c])
        blk.mm(pb[0][:, :], C["ONES128"][:, :], XB[:, c * TB:(c + 1) * TB],
               start=(c == 0), stop=(c == NCH - 1))
    v = None
    for c in range(NCH):
        blk.wait("tensor", "ea", sq_v[c])
        v = blk.mm(pb[1][:, :], C["ONES128"][:, :], SQ[:, c * TB:(c + 1) * TB],
                   start=(c == 0), stop=(c == NCH - 1),
                   inc=("mm" if c == NCH - 1 else None))
    blk.wait("vector", "mm", v)
    blk.add("vector", lambda e: e.tensor_scalar(
        MU[:, :], pb[0][:, :], 1.0 / D, None, ALU.mult))
    blk.add("vector", lambda e: e.tensor_mul(TF1[:, :], MU[:, :], MU[:, :]))
    v = blk.add("vector", lambda e: e.scalar_tensor_tensor(
        TF2[:, :], pb[1][:, :], 1.0 / D, TF1[:, :], ALU.mult, ALU.subtract),
        inc="ev")
    blk.wait("scalar", "ev", v)
    v = blk.add("scalar", lambda e: e.activation(
        TF1[:, :], TF2[:, :], AF.Sqrt, bias=C["EPSC"][:, 0:1]), inc="ea")
    blk.wait("vector", "ea", v)
    blk.add("vector", lambda e: e.reciprocal(RSTD[:, :], TF1[:, :]))
    out_v = None
    cb_mm = None
    for c in range(NCH):
        dstap = dst_fn(c) if dst_fn is not None else \
            dst[:, c * TB:(c + 1) * TB]
        blk.add("vector", lambda e, c=c: e.tensor_sub(
            TF2[:, :], src[:, c * TB:(c + 1) * TB], MU[:, :]))
        if ln_w_col is None:
            if fp32_cb is None:
                out_v = blk.add("vector", lambda e, c=c, dstap=dstap:
                                e.tensor_mul(dstap, TF2[:, :], RSTD[:, :]),
                                inc=("ev" if c == NCH - 1 else None))
            else:
                if c > 0 and cb_mm is not None:
                    blk.wait("vector", "mm", cb_mm)
                fv = blk.add("vector", lambda e, c=c: e.tensor_mul(
                    TF1[:, :], TF2[:, :], RSTD[:, :]), inc="ev")
                out_v = blk.add("vector", lambda e, c=c, dstap=dstap:
                                e.tensor_copy(dstap, TF1[:, :]),
                                inc=("ev" if c == NCH - 1 else None))
                cb_mm = fp32_cb(blk, c, TF1, fv)
        else:
            blk.add("vector", lambda e, c=c: e.tensor_mul(
                TF2[:, :], TF2[:, :], RSTD[:, :]))
            out_v = blk.add("vector", lambda e, c=c, dstap=dstap:
                            e.tensor_scalar(dstap, TF2[:, :],
                                            ln_w_col[:, c:c + 1],
                                            ln_b_col[:, c:c + 1],
                                            ALU.mult, ALU.add),
                            inc=("ev" if c == NCH - 1 else None))
    blk.emit()
    return out_v


def stream_mm(ctx, blk, specs, ws, n_kc, moving, psum_of, evac,
              stream_engine="gpsimd"):
    nb = len(specs)
    mm_tgt = [None] * nb
    ev_tgt = [None] * nb
    for i in range(nb):
        buf = ws[i % 2]
        if i >= 2:
            blk.wait(stream_engine, "mm", mm_tgt[i - 2])
        ld = blk.dma(stream_engine,
                     buf[:, 0:n_kc * 128].rearrange("p (c n) -> p c n",
                                                    c=n_kc),
                     specs[i], "dg")
        blk.wait("tensor", "dg", ld)
        if i >= 4:
            blk.wait("tensor", ev_tgt[i - 4][0], ev_tgt[i - 4][1])
        pbank = psum_of(i)
        for kc in range(n_kc):
            last = kc == n_kc - 1
            mm_tgt[i] = blk.mm(pbank, buf[:, kc * 128:(kc + 1) * 128],
                               moving(i, kc), start=(kc == 0), stop=last,
                               inc=("mm" if last else None))
        ev_tgt[i] = evac(blk, i, mm_tgt[i])
    return mm_tgt, ev_tgt



def alloc_stage_A(nc):
    sb = nc.alloc_sbuf_tensor
    S = {}
    S["XF"] = sb("A_XF", [128, NCH * TB], F32)
    S["XLN"] = sb("A_XLN", [128, NCH * TB], BF16)
    S["XB"] = sb("A_XB", [128, NCH * TB], BF16)
    S["SQ"] = sb("A_SQ", [128, NCH * TB], BF16)
    S["MU"] = sb("A_MU", [128, TB], F32)
    S["RSTD"] = sb("A_RSTD", [128, TB], F32)
    S["TF1"] = sb("A_TF1", [128, TB], F32)
    S["TF2"] = sb("A_TF2", [128, TB], F32)
    S["TF3"] = None
    S["T16"] = sb("A_T16", [16, 1], F32)
    S["TF3"] = S["MU"]
    S["KC"] = sb("A_KC", [128, NCH * T], BF16)
    S["VT"] = sb("A_VT", [128, 16 * 16 * 80], BF16)
    S["KRA"] = sb("A_KRA", [97, T], BF16)
    S["QC"] = sb("A_QC", [128, NCH * TB], BF16)
    S["QRA"] = sb("A_QRA", [97, H * TB], BF16)
    S["SQR"] = sb("A_SQR", [32, 2 * TB], BF16)
    S["KN2H"] = sb("A_KN2H", [16, T], F32)
    S["KMS16"] = sb("A_KMS16", [16, 1], F32)
    S["QN16"] = sb("A_QN16", [16, TB], F32)
    S["B16"] = sb("A_B16", [16, TB], F32)
    S["RD"] = [sb(f"A_RD{i}", [1, TB], F32) for i in range(2)]
    S["RDB"] = [sb(f"A_RDB{i}", [64, TB], F32) for i in range(2)]
    S["PT"] = [sb(f"A_PT{i}", [128, TB], BF16) for i in range(2)]
    S["PTM"] = [sb(f"A_PTM{i}", [128, TB], BF16) for i in range(2)]
    S["AO"] = S["XB"]
    S["WS"] = [sb(f"A_WS{i}", [128, NCH * 128], BF16) for i in range(2)]
    S["WT"] = None
    S["WM"] = [sb(f"A_WM{i}", [128, D], BF16) for i in range(2)]
    S["WT"] = S["WM"]
    return S


def emit_compose_mats(ctx, S, C, pb, io):
    nc = ctx.nc
    wdq, wuq, wqr = io["L0_attn_W_DQ"], io["L0_attn_W_UQ"], io["L0_attn_W_QR"]
    wdkv, wuk, wuv = io["L0_attn_W_DKV"], io["L0_attn_W_UK"], io["L0_attn_W_UV"]
    wdqt, wdkvt = io["scr_WDQT"], io["scr_WDKVT"]
    m1, m2, m3, m4 = io["scr_M1"], io["scr_M2"], io["scr_M3"], io["scr_M4"]
    WS, WT = S["WS"], S["WT"]

    blk = Blk(ctx)
    st_t = {}
    ev_hist = {}
    for i in range(34):
        buf = WS[i % 2]
        obuf = WT[i % 2]
        if i >= 2:
            blk.wait("gpsimd", "mm", st_t[i - 2][0])
        if i < 32:
            src = wdq[:, i * 128:(i + 1) * 128].rearrange(
                "(c p) n -> p c n", c=NCH)
        else:
            src = wdkv[:, (i - 32) * 128:(i - 31) * 128].rearrange(
                "(c p) n -> p c n", c=NCH)
        ld = blk.dma("gpsimd", buf[:, :].rearrange("p (c n) -> p c n", c=NCH),
                     src, "dg")
        blk.wait("tensor", "dg", ld)
        if i >= 2:
            blk.wait("vector", "ds", st_t[i - 2][2])
        last_mm = None
        vv = None
        for c in range(NCH):
            pbb = pb[4 + c % 2]
            prev = ev_hist.get((i, c - 2)) or ev_hist.get((i - 1, NCH - 2 + c))
            if prev:
                blk.wait("tensor", "ev", prev)
            last_mm = blk.tr(pbb.bitcast(BF16)[:, 0:128],
                             buf[:, c * 128:(c + 1) * 128], C["IDB"][:, :],
                             inc="mm")
            blk.wait("vector", "mm", last_mm)
            vv = blk.add("vector", lambda e, obuf=obuf, c=c, pbb=pbb:
                         e.tensor_copy(obuf[:, c * 128:(c + 1) * 128],
                                       pbb.bitcast(BF16)[:, 0:128]), inc="ev")
            ev_hist[(i, c)] = vv
        blk.wait("sync", "ev", vv)
        if i < 32:
            dst = wdqt[i * 128:(i + 1) * 128, :]
        else:
            dst = wdkvt[(i - 32) * 128:(i - 31) * 128, :]
        sv = blk.dma("sync", dst, obuf[:, :], "ds")
        st_t[i] = (last_mm, vv, sv)
    blk.emit()

    for eb in range(3):
        blk = Blk(ctx)
        blk.wait("gpsimd", "ds", ctx.cnt["ds"])
        mm_last = {}
        for fc in range(32):
            buf, mbuf = WS[fc % 2], S["WM"][fc % 2]
            if fc >= 2:
                blk.wait("gpsimd", "mm", mm_last[fc - 2])
            ld1 = blk.dma("gpsimd",
                          buf[:, :].rearrange("p (c n) -> p c n", c=NCH),
                          wdqt[fc * 128:(fc + 1) * 128, :].rearrange(
                              "p (c n) -> p c n", c=NCH), "dg")
            if eb < 2:
                msrc = wuq[fc * 128:(fc + 1) * 128, eb * 512:(eb + 1) * 512]
            else:
                msrc = wqr[fc * 128:(fc + 1) * 128, :]
            ld2 = blk.dma("gpsimd", mbuf[:, 0:512], msrc, "dg")
            blk.wait("tensor", "dg", max(ld1, ld2))
            for dch in range(NCH):
                mm_last[fc] = blk.mm(
                    pb[dch][:, :], buf[:, dch * 128:(dch + 1) * 128],
                    mbuf[:, 0:512], start=(fc == 0), stop=(fc == 31),
                    inc=("mm" if dch == NCH - 1 else None))
        XB = S["XB"]
        for dch in range(NCH):
            blk.wait("vector", "mm", mm_last[31])
            v = blk.add("vector", lambda e, dch=dch: e.tensor_copy(
                XB[:, dch * 512:(dch + 1) * 512], pb[dch][:, :]), inc="ev")
            blk.wait("sync", "ev", v)
            if eb < 2:
                dst = m1[dch * 128:(dch + 1) * 128, eb * 512:(eb + 1) * 512]
            else:
                dst = m2[dch * 128:(dch + 1) * 128, :]
            blk.dma("sync", dst, XB[:, dch * 512:(dch + 1) * 512], "ds")
        blk.wait("sync", "ds", ctx.cnt["ds"])
        blk.emit()

    blk = Blk(ctx)
    ld1 = blk.dma("gpsimd", WS[0][:, 0:D].rearrange("p (c n) -> p c n", c=NCH),
                  wdkvt[0:128, :].rearrange("p (c n) -> p c n", c=NCH), "dg")
    ld2 = blk.dma("gpsimd", WS[1][:, 0:D].rearrange("p (c n) -> p c n", c=NCH),
                  wdkvt[128:256, :].rearrange("p (c n) -> p c n", c=NCH), "dg")
    XB = S["XB"]
    prev_ev = {}
    st_x = {}
    wm_last = {}
    wm2 = [S["WM"][0][:, 0:1024], S["WM"][1][:, 0:1024]]
    for mi, (wsrc, mdst) in enumerate([(wuk, m3), (wuv, m4)]):
        for eb in range(2):
            pi = mi * 2 + eb
            mb = wm2[pi % 2]
            if pi >= 2:
                blk.wait("gpsimd", "mm", wm_last[pi - 2])
            ld3 = blk.dma("gpsimd",
                          mb.rearrange("p (c n) -> p c n", c=2),
                          wsrc[:, eb * 512:(eb + 1) * 512].rearrange(
                              "(c p) n -> p c n", c=2), "dg")
            blk.wait("tensor", "dg", max(ld1, ld2, ld3))
            for dch in range(NCH):
                b = dch % 4
                if b in prev_ev:
                    blk.wait("tensor", "ev", prev_ev[b])
                v = None
                for k2 in range(2):
                    v = blk.mm(pb[b][:, :],
                               WS[k2][:, dch * 128:(dch + 1) * 128],
                               mb[:, k2 * 512:(k2 + 1) * 512],
                               start=(k2 == 0), stop=(k2 == 1),
                               inc=("mm" if k2 == 1 else None))
                wm_last[pi] = v
                blk.wait("vector", "mm", v)
                if dch in st_x:
                    blk.wait("vector", "ds", st_x[dch])
                v = blk.add("vector", lambda e, dch=dch, b=b: e.tensor_copy(
                    XB[:, dch * 512:(dch + 1) * 512], pb[b][:, :]), inc="ev")
                prev_ev[b] = v
                blk.wait("sync", "ev", v)
                st_x[dch] = blk.dma(
                    "sync",
                    mdst[dch * 128:(dch + 1) * 128, eb * 512:(eb + 1) * 512],
                    XB[:, dch * 512:(dch + 1) * 512], "ds")
    blk.wait("sync", "ds", ctx.cnt["ds"])
    blk.emit()


def emit_stage_A(ctx, S, C, pb, io):
    nc = ctx.nc
    xT = io["xT"]

    emit_compose_mats(ctx, S, C, pb, io)

    blk = Blk(ctx)
    blk.add("gpsimd", lambda e: e.memset(S["KRA"][32:33, :], 1.0))
    blk.add("gpsimd", lambda e: e.memset(S["KRA"][96:97, :], 1.0))
    blk.add("gpsimd", lambda e: e.memset(
        S["VT"][:, :].rearrange("p (a b) -> p a b", b=80)[:, :, 64:65], 1.0))
    blk.dma("gpsimd", S["XF"][:, :].rearrange("p (c n) -> p c n", c=NCH),
            xT[:, 0:TB].rearrange("(c p) n -> p c n", c=NCH), "dg")
    blk.wait("gpsimd", "dg", ctx.cnt["dg"])
    blk.emit()

    for tb in range(NTB):
        _stage_A_tb(ctx, S, C, pb, io, tb)


def _stage_A_tb(ctx, S, C, pb, io, tb):
    xT, x1T = io["xT"], io["x1T"]
    m1, m2, m3, m4 = io["scr_M1"], io["scr_M2"], io["scr_M3"], io["scr_M4"]
    wkr, wo = io["L0_attn_W_KR"], io["L0_attn_W_O"]
    XF, XLN, SQ = S["XF"], S["XLN"], S["SQ"]
    KC, VT, KRA, QC, QRA, SQR = (S["KC"], S["VT"], S["KRA"], S["QC"],
                                 S["QRA"], S["SQR"])

    emit_ln(ctx, S, C, pb, XF, XLN)

    blk = Blk(ctx)
    m3_specs = [m3[:, oc * 128:(oc + 1) * 128].rearrange("(c p) n -> p c n",
                                                         c=NCH)
                for oc in range(NCH)]

    def kc_evac(blk, i, mv):
        blk.wait("vector", "mm", mv)
        return ("ev", blk.add("vector", lambda e, i=i: e.tensor_copy(
            KC[:, i * T + tb * TB:i * T + tb * TB + TB], pb[i % 4][:, :]),
            inc="ev"))

    stream_mm(ctx, blk, m3_specs, S["WS"], NCH,
              moving=lambda i, kc: XLN[:, kc * TB:(kc + 1) * TB],
              psum_of=lambda i: pb[i % 4][:, :], evac=kc_evac)
    ld = blk.dma("gpsimd", S["WM"][0][:, 0:NCH * 32].rearrange(
        "p (c n) -> p c n", c=NCH),
        wkr[:, :].rearrange("(c p) n -> p c n", c=NCH), "dg")
    blk.wait("tensor", "dg", ld)
    for kc in range(NCH):
        v = blk.mm(pb[4][0:32, :], S["WM"][0][:, kc * 32:(kc + 1) * 32],
                   XLN[:, kc * TB:(kc + 1) * TB],
                   start=(kc == 0), stop=(kc == NCH - 1),
                   inc=("mm" if kc == NCH - 1 else None))
    blk.wait("vector", "mm", v)
    blk.add("vector", lambda e: e.tensor_mul(
        KRA[0:32, tb * TB:(tb + 1) * TB], pb[4][0:32, :],
        C["SCK"][:, tb * TB:(tb + 1) * TB]), inc="ev")
    blk.add("vector", lambda e: e.tensor_mul(
        KRA[64:96, tb * TB:(tb + 1) * TB], pb[4][0:32, :],
        C["SCK"][:, tb * TB:(tb + 1) * TB]), inc="ev")
    blk.emit()

    blk = Blk(ctx)
    m4ld = {}
    for kc in range(NCH):
        mb = S["WM"][kc % 2]
        if kc >= 2:
            blk.wait("gpsimd", "mm", m4ld[kc - 2])
        ld = blk.dma("gpsimd", mb[:, 0:D], m4[kc * 128:(kc + 1) * 128, :],
                     "dg")
        blk.wait("tensor", "dg", ld)
        lastmm = None
        for tt in range(4):
            for nv in range(2):
                i = tt * 2 + nv
                last_of_kc = (tt == 3 and nv == 1)
                lastmm = blk.mm(
                    pb[i][:, :],
                    XLN[:, kc * TB + tt * 128:kc * TB + tt * 128 + 128],
                    mb[:, nv * 512:(nv + 1) * 512],
                    start=(kc == 0), stop=(kc == NCH - 1),
                    inc=("mm" if (kc == NCH - 1 or last_of_kc) else None))
                if kc == NCH - 1:
                    kt = tb * 4 + tt
                    blk.wait("vector", "mm", lastmm)
                    dst = VT[:, kt * 1280 + nv * 640:kt * 1280 + (nv + 1) * 640] \
                        .rearrange("p (h n) -> p h n", h=8)[:, :, 0:64]
                    src = pb[i][:, :].rearrange("p (h n) -> p h n", h=8)
                    blk.add("vector",
                            lambda e, dst=dst, src=src:
                            e.tensor_copy(dst, src), inc="ev")
        m4ld[kc] = lastmm
    blk.emit()

    blk = Blk(ctx)
    sqv = []
    for oc in range(NCH):
        v = blk.add("scalar", lambda e, oc=oc: e.activation(
            SQ[:, oc * TB:(oc + 1) * TB],
            KC[:, oc * T + tb * TB:oc * T + tb * TB + TB], AF.Square),
            inc="ea")
        sqv.append(v)
    vkr = blk.add("scalar", lambda e: e.activation(
        SQR[0:32, 0:TB], KRA[0:32, tb * TB:(tb + 1) * TB], AF.Square),
        inc="ea")
    for oc in range(NCH):
        blk.wait("tensor", "ea", sqv[oc])
        blk.mm(pb[7][0:16, :], C["EHALL"][:, oc * 16:(oc + 1) * 16],
               SQ[:, oc * TB:(oc + 1) * TB], start=(oc == 0), stop=False)
    blk.wait("tensor", "ea", vkr)
    v = blk.mm(pb[7][0:16, :], C["O32X16"][:, :], SQR[0:32, 0:TB],
               start=False, stop=True, inc="mm")
    blk.wait("vector", "mm", v)
    v = blk.add("vector", lambda e: e.tensor_copy(
        S["KN2H"][:, tb * TB:(tb + 1) * TB], pb[7][0:16, :]), inc="ev")
    blk.wait("vector", "ev", v)
    v = blk.add("vector", lambda e: e.reduce_max(
        S["T16"][:, :], S["KN2H"][:, 0:(tb + 1) * TB], axis=AX.X), inc="ev")
    blk.wait("scalar", "ev", v)
    blk.add("scalar", lambda e: e.activation(S["KMS16"][:, :], S["T16"][:, :],
                                             AF.Sqrt), inc="ea")
    blk.emit()

    blk = Blk(ctx)
    q_specs = [m1[:, oc * 128:(oc + 1) * 128].rearrange("(c p) n -> p c n",
                                                        c=NCH)
               for oc in range(NCH)] + \
              [m2[:, ocr * 128:(ocr + 1) * 128].rearrange("(c p) n -> p c n",
                                                          c=NCH)
               for ocr in range(4)]

    def q_evac(blk, i, mv):
        blk.wait("vector", "mm", mv)
        if i < NCH:
            return ("ev", blk.add("vector", lambda e, i=i: e.tensor_copy(
                QC[:, i * TB:(i + 1) * TB], pb[i % 4][:, :]), inc="ev"))
        ocr = i - NCH
        last = None
        for j in range(4):
            h = ocr * 4 + j
            blk.add("vector", lambda e, h=h, j=j, i=i: e.tensor_mul(
                QRA[0:32, h * TB:(h + 1) * TB],
                pb[i % 4][32 * j:32 * (j + 1), :],
                C["SCQ"][32 * j:32 * (j + 1), tb * TB:(tb + 1) * TB]))
            last = blk.add("vector", lambda e, h=h, j=j, i=i: e.tensor_mul(
                QRA[64:96, h * TB:(h + 1) * TB],
                pb[i % 4][32 * j:32 * (j + 1), :],
                C["SCQ"][32 * j:32 * (j + 1), tb * TB:(tb + 1) * TB]),
                inc=("ev" if j == 3 else None))
        return ("ev", last)

    stream_mm(ctx, blk, q_specs, S["WS"], NCH,
              moving=lambda i, kc: XLN[:, kc * TB:(kc + 1) * TB],
              psum_of=lambda i: pb[i % 4][:, :], evac=q_evac)
    blk.emit()

    blk = Blk(ctx)
    sqv = []
    for oc in range(NCH):
        v = blk.add("scalar", lambda e, oc=oc: e.activation(
            SQ[:, oc * TB:(oc + 1) * TB], QC[:, oc * TB:(oc + 1) * TB],
            AF.Square), inc="ea")
        sqv.append(v)
    for oc in range(NCH):
        blk.wait("tensor", "ea", sqv[oc])
        blk.mm(pb[4][0:16, :], C["EHALL"][:, oc * 16:(oc + 1) * 16],
               SQ[:, oc * TB:(oc + 1) * TB], start=(oc == 0), stop=False)
    v = None
    qr_mm = {}
    for h in range(H):
        if h >= 2:
            blk.wait("scalar", "mm", qr_mm[h - 2])
        vh = blk.add("scalar", lambda e, h=h: e.activation(
            SQR[0:32, (h % 2) * TB:(h % 2 + 1) * TB],
            QRA[0:32, h * TB:(h + 1) * TB], AF.Square), inc="ea")
        blk.wait("tensor", "ea", vh)
        v = blk.mm(pb[4][0:16, :], C["SELH"][:, h * 16:(h + 1) * 16],
                   SQR[0:32, (h % 2) * TB:(h % 2 + 1) * TB],
                   start=False, stop=(h == H - 1), inc="mm")
        qr_mm[h] = v
    blk.wait("scalar", "mm", v)
    v = blk.add("scalar", lambda e: e.activation(
        S["QN16"][:, :], pb[4][0:16, :], AF.Sqrt), inc="ea")
    blk.wait("vector", "ea", v)
    v = blk.add("vector", lambda e: e.tensor_scalar(
        S["B16"][:, :], S["QN16"][:, :], S["KMS16"][:, 0:1], -SQRT96,
        ALU.mult, ALU.mult), inc="ev")
    blk.wait("gpsimd", "ev", v)
    for h in range(H):
        blk.dma("gpsimd", QRA[32:33, h * TB:(h + 1) * TB],
                S["B16"][h:h + 1, :], "dg")
        blk.dma("gpsimd", QRA[96:97, h * TB:(h + 1) * TB],
                S["B16"][h:h + 1, :], "dg")
    blk.wait("tensor", "dg", ctx.cnt["dg"])
    blk.emit()

    nkt = 4 * (tb + 1)
    blk = Blk(ctx)
    AO = S["AO"]
    ea_t, prod_t, mm_t, pv_t, pv_in = {}, {}, {}, {}, {}
    ao_t, cp_t = {}, {}
    diag_gs = []

    _NOPV = os.environ.get("K_ATT_NOPV", "0") == "1"
    _NOMM2 = os.environ.get("K_ATT_NOMM2", "0") == "1"

    def emit_pv(h, kt, g):
        if _NOPV:
            pv_t[g] = ctx.cnt["mm"]
            return
        sem, tgt = prod_t[g]
        blk.wait("tensor", sem, tgt)
        if kt == 0 and h >= 2:
            blk.wait("tensor", "ev", ao_t[h - 2])
        pv_t[g] = blk.mm(pb[4 + h % 2][0:65, :],
                         VT[:, (kt * 16 + h) * 80:(kt * 16 + h) * 80 + 65],
                         pv_in[g][:, :], start=(kt == 0),
                         stop=(kt == nkt - 1), inc="mm")

    g = -1
    _H = int(os.environ.get("K_ATT_H", "0")) or H
    _EPI = os.environ.get("K_ATT_EPI", "1") == "1"
    for h in range(_H):
        hc, hr = h // 2, (h % 2) * 64
        for kt in range(nkt):
            g += 1
            sb_ = pb[g % 2]
            if g >= 2:
                blk.wait("tensor", "ea", ea_t[g - 2])
            blk.mm(sb_[:, :],
                   KC[hr:hr + 64, hc * T + kt * 128:hc * T + (kt + 1) * 128],
                   QC[hr:hr + 64, hc * TB:(hc + 1) * TB],
                   start=True, stop=_NOMM2)
            if _NOMM2:
                mm_t[g] = blk.add(
                    "tensor", lambda e: e.ldweights(C["IDB"][:, :]), inc="mm")
            else:
                qb_ = (h % 2) * 64
                mm_t[g] = blk.mm(sb_[:, :],
                                 KRA[qb_:qb_ + 33, kt * 128:(kt + 1) * 128],
                                 QRA[qb_:qb_ + 33, h * TB:(h + 1) * TB],
                                 start=False, stop=True, inc="mm")
            diag = kt >= 4 * tb
            pt = S["PT"][g % 2]
            blk.wait("scalar", "mm", mm_t[g])
            if g >= 2:
                blk.wait("scalar", "mm", pv_t[g - 2])
            ea_t[g] = blk.add("scalar", lambda e, sb_=sb_, pt=pt: e.activation(
                pt[:, :], sb_[:, :], AF.Exp, scale=INV_SQRT96), inc="ea")
            if diag:
                nd = len(diag_gs)
                ptm = S["PTM"][nd % 2]
                blk.wait("vector", "ea", ea_t[g])
                if nd >= 2:
                    blk.wait("vector", "mm", pv_t[diag_gs[nd - 2]])
                r = kt - 4 * tb
                vv = blk.add("vector", lambda e, pt=pt, ptm=ptm, r=r:
                             e.tensor_mul(ptm[:, :], pt[:, :],
                                          C["DMASK"][:, r * TB:(r + 1) * TB]),
                             inc="ev")
                prod_t[g] = ("ev", vv)
                pv_in[g] = ptm
                diag_gs.append(g)
            else:
                prod_t[g] = ("ea", ea_t[g])
                pv_in[g] = pt
            if kt >= 1:
                emit_pv(h, kt - 1, g - 1)
        emit_pv(h, nkt - 1, g)
        if not _EPI:
            continue
        ob = pb[4 + h % 2]
        rb = pb[6 + h % 2]
        blk.wait("vector", "mm", pv_t[g])
        rdv = blk.add("vector", lambda e, ob=ob, h=h: e.reciprocal(
            S["RD"][h % 2][:, :], ob[64:65, :]), inc="ev")
        blk.wait("tensor", "ev", rdv)
        if h >= 2:
            blk.wait("tensor", "ev", cp_t[h - 2])
        rbv = blk.mm(rb[0:64, :], C["ONESR"][0:1, 0:64],
                     S["RD"][h % 2][0:1, :], start=True, stop=True, inc="mm")
        blk.wait("vector", "mm", rbv)
        cp_t[h] = blk.add("vector", lambda e, rb=rb, h=h: e.tensor_copy(
            S["RDB"][h % 2][:, :], rb[0:64, :]), inc="ev")
        ao_t[h] = blk.add("vector", lambda e, ob=ob, hc=hc, hr=hr, h=h:
                          e.tensor_mul(AO[hr:hr + 64, hc * TB:(hc + 1) * TB],
                                       ob[0:64, :], S["RDB"][h % 2][:, :]),
                          inc="ev")
    blk.emit()

    blk = Blk(ctx)
    wo_specs = [wo[:, oc * 128:(oc + 1) * 128].rearrange("(c p) n -> p c n",
                                                         c=NCH)
                for oc in range(NCH)]
    st_t = {}

    def wo_evac(blk, i, mv):
        blk.wait("vector", "mm", mv)
        if i >= 2:
            blk.wait("vector", "ds", st_t[i - 2])
        tfo = S["TF1"] if i % 2 == 0 else S["TF3"]
        v = blk.add("vector", lambda e, i=i, tfo=tfo: e.tensor_add(
            tfo[:, :], pb[i % 4][:, :], XF[:, i * TB:(i + 1) * TB]), inc="ev")
        blk.wait("sync", "ev", v)
        st_t[i] = blk.dma(
            "sync", x1T[i * 128:(i + 1) * 128, tb * TB:(tb + 1) * TB],
            tfo[:, :], "ds")
        return ("ev", v)

    stream_mm(ctx, blk, wo_specs, S["WS"], NCH,
              moving=lambda i, kc: AO[:, kc * TB:(kc + 1) * TB],
              psum_of=lambda i: pb[i % 4][:, :], evac=wo_evac)
    blk.emit()

    blk = Blk(ctx)
    if tb + 1 < NTB:
        blk.dma("gpsimd", S["XF"][:, :].rearrange("p (c n) -> p c n", c=NCH),
                xT[:, (tb + 1) * TB:(tb + 2) * TB].rearrange(
                    "(c p) n -> p c n", c=NCH), "dg")
        blk.wait("gpsimd", "dg", ctx.cnt["dg"])
    else:
        blk.wait("sync", "ds", ctx.cnt["ds"])
        blk.wait("gpsimd", "dg", ctx.cnt["dg"])
    blk.emit()



CAP = 2560
BIG = float(2 ** 26)
RG8 = [[0, 1, 2, 3, 4, 5, 6, 7]]


def alloc_stage_B(nc):
    sb = nc.alloc_sbuf_tensor
    S = {}
    A2 = sb("B_A2", [128, 2 * NCH * TB], F32)
    S["XF"] = A2.ap()[:, 0:NCH * TB]
    S["XB"] = A2.ap().bitcast(BF16)[:, 2 * NCH * TB:3 * NCH * TB]
    S["SQ"] = A2.ap().bitcast(BF16)[:, 3 * NCH * TB:4 * NCH * TB]
    S["HT"] = A2.ap().bitcast(BF16)[:, 0:32 * TB]
    S["MU"] = sb("B_MU", [128, TB], F32)
    S["RSTD"] = sb("B_RSTD", [128, TB], F32)
    S["TF1"] = sb("B_TF1", [128, TB], F32)
    S["TF2"] = sb("B_TF2", [128, TB], F32)
    A1 = sb("B_A1", [128, NCH * CAP], BF16)
    S["XLNB"] = A1.ap()[:, 0:NCH * T]
    S["BUF"] = A1.ap()
    S["TOK"] = sb("B_TOK", [128, D], BF16)
    S["RWF"] = sb("B_RWF", [128, NCH * 8], F32)
    S["RBROW"] = sb("B_RBROW", [1, 8], BF16)
    S["LGT"] = sb("B_LGT", [128, 16 * 8], F32)
    S["GT"] = sb("B_GT", [128, 16 * 8], F32)
    S["MK1"] = sb("B_MK1", [128, 16 * 8], F32)
    S["MK2"] = sb("B_MK2", [128, 16 * 8], F32)
    S["MKB"] = sb("B_MKB", [128, 16 * 8], BF16)
    S["T8A"] = sb("B_T8A", [128, 8], F32)
    S["T8B"] = sb("B_T8B", [128, 8], F32)
    S["T8C"] = sb("B_T8C", [128, 8], F32)
    S["T8D"] = sb("B_T8D", [128, 8], F32)
    S["TC1"] = sb("B_TC1", [128, 1], F32)
    S["TC2"] = sb("B_TC2", [128, 1], F32)
    S["MKE"] = sb("B_MKE", [8, T], F32)
    S["CUM"] = sb("B_CUM", [8, T], F32)
    S["GCUM"] = sb("B_GCUM", [8, T], F32)
    S["DSTF"] = sb("B_DSTF", [8, T], F32)
    S["CNT8"] = sb("B_CNT8", [8, 1], F32)
    S["CNT64"] = sb("B_CNT64", [64, 1], F32)
    S["OFFP"] = sb("B_OFFP", [8, 1], F32)
    S["V8A"] = sb("B_V8A", [8, 4], F32)
    S["ROW8"] = sb("B_ROW8", [1, 16], F32)
    S["GOB"] = sb("B_GOB", [128, 8], F32)
    S["DELB"] = sb("B_DELB", [128, 8], F32)
    S["IDF1"] = sb("B_IDF1", [128, 20], F32)
    S["IDF2"] = sb("B_IDF2", [128, 20], F32)
    S["SIOTA"] = sb("B_SIOTA", [128, 20], F32)
    S["IDXG"] = sb("B_IDXG", [128, 20], I32)
    S["IDXL1"] = sb("B_IDXL1", [128, 16], I32)
    S["IDXL2"] = sb("B_IDXL2", [128, 16], I32)
    S["IDX1"] = sb("B_IDX1", [128, 16], I32)
    S["IDX2"] = sb("B_IDX2", [128, 16], I32)
    S["W1TOK"] = sb("B_W1TOK", [128, 16], F32)
    S["W2TOK"] = sb("B_W2TOK", [128, 16], F32)
    S["YF"] = sb("B_YF", [128, NCH * TB], BF16)
    S["B1S"] = sb("B_B1S", [128, 32], F32)
    S["B2S"] = sb("B_B2S", [128, 8], F32)
    S["BW"] = [sb(f"B_BW{i}", [128, 32 * 128], BF16) for i in range(2)]
    S["G1"] = sb("B_G1", [128, D], BF16)
    S["G2"] = sb("B_G2", [128, D], BF16)
    S["GS"] = sb("B_GS", [128, D], F32)
    S["XR"] = sb("B_XR", [128, NCH * 128], F32)
    S["OT"] = [sb(f"B_OT{i}", [128, 128], F32) for i in range(2)]
    S["SEL8F"] = sb("B_SEL8F", [64, 8], F32)
    S["SELLO"] = sb("B_SELLO", [64, 8], F32)
    S["SELPF"] = sb("B_SELPF", [64, 8], F32)
    S["TRI8S"] = sb("B_TRI8S", [8, 8], F32)
    S["SH8"] = sb("B_SH8", [8, 8], F32)
    S["R4096"] = sb("B_R4096", [8, 1], F32)
    S["RBM1"] = sb("B_RBM1", [128, 8], F32)
    return S


def emit_stage_B(ctx, S, C, pb, io):
    nc = ctx.nc
    x1T, x2T = io["x1T"], io["x2T"]
    rw, rb = io["L0_ffn_router_w"], io["L0_ffn_router_b"]
    w1e, w2e, b1e, b2e = io["w1_e"], io["w2_e"], io["b1_e"], io["b2_e"]
    myxln, mycnt = io["scr_myxln"], io["scr_mycnt"]
    cnt64d = io["scr_cnt64"]
    sendb, sball = io["scr_sendb"], io["scr_sball"]
    yd, yall = io["scr_yd"], io["scr_yall"]

    blk = Blk(ctx)
    blk.dma("sync", S["RWF"][:, :].rearrange("p (c n) -> p c n", c=NCH),
            rw[:, :].rearrange("(c p) n -> p c n", c=NCH), "ds")
    blk.dma("sync", S["SEL8F"][:, :], io["sel8f"][:, :], "ds")
    blk.dma("sync", S["SELLO"][:, :], io["sello"][:, :], "ds")
    blk.dma("sync", S["SELPF"][:, :], io["selp"][:, :], "ds")
    blk.dma("sync", S["TRI8S"][:, :], io["tri8s"][:, :], "ds")
    blk.dma("sync", S["SH8"][:, :], io["sh8"][:, :], "ds")
    blk.dma("sync", S["R4096"][:, :], io["r4096"][:, :], "ds")
    blk.dma("sync", S["RBM1"][:, :], io["rbasem1"][:, :], "ds")
    blk.dma("sync", S["SIOTA"][:, :], io["siota"][:, :], "ds")
    blk.dma("gpsimd", S["RBROW"][:, :], rb[None, :], "dg")
    blk.dma("gpsimd", S["XR"][0:32, 0:128],
            b1e[:].rearrange("(a b) -> a b", b=128), "dg")
    blk.dma("gpsimd", S["XR"][0:8, 128:256],
            b2e[:].rearrange("(a b) -> a b", b=128), "dg")
    blk.wait("tensor", "dg", ctx.cnt["dg"])
    blk.wait("tensor", "ds", ctx.cnt["ds"])
    blk.tr(pb[6][:, 0:32], S["XR"][0:32, 0:128], C["IDF"][0:32, 0:32])
    v2 = blk.tr(pb[7][:, 0:8], S["XR"][0:8, 128:256], C["IDF"][0:8, 0:8],
                inc="mm")
    blk.wait("vector", "mm", v2)
    blk.add("vector", lambda e: e.tensor_copy(S["B1S"][:, :], pb[6][:, 0:32]))
    blk.add("vector", lambda e: e.tensor_copy(S["B2S"][:, :], pb[7][:, 0:8]))
    blk.emit()

    for tb in range(NTB):
        blk0 = Blk(ctx)
        blk0.dma("gpsimd", S["XF"][:, :].rearrange("p (c n) -> p c n", c=NCH),
                 x1T[:, tb * TB:(tb + 1) * TB].rearrange("(c p) n -> p c n",
                                                         c=NCH), "dg")
        blk0.wait("gpsimd", "dg", ctx.cnt["dg"])
        blk0.emit()

        def logits_cb(blk, c, tf1, fv, tb=tb):
            blk.wait("tensor", "ev", fv)
            v = None
            for tt in range(4):
                v = blk.mm(pb[2 + tt][:, 0:8],
                           tf1[:, tt * 128:(tt + 1) * 128],
                           S["RWF"][:, c * 8:(c + 1) * 8],
                           start=(c == 0), stop=False,
                           inc=("mm" if tt == 3 else None))
            return v

        emit_ln(ctx, S, C, pb, S["XF"], None,
                dst_fn=lambda c, tb=tb: S["XLNB"][:, c * T + tb * TB:
                                                  c * T + (tb + 1) * TB],
                fp32_cb=logits_cb)

        blk = Blk(ctx)
        for tt in range(4):
            v = blk.mm(pb[2 + tt][:, 0:8], C["ONES128"][0:1, :],
                       S["RBROW"][0:1, :], start=False, stop=True, inc="mm")
            blk.wait("vector", "mm", v)
            blk.add("vector", lambda e, tt=tt, tb=tb: e.tensor_copy(
                S["LGT"][:, (tb * 4 + tt) * 8:(tb * 4 + tt + 1) * 8],
                pb[2 + tt][:, 0:8]), inc="ev")
        blk.emit()

        blk = Blk(ctx)
        pr_ev = {}
        for tt in range(4):
            for c in range(NCH):
                bk = c % 2
                if pr_ev.get(bk):
                    blk.wait("tensor", "ev", pr_ev[bk])
                v = blk.tr(pb[bk].bitcast(BF16)[:, 0:128],
                           S["XLNB"][:, c * T + tb * TB + tt * 128:
                                     c * T + tb * TB + (tt + 1) * 128],
                           C["IDB"][:, :], inc="mm")
                blk.wait("vector", "mm", v)
                pr_ev[bk] = blk.add(
                    "vector", lambda e, c=c, bk=bk: e.tensor_copy(
                        S["TOK"][:, c * 128:(c + 1) * 128],
                        pb[bk].bitcast(BF16)[:, 0:128]), inc="ev")
            blk.wait("sync", "ev", ctx.cnt["ev"])
            sv = blk.dma("sync",
                         myxln[tb * TB + tt * 128:tb * TB + (tt + 1) * 128, :],
                         S["TOK"][:, :], "ds")
            blk.wait("vector", "ds", sv)
        blk.emit()

        blk = Blk(ctx)
        for tt in range(4):
            j = tb * 4 + tt
            LG = S["LGT"][:, j * 8:(j + 1) * 8]
            blk.vf(lambda e, LG=LG: e.reduce_max(
                S["TC1"][:, :], LG, axis=AX.X))
            blk.vf(lambda e, LG=LG, j=j: e.tensor_scalar(
                S["MK1"][:, j * 8:(j + 1) * 8], LG, S["TC1"][:, 0:1], None,
                ALU.is_equal))
            blk.vf(lambda e, LG=LG, j=j: e.scalar_tensor_tensor(
                S["T8A"][:, :], S["MK1"][:, j * 8:(j + 1) * 8], -1e30, LG,
                ALU.mult, ALU.add))
            blk.vf(lambda e: e.reduce_max(
                S["TC2"][:, :], S["T8A"][:, :], axis=AX.X))
            blk.vf(lambda e, j=j: e.tensor_scalar(
                S["MK2"][:, j * 8:(j + 1) * 8], S["T8A"][:, :],
                S["TC2"][:, 0:1], None, ALU.is_equal))
            blk.vf(lambda e, j=j: e.tensor_add(
                S["MKB"][:, j * 8:(j + 1) * 8],
                S["MK1"][:, j * 8:(j + 1) * 8],
                S["MK2"][:, j * 8:(j + 1) * 8]))
            vv = blk.vf(lambda e: e.tensor_scalar(
                S["TC2"][:, :], S["TC1"][:, :], -1.0, None, ALU.mult))
            blk.wait("scalar", "ev", vv)
            av = blk.add("scalar", lambda e, LG=LG: e.activation(
                S["T8B"][:, :], LG, AF.Exp, bias=S["TC2"][:, 0:1]), inc="ea")
            blk.wait("vector", "ea", av)
            blk.vf(lambda e: e.reduce_sum(
                S["TC1"][:, :], S["T8B"][:, :], axis=AX.X))
            blk.vf(lambda e: e.reciprocal(
                S["TC2"][:, 0:1], S["TC1"][:, 0:1]))
            blk.vf(lambda e, j=j: e.tensor_scalar(
                S["GT"][:, j * 8:(j + 1) * 8], S["T8B"][:, :],
                S["TC2"][:, 0:1], None, ALU.mult))
        blk.emit()

    blk = Blk(ctx)
    pr = {}
    for j in range(16):
        bk = j % 2
        if pr.get(bk):
            blk.wait("tensor", "ev", pr[bk])
        v = blk.tr(pb[bk].bitcast(BF16)[0:8, 0:128],
                   S["MKB"][:, j * 8:(j + 1) * 8], C["IDB"][:, :], inc="mm")
        blk.wait("vector", "mm", v)
        pr[bk] = blk.add("vector", lambda e, j=j, bk=bk: e.tensor_copy(
            S["MKE"][0:8, j * 128:(j + 1) * 128],
            pb[bk].bitcast(BF16)[0:8, 0:128]), inc="ev")
    blk.wait("vector", "ev", ctx.cnt["ev"])
    v = blk.add("vector", lambda e: e.tensor_tensor_scan(
        S["CUM"][:, :], S["MKE"][:, :], S["MKE"][:, :], 0.0,
        ALU.add, ALU.bypass), inc="ev")
    blk.wait("vector", "ev", v)
    v2 = blk.add("vector", lambda e: e.tensor_copy(
        S["CNT8"][:, :], S["CUM"][:, T - 1:T]), inc="ev")
    blk.wait("sync", "ev", ctx.cnt["ev"])
    s2 = blk.dma("sync", mycnt[:, :], S["CNT8"][:, :], "ds")
    blk.wait("gpsimd", "ds", s2)
    blk.add("gpsimd", lambda e: e.collective_compute(
        "AllGather", ALU.bypass, replica_groups=RG8,
        ins=[mycnt.ap().opt()], outs=[cnt64d.ap().opt()]), inc="cc")
    blk.wait("gpsimd", "cc", ctx.cnt["cc"])
    l2 = blk.dma("gpsimd", S["CNT64"][:, :], cnt64d[:, :], "dg")
    blk.wait("gpsimd", "dg", l2)
    blk.emit()

    blk = Blk(ctx)
    vv = blk.mm(pb[5][0:8, 0:1], S["TRI8S"][:, :], S["CNT8"][:, :],
                start=True, stop=True, inc="mm")
    blk.wait("vector", "mm", vv)
    blk.vf(lambda e: e.tensor_copy(S["V8A"][0:8, 0:1], pb[5][0:8, 0:1]))
    dst_v = blk.vf(lambda e: e.tensor_scalar(
        S["DSTF"][:, :], S["CUM"][:, :], S["V8A"][0:8, 0:1], -1.0,
        ALU.add, ALU.add))
    blk.wait("tensor", "ev", dst_v)
    sc_prev = None
    cp_ev = {}
    for j in range(16):
        bk = j % 2
        if cp_ev.get(bk):
            blk.wait("tensor", "ev", cp_ev[bk])
        v = blk.tr(pb[bk][:, 0:8], S["DSTF"][0:8, j * 128:(j + 1) * 128],
                   C["IDF"][0:8, 0:8], inc="mm")
        blk.wait("vector", "mm", v)
        cp_ev[bk] = blk.vf(lambda e, bk=bk: e.tensor_copy(
            S["T8D"][:, :], pb[bk][:, 0:8]))
        blk.vf(lambda e, j=j: e.tensor_mul(
            S["T8C"][:, :], S["MK1"][:, j * 8:(j + 1) * 8], S["T8D"][:, :]))
        blk.vf(lambda e: e.reduce_sum(
            S["TC1"][:, :], S["T8C"][:, :], axis=AX.X))
        blk.vf(lambda e, j=j: e.tensor_copy(
            S["IDXL1"][:, j:j + 1], S["TC1"][:, :]))
        blk.vf(lambda e, j=j: e.tensor_mul(
            S["T8C"][:, :], S["MK2"][:, j * 8:(j + 1) * 8], S["T8D"][:, :]))
        blk.vf(lambda e: e.reduce_sum(
            S["TC1"][:, :], S["T8C"][:, :], axis=AX.X))
        v = blk.vf(lambda e, j=j: e.tensor_copy(
            S["IDXL2"][:, j:j + 1], S["TC1"][:, :]))
        blk.wait("gpsimd", "ev", v)
        if sc_prev is not None:
            blk.wait("gpsimd", "dg", sc_prev)
        lv = blk.dma("gpsimd", S["TOK"][:, :],
                     myxln[j * 128:(j + 1) * 128, :], "dg")
        blk.wait("gpsimd", "dg", lv)
        blk.add("gpsimd", lambda e, j=j: e.indirect_dma_start(
            sendb[0:128, :],
            IndirectOffsetOnAxis(S["IDXL1"][:, j:j + 1], 0),
            S["TOK"][:, :], None), inc="dg", n=16)
        sc_prev = blk.add("gpsimd", lambda e, j=j: e.indirect_dma_start(
            sendb[0:128, :],
            IndirectOffsetOnAxis(S["IDXL2"][:, j:j + 1], 0),
            S["TOK"][:, :], None), inc="dg", n=16)
    blk.wait("gpsimd", "dg", ctx.cnt["dg"])
    blk.add("gpsimd", lambda e: e.collective_compute(
        "AllGather", ALU.bypass, replica_groups=RG8,
        ins=[sendb.ap().opt()], outs=[sball.ap().opt()]), inc="cc")
    blk.wait("gpsimd", "cc", ctx.cnt["cc"])
    v = blk.mm(pb[4][0:8, 0:1], S["SELPF"][:, :], S["CNT64"][:, :],
               start=True, stop=True, inc="mm")
    blk.wait("vector", "mm", v)
    blk.vf(lambda e: e.tensor_copy(S["OFFP"][:, :], pb[4][0:8, 0:1]))
    blk.vf(lambda e: e.tensor_scalar(
        S["GCUM"][:, :], S["CUM"][:, :], S["OFFP"][0:8, 0:1], None, ALU.add))
    blk.emit()

    blk = Blk(ctx)
    v = blk.mm(pb[0][0:8, 0:1], S["SEL8F"][:, :], S["CNT64"][:, :],
               start=True, stop=True)
    v = blk.mm(pb[1][0:8, 0:1], S["SELLO"][:, :], S["CNT64"][:, :],
               start=True, stop=True, inc="mm")
    blk.wait("vector", "mm", v)
    blk.vf(lambda e: e.tensor_copy(S["V8A"][0:8, 0:1],
                                   pb[0][0:8, 0:1]))
    vv = blk.vf(lambda e: e.tensor_copy(
        S["V8A"][0:8, 1:2], pb[1][0:8, 0:1]))
    blk.wait("tensor", "ev", vv)
    v = blk.mm(pb[2][0:8, 0:1], S["TRI8S"][:, :], S["V8A"][0:8, 0:1],
               start=True, stop=True, inc="mm")
    blk.wait("vector", "mm", v)
    blk.vf(lambda e: e.tensor_copy(S["V8A"][0:8, 2:3], pb[2][0:8, 0:1]))
    blk.vf(lambda e: e.tensor_sub(
        S["V8A"][0:8, 3:4], S["V8A"][0:8, 1:2], S["V8A"][0:8, 2:3]))
    vv = blk.vf(lambda e: e.tensor_add(
        S["V8A"][0:8, 3:4], S["V8A"][0:8, 3:4], S["R4096"][0:8, 0:1]))
    blk.wait("tensor", "ev", vv)
    v = blk.mm(pb[3][0:8, 0:1], S["SH8"][:, :], S["V8A"][0:8, 3:4],
               start=True, stop=True, inc="mm")
    blk.wait("vector", "mm", v)
    vv = blk.vf(lambda e: e.tensor_sub(
        S["V8A"][0:8, 1:2], S["V8A"][0:8, 3:4], pb[3][0:8, 0:1]))
    blk.wait("tensor", "ev", vv)
    v = blk.tr(pb[4][0:1, 0:8], S["V8A"][0:8, 2:3], C["IDF"][0:8, 0:8])
    v = blk.tr(pb[5][0:1, 0:8], S["V8A"][0:8, 1:2], C["IDF"][0:8, 0:8],
               inc="mm")
    blk.wait("vector", "mm", v)
    blk.vf(lambda e: e.tensor_copy(S["ROW8"][0:1, 0:8], pb[4][0:1, 0:8]))
    vv = blk.vf(lambda e: e.tensor_copy(
        S["ROW8"][0:1, 8:16], pb[5][0:1, 0:8]))
    blk.wait("tensor", "ev", vv)
    v = blk.mm(pb[6][:, 0:8], C["ONESR"][0:1, :], S["ROW8"][0:1, 0:8],
               start=True, stop=True)
    v = blk.mm(pb[7][:, 0:8], C["ONESR"][0:1, :], S["ROW8"][0:1, 8:16],
               start=True, stop=True, inc="mm")
    blk.wait("vector", "mm", v)
    blk.vf(lambda e: e.tensor_copy(S["GOB"][:, :], pb[6][:, 0:8]))
    blk.vf(lambda e: e.tensor_copy(S["DELB"][:, :], pb[7][:, 0:8]))
    blk.vf(lambda e: e.tensor_copy(S["IDF1"][:, :], S["SIOTA"][:, :]))
    cur, nxt = "IDF1", "IDF2"
    for r in range(8):
        blk.vf(lambda e, r=r: e.tensor_scalar(
            S["GS"][:, 0:20], S["SIOTA"][:, :], S["GOB"][:, r:r + 1], None,
            ALU.is_ge))
        blk.vf(lambda e, r=r, cur=cur, nxt=nxt:
               e.scalar_tensor_tensor(
                   S[nxt][:, :], S["GS"][:, 0:20], S["DELB"][:, r:r + 1],
                   S[cur][:, :], ALU.mult, ALU.add))
        cur, nxt = nxt, cur
    blk.vf(lambda e, cur=cur, nxt=nxt: e.tensor_scalar_min(
        S[nxt][:, :], S[cur][:, :], 32767.0))
    v = blk.vf(lambda e, nxt=nxt: e.tensor_copy(
        S["IDXG"][:, :], S[nxt][:, :]))
    blk.wait("gpsimd", "ev", v)
    pr = {}
    g_prev = None
    for tt in range(20):
        if g_prev is not None:
            blk.wait("gpsimd", "mm", g_prev)
        lv = blk.add("gpsimd", lambda e, tt=tt: e.indirect_dma_start(
            S["TOK"][:, :], None, sball[0:128, :],
            IndirectOffsetOnAxis(S["IDXG"][:, tt:tt + 1], 0)),
            inc="dg", n=16)
        blk.wait("tensor", "dg", lv)
        for c in range(NCH):
            bk = c % 2
            if pr.get(bk):
                blk.wait("tensor", "ev", pr[bk])
            g_prev = blk.tr(pb[bk].bitcast(BF16)[:, 0:128],
                            S["TOK"][:, c * 128:(c + 1) * 128],
                            C["IDB"][:, :], inc="mm")
            blk.wait("vector", "mm", g_prev)
            pr[bk] = blk.add("vector", lambda e, c=c, tt=tt, bk=bk:
                             e.tensor_copy(
                                 S["BUF"][:, c * CAP + tt * 128:
                                          c * CAP + (tt + 1) * 128],
                                 pb[bk].bitcast(BF16)[:, 0:128]), inc="ev")
    blk.emit()

    for tb in range(5):
        blk = Blk(ctx)
        w1_specs = [w1e[:, hc * 128:(hc + 1) * 128].rearrange(
            "(c p) n -> p c n", c=NCH) for hc in range(32)]

        def h_evac(blk, i, mv, tb=tb):
            blk.wait("scalar", "mm", mv)
            return ("ea", blk.add("scalar", lambda e, i=i: e.activation(
                S["HT"][:, i * TB:(i + 1) * TB], pb[i % 4][:, :], AF.Gelu,
                bias=S["B1S"][:, i:i + 1]), inc="ea"))

        stream_mm(ctx, blk, w1_specs, S["BW"], NCH,
                  moving=lambda i, kc, tb=tb:
                  S["BUF"][:, kc * CAP + tb * TB:kc * CAP + (tb + 1) * TB],
                  psum_of=lambda i: pb[i % 4][:, :], evac=h_evac)
        blk.emit()
        blk = Blk(ctx)
        w2_specs = [w2e[:, oc * 128:(oc + 1) * 128].rearrange(
            "(c p) n -> p c n", c=32) for oc in range(NCH)]

        def y_evac(blk, i, mv, tb=tb):
            blk.wait("scalar", "mm", mv)
            return ("ea", blk.add("scalar", lambda e, i=i: e.activation(
                S["YF"][:, i * TB:(i + 1) * TB], pb[i % 4][:, :], AF.Identity,
                bias=S["B2S"][:, i:i + 1]), inc="ea"))

        stream_mm(ctx, blk, w2_specs, S["BW"], 32,
                  moving=lambda i, kc: S["HT"][:, kc * TB:(kc + 1) * TB],
                  psum_of=lambda i: pb[i % 4][:, :], evac=y_evac)
        blk.emit()
        blk = Blk(ctx)
        blk.wait("tensor", "ea", ctx.cnt["ea"])
        pr = {}
        for tt in range(4):
            for c in range(NCH):
                bk = c % 2
                if pr.get(bk):
                    blk.wait("tensor", "ev", pr[bk])
                v = blk.tr(pb[bk].bitcast(BF16)[:, 0:128],
                           S["YF"][:, c * TB + tt * 128:
                                   c * TB + (tt + 1) * 128],
                           C["IDB"][:, :], inc="mm")
                blk.wait("vector", "mm", v)
                pr[bk] = blk.add("vector", lambda e, c=c, bk=bk:
                                 e.tensor_copy(
                                     S["TOK"][:, c * 128:(c + 1) * 128],
                                     pb[bk].bitcast(BF16)[:, 0:128]),
                                 inc="ev")
            blk.wait("sync", "ev", ctx.cnt["ev"])
            sv = blk.dma("sync",
                         yd[tb * TB + tt * 128:tb * TB + (tt + 1) * 128, :],
                         S["TOK"][:, :], "ds")
            blk.wait("vector", "ds", sv)
        blk.emit()

    blk = Blk(ctx)
    blk.wait("gpsimd", "ds", ctx.cnt["ds"])
    blk.add("gpsimd", lambda e: e.collective_compute(
        "AllGather", ALU.bypass, replica_groups=RG8,
        ins=[yd.ap().opt()], outs=[yall.ap().opt()]), inc="cc")
    blk.wait("gpsimd", "cc", ctx.cnt["cc"])
    blk.emit()

    for j in range(16):
        blk = Blk(ctx)
        v = blk.tr(pb[0][:, 0:8], S["GCUM"][0:8, j * 128:(j + 1) * 128],
                   C["IDF"][0:8, 0:8], inc="mm")
        blk.wait("vector", "mm", v)
        blk.vf(lambda e: e.tensor_copy(S["T8A"][:, :], pb[0][:, 0:8]))
        blk.vf(lambda e: e.tensor_scalar(
            S["T8B"][:, :], S["T8A"][:, :], float(CAP), None, ALU.is_le))
        blk.vf(lambda e: e.tensor_add(
            S["T8D"][:, :], S["T8A"][:, :], S["RBM1"][:, :]))
        blk.vf(lambda e: e.tensor_copy(S["T8A"][:, :], S["T8D"][:, :]))
        for k, (MKk, IDXk, WTOKk) in enumerate(
                [(S["MK1"], S["IDX1"], S["W1TOK"]),
                 (S["MK2"], S["IDX2"], S["W2TOK"])]):
            blk.vf(lambda e, MKk=MKk, j=j: e.tensor_mul(
                S["T8C"][:, :], MKk[:, j * 8:(j + 1) * 8], S["T8B"][:, :]))
            blk.vf(lambda e: e.tensor_mul(
                S["T8D"][:, :], S["T8C"][:, :], S["T8A"][:, :]))
            blk.vf(lambda e: e.reduce_sum(
                S["TC1"][:, :], S["T8D"][:, :], axis=AX.X))
            blk.vf(lambda e, IDXk=IDXk, j=j: e.tensor_copy(
                IDXk[:, j:j + 1], S["TC1"][:, :]))
            blk.vf(lambda e, j=j: e.tensor_mul(
                S["T8D"][:, :], S["T8C"][:, :],
                S["GT"][:, j * 8:(j + 1) * 8]))
            blk.vf(lambda e, WTOKk=WTOKk, j=j: e.reduce_sum(
                WTOKk[:, j:j + 1], S["T8D"][:, :], axis=AX.X))
        blk.wait("gpsimd", "ev", ctx.cnt["ev"])
        blk.add("gpsimd", lambda e, j=j: e.indirect_dma_start(
            S["G1"][:, :], None, yall[0:128, :],
            IndirectOffsetOnAxis(S["IDX1"][:, j:j + 1], 0)), inc="dg", n=16)
        blk.add("gpsimd", lambda e, j=j: e.indirect_dma_start(
            S["G2"][:, :], None, yall[0:128, :],
            IndirectOffsetOnAxis(S["IDX2"][:, j:j + 1], 0)), inc="dg", n=16)
        lv = blk.dma("gpsimd", S["XR"][:, :].rearrange("p (c n) -> p c n",
                                                       c=NCH),
                     x1T[:, j * 128:(j + 1) * 128].rearrange(
                         "(c p) n -> p c n", c=NCH), "dg")
        blk.wait("vector", "dg", ctx.cnt["dg"])
        blk.add("vector", lambda e, j=j: e.tensor_scalar(
            S["GS"][:, :], S["G1"][:, :], S["W1TOK"][:, j:j + 1], None,
            ALU.mult))
        v = blk.add("vector", lambda e, j=j: e.scalar_tensor_tensor(
            S["GS"][:, :], S["G2"][:, :], S["W2TOK"][:, j:j + 1],
            S["GS"][:, :], ALU.mult, ALU.add), inc="ev")
        blk.wait("tensor", "ev", v)
        st_prev = {}
        ev_prev = {}
        for c in range(NCH):
            bk = 2 + c % 2
            if ev_prev.get(bk):
                blk.wait("tensor", "ev", ev_prev[bk])
            v = blk.tr(pb[bk][:, 0:128],
                       S["GS"][:, c * 128:(c + 1) * 128], C["IDF"][:, :],
                       inc="mm")
            blk.wait("vector", "mm", v)
            ot = S["OT"][c % 2]
            if st_prev.get(c % 2):
                blk.wait("vector", "ds", st_prev[c % 2])
            vv = blk.add("vector", lambda e, c=c, bk=bk, ot=ot: e.tensor_add(
                ot[:, :], pb[bk][:, 0:128],
                S["XR"][:, c * 128:(c + 1) * 128]), inc="ev")
            ev_prev[bk] = vv
            blk.wait("sync", "ev", vv)
            st_prev[c % 2] = blk.dma(
                "sync", x2T[c * 128:(c + 1) * 128, j * 128:(j + 1) * 128],
                ot[:, :], "ds")
        blk.emit()

    if "dbg" in io:
        blk = Blk(ctx)
        for nm, hd in io["dbg"].items():
            if nm == "BUFS":
                blk.dma("sync", hd[:, :], S["BUF"][:, 0:NCH * CAP], "ds")
            else:
                blk.dma("sync", hd[:, :], S[nm][:, :] if not isinstance(
                    S[nm], list) else S[nm][0][:, :], "ds")
        blk.wait("sync", "ds", ctx.cnt["ds"])
        blk.emit()

    blk = Blk(ctx)
    blk.wait("sync", "ds", ctx.cnt["ds"])
    blk.wait("gpsimd", "dg", ctx.cnt["dg"])
    blk.emit()



CW_C1, CW_C2, CW_C3 = 6.28125, 0.0019378662, -2.5590314e-06
INV2PI = 0.15915494
MAGIC = 12582912.0
TAY_A = float(0.125 / np.sqrt(2.0))
TAY_B = float(1.0 / np.sqrt(2.0))
PIO2 = float(np.float32(np.pi / 2))
TWO_PI = float(np.float32(2 * np.pi))
PI_F = float(np.float32(np.pi))


def alloc_stage_C(nc):
    sb = nc.alloc_sbuf_tensor
    S = {}
    A2 = sb("C_A2", [128, 2 * NCH * TB], F32)
    S["XF"] = A2.ap()[:, 0:NCH * TB]
    S["LN1F"] = A2.ap()[:, NCH * TB:2 * NCH * TB]
    S["XB"] = sb("C_XB", [128, NCH * TB], BF16)
    S["SQ"] = sb("C_SQ", [128, NCH * TB], BF16)
    S["MU"] = sb("C_MU", [128, TB], F32)
    S["RSTD"] = sb("C_RSTD", [128, TB], F32)
    S["TF1"] = sb("C_TF1", [128, TB], F32)
    S["TF2"] = sb("C_TF2", [128, TB], F32)
    S["TF3"] = None
    S["XLN2"] = sb("C_XLN2", [128, NCH * TB], BF16)
    S["TF3"] = S["MU"]
    QB = sb("C_QB", [128, NCH * T], BF16)
    KB = sb("C_KB", [128, NCH * T], BF16)
    S["QB"] = QB.ap()
    S["KB"] = KB.ap()
    S["VT"] = sb("C_VT", [128, 16 * 16 * 80], BF16)
    S["AO"] = S["XB"]
    S["CT"] = sb("C_CT", [128, T], BF16)
    S["ST"] = sb("C_ST", [128, T], BF16)
    qf = QB.ap().bitcast(F32)
    kf = KB.ap().bitcast(F32)
    S["A32A"] = qf[0:32, 0:T]
    S["A32B"] = qf[0:32, T:2 * T]
    S["A32C"] = qf[0:32, 2 * T:3 * T]
    S["SINT"] = qf[0:32, 3 * T:4 * T]
    S["COST"] = kf[0:32, 0:T]
    S["IOTA32"] = kf[0:32, T:2 * T]
    S["TH"] = sb("C_TH", [32, 4], F32)
    S["TAYB"] = sb("C_TAYB", [128, 1], F32)
    S["LNW"] = sb("C_LNW", [128, NCH], F32)
    S["LNB"] = sb("C_LNB", [128, NCH], F32)
    S["VSH"] = sb("C_VSH", [65, 16], F32)
    S["RD"] = [sb(f"C_RD{i}", [1, TB], F32) for i in range(2)]
    S["RDB"] = [sb(f"C_RDB{i}", [64, TB], F32) for i in range(2)]
    S["NUM"] = [sb(f"C_NUM{i}", [64, TB], BF16) for i in range(2)]
    S["PT"] = [sb(f"C_PT{i}", [128, TB], BF16) for i in range(2)]
    S["WS"] = [sb(f"C_WS{i}", [128, NCH * 128], BF16) for i in range(2)]
    S["WM"] = S["WS"]
    S["BASE32"] = sb("C_BASE32", [32, 1], F32)
    S["ECX"] = sb("C_ECX", [32, 128], F32)
    S["ESX"] = sb("C_ESX", [32, 128], F32)
    S["PERM"] = sb("C_PERM", [128, 128], BF16)
    return S


def emit_stage_C(ctx, S, C, pb, io):
    nc = ctx.nc
    x2T, x3T = io["x2T"], io["x3T"]
    wq, wkv, wo = io["L1_attn_w_q"], io["L1_attn_w_kv"], io["L1_attn_w_o"]
    lnw, lnb, theta = io["L1_attn_ln_w"], io["L1_attn_ln_b"], \
        io["L1_attn_theta"]

    blk = Blk(ctx)
    blk.dma("sync", S["IOTA32"][:, :], io["c_iota32"][:, :], "ds")
    blk.dma("sync", S["BASE32"][:, :], io["c_base32"][:, :], "ds")
    blk.dma("sync", S["ECX"][:, :], io["c_ec"][:, :], "ds")
    blk.dma("sync", S["ESX"][:, :], io["c_es"][:, :], "ds")
    blk.dma("sync", S["PERM"][:, :], io["c_perm"][:, :], "ds")
    blk.dma("gpsimd", S["TH"][0:1, 0:1], theta[None, None], "dg")
    blk.add("gpsimd", lambda e: e.memset(S["TAYB"][:, :], TAY_B))
    blk.dma("gpsimd", S["XF"][0:8, 0:128],
            lnw[:].rearrange("(a b) -> a b", b=128), "dg")
    blk.dma("gpsimd", S["XF"][0:8, 128:256],
            lnb[:].rearrange("(a b) -> a b", b=128), "dg")
    blk.wait("tensor", "dg", ctx.cnt["dg"])
    blk.wait("tensor", "ds", ctx.cnt["ds"])
    blk.tr(pb[0][:, 0:8], S["XF"][0:8, 0:128], C["IDF"][0:8, 0:8])
    v = blk.tr(pb[1][:, 0:8], S["XF"][0:8, 128:256], C["IDF"][0:8, 0:8],
               inc="mm")
    blk.wait("vector", "mm", v)
    blk.vf(lambda e: e.tensor_copy(S["LNW"][:, :], pb[0][:, 0:8]))
    blk.vf(lambda e: e.tensor_copy(S["LNB"][:, :], pb[1][:, 0:8]))
    v = blk.mm(pb[2][0:32, 0:1], C["ONESR"][0:1, 0:32], S["TH"][0:1, 0:1],
               start=True, stop=True, inc="mm")
    blk.wait("vector", "mm", v)
    blk.vf(lambda e: e.tensor_scalar(
        S["TH"][0:32, 1:2], pb[2][0:32, 0:1], 1.0 / 220.0, None, ALU.mult))
    blk.vf(lambda e: e.scalar_tensor_tensor(
        S["TH"][0:32, 3:4], S["TH"][0:32, 1:2], -220.0, pb[2][0:32, 0:1],
        ALU.mult, ALU.add))
    blk.vf(lambda e: e.scalar_tensor_tensor(
        S["TH"][0:32, 1:2], S["TH"][0:32, 3:4], 1.0 / 220.0,
        S["TH"][0:32, 1:2], ALU.mult, ALU.add))
    blk.vf(lambda e: e.tensor_mul(
        S["TH"][0:32, 2:3], S["BASE32"][:, :], S["TH"][0:32, 1:2]))
    blk.vf(lambda e: e.tensor_scalar(
        S["A32A"][:, :], S["IOTA32"][:, :], S["TH"][0:32, 2:3], None,
        ALU.mult))
    for (dst, shift) in ((S["SINT"], 0.0), (S["COST"], 0.25)):
        blk.vf(lambda e, shift=shift: e.tensor_scalar(
            S["A32B"][:, :], S["A32A"][:, :], INV2PI, shift + MAGIC,
            ALU.mult, ALU.add))
        blk.vf(lambda e: e.tensor_scalar(
            S["A32B"][:, :], S["A32B"][:, :], -MAGIC, None, ALU.add))
        blk.vf(lambda e: e.scalar_tensor_tensor(
            S["A32C"][:, :], S["A32B"][:, :], -CW_C1, S["A32A"][:, :],
            ALU.mult, ALU.add))
        blk.vf(lambda e: e.scalar_tensor_tensor(
            S["A32C"][:, :], S["A32B"][:, :], -CW_C2, S["A32C"][:, :],
            ALU.mult, ALU.add))
        blk.vf(lambda e: e.scalar_tensor_tensor(
            S["A32C"][:, :], S["A32B"][:, :], -CW_C3, S["A32C"][:, :],
            ALU.mult, ALU.add))
        if shift:
            blk.vf(lambda e: e.tensor_scalar(
                S["A32C"][:, :], S["A32C"][:, :], PIO2, None, ALU.add))
        blk.vf(lambda e: e.tensor_scalar(
            S["A32B"][:, :], S["A32C"][:, :], PI_F, None, ALU.is_gt))
        blk.vf(lambda e: e.scalar_tensor_tensor(
            S["A32C"][:, :], S["A32B"][:, :], -TWO_PI, S["A32C"][:, :],
            ALU.mult, ALU.add))
        blk.vf(lambda e: e.tensor_scalar(
            S["A32B"][:, :], S["A32C"][:, :], -PI_F, None, ALU.is_lt))
        v = blk.vf(lambda e: e.scalar_tensor_tensor(
            S["A32C"][:, :], S["A32B"][:, :], TWO_PI, S["A32C"][:, :],
            ALU.mult, ALU.add))
        blk.wait("scalar", "ev", v)
        blk.add("scalar", lambda e, dst=dst: e.activation(
            dst[:, :], S["A32C"][:, :], AF.Sin), inc="ea")
    blk.wait("tensor", "ea", ctx.cnt["ea"])
    ex_ev = {}
    for half, (tab, exp_, dst) in enumerate(
            [(S["COST"], S["ECX"], S["CT"]), (S["SINT"], S["ESX"], S["ST"])]):
        for sl in range(4):
            b = (half * 4 + sl) % 4
            if ex_ev.get(b):
                blk.wait("tensor", "ev", ex_ev[b])
            v = blk.mm(pb[b][:, :], exp_[:, :],
                       tab[:, sl * TB:(sl + 1) * TB], start=True, stop=True,
                       inc="mm")
            blk.wait("vector", "mm", v)
            ex_ev[b] = blk.add("vector", lambda e, dst=dst, sl=sl, b=b:
                               e.tensor_copy(dst[:, sl * TB:(sl + 1) * TB],
                                             pb[b][:, :]), inc="ev")
    blk.emit()

    for tb in range(NTB):
        blk0 = Blk(ctx)
        blk0.dma("gpsimd", S["XF"][:, :].rearrange("p (c n) -> p c n", c=NCH),
                 x2T[:, tb * TB:(tb + 1) * TB].rearrange(
                     "(c p) n -> p c n", c=NCH), "dg")
        blk0.wait("gpsimd", "dg", ctx.cnt["dg"])
        blk0.emit()
        emit_ln(ctx, S, C, pb, S["XF"], None,
                dst_fn=lambda c: S["LN1F"][:, c * TB:(c + 1) * TB])
        emit_ln(ctx, S, C, pb, S["LN1F"], S["XLN2"],
                ln_w_col=S["LNW"], ln_b_col=S["LNB"])

        blk = Blk(ctx)
        q_specs = [wq[:, oc * 128:(oc + 1) * 128].rearrange(
            "(c p) n -> p c n", c=NCH) for oc in range(NCH)] + \
            [wkv[:, oc * 128:(oc + 1) * 128].rearrange(
                "(c p) n -> p c n", c=NCH) for oc in range(NCH)]

        def qk_evac(blk, i, mv, tb=tb):
            blk.wait("vector", "mm", mv)
            dst = S["QB"] if i < NCH else S["KB"]
            oc = i % NCH
            return ("ev", blk.add("vector", lambda e, dst=dst, oc=oc:
                                  e.tensor_copy(
                                      dst[:, oc * T + tb * TB:
                                          oc * T + tb * TB + TB],
                                      pb[i % 4][:, :]), inc="ev"))

        stream_mm(ctx, blk, q_specs, S["WS"], NCH,
                  moving=lambda i, kc: S["XLN2"][:, kc * TB:(kc + 1) * TB],
                  psum_of=lambda i: pb[i % 4][:, :], evac=qk_evac)
        blk.emit()

        blk = Blk(ctx)
        m4ld = {}
        for kc in range(NCH):
            mb = S["WM"][kc % 2]
            if kc >= 2:
                blk.wait("gpsimd", "mm", m4ld[kc - 2])
            ld = blk.dma("gpsimd", mb[:, 0:D],
                         wkv[kc * 128:(kc + 1) * 128, D:2 * D], "dg")
            blk.wait("tensor", "dg", ld)
            lastmm = None
            for tt in range(4):
                for nv in range(2):
                    i = tt * 2 + nv
                    last_of_kc = (tt == 3 and nv == 1)
                    lastmm = blk.mm(
                        pb[i][:, :],
                        S["XLN2"][:, kc * TB + tt * 128:
                                  kc * TB + tt * 128 + 128],
                        mb[:, nv * 512:(nv + 1) * 512],
                        start=(kc == 0), stop=(kc == NCH - 1),
                        inc=("mm" if (kc == NCH - 1 or last_of_kc)
                             else None))
                    if kc == NCH - 1:
                        kt = tb * 4 + tt
                        blk.wait("vector", "mm", lastmm)
                        dst = S["VT"][:, kt * 1280 + nv * 640:
                                      kt * 1280 + (nv + 1) * 640] \
                            .rearrange("p (h n) -> p h n", h=8)[:, :, 0:64]
                        srcp = pb[i][:, :].rearrange("p (h n) -> p h n", h=8)
                        blk.add("vector", lambda e, dst=dst, srcp=srcp:
                                e.tensor_copy(dst, srcp), inc="ev")
            m4ld[kc] = lastmm
        blk.emit()

    blk = Blk(ctx)
    blk.add("gpsimd", lambda e: e.memset(
        S["VT"][:, :].rearrange("p (a b) -> p a b", b=80)[:, :, 64:65], 1.0))
    gv = blk.add("gpsimd", lambda e: e.engine_nop(), inc="eg")
    blk.wait("tensor", "eg", gv)
    for h in range(H):
        for kt in range(16):
            blk.mm(pb[h % 2][0:65, h // 2:h // 2 + 1],
                   S["VT"][:, (kt * 16 + h) * 80:(kt * 16 + h) * 80 + 65],
                   C["ONESC"][:, 0:1],
                   start=(kt == 0), stop=(kt == 15),
                   inc=("mm" if kt == 15 else None))
    blk.wait("vector", "mm", ctx.cnt["mm"])
    blk.vf(lambda e: e.tensor_scalar(
        S["VSH"][:, 0:8], pb[0][0:65, 0:8], 0.5, None, ALU.mult))
    blk.vf(lambda e: e.tensor_scalar(
        S["VSH"][:, 8:16], pb[1][0:65, 0:8], 0.5, None, ALU.mult))
    blk.emit()

    for X in (S["QB"], S["KB"]):
        blk = Blk(ctx)
        pr = {}
        for c in range(NCH):
            for sl in range(4):
                i = c * 4 + sl
                bk = i % 2
                if pr.get(bk):
                    blk.wait("tensor", "ev", pr[bk])
                v = blk.mm(pb[bk][:, :], S["PERM"][:, :],
                           X[:, c * T + sl * TB:c * T + (sl + 1) * TB],
                           start=True, stop=True, inc="mm")
                blk.wait("vector", "mm", v)
                blk.add("vector", lambda e, X=X, c=c, sl=sl: e.tensor_mul(
                    S["TF1"][:, :],
                    X[:, c * T + sl * TB:c * T + (sl + 1) * TB],
                    S["CT"][:, sl * TB:(sl + 1) * TB]))
                blk.add("vector", lambda e, bk=bk, sl=sl: e.tensor_mul(
                    S["TF2"][:, :], pb[bk][:, :],
                    S["ST"][:, sl * TB:(sl + 1) * TB]))
                pr[bk] = blk.add("vector", lambda e, X=X, c=c, sl=sl:
                                 e.tensor_add(
                                     X[:, c * T + sl * TB:
                                       c * T + (sl + 1) * TB],
                                     S["TF1"][:, :], S["TF2"][:, :]),
                                 inc="ev")
        blk.emit()

    for tb in range(NTB):
        blk0 = Blk(ctx)
        blk0.dma("gpsimd", S["XF"][:, :].rearrange("p (c n) -> p c n", c=NCH),
                 x2T[:, tb * TB:(tb + 1) * TB].rearrange(
                     "(c p) n -> p c n", c=NCH), "dg")
        blk0.wait("gpsimd", "dg", ctx.cnt["dg"])
        blk0.emit()

        blk = Blk(ctx)
        AO = S["AO"]
        ea_t, mm_t, pv_t = {}, {}, {}
        ao_t, cp_t = {}, {}
        g = -1
        nkt = 16

        def emit_pv(h, kt, g):
            blk.wait("tensor", "ea", ea_t[g])
            if kt == 0 and h >= 2:
                blk.wait("tensor", "ev", ao_t[h - 2])
            pv_t[g] = blk.mm(pb[4 + h % 2][0:65, :],
                             S["VT"][:, (kt * 16 + h) * 80:
                                     (kt * 16 + h) * 80 + 65],
                             S["PT"][g % 2][:, :], start=(kt == 0),
                             stop=(kt == nkt - 1), inc="mm")

        for h in range(H):
            hc, hr = h // 2, (h % 2) * 64
            for kt in range(nkt):
                g += 1
                sb_ = pb[g % 2]
                if g >= 2:
                    blk.wait("tensor", "ea", ea_t[g - 2])
                mm_t[g] = blk.mm(
                    sb_[:, :],
                    S["KB"][hr:hr + 64, hc * T + kt * 128:
                            hc * T + (kt + 1) * 128],
                    S["QB"][hr:hr + 64, hc * T + tb * TB:
                            hc * T + (tb + 1) * TB],
                    start=True, stop=True, inc="mm")
                pt = S["PT"][g % 2]
                blk.wait("scalar", "mm", mm_t[g])
                if g >= 2:
                    blk.wait("scalar", "mm", pv_t[g - 2])
                ea_t[g] = blk.add("scalar", lambda e, sb_=sb_, pt=pt:
                                  e.activation(pt[:, :], sb_[:, :], AF.Square,
                                               bias=S["TAYB"][:, 0:1],
                                               scale=TAY_A), inc="ea")
                if kt >= 1:
                    emit_pv(h, kt - 1, g - 1)
            emit_pv(h, nkt - 1, g)
            ob = pb[4 + h % 2]
            rb = pb[6 + h % 2]
            blk.wait("scalar", "mm", pv_t[g])
            if h >= 2:
                blk.wait("scalar", "ev", ao_t[h - 2])
            vcol = (h % 2) * 8 + h // 2
            nv = blk.add("scalar", lambda e, ob=ob, h=h, vcol=vcol:
                         e.activation(S["NUM"][h % 2][:, :], ob[0:64, :],
                                      AF.Identity,
                                      bias=S["VSH"][0:64, vcol:vcol + 1]),
                         inc="ea")
            blk.wait("vector", "mm", pv_t[g])
            rdv = blk.vf(lambda e, ob=ob, h=h: e.tensor_scalar(
                S["RD"][h % 2][:, :], ob[64:65, :], 1024.0, None, ALU.add))
            rdv = blk.vf(lambda e, h=h: e.reciprocal(
                S["RD"][h % 2][:, :], S["RD"][h % 2][:, :]))
            blk.wait("tensor", "ev", rdv)
            if h >= 2:
                blk.wait("tensor", "ev", cp_t[h - 2])
            rbv = blk.mm(rb[0:64, :], C["ONESR"][0:1, 0:64],
                         S["RD"][h % 2][0:1, :], start=True, stop=True,
                         inc="mm")
            blk.wait("vector", "mm", rbv)
            cp_t[h] = blk.add("vector", lambda e, rb=rb, h=h: e.tensor_copy(
                S["RDB"][h % 2][:, :], rb[0:64, :]), inc="ev")
            blk.wait("vector", "ea", nv)
            ao_t[h] = blk.add("vector", lambda e, h=h, hc=hc, hr=hr:
                              e.tensor_mul(
                                  AO[hr:hr + 64, hc * TB:(hc + 1) * TB],
                                  S["NUM"][h % 2][:, :],
                                  S["RDB"][h % 2][:, :]), inc="ev")
        blk.emit()

        blk = Blk(ctx)
        wo_specs = [wo[:, oc * 128:(oc + 1) * 128].rearrange(
            "(c p) n -> p c n", c=NCH) for oc in range(NCH)]
        st_t = {}

        def wo_evac(blk, i, mv, tb=tb):
            blk.wait("vector", "mm", mv)
            if i >= 2:
                blk.wait("vector", "ds", st_t[i - 2])
            tfo = S["TF1"] if i % 2 == 0 else S["TF3"]
            v = blk.add("vector", lambda e, i=i, tfo=tfo: e.tensor_add(
                tfo[:, :], pb[i % 4][:, :], S["XF"][:, i * TB:(i + 1) * TB]),
                inc="ev")
            blk.wait("sync", "ev", v)
            st_t[i] = blk.dma(
                "sync", x3T[i * 128:(i + 1) * 128, tb * TB:(tb + 1) * TB],
                tfo[:, :], "ds")
            return ("ev", v)

        stream_mm(ctx, blk, wo_specs, S["WS"], NCH,
                  moving=lambda i, kc: AO[:, kc * TB:(kc + 1) * TB],
                  psum_of=lambda i: pb[i % 4][:, :], evac=wo_evac)
        blk.emit()

    if "dbgC" in io:
        blk = Blk(ctx)
        for nm, hd in io["dbgC"].items():
            blk.dma("sync", hd[:, :], S[nm][:, :] if nm != "AO"
                    else S["AO"][:, :], "ds")
        blk.emit()

    blk = Blk(ctx)
    blk.wait("sync", "ds", ctx.cnt["ds"])
    blk.emit()


C_SHAPES = {"L1_attn_ln_w": [1024], "L1_attn_ln_b": [1024],
            "L1_attn_w_q": [1024, 1024], "L1_attn_w_kv": [1024, 2048],
            "L1_attn_w_o": [1024, 1024], "L1_attn_theta": [1]}


def declare_stage_C_io(nc, io):
    for name, shape in C_SHAPES.items():
        io[name] = nc.dram_tensor(name, shape, F32, kind="ExternalInput")
    if os.environ.get("K_DEBUG_OUT", "0") == "1":
        io["dbgC"] = {}
        for nm, shape, dt in [("CT", [128, T], BF16), ("ST", [128, T], BF16),
                              ("QB", [128, NCH * T], BF16),
                              ("KB", [128, NCH * T], BF16),
                              ("VSH", [65, 16], F32),
                              ("VT", [128, 16 * 16 * 80], BF16),
                              ("XLN2", [128, NCH * TB], BF16),
                              ("AO", [128, NCH * TB], BF16)]:
            io["dbgC"][nm] = nc.dram_tensor("dbgC_" + nm, shape, dt,
                                            kind="ExternalOutput")


def stage_C_in_map(x2_batch, inputs):
    m = {}
    if x2_batch is not None:
        m["x2T"] = np.ascontiguousarray(x2_batch.T.astype(np.float32))
    for k in C_SHAPES:
        if k == "L1_attn_theta":
            m[k] = np.asarray(inputs[k], np.float32).reshape(1)
        else:
            m[k] = np.asarray(inputs[k], np.float32)
    for k, v in CONSTS.items():
        m["c_" + k] = v
    return m


def build_program_C():
    nc = bass.Bass("TRN2", target_bir_lowering=False)
    io = {}
    io["x2T"] = nc.dram_tensor("x2T", [D, T], F32, kind="ExternalInput")
    io["x3T"] = nc.dram_tensor("x3T", [D, T], F32, kind="ExternalOutput")
    declare_stage_C_io(nc, io)
    for k, v in CONSTS.items():
        io["c_" + k] = nc.dram_tensor("c_" + k, list(v.shape),
                                      BF16 if v.dtype == BF else F32,
                                      kind="ExternalInput")
    ctx = Ctx(nc)
    for s in ("dg", "ds", "mm", "ev", "ea", "eg"):
        ctx.add_sem(s)
    C = alloc_consts(nc, CORE_CONSTS)
    S = alloc_stage_C(nc)
    pb = [nc.alloc_psum_tensor(f"pb{i}", [128, TB], F32) for i in range(8)]
    blk = Blk(ctx)
    load_consts(blk, C, {k: io["c_" + k] for k in CONSTS})
    blk.wait("sync", "ds", ctx.cnt["ds"])
    blk.emit()
    emit_stage_C(ctx, S, C, pb, io)
    return nc



def alloc_stage_D(nc):
    sb = nc.alloc_sbuf_tensor
    S = {}
    A2 = sb("D_A2", [128, 2 * NCH * TB], F32)
    S["XF"] = A2.ap()[:, 0:NCH * TB]
    S["XB"] = A2.ap().bitcast(BF16)[:, 2 * NCH * TB:3 * NCH * TB]
    S["SQ"] = A2.ap().bitcast(BF16)[:, 3 * NCH * TB:4 * NCH * TB]
    S["MU"] = sb("D_MU", [128, TB], F32)
    S["RSTD"] = sb("D_RSTD", [128, TB], F32)
    S["TF1"] = sb("D_TF1", [128, TB], F32)
    S["TF2"] = sb("D_TF2", [128, TB], F32)
    S["XLND"] = sb("D_XLND", [128, NCH * T], BF16)
    S["HALL"] = sb("D_HALL", [128, 16 * 4 * TB], BF16)
    S["SHH"] = sb("D_SHH", [128, NCH * TB], BF16)
    S["TM1"] = sb("D_TM1", [128, TB], BF16)
    S["TM2"] = sb("D_TM2", [128, TB], BF16)
    S["TM3"] = sb("D_TM3", [128, TB], BF16)
    S["GWTF"] = sb("D_GWTF", [128, NCH * 16], F32)
    S["LGT"] = sb("D_LGT", [128, 16 * 16], F32)
    S["GTW"] = sb("D_GTW", [128, 16 * 16], F32)
    S["GTE"] = sb("D_GTE", [16, T], F32)
    S["GTEB"] = sb("D_GTEB", [16, T], BF16)
    S["EBB"] = sb("D_EBB", [128, 16], F32)
    S["T16A"] = sb("D_T16A", [128, 16], F32)
    S["T16B"] = sb("D_T16B", [128, 16], F32)
    S["TC1"] = sb("D_TC1", [128, 1], F32)
    S["TC2"] = sb("D_TC2", [128, 1], F32)
    S["B1SD"] = sb("D_B1SD", [128, 64], F32)
    S["B3SD"] = sb("D_B3SD", [128, 64], F32)
    S["SB1D"] = sb("D_SB1D", [128, 8], F32)
    S["SB3D"] = sb("D_SB3D", [128, 8], F32)
    S["B2SB"] = sb("D_B2SB", [16, D], BF16)
    S["SB2R"] = sb("D_SB2R", [1, D], BF16)
    S["ONER"] = sb("D_ONER", [1, TB], BF16)
    S["GBC"] = sb("D_GBC", [128, TB], F32)
    S["GROW"] = sb("D_GROW", [1, TB], F32)
    S["WS"] = [sb(f"D_WS{i}", [128, NCH * 128], BF16) for i in range(2)]
    S["OT"] = [sb(f"D_OT{i}", [128, TB], F32) for i in range(2)]
    S["STG"] = sb("D_STG", [128, NCH * 128], BF16)
    S["STGF"] = sb("D_STGF", [128, 5 * 128], F32)
    return S


def emit_prep_D(ctx, S, C, pb, io):
    nc = ctx.nc
    w1p, w3p, w2p = io["w1_pair"], io["w3_pair"], io["w2_pair"]
    shm = io["sh_one"]
    myw1t, myw3t, myw2t, mysht = (io["scr_myw1t"], io["scr_myw3t"],
                                  io["scr_myw2t"], io["scr_mysht"])
    allw1t, allw3t, allw2t, allsh = (io["scr_allw1t"], io["scr_allw3t"],
                                     io["scr_allw2t"], io["scr_allsh"])
    blk = Blk(ctx)
    jobs = []
    for le in range(2):
        jobs.append((lambda rb, le=le: w1p[le, :, rb * 128:(rb + 1) * 128]
                     .rearrange("(c p) n -> p c n", c=4),
                     myw1t, le * 1024, 8, 4))
        jobs.append((lambda rb, le=le: w3p[le, :, rb * 128:(rb + 1) * 128]
                     .rearrange("(c p) n -> p c n", c=4),
                     myw3t, le * 1024, 8, 4))
        jobs.append((lambda rb, le=le: w2p[le, :, rb * 128:(rb + 1) * 128]
                     .rearrange("(c p) n -> p c n", c=8),
                     myw2t, le * 512, 4, 8))
    jobs.append((lambda rb: shm[:, rb * 128:(rb + 1) * 128]
                 .rearrange("(c p) n -> p c n", c=8),
                 mysht, 0, 8, 8))
    st_t = {}
    prev_ev = {}
    i = 0
    for (srcf, dstd, dro, nrb, nfc) in jobs:
        for rb in range(nrb):
            buf = S["WS"][i % 2]
            obuf = S["STG"]
            if i >= 2:
                blk.wait("gpsimd", "mm", st_t[i - 2][0])
            ld = blk.dma("gpsimd", buf[:, 0:nfc * 128].rearrange(
                "p (c n) -> p c n", c=nfc), srcf(rb), "dg")
            blk.wait("tensor", "dg", ld)
            if i >= 1:
                blk.wait("vector", "ds", st_t[i - 1][1])
            lastmm = None
            vv = None
            for c in range(nfc):
                pbb = pb[4 + c % 2]
                if prev_ev.get(c % 2):
                    blk.wait("tensor", "ev", prev_ev[c % 2])
                lastmm = blk.tr(pbb.bitcast(BF16)[:, 0:128],
                                buf[:, c * 128:(c + 1) * 128],
                                C["IDB"][:, :], inc="mm")
                blk.wait("vector", "mm", lastmm)
                vv = blk.add("vector", lambda e, obuf=obuf, c=c, pbb=pbb:
                             e.tensor_copy(obuf[:, c * 128:(c + 1) * 128],
                                           pbb.bitcast(BF16)[:, 0:128]),
                             inc="ev")
                prev_ev[c % 2] = vv
            blk.wait("sync", "ev", vv)
            sv = blk.dma("sync",
                         dstd[dro + rb * 128:dro + (rb + 1) * 128,
                              0:nfc * 128],
                         obuf[:, 0:nfc * 128], "ds")
            st_t[i] = (lastmm, sv)
            i += 1
    blk.wait("gpsimd", "ds", ctx.cnt["ds"])
    for (src, dst) in [(myw1t, allw1t), (myw3t, allw3t), (myw2t, allw2t),
                       (mysht, allsh)]:
        blk.add("gpsimd", lambda e, src=src, dst=dst: e.collective_compute(
            "AllGather", ALU.bypass, replica_groups=RG8,
            ins=[src.ap().opt()], outs=[dst.ap().opt()]), inc="cc")
    blk.wait("gpsimd", "cc", ctx.cnt["cc"])
    blk.emit()


def emit_stage_D(ctx, S, C, pb, io):
    nc = ctx.nc
    x3T, outT = io["x3T"], io["outT"]
    allw1t, allw3t, allw2t, allsh = (io["scr_allw1t"], io["scr_allw3t"],
                                     io["scr_allw2t"], io["scr_allsh"])
    gted = io["scr_gted"]

    blk = Blk(ctx)
    gw_ev = {}
    for cb in range(NCH):
        ld = blk.dma("gpsimd", S["STGF"][0:16, 0:128],
                     io["L1_ffn_gate_w"][:, cb * 128:(cb + 1) * 128], "dg")
        blk.wait("tensor", "dg", ld)
        if cb >= 1:
            blk.wait("tensor", "ev", gw_ev[cb - 1])
        v = blk.tr(pb[0][:, 0:16], S["STGF"][0:16, 0:128],
                   C["IDF"][0:16, 0:16], inc="mm")
        blk.wait("vector", "mm", v)
        gw_ev[cb] = blk.add("vector", lambda e, cb=cb: e.tensor_copy(
            S["GWTF"][:, cb * 16:(cb + 1) * 16], pb[0][:, 0:16]), inc="ev")
        blk.wait("gpsimd", "ev", gw_ev[cb])
    ld = blk.dma("gpsimd", S["STGF"][0:64, 0:128],
                 io["L1_ffn_b1"][:, :].rearrange("a (b n) -> (a b) n", b=4),
                 "dg")
    ld = blk.dma("gpsimd", S["STGF"][0:64, 128:256],
                 io["L1_ffn_b3"][:, :].rearrange("a (b n) -> (a b) n", b=4),
                 "dg")
    ld = blk.dma("gpsimd", S["STGF"][0:8, 256:384],
                 io["L1_ffn_sb1"][:].rearrange("(a n) -> a n", n=128), "dg")
    ld = blk.dma("gpsimd", S["STGF"][0:8, 384:512],
                 io["L1_ffn_sb3"][:].rearrange("(a n) -> a n", n=128), "dg")
    ld = blk.dma("gpsimd", S["B2SB"][:, :], io["L1_ffn_b2"][:, :], "dg")
    ld = blk.dma("gpsimd", S["SB2R"][:, :], io["L1_ffn_sb2"][None, :], "dg")
    ld = blk.dma("gpsimd", S["STGF"][0:1, 512:528],
                 io["L1_ffn_expert_bias"][None, :], "dg")
    blk.add("gpsimd", lambda e: e.memset(S["ONER"][:, :], 1.0))
    blk.wait("tensor", "dg", ctx.cnt["dg"])
    v = blk.tr(pb[1][:, 0:64], S["STGF"][0:64, 0:128], C["IDF"][0:64, 0:64])
    v = blk.tr(pb[2][:, 0:64], S["STGF"][0:64, 128:256],
               C["IDF"][0:64, 0:64])
    v = blk.tr(pb[3][:, 0:8], S["STGF"][0:8, 256:384], C["IDF"][0:8, 0:8])
    v = blk.tr(pb[6][:, 0:8], S["STGF"][0:8, 384:512],
               C["IDF"][0:8, 0:8], inc="mm")
    vb = blk.mm(pb[7][:, 0:16], C["ONESR"][0:1, :], S["STGF"][0:1, 512:528],
                start=True, stop=True, inc="mm")
    blk.wait("vector", "mm", vb)
    blk.vf(lambda e: e.tensor_copy(S["B1SD"][:, :], pb[1][:, 0:64]))
    blk.vf(lambda e: e.tensor_copy(S["B3SD"][:, :], pb[2][:, 0:64]))
    blk.vf(lambda e: e.tensor_copy(S["SB1D"][:, :], pb[3][:, 0:8]))
    blk.vf(lambda e: e.tensor_copy(S["SB3D"][:, :], pb[6][:, 0:8]))
    blk.vf(lambda e: e.tensor_copy(S["EBB"][:, :], pb[7][:, 0:16]))
    blk.emit()

    for tb in range(NTB):
        blk0 = Blk(ctx)
        blk0.dma("gpsimd", S["XF"][:, :].rearrange("p (c n) -> p c n", c=NCH),
                 x3T[:, tb * TB:(tb + 1) * TB].rearrange("(c p) n -> p c n",
                                                         c=NCH), "dg")
        blk0.wait("gpsimd", "dg", ctx.cnt["dg"])
        blk0.emit()

        def logits_cb(blk, c, tf1, fv, tb=tb):
            blk.wait("tensor", "ev", fv)
            v = None
            for tt in range(4):
                last = (c == NCH - 1)
                v = blk.mm(pb[2 + tt][:, 0:16],
                           tf1[:, tt * 128:(tt + 1) * 128],
                           S["GWTF"][:, c * 16:(c + 1) * 16],
                           start=(c == 0), stop=last,
                           inc=("mm" if tt == 3 else None))
            return v

        emit_ln(ctx, S, C, pb, S["XF"], None,
                dst_fn=lambda c, tb=tb: S["XLND"][:, c * T + tb * TB:
                                                  c * T + (tb + 1) * TB],
                fp32_cb=logits_cb)

        blk = Blk(ctx)
        blk.wait("vector", "mm", ctx.cnt["mm"])
        for tt in range(4):
            blk.add("vector", lambda e, tt=tt, tb=tb: e.tensor_copy(
                S["LGT"][:, (tb * 4 + tt) * 16:(tb * 4 + tt + 1) * 16],
                pb[2 + tt][:, 0:16]), inc="ev")
        for tt in range(4):
            j = tb * 4 + tt
            LG = S["LGT"][:, j * 16:(j + 1) * 16]
            blk.vf(lambda e, LG=LG: e.tensor_add(
                S["T16A"][:, :], LG, S["EBB"][:, :]))
            blk.vf(lambda e: e.reduce_max(S["TC1"][:, :], S["T16A"][:, :],
                                          axis=AX.X))
            blk.vf(lambda e: e.tensor_scalar(
                S["T16B"][:, :], S["T16A"][:, :], S["TC1"][:, 0:1], None,
                ALU.is_equal))
            blk.vf(lambda e: e.scalar_tensor_tensor(
                S["T16A"][:, :], S["T16B"][:, :], -1e30, S["T16A"][:, :],
                ALU.mult, ALU.add))
            blk.vf(lambda e: e.reduce_max(S["TC1"][:, :], S["T16A"][:, :],
                                          axis=AX.X))
            blk.vf(lambda e: e.tensor_scalar(
                S["T16A"][:, :], S["T16A"][:, :], S["TC1"][:, 0:1], None,
                ALU.is_equal))
            mv = blk.vf(lambda e: e.tensor_add(
                S["T16B"][:, :], S["T16B"][:, :], S["T16A"][:, :]))
            blk.wait("scalar", "ev", mv)
            av = blk.add("scalar", lambda e, LG=LG: e.activation(
                S["T16A"][:, :], LG, AF.Sigmoid), inc="ea")
            blk.wait("vector", "ea", av)
            blk.vf(lambda e: e.tensor_mul(
                S["T16A"][:, :], S["T16A"][:, :], S["T16B"][:, :]))
            blk.vf(lambda e: e.reduce_sum(S["TC1"][:, :], S["T16A"][:, :],
                                          axis=AX.X))
            blk.vf(lambda e: e.tensor_scalar(
                S["TC1"][:, :], S["TC1"][:, :], 1e-10, None, ALU.add))
            blk.vf(lambda e: e.reciprocal(S["TC2"][:, 0:1],
                                          S["TC1"][:, 0:1]))
            blk.vf(lambda e, j=j: e.tensor_scalar(
                S["GTW"][:, j * 16:(j + 1) * 16], S["T16A"][:, :],
                S["TC2"][:, 0:1], None, ALU.mult))
        blk.emit()

    blk = Blk(ctx)
    pr = {}
    for j in range(16):
        bk = j % 2
        if pr.get(bk):
            blk.wait("tensor", "ev", pr[bk])
        v = blk.tr(pb[bk][0:16, 0:128], S["GTW"][:, j * 16:(j + 1) * 16],
                   C["IDF"][:, :], inc="mm")
        blk.wait("vector", "mm", v)
        pr[bk] = blk.add("vector", lambda e, j=j, bk=bk: e.tensor_copy(
            S["GTE"][0:16, j * 128:(j + 1) * 128], pb[bk][0:16, 0:128]),
            inc="ev")
    blk.wait("vector", "ev", ctx.cnt["ev"])
    v = blk.add("vector", lambda e: e.tensor_copy(S["GTEB"][:, :],
                                                  S["GTE"][:, :]), inc="ev")
    blk.wait("sync", "ev", v)
    blk.dma("sync", gted[:, :], S["GTE"][:, :], "ds")
    blk.wait("sync", "ds", ctx.cnt["ds"])
    blk.emit()

    for tb in range(NTB):
        blk0 = Blk(ctx)
        blk0.dma("gpsimd", S["XF"][:, :].rearrange("p (c n) -> p c n", c=NCH),
                 x3T[:, tb * TB:(tb + 1) * TB].rearrange("(c p) n -> p c n",
                                                         c=NCH), "dg")
        blk0.wait("gpsimd", "dg", ctx.cnt["dg"])
        blk0.emit()
        blk = Blk(ctx)
        specs = []
        for e in range(16):
            for oc in range(4):
                specs.append(("w1", e, oc))
                specs.append(("w3", e, oc))
        for oc in range(NCH):
            specs.append(("sw1", 0, oc))
            specs.append(("sw3", 0, oc))

        gb_ev = [None]
        pair_ev = [None]

        def spec_src(i):
            kind, e, oc = specs[i]
            if kind == "w1":
                return allw1t[e * 1024:(e + 1) * 1024,
                              oc * 128:(oc + 1) * 128].rearrange(
                    "(c p) n -> p c n", c=NCH)
            if kind == "w3":
                return allw3t[e * 1024:(e + 1) * 1024,
                              oc * 128:(oc + 1) * 128].rearrange(
                    "(c p) n -> p c n", c=NCH)
            base = 0 if kind == "sw1" else 1024
            return allsh[base:base + 1024,
                         oc * 128:(oc + 1) * 128].rearrange(
                "(c p) n -> p c n", c=NCH)

        def a_evac(blk, i, mv, tb=tb):
            kind, e, oc = specs[i]
            if kind == "w1":
                blk.wait("scalar", "mm", mv)
                if pair_ev[0] is not None:
                    blk.wait("scalar", "ev", pair_ev[0])
                return ("ea", blk.add("scalar", lambda e_, e=e, oc=oc, i=i:
                                      e_.activation(
                                          S["TM1"][:, :], pb[i % 4][:, :],
                                          AF.Silu,
                                          bias=S["B1SD"][:, e * 4 + oc:
                                                         e * 4 + oc + 1]),
                                      inc="ea"))
            if kind == "sw1":
                blk.wait("scalar", "mm", mv)
                if pair_ev[0] is not None:
                    blk.wait("scalar", "ev", pair_ev[0])
                return ("ea", blk.add("scalar", lambda e_, oc=oc, i=i:
                                      e_.activation(
                                          S["TM1"][:, :], pb[i % 4][:, :],
                                          AF.Silu,
                                          bias=S["SB1D"][:, oc:oc + 1]),
                                      inc="ea"))
            if kind == "sw3":
                blk.wait("scalar", "mm", mv)
                v = blk.add("scalar", lambda e_, oc=oc, i=i: e_.activation(
                    S["TM2"][:, :], pb[i % 4][:, :], AF.Identity,
                    bias=S["SB3D"][:, oc:oc + 1]), inc="ea")
                blk.wait("vector", "ea", v)
                rr = blk.add(
                    "vector", lambda e_, oc=oc: e_.tensor_mul(
                        S["SHH"][:, oc * TB:(oc + 1) * TB],
                        S["TM1"][:, :], S["TM2"][:, :]), inc="ev")
                pair_ev[0] = rr
                return ("ev", rr)
            blk.wait("scalar", "mm", mv)
            v = blk.add("scalar", lambda e_, e=e, oc=oc, i=i: e_.activation(
                S["TM2"][:, :], pb[i % 4][:, :], AF.Identity,
                bias=S["B3SD"][:, e * 4 + oc:e * 4 + oc + 1]), inc="ea")
            if oc == 0:
                gl = blk.dma("gpsimd", S["GROW"][:, :],
                             gted[e:e + 1, tb * TB:(tb + 1) * TB], "dg")
                blk.wait("tensor", "dg", gl)
                if gb_ev[0] is not None:
                    blk.wait("tensor", "ev", gb_ev[0])
                gv = blk.mm(pb[6][:, :], C["ONESR"][0:1, :],
                            S["GROW"][0:1, :], start=True, stop=True,
                            inc="mm")
                blk.wait("vector", "mm", gv)
                gb_ev[0] = blk.add("vector", lambda e_: e_.tensor_copy(
                    S["GBC"][:, :], pb[6][:, :]), inc="ev")
            blk.wait("vector", "ea", v)
            blk.add("vector", lambda e_: e_.tensor_mul(
                S["TM3"][:, :], S["TM1"][:, :], S["TM2"][:, :]))
            rr = blk.add(
                "vector", lambda e_, e=e, oc=oc: e_.tensor_mul(
                    S["HALL"][:, (e * 4 + oc) * TB:(e * 4 + oc + 1) * TB],
                    S["TM3"][:, :], S["GBC"][:, :]), inc="ev")
            pair_ev[0] = rr
            return ("ev", rr)

        stream_mm(ctx, blk, [spec_src(i) for i in range(len(specs))],
                  S["WS"], NCH,
                  moving=lambda i, kc, tb=tb: S["XLND"][:, kc * T + tb * TB:
                                                        kc * T + tb * TB + TB],
                  psum_of=lambda i: pb[i % 4][:, :], evac=a_evac)
        blk.emit()

        blk = Blk(ctx)
        st_t = {}
        ld_t = {}
        mm_t = {}
        for dch in range(NCH):
            bank = dch % 2
            for e in range(17):
                i = dch * 17 + e
                buf = S["WS"][i % 2]
                if i >= 2:
                    blk.wait("gpsimd", "mm", mm_t[i - 2])
                if e < 16:
                    srcw = allw2t[e * 512:(e + 1) * 512,
                                  dch * 128:(dch + 1) * 128].rearrange(
                        "(c p) n -> p c n", c=4)
                    nfc = 4
                else:
                    srcw = allsh[2048 + 0:2048 + 1024,
                                 dch * 128:(dch + 1) * 128].rearrange(
                        "(c p) n -> p c n", c=NCH)
                    nfc = NCH
                ld = blk.dma("gpsimd", buf[:, 0:nfc * 128].rearrange(
                    "p (c n) -> p c n", c=nfc), srcw, "dg")
                blk.wait("tensor", "dg", ld)
                if e == 0 and dch >= 2:
                    blk.wait("tensor", "ev", st_t[dch - 2][0])
                for fc in range(nfc):
                    if e < 16:
                        movs = S["HALL"][:, (e * 4 + fc) * TB:
                                         (e * 4 + fc + 1) * TB]
                    else:
                        movs = S["SHH"][:, fc * TB:(fc + 1) * TB]
                    mm_t[i] = blk.mm(pb[bank][:, :],
                                     buf[:, fc * 128:(fc + 1) * 128],
                                     movs, start=(e == 0 and fc == 0),
                                     stop=False,
                                     inc=("mm" if fc == nfc - 1 else None))
            blk.mm(pb[bank][:, :], S["B2SB"][:, dch * 128:(dch + 1) * 128],
                   S["GTEB"][:, tb * TB:(tb + 1) * TB],
                   start=False, stop=False)
            v = blk.mm(pb[bank][:, :],
                       S["SB2R"][0:1, dch * 128:(dch + 1) * 128],
                       S["ONER"][0:1, :], start=False, stop=True, inc="mm")
            blk.wait("vector", "mm", v)
            if dch >= 2:
                blk.wait("vector", "ds", st_t[dch - 2][1])
            ot = S["OT"][dch % 2]
            vv = blk.add("vector", lambda e_, dch=dch, bank=bank, ot=ot:
                         e_.tensor_add(ot[:, :], pb[bank][:, :],
                                       S["XF"][:, dch * TB:(dch + 1) * TB]),
                         inc="ev")
            blk.wait("sync", "ev", vv)
            sv = blk.dma("sync",
                         outT[dch * 128:(dch + 1) * 128,
                              tb * TB:(tb + 1) * TB], ot[:, :], "ds")
            st_t[dch] = (vv, sv)
        blk.emit()

    if "dbgD" in io:
        blk = Blk(ctx)
        for nm, hd in io["dbgD"].items():
            blk.dma("sync", hd[:, :], S[nm][:, :], "ds")
        blk.emit()

    blk = Blk(ctx)
    blk.wait("sync", "ds", ctx.cnt["ds"])
    blk.wait("gpsimd", "dg", ctx.cnt["dg"])
    blk.emit()


D_SHAPES = {"L1_ffn_gate_w": [16, 1024], "L1_ffn_expert_bias": [16],
            "L1_ffn_b1": [16, 512], "L1_ffn_b3": [16, 512],
            "L1_ffn_b2": [16, 1024], "L1_ffn_sb1": [1024],
            "L1_ffn_sb3": [1024], "L1_ffn_sb2": [1024],
            "w1_pair": [2, 512, 1024], "w3_pair": [2, 512, 1024],
            "w2_pair": [2, 1024, 512], "sh_one": [1024, 1024]}


def declare_stage_D_io(nc, io):
    for name, shape in D_SHAPES.items():
        io[name] = nc.dram_tensor(name, shape, F32, kind="ExternalInput")
    if os.environ.get("K_DEBUG_OUT", "0") == "1":
        io["dbgD"] = {}
        for nm, shape, dt in [("GTW", [128, 256], F32),
                              ("LGT", [128, 256], F32),
                              ("GWTF", [128, 128], F32),
                              ("EBB", [128, 16], F32),
                              ("B1SD", [128, 64], F32),
                              ("SB1D", [128, 8], F32),
                              ("SHH", [128, NCH * TB], BF16),
                              ("HALL", [128, 64 * TB], BF16),
                              ("XLND", [128, NCH * T], BF16)]:
            io["dbgD"][nm] = nc.dram_tensor("dbgD_" + nm, shape, dt,
                                            kind="ExternalOutput")
    io["scr_myw1t"] = nc.dram_tensor("scr_myw1t", [2048, 512], BF16)
    io["scr_myw3t"] = nc.dram_tensor("scr_myw3t", [2048, 512], BF16)
    io["scr_myw2t"] = nc.dram_tensor("scr_myw2t", [1024, 1024], BF16)
    io["scr_mysht"] = nc.dram_tensor("scr_mysht", [1024, 1024], BF16)
    io["scr_allw1t"] = nc.dram_tensor("scr_allw1t", [16384, 512], BF16)
    io["scr_allw3t"] = nc.dram_tensor("scr_allw3t", [16384, 512], BF16)
    io["scr_allw2t"] = nc.dram_tensor("scr_allw2t", [8192, 1024], BF16)
    io["scr_allsh"] = nc.dram_tensor("scr_allsh", [8192, 1024], BF16)
    io["scr_gted"] = nc.dram_tensor("scr_gted", [16, T], F32)


def stage_D_in_map(core, x3_batch, inputs):
    m = {}
    if x3_batch is not None:
        m["x3T"] = np.ascontiguousarray(x3_batch.T.astype(np.float32))
    for k in ("L1_ffn_gate_w", "L1_ffn_expert_bias", "L1_ffn_b1",
              "L1_ffn_b3", "L1_ffn_b2", "L1_ffn_sb1", "L1_ffn_sb3",
              "L1_ffn_sb2"):
        m[k] = np.asarray(inputs[k], np.float32)
    m["w1_pair"] = np.ascontiguousarray(
        np.asarray(inputs["L1_ffn_w1"], np.float32)[2 * core:2 * core + 2])
    m["w3_pair"] = np.ascontiguousarray(
        np.asarray(inputs["L1_ffn_w3"], np.float32)[2 * core:2 * core + 2])
    m["w2_pair"] = np.ascontiguousarray(
        np.asarray(inputs["L1_ffn_w2"], np.float32)[2 * core:2 * core + 2])
    if core == 0:
        sh = np.asarray(inputs["L1_ffn_sw1"], np.float32)
    elif core == 1:
        sh = np.asarray(inputs["L1_ffn_sw3"], np.float32)
    elif core == 2:
        sh = np.asarray(inputs["L1_ffn_sw2"], np.float32)
    else:
        sh = np.zeros((1024, 1024), np.float32)
    m["sh_one"] = np.ascontiguousarray(sh)
    for k, v in CONSTS.items():
        m["c_" + k] = v
    return m


def build_program_D():
    nc = bass.Bass("TRN2", target_bir_lowering=False)
    io = {}
    io["x3T"] = nc.dram_tensor("x3T", [D, T], F32, kind="ExternalInput")
    io["outT"] = nc.dram_tensor("outT", [D, T], F32, kind="ExternalOutput")
    declare_stage_D_io(nc, io)
    for k, v in CONSTS.items():
        io["c_" + k] = nc.dram_tensor("c_" + k, list(v.shape),
                                      BF16 if v.dtype == BF else F32,
                                      kind="ExternalInput")
    ctx = Ctx(nc)
    for s in ("dg", "ds", "mm", "ev", "ea", "cc"):
        ctx.add_sem(s)
    C = alloc_consts(nc, CORE_CONSTS)
    S = alloc_stage_D(nc)
    pb = [nc.alloc_psum_tensor(f"pb{i}", [128, TB], F32) for i in range(8)]
    blk = Blk(ctx)
    load_consts(blk, C, {k: io["c_" + k] for k in CONSTS})
    blk.wait("sync", "ds", ctx.cnt["ds"])
    blk.emit()
    emit_prep_D(ctx, S, C, pb, io)
    emit_stage_D(ctx, S, C, pb, io)
    return nc



A_WEIGHTS = ["L0_attn_W_DQ", "L0_attn_W_UQ", "L0_attn_W_DKV", "L0_attn_W_UK",
             "L0_attn_W_UV", "L0_attn_W_QR", "L0_attn_W_KR", "L0_attn_W_O"]
A_SHAPES = {"L0_attn_W_DQ": [1024, 4096], "L0_attn_W_UQ": [4096, 1024],
            "L0_attn_W_DKV": [1024, 256], "L0_attn_W_UK": [256, 1024],
            "L0_attn_W_UV": [256, 1024], "L0_attn_W_QR": [4096, 512],
            "L0_attn_W_KR": [1024, 32], "L0_attn_W_O": [1024, 1024]}


def build_program_A():
    nc = bass.Bass("TRN2", target_bir_lowering=False)
    io = {}
    io["xT"] = nc.dram_tensor("xT", [D, T], F32, kind="ExternalInput")
    io["x1T"] = nc.dram_tensor("x1T", [D, T], F32, kind="ExternalOutput")
    for name in A_WEIGHTS:
        io[name] = nc.dram_tensor(name, A_SHAPES[name], F32,
                                  kind="ExternalInput")
    for k, v in CONSTS.items():
        io["c_" + k] = nc.dram_tensor("c_" + k, list(v.shape),
                                      BF16 if v.dtype == BF else F32,
                                      kind="ExternalInput")
    io["scr_WDQT"] = nc.dram_tensor("scr_WDQT", [4096, 1024], BF16)
    io["scr_WDKVT"] = nc.dram_tensor("scr_WDKVT", [256, 1024], BF16)
    io["scr_M1"] = nc.dram_tensor("scr_M1", [1024, 1024], BF16)
    io["scr_M2"] = nc.dram_tensor("scr_M2", [1024, 512], BF16)
    io["scr_M3"] = nc.dram_tensor("scr_M3", [1024, 1024], BF16)
    io["scr_M4"] = nc.dram_tensor("scr_M4", [1024, 1024], BF16)

    ctx = Ctx(nc)
    for s in ("dg", "ds", "mm", "ev", "ea"):
        ctx.add_sem(s)
    C = alloc_consts(nc, A_CONSTS)
    S = alloc_stage_A(nc)
    pb = [nc.alloc_psum_tensor(f"pb{i}", [128, TB], F32) for i in range(8)]

    blk = Blk(ctx)
    load_consts(blk, C, {k: io["c_" + k] for k in CONSTS})
    blk.wait("sync", "ds", ctx.cnt["ds"])
    blk.emit()

    emit_stage_A(ctx, S, C, pb, io)
    return nc


def stage_A_in_map(x_batch, inputs):
    m = {"xT": np.ascontiguousarray(x_batch.T)}
    for k in A_WEIGHTS:
        m[k] = np.asarray(inputs[k], dtype=np.float32)
    for k, v in CONSTS.items():
        m["c_" + k] = v
    return m


B_SHAPES = {"L0_ffn_router_w": [1024, 8], "L0_ffn_router_b": [8],
            "w1_e": [1024, 4096], "b1_e": [4096],
            "w2_e": [4096, 1024], "b2_e": [1024],
            "sel8f": [64, 8], "sello": [64, 8], "selp": [64, 8],
            "tri8s": [8, 8], "sh8": [8, 8], "r4096": [8, 1],
            "rbasem1": [128, 8], "siota": [128, 20]}


def declare_stage_B_io(nc, io):
    dbg = os.environ.get("K_DEBUG_OUT", "0") == "1"
    kind = "ExternalOutput" if dbg else "Internal"
    for name, shape in B_SHAPES.items():
        io[name] = nc.dram_tensor(name, shape, F32, kind="ExternalInput")
    io["scr_myxln"] = nc.dram_tensor("scr_myxln", [T, D], BF16, kind=kind)
    io["scr_mycnt"] = nc.dram_tensor("scr_mycnt", [8, 1], F32)
    io["scr_cnt64"] = nc.dram_tensor("scr_cnt64", [64, 1], F32)
    io["scr_sendb"] = nc.dram_tensor("scr_sendb", [2 * T, D], BF16)
    io["scr_sball"] = nc.dram_tensor("scr_sball", [16 * T, D], BF16)
    io["scr_yd"] = nc.dram_tensor("scr_yd", [CAP, D], BF16)
    io["scr_yall"] = nc.dram_tensor("scr_yall", [8 * CAP, D], BF16)
    if dbg:
        io["dbg"] = {}
        for nm, shape, dt in [("CNT64", [64, 1], F32), ("GCUM", [8, T], F32),
                              ("IDXL1", [128, 16], I32),
                              ("IDXL2", [128, 16], I32),
                              ("IDXG", [128, 20], I32),
                              ("IDX1", [128, 16], I32),
                              ("IDX2", [128, 16], I32),
                              ("W1TOK", [128, 16], F32),
                              ("W2TOK", [128, 16], F32),
                              ("GT", [128, 128], F32),
                              ("LGT", [128, 128], F32),
                              ("MK1", [128, 128], F32),
                              ("MK2", [128, 128], F32),
                              ("G1", [128, D], BF16),
                              ("G2", [128, D], BF16),
                              ("BUFS", [128, NCH * CAP], BF16)]:
            io["dbg"][nm] = nc.dram_tensor("dbg_" + nm, shape, dt,
                                           kind="ExternalOutput")


def stage_B_in_map(core, x1_batch, inputs):
    m = {}
    if x1_batch is not None:
        m["x1T"] = np.ascontiguousarray(x1_batch.T.astype(np.float32))
    m["L0_ffn_router_w"] = np.asarray(inputs["L0_ffn_router_w"], np.float32)
    m["L0_ffn_router_b"] = np.asarray(inputs["L0_ffn_router_b"], np.float32)
    m["w1_e"] = np.ascontiguousarray(np.asarray(inputs["L0_ffn_w1"],
                                                np.float32)[core])
    m["b1_e"] = np.ascontiguousarray(np.asarray(inputs["L0_ffn_b1"],
                                                np.float32)[core])
    m["w2_e"] = np.ascontiguousarray(np.asarray(inputs["L0_ffn_w2"],
                                                np.float32)[core])
    m["b2_e"] = np.ascontiguousarray(np.asarray(inputs["L0_ffn_b2"],
                                                np.float32)[core])
    sel8f = np.zeros((64, 8), dtype=np.float32)
    sello = np.zeros((64, 8), dtype=np.float32)
    selp = np.zeros((64, 8), dtype=np.float32)
    for r in range(8):
        sel8f[8 * r + core, r] = 1.0
        for e in range(8):
            if e < core:
                sello[8 * r + e, r] = 1.0
            if r < core:
                selp[8 * r + e, e] = 1.0
    m["sel8f"] = sel8f
    m["sello"] = sello
    m["selp"] = selp
    m["tri8s"] = np.triu(np.ones((8, 8), np.float32), 1).copy()
    sh8 = np.zeros((8, 8), np.float32)
    for mm_ in range(1, 8):
        sh8[mm_ - 1, mm_] = 1.0
    m["sh8"] = sh8
    m["r4096"] = (np.arange(8, dtype=np.float32) * 4096).reshape(8, 1)
    rb = np.zeros((128, 8), np.float32)
    rb[:, :] = (np.arange(8) * CAP - 1.0)[None, :]
    m["rbasem1"] = rb
    si = np.zeros((128, 20), np.float32)
    si[:, :] = np.arange(128)[:, None] + 128 * np.arange(20)[None, :]
    m["siota"] = si
    for k, v in CONSTS.items():
        m["c_" + k] = v
    return m


def build_program_B():
    nc = bass.Bass("TRN2", target_bir_lowering=False)
    io = {}
    io["x1T"] = nc.dram_tensor("x1T", [D, T], F32, kind="ExternalInput")
    io["x2T"] = nc.dram_tensor("x2T", [D, T], F32, kind="ExternalOutput")
    declare_stage_B_io(nc, io)
    for k, v in CONSTS.items():
        io["c_" + k] = nc.dram_tensor("c_" + k, list(v.shape),
                                      BF16 if v.dtype == BF else F32,
                                      kind="ExternalInput")
    ctx = Ctx(nc)
    for s in ("dg", "ds", "mm", "ev", "ea", "cc"):
        ctx.add_sem(s)
    C = alloc_consts(nc, CORE_CONSTS)
    S = alloc_stage_B(nc)
    pb = [nc.alloc_psum_tensor(f"pb{i}", [128, TB], F32) for i in range(8)]
    blk = Blk(ctx)
    load_consts(blk, C, {k: io["c_" + k] for k in CONSTS})
    blk.wait("sync", "ds", ctx.cnt["ds"])
    blk.emit()
    emit_stage_B(ctx, S, C, pb, io)
    return nc



class _StageAlloc:

    def __init__(self, nc, stack):
        self.nc = nc
        self.stack = stack

    def alloc_sbuf_tensor(self, *a, **kw):
        return self.stack.enter_context(self.nc.sbuf_tensor(*a, **kw))


def _drain(ctx, bar=None):
    blk = Blk(ctx)
    for sem in ("dg", "ds", "cc"):
        if sem in ctx.cnt:
            blk.wait("gpsimd", sem, ctx.cnt[sem])
    if bar is not None:
        src, dst = bar
        blk.add("gpsimd", lambda e: e.collective_compute(
            "AllGather", ALU.bypass, replica_groups=RG8,
            ins=[src.ap().opt()], outs=[dst.ap().opt()]), inc="cc")
        blk.wait("gpsimd", "cc", ctx.cnt["cc"])
    blk.emit()


def build_program_fused():
    nc = bass.Bass("TRN2", target_bir_lowering=False)
    io = {}
    io["xT"] = nc.dram_tensor("xT", [D, T], F32, kind="ExternalInput")
    xkind = ("ExternalOutput" if os.environ.get("K_XOUT", "0") == "1"
             else "Internal")
    io["x1T"] = nc.dram_tensor("x1T", [D, T], F32, kind=xkind)
    io["x2T"] = nc.dram_tensor("x2T", [D, T], F32, kind=xkind)
    io["x3T"] = nc.dram_tensor("x3T", [D, T], F32, kind=xkind)
    io["outT"] = nc.dram_tensor("outT", [D, T], F32, kind="ExternalOutput")
    for name in A_WEIGHTS:
        io[name] = nc.dram_tensor(name, A_SHAPES[name], F32,
                                  kind="ExternalInput")
    io["scr_WDQT"] = nc.dram_tensor("scr_WDQT", [4096, 1024], BF16)
    io["scr_WDKVT"] = nc.dram_tensor("scr_WDKVT", [256, 1024], BF16)
    io["scr_M1"] = nc.dram_tensor("scr_M1", [1024, 1024], BF16)
    io["scr_M2"] = nc.dram_tensor("scr_M2", [1024, 512], BF16)
    io["scr_M3"] = nc.dram_tensor("scr_M3", [1024, 1024], BF16)
    io["scr_M4"] = nc.dram_tensor("scr_M4", [1024, 1024], BF16)
    declare_stage_B_io(nc, io)
    declare_stage_C_io(nc, io)
    declare_stage_D_io(nc, io)
    for k, v in CONSTS.items():
        io["c_" + k] = nc.dram_tensor("c_" + k, list(v.shape),
                                      BF16 if v.dtype == BF else F32,
                                      kind="ExternalInput")
    cio = {k: io["c_" + k] for k in CONSTS}
    bars = {}
    for st in ("A", "B", "C"):
        bars[st] = (nc.dram_tensor(f"bar_{st}_src", [1, 32], F32,
                                   kind="Internal"),
                    nc.dram_tensor(f"bar_{st}_dst", [8, 32], F32,
                                   kind="Internal"))
    pb = [nc.alloc_psum_tensor(f"pb{i}", [128, TB], F32) for i in range(8)]
    C = alloc_consts(nc, CORE_CONSTS)

    with contextlib.ExitStack() as stk:
        shim = _StageAlloc(nc, stk)
        ctx = Ctx(nc, prefix="A_")
        for s in ("dg", "ds", "mm", "ev", "ea", "cc"):
            ctx.add_sem(s)
        CA = dict(C)
        CA.update(alloc_consts(shim,
                               [k for k in A_CONSTS if k not in CORE_CONSTS]))
        S = alloc_stage_A(shim)
        blk = Blk(ctx)
        load_consts(blk, CA, cio)
        blk.wait("sync", "ds", ctx.cnt["ds"])
        blk.emit()
        emit_stage_A(ctx, S, CA, pb, io)
        _drain(ctx, bar=bars["A"])

    with contextlib.ExitStack() as stk:
        shim = _StageAlloc(nc, stk)
        ctx = Ctx(nc, prefix="B_")
        for s in ("dg", "ds", "mm", "ev", "ea", "cc"):
            ctx.add_sem(s)
        S = alloc_stage_B(shim)
        emit_stage_B(ctx, S, C, pb, io)
        _drain(ctx, bar=bars["B"])

    with contextlib.ExitStack() as stk:
        shim = _StageAlloc(nc, stk)
        ctx = Ctx(nc, prefix="C_")
        for s in ("dg", "ds", "mm", "ev", "ea", "eg", "cc"):
            ctx.add_sem(s)
        S = alloc_stage_C(shim)
        emit_stage_C(ctx, S, C, pb, io)
        _drain(ctx, bar=bars["C"])

    with contextlib.ExitStack() as stk:
        shim = _StageAlloc(nc, stk)
        ctx = Ctx(nc, prefix="D_")
        for s in ("dg", "ds", "mm", "ev", "ea", "cc"):
            ctx.add_sem(s)
        S = alloc_stage_D(shim)
        emit_prep_D(ctx, S, C, pb, io)
        emit_stage_D(ctx, S, C, pb, io)
        _drain(ctx)
    return nc


def kernel(**inputs):
    x = np.asarray(inputs["L0_attn_x"], dtype=np.float32)
    n = 8
    ncF = build_program_fused()
    maps = []
    for c in range(n):
        m = {}
        m.update(stage_A_in_map(x[c], inputs))
        m.update(stage_B_in_map(c, None, inputs))
        m.update(stage_C_in_map(None, inputs))
        m.update(stage_D_in_map(c, None, inputs))
        maps.append(m)
    res = run_bass_kernel_spmd(ncF, maps, core_ids=list(range(n)))
    out = np.stack([np.asarray(res.results[c]["outT"]).T
                    for c in range(n)], axis=0)
    return out.astype(np.float32)


# version 03 (final)
# speedup vs baseline: 1.2319x; 1.1070x over previous
import contextlib
import numpy as np
import ml_dtypes
import concourse.bass as bass
import concourse.mybir as mybir
from concourse.bass import IndirectOffsetOnAxis
from concourse.bass_utils import run_bass_kernel_spmd

F32 = mybir.dt.float32
BF16 = mybir.dt.bfloat16
I32 = mybir.dt.int32
AF = mybir.ActivationFunctionType
ALU = mybir.AluOpType
AX = mybir.AxisListType
BF = ml_dtypes.bfloat16

T = 2048
D = 1024
NCH = 8
TB = 512
NTB = T // TB
H = 16
INV_SQRT96 = float(1.0 / np.sqrt(96.0))
SQRT96 = float(np.sqrt(96.0))

ENGINES = ("tensor", "vector", "scalar", "gpsimd", "sync")

import os
_BLK_LIMIT = int(os.environ.get("K_BLK_LIMIT", "0")) or None
_BLK_COUNT = [0]



def host_constants():
    c = {}
    c["ident_bf"] = np.eye(128, dtype=BF)
    c["ident_f32"] = np.eye(128, dtype=np.float32)
    c["ones_col_bf"] = np.ones((128, 1), dtype=BF)
    c["ones128_bf"] = np.ones((128, 128), dtype=BF)
    c["ones_f32_row"] = np.ones((1, 128), dtype=np.float32)
    c["eps_col"] = np.full((128, 1), 1e-5, dtype=np.float32)
    m = np.zeros((128, 4 * TB), dtype=BF)
    for r in range(4):
        j = np.arange(128)[:, None] + 128 * r
        i = np.arange(TB)[None, :]
        m[:, r * TB:(r + 1) * TB] = (j <= i).astype(BF)
    c["diag_masks"] = m
    t = np.arange(T, dtype=np.float32)
    inv = (1.0 / 10000.0 ** (np.arange(0, 32, 2, dtype=np.float32) / 32)) \
        .astype(np.float32)
    freqs = t[:, None] * inv[None, :]
    emb = np.concatenate([freqs, freqs], -1).astype(np.float32)
    scale = (np.cos(emb) + np.sin(emb)).astype(np.float32)
    c["scaleq"] = np.ascontiguousarray(np.tile(scale.T, (4, 1)).astype(np.float32))
    c["scalek"] = np.ascontiguousarray(scale.T.astype(np.float32))
    eh = np.zeros((128, NCH * 16), dtype=BF)
    for cc in range(NCH):
        for p in range(128):
            eh[p, cc * 16 + 2 * cc + (p >= 64)] = 1.0
    c["ehall"] = eh
    c["ones32x16"] = np.ones((32, 16), dtype=BF)
    selh = np.zeros((32, 16 * 16), dtype=BF)
    for h in range(16):
        selh[:, h * 16 + h] = 1.0
    c["selh"] = selh
    c["iota32"] = np.broadcast_to(np.arange(T, dtype=np.float32),
                                  (32, T)).copy()
    c["base32"] = np.array([0.0, 0.02063896507024765, 0.04340777546167374, 0.06852622330188751, 0.09623675793409348, 0.1268068552017212, 0.1605316549539566, 0.19773666560649872, 0.23878106474876404, 0.2840610444545746, 0.33401361107826233, 0.38912108540534973, 0.4499153792858124, 0.5169833302497864, 0.5909722447395325, 0.6725966334342957, 0.7626441121101379, 0.8619838953018188, 0.9715754389762878, 1.0924760103225708, 1.2258524894714355, 1.3729935884475708, 1.535318374633789, 1.7143945693969727, 1.9119499921798706, 2.129892349243164, 2.3703253269195557, 2.6355690956115723, 2.92818546295166, 3.2509994506835938, 3.6071248054504395, 4.000000953674316], dtype=np.float32).reshape(32, 1)
    ec = np.zeros((32, 128), dtype=np.float32)
    es = np.zeros((32, 128), dtype=np.float32)
    for dd in range(128):
        m_ = (dd % 64) // 2
        ec[m_, dd] = 1.0
        es[m_, dd] = 1.0 if dd % 2 == 1 else -1.0
    c["ec"] = ec
    c["es"] = es
    perm = np.zeros((128, 128), dtype=BF)
    for dd in range(0, 128, 2):
        perm[dd + 1, dd] = 1.0
        perm[dd, dd + 1] = 1.0
    c["perm"] = perm
    return c


CONSTS = host_constants()



class Ctx:
    def __init__(self, nc, prefix=""):
        self.nc = nc
        self.prefix = prefix
        self.semh = {}
        self.cnt = {}

    def add_sem(self, name):
        self.semh[name] = self.nc.semaphore(self.prefix + name).__enter__()
        self.cnt[name] = 0

    def bump(self, name, n=1):
        self.cnt[name] += n
        return self.cnt[name]


class Blk:
    def __init__(self, ctx):
        self.ctx = ctx
        self.ops = {e: [] for e in ENGINES}

    def add(self, eng, fn, inc=None, n=1):
        if inc is None:
            self.ops[eng].append(fn)
            return None
        h = self.ctx.semh[inc]
        v = self.ctx.bump(inc, n)
        self.ops[eng].append(lambda e, f=fn, h=h, n=n: f(e).then_inc(h, n))
        return v

    def wait(self, eng, sem, val):
        if val is None or val <= 0:
            return
        assert val <= self.ctx.cnt[sem], (sem, val, self.ctx.cnt[sem])
        h = self.ctx.semh[sem]
        self.ops[eng].append(lambda e, h=h, v=val: e.wait_ge(h, v))

    def vf(self, fn):
        v = self.add("vector", fn, inc="ev")
        self.wait("vector", "ev", v)
        return v

    def dma(self, queue, out_ap, in_ap, sem):
        return self.add(queue, lambda e, o=out_ap, i=in_ap: e.dma_start(o, i),
                        inc=sem, n=16)

    def mm(self, out, lhsT, rhs, start, stop, inc=None):
        return self.add(
            "tensor",
            lambda e, o=out, l=lhsT, r=rhs, s=start, p=stop:
            e.matmul(o, l, r, start=s, stop=p, skip_group_check=True),
            inc=inc)

    def tr(self, out, in_, ident, inc=None):
        return self.add(
            "tensor",
            lambda e, o=out, i=in_, d=ident: e.transpose(o, i, d), inc=inc)

    def emit(self):
        _BLK_COUNT[0] += 1
        if _BLK_LIMIT is not None and _BLK_COUNT[0] > _BLK_LIMIT:
            return
        nc = self.ctx.nc
        ops = self.ops
        with nc.Block() as blk:
            for name in ENGINES:
                if not ops[name]:
                    continue

                def body(eng, lst=tuple(ops[name])):
                    for f in lst:
                        f(eng)
                getattr(blk, name)(body)



CONST_DEFS = {
    "ONESC": ("ones_col_bf", [128, 1], BF16),
    "ONES128": ("ones128_bf", [128, 128], BF16),
    "ONESR": ("ones_f32_row", [1, 128], F32),
    "EPSC": ("eps_col", [128, 1], F32),
    "DMASK": ("diag_masks", [128, 4 * TB], BF16),
    "SCQ": ("scaleq", [128, T], F32),
    "SCK": ("scalek", [32, T], F32),
    "EHALL": ("ehall", [128, NCH * 16], BF16),
    "O32X16": ("ones32x16", [32, 16], BF16),
    "SELH": ("selh", [32, 16 * 16], BF16),
    "IDB": ("ident_bf", [128, 128], BF16),
    "IDF": ("ident_f32", [128, 128], F32),
}
CORE_CONSTS = ["ONESC", "ONES128", "ONESR", "EPSC", "IDB", "IDF"]
A_CONSTS = CORE_CONSTS + ["DMASK", "SCQ", "SCK", "EHALL", "O32X16", "SELH"]


def alloc_consts(nc, keys=None):
    sb = nc.alloc_sbuf_tensor
    C = {}
    for k in (keys or list(CONST_DEFS)):
        ion, shape, dt = CONST_DEFS[k]
        C[k] = sb("K_" + k, shape, dt)
    return C


def load_consts(blk, C, cio):
    for sbn in C:
        ion = CONST_DEFS[sbn][0]
        blk.dma("sync", C[sbn][:, :], cio[ion][:, :], "ds")


def emit_ln(ctx, S, C, pb, src, dst, ln_w_col=None, ln_b_col=None,
            dst_fn=None, fp32_cb=None):
    XB, SQ, MU, RSTD = S["XB"], S["SQ"], S["MU"], S["RSTD"]
    TF1, TF2 = S["TF1"], S["TF2"]
    blk = Blk(ctx)
    cast_v = []
    for c in range(NCH):
        v = blk.add("vector", lambda e, c=c: e.tensor_copy(
            XB[:, c * TB:(c + 1) * TB], src[:, c * TB:(c + 1) * TB]), inc="ev")
        cast_v.append(v)
    sq_v = []
    for c in range(NCH):
        blk.wait("scalar", "ev", cast_v[c])
        v = blk.add("scalar", lambda e, c=c: e.activation(
            SQ[:, c * TB:(c + 1) * TB], XB[:, c * TB:(c + 1) * TB],
            AF.Square), inc="ea")
        sq_v.append(v)
    for c in range(NCH):
        blk.wait("tensor", "ev", cast_v[c])
        blk.mm(pb[0][:, :], C["ONES128"][:, :], XB[:, c * TB:(c + 1) * TB],
               start=(c == 0), stop=(c == NCH - 1))
    v = None
    for c in range(NCH):
        blk.wait("tensor", "ea", sq_v[c])
        v = blk.mm(pb[1][:, :], C["ONES128"][:, :], SQ[:, c * TB:(c + 1) * TB],
                   start=(c == 0), stop=(c == NCH - 1),
                   inc=("mm" if c == NCH - 1 else None))
    blk.wait("vector", "mm", v)
    blk.add("vector", lambda e: e.tensor_scalar(
        MU[:, :], pb[0][:, :], 1.0 / D, None, ALU.mult))
    blk.add("vector", lambda e: e.tensor_mul(TF1[:, :], MU[:, :], MU[:, :]))
    v = blk.add("vector", lambda e: e.scalar_tensor_tensor(
        TF2[:, :], pb[1][:, :], 1.0 / D, TF1[:, :], ALU.mult, ALU.subtract),
        inc="ev")
    blk.wait("scalar", "ev", v)
    v = blk.add("scalar", lambda e: e.activation(
        TF1[:, :], TF2[:, :], AF.Sqrt, bias=C["EPSC"][:, 0:1]), inc="ea")
    blk.wait("vector", "ea", v)
    blk.add("vector", lambda e: e.reciprocal(RSTD[:, :], TF1[:, :]))
    out_v = None
    cb_mm = None
    for c in range(NCH):
        dstap = dst_fn(c) if dst_fn is not None else \
            dst[:, c * TB:(c + 1) * TB]
        blk.add("vector", lambda e, c=c: e.tensor_sub(
            TF2[:, :], src[:, c * TB:(c + 1) * TB], MU[:, :]))
        if ln_w_col is None:
            if fp32_cb is None:
                out_v = blk.add("vector", lambda e, c=c, dstap=dstap:
                                e.tensor_mul(dstap, TF2[:, :], RSTD[:, :]),
                                inc=("ev" if c == NCH - 1 else None))
            else:
                if c > 0 and cb_mm is not None:
                    blk.wait("vector", "mm", cb_mm)
                fv = blk.add("vector", lambda e, c=c: e.tensor_mul(
                    TF1[:, :], TF2[:, :], RSTD[:, :]), inc="ev")
                out_v = blk.add("vector", lambda e, c=c, dstap=dstap:
                                e.tensor_copy(dstap, TF1[:, :]),
                                inc=("ev" if c == NCH - 1 else None))
                cb_mm = fp32_cb(blk, c, TF1, fv)
        else:
            blk.add("vector", lambda e, c=c: e.tensor_mul(
                TF2[:, :], TF2[:, :], RSTD[:, :]))
            out_v = blk.add("vector", lambda e, c=c, dstap=dstap:
                            e.tensor_scalar(dstap, TF2[:, :],
                                            ln_w_col[:, c:c + 1],
                                            ln_b_col[:, c:c + 1],
                                            ALU.mult, ALU.add),
                            inc=("ev" if c == NCH - 1 else None))
    blk.emit()
    return out_v


def stream_mm(ctx, blk, specs, ws, n_kc, moving, psum_of, evac,
              stream_engine="gpsimd"):
    nb = len(specs)
    mm_tgt = [None] * nb
    ev_tgt = [None] * nb
    for i in range(nb):
        buf = ws[i % 2]
        if i >= 2:
            blk.wait(stream_engine, "mm", mm_tgt[i - 2])
        ld = blk.dma(stream_engine,
                     buf[:, 0:n_kc * 128].rearrange("p (c n) -> p c n",
                                                    c=n_kc),
                     specs[i], "dg")
        blk.wait("tensor", "dg", ld)
        if i >= 4:
            blk.wait("tensor", ev_tgt[i - 4][0], ev_tgt[i - 4][1])
        pbank = psum_of(i)
        for kc in range(n_kc):
            last = kc == n_kc - 1
            mm_tgt[i] = blk.mm(pbank, buf[:, kc * 128:(kc + 1) * 128],
                               moving(i, kc), start=(kc == 0), stop=last,
                               inc=("mm" if last else None))
        ev_tgt[i] = evac(blk, i, mm_tgt[i])
    return mm_tgt, ev_tgt



def alloc_stage_A(nc):
    sb = nc.alloc_sbuf_tensor
    S = {}
    S["XF"] = sb("A_XF", [128, NCH * TB], F32)
    S["XLN"] = sb("A_XLN", [128, NCH * TB], BF16)
    S["XB"] = sb("A_XB", [128, NCH * TB], BF16)
    S["SQ"] = sb("A_SQ", [128, NCH * TB], BF16)
    S["MU"] = sb("A_MU", [128, TB], F32)
    S["RSTD"] = sb("A_RSTD", [128, TB], F32)
    S["TF1"] = sb("A_TF1", [128, TB], F32)
    S["TF2"] = sb("A_TF2", [128, TB], F32)
    S["TF3"] = None
    S["T16"] = sb("A_T16", [16, 1], F32)
    S["TF3"] = S["MU"]
    S["KC"] = sb("A_KC", [128, NCH * T], BF16)
    S["VT"] = sb("A_VT", [128, 16 * 16 * 80], BF16)
    S["KRA"] = sb("A_KRA", [97, T], BF16)
    S["QC"] = sb("A_QC", [128, NCH * TB], BF16)
    S["QRA"] = sb("A_QRA", [97, H * TB], BF16)
    S["SQR"] = sb("A_SQR", [32, 2 * TB], BF16)
    S["KN2H"] = sb("A_KN2H", [16, T], F32)
    S["KMS16"] = sb("A_KMS16", [16, 1], F32)
    S["QN16"] = sb("A_QN16", [16, TB], F32)
    S["B16"] = sb("A_B16", [16, TB], F32)
    S["RD"] = [sb(f"A_RD{i}", [1, TB], F32) for i in range(2)]
    S["RDB"] = [sb(f"A_RDB{i}", [64, TB], F32) for i in range(2)]
    S["PT"] = [sb(f"A_PT{i}", [128, TB], BF16) for i in range(2)]
    S["PTM"] = [sb(f"A_PTM{i}", [128, TB], BF16) for i in range(2)]
    S["AO"] = S["XB"]
    S["WS"] = [sb(f"A_WS{i}", [128, NCH * 128], BF16) for i in range(2)]
    S["WT"] = None
    S["WM"] = [sb(f"A_WM{i}", [128, D], BF16) for i in range(2)]
    S["WT"] = S["WM"]
    return S


def emit_compose_mats(ctx, S, C, pb, io):
    nc = ctx.nc
    wdq, wuq, wqr = io["L0_attn_W_DQ"], io["L0_attn_W_UQ"], io["L0_attn_W_QR"]
    wdkv, wuk, wuv = io["L0_attn_W_DKV"], io["L0_attn_W_UK"], io["L0_attn_W_UV"]
    wdqt, wdkvt = io["scr_WDQT"], io["scr_WDKVT"]
    m1, m2, m3, m4 = io["scr_M1"], io["scr_M2"], io["scr_M3"], io["scr_M4"]
    WS, WT = S["WS"], S["WT"]

    blk = Blk(ctx)
    st_t = {}
    ev_hist = {}
    for i in range(34):
        buf = WS[i % 2]
        obuf = WT[i % 2]
        if i >= 2:
            blk.wait("gpsimd", "mm", st_t[i - 2][0])
        if i < 32:
            src = wdq[:, i * 128:(i + 1) * 128].rearrange(
                "(c p) n -> p c n", c=NCH)
        else:
            src = wdkv[:, (i - 32) * 128:(i - 31) * 128].rearrange(
                "(c p) n -> p c n", c=NCH)
        ld = blk.dma("gpsimd", buf[:, :].rearrange("p (c n) -> p c n", c=NCH),
                     src, "dg")
        blk.wait("tensor", "dg", ld)
        if i >= 2:
            blk.wait("vector", "ds", st_t[i - 2][2])
        last_mm = None
        vv = None
        for c in range(NCH):
            pbb = pb[4 + c % 2]
            prev = ev_hist.get((i, c - 2)) or ev_hist.get((i - 1, NCH - 2 + c))
            if prev:
                blk.wait("tensor", "ev", prev)
            last_mm = blk.tr(pbb.bitcast(BF16)[:, 0:128],
                             buf[:, c * 128:(c + 1) * 128], C["IDB"][:, :],
                             inc="mm")
            blk.wait("vector", "mm", last_mm)
            vv = blk.add("vector", lambda e, obuf=obuf, c=c, pbb=pbb:
                         e.tensor_copy(obuf[:, c * 128:(c + 1) * 128],
                                       pbb.bitcast(BF16)[:, 0:128]), inc="ev")
            ev_hist[(i, c)] = vv
        blk.wait("sync", "ev", vv)
        if i < 32:
            dst = wdqt[i * 128:(i + 1) * 128, :]
        else:
            dst = wdkvt[(i - 32) * 128:(i - 31) * 128, :]
        sv = blk.dma("sync", dst, obuf[:, :], "ds")
        st_t[i] = (last_mm, vv, sv)
    blk.emit()

    for eb in range(3):
        blk = Blk(ctx)
        blk.wait("gpsimd", "ds", ctx.cnt["ds"])
        mm_last = {}
        for fc in range(32):
            buf, mbuf = WS[fc % 2], S["WM"][fc % 2]
            if fc >= 2:
                blk.wait("gpsimd", "mm", mm_last[fc - 2])
            ld1 = blk.dma("gpsimd",
                          buf[:, :].rearrange("p (c n) -> p c n", c=NCH),
                          wdqt[fc * 128:(fc + 1) * 128, :].rearrange(
                              "p (c n) -> p c n", c=NCH), "dg")
            if eb < 2:
                msrc = wuq[fc * 128:(fc + 1) * 128, eb * 512:(eb + 1) * 512]
            else:
                msrc = wqr[fc * 128:(fc + 1) * 128, :]
            ld2 = blk.dma("gpsimd", mbuf[:, 0:512], msrc, "dg")
            blk.wait("tensor", "dg", max(ld1, ld2))
            for dch in range(NCH):
                mm_last[fc] = blk.mm(
                    pb[dch][:, :], buf[:, dch * 128:(dch + 1) * 128],
                    mbuf[:, 0:512], start=(fc == 0), stop=(fc == 31),
                    inc=("mm" if dch == NCH - 1 else None))
        XB = S["XB"]
        for dch in range(NCH):
            blk.wait("vector", "mm", mm_last[31])
            v = blk.add("vector", lambda e, dch=dch: e.tensor_copy(
                XB[:, dch * 512:(dch + 1) * 512], pb[dch][:, :]), inc="ev")
            blk.wait("sync", "ev", v)
            if eb < 2:
                dst = m1[dch * 128:(dch + 1) * 128, eb * 512:(eb + 1) * 512]
            else:
                dst = m2[dch * 128:(dch + 1) * 128, :]
            blk.dma("sync", dst, XB[:, dch * 512:(dch + 1) * 512], "ds")
        blk.wait("sync", "ds", ctx.cnt["ds"])
        blk.emit()

    blk = Blk(ctx)
    ld1 = blk.dma("gpsimd", WS[0][:, 0:D].rearrange("p (c n) -> p c n", c=NCH),
                  wdkvt[0:128, :].rearrange("p (c n) -> p c n", c=NCH), "dg")
    ld2 = blk.dma("gpsimd", WS[1][:, 0:D].rearrange("p (c n) -> p c n", c=NCH),
                  wdkvt[128:256, :].rearrange("p (c n) -> p c n", c=NCH), "dg")
    XB = S["XB"]
    prev_ev = {}
    st_x = {}
    wm_last = {}
    wm2 = [S["WM"][0][:, 0:1024], S["WM"][1][:, 0:1024]]
    for mi, (wsrc, mdst) in enumerate([(wuk, m3), (wuv, m4)]):
        for eb in range(2):
            pi = mi * 2 + eb
            mb = wm2[pi % 2]
            if pi >= 2:
                blk.wait("gpsimd", "mm", wm_last[pi - 2])
            ld3 = blk.dma("gpsimd",
                          mb.rearrange("p (c n) -> p c n", c=2),
                          wsrc[:, eb * 512:(eb + 1) * 512].rearrange(
                              "(c p) n -> p c n", c=2), "dg")
            blk.wait("tensor", "dg", max(ld1, ld2, ld3))
            for dch in range(NCH):
                b = dch % 4
                if b in prev_ev:
                    blk.wait("tensor", "ev", prev_ev[b])
                v = None
                for k2 in range(2):
                    v = blk.mm(pb[b][:, :],
                               WS[k2][:, dch * 128:(dch + 1) * 128],
                               mb[:, k2 * 512:(k2 + 1) * 512],
                               start=(k2 == 0), stop=(k2 == 1),
                               inc=("mm" if k2 == 1 else None))
                wm_last[pi] = v
                blk.wait("vector", "mm", v)
                if dch in st_x:
                    blk.wait("vector", "ds", st_x[dch])
                v = blk.add("vector", lambda e, dch=dch, b=b: e.tensor_copy(
                    XB[:, dch * 512:(dch + 1) * 512], pb[b][:, :]), inc="ev")
                prev_ev[b] = v
                blk.wait("sync", "ev", v)
                st_x[dch] = blk.dma(
                    "sync",
                    mdst[dch * 128:(dch + 1) * 128, eb * 512:(eb + 1) * 512],
                    XB[:, dch * 512:(dch + 1) * 512], "ds")
    blk.wait("sync", "ds", ctx.cnt["ds"])
    blk.emit()


def emit_stage_A(ctx, S, C, pb, io):
    nc = ctx.nc
    xT = io["xT"]

    emit_compose_mats(ctx, S, C, pb, io)

    blk = Blk(ctx)
    blk.add("gpsimd", lambda e: e.memset(S["KRA"][32:33, :], 1.0))
    blk.add("gpsimd", lambda e: e.memset(S["KRA"][96:97, :], 1.0))
    blk.add("gpsimd", lambda e: e.memset(
        S["VT"][:, :].rearrange("p (a b) -> p a b", b=80)[:, :, 64:65], 1.0))
    blk.dma("gpsimd", S["XF"][:, :].rearrange("p (c n) -> p c n", c=NCH),
            xT[:, 0:TB].rearrange("(c p) n -> p c n", c=NCH), "dg")
    blk.wait("gpsimd", "dg", ctx.cnt["dg"])
    blk.emit()

    for tb in range(NTB):
        _stage_A_tb(ctx, S, C, pb, io, tb)


def _stage_A_tb(ctx, S, C, pb, io, tb):
    xT, x1T = io["xT"], io["x1T"]
    m1, m2, m3, m4 = io["scr_M1"], io["scr_M2"], io["scr_M3"], io["scr_M4"]
    wkr, wo = io["L0_attn_W_KR"], io["L0_attn_W_O"]
    XF, XLN, SQ = S["XF"], S["XLN"], S["SQ"]
    KC, VT, KRA, QC, QRA, SQR = (S["KC"], S["VT"], S["KRA"], S["QC"],
                                 S["QRA"], S["SQR"])

    emit_ln(ctx, S, C, pb, XF, XLN)

    blk = Blk(ctx)
    m3_specs = [m3[:, oc * 128:(oc + 1) * 128].rearrange("(c p) n -> p c n",
                                                         c=NCH)
                for oc in range(NCH)]

    def kc_evac(blk, i, mv):
        blk.wait("vector", "mm", mv)
        return ("ev", blk.add("vector", lambda e, i=i: e.tensor_copy(
            KC[:, i * T + tb * TB:i * T + tb * TB + TB], pb[i % 4][:, :]),
            inc="ev"))

    stream_mm(ctx, blk, m3_specs, S["WS"], NCH,
              moving=lambda i, kc: XLN[:, kc * TB:(kc + 1) * TB],
              psum_of=lambda i: pb[i % 4][:, :], evac=kc_evac)
    ld = blk.dma("gpsimd", S["WM"][0][:, 0:NCH * 32].rearrange(
        "p (c n) -> p c n", c=NCH),
        wkr[:, :].rearrange("(c p) n -> p c n", c=NCH), "dg")
    blk.wait("tensor", "dg", ld)
    for kc in range(NCH):
        v = blk.mm(pb[4][0:32, :], S["WM"][0][:, kc * 32:(kc + 1) * 32],
                   XLN[:, kc * TB:(kc + 1) * TB],
                   start=(kc == 0), stop=(kc == NCH - 1),
                   inc=("mm" if kc == NCH - 1 else None))
    blk.wait("vector", "mm", v)
    blk.add("vector", lambda e: e.tensor_mul(
        KRA[0:32, tb * TB:(tb + 1) * TB], pb[4][0:32, :],
        C["SCK"][:, tb * TB:(tb + 1) * TB]), inc="ev")
    blk.add("vector", lambda e: e.tensor_mul(
        KRA[64:96, tb * TB:(tb + 1) * TB], pb[4][0:32, :],
        C["SCK"][:, tb * TB:(tb + 1) * TB]), inc="ev")
    blk.emit()

    blk = Blk(ctx)
    m4ld = {}
    for kc in range(NCH):
        mb = S["WM"][kc % 2]
        if kc >= 2:
            blk.wait("gpsimd", "mm", m4ld[kc - 2])
        ld = blk.dma("gpsimd", mb[:, 0:D], m4[kc * 128:(kc + 1) * 128, :],
                     "dg")
        blk.wait("tensor", "dg", ld)
        lastmm = None
        for tt in range(4):
            for nv in range(2):
                i = tt * 2 + nv
                last_of_kc = (tt == 3 and nv == 1)
                lastmm = blk.mm(
                    pb[i][:, :],
                    XLN[:, kc * TB + tt * 128:kc * TB + tt * 128 + 128],
                    mb[:, nv * 512:(nv + 1) * 512],
                    start=(kc == 0), stop=(kc == NCH - 1),
                    inc=("mm" if (kc == NCH - 1 or last_of_kc) else None))
                if kc == NCH - 1:
                    kt = tb * 4 + tt
                    blk.wait("vector", "mm", lastmm)
                    dst = VT[:, kt * 1280 + nv * 640:kt * 1280 + (nv + 1) * 640] \
                        .rearrange("p (h n) -> p h n", h=8)[:, :, 0:64]
                    src = pb[i][:, :].rearrange("p (h n) -> p h n", h=8)
                    blk.add("vector",
                            lambda e, dst=dst, src=src:
                            e.tensor_copy(dst, src), inc="ev")
        m4ld[kc] = lastmm
    blk.emit()

    blk = Blk(ctx)
    sqv = []
    for oc in range(NCH):
        v = blk.add("scalar", lambda e, oc=oc: e.activation(
            SQ[:, oc * TB:(oc + 1) * TB],
            KC[:, oc * T + tb * TB:oc * T + tb * TB + TB], AF.Square),
            inc="ea")
        sqv.append(v)
    vkr = blk.add("scalar", lambda e: e.activation(
        SQR[0:32, 0:TB], KRA[0:32, tb * TB:(tb + 1) * TB], AF.Square),
        inc="ea")
    for oc in range(NCH):
        blk.wait("tensor", "ea", sqv[oc])
        blk.mm(pb[7][0:16, :], C["EHALL"][:, oc * 16:(oc + 1) * 16],
               SQ[:, oc * TB:(oc + 1) * TB], start=(oc == 0), stop=False)
    blk.wait("tensor", "ea", vkr)
    v = blk.mm(pb[7][0:16, :], C["O32X16"][:, :], SQR[0:32, 0:TB],
               start=False, stop=True, inc="mm")
    blk.wait("vector", "mm", v)
    v = blk.add("vector", lambda e: e.tensor_copy(
        S["KN2H"][:, tb * TB:(tb + 1) * TB], pb[7][0:16, :]), inc="ev")
    blk.wait("vector", "ev", v)
    v = blk.add("vector", lambda e: e.reduce_max(
        S["T16"][:, :], S["KN2H"][:, 0:(tb + 1) * TB], axis=AX.X), inc="ev")
    blk.wait("scalar", "ev", v)
    blk.add("scalar", lambda e: e.activation(S["KMS16"][:, :], S["T16"][:, :],
                                             AF.Sqrt), inc="ea")
    blk.emit()

    blk = Blk(ctx)
    q_specs = [m1[:, oc * 128:(oc + 1) * 128].rearrange("(c p) n -> p c n",
                                                        c=NCH)
               for oc in range(NCH)] + \
              [m2[:, ocr * 128:(ocr + 1) * 128].rearrange("(c p) n -> p c n",
                                                          c=NCH)
               for ocr in range(4)]

    def q_evac(blk, i, mv):
        blk.wait("vector", "mm", mv)
        if i < NCH:
            return ("ev", blk.add("vector", lambda e, i=i: e.tensor_copy(
                QC[:, i * TB:(i + 1) * TB], pb[i % 4][:, :]), inc="ev"))
        ocr = i - NCH
        last = None
        for j in range(4):
            h = ocr * 4 + j
            blk.add("vector", lambda e, h=h, j=j, i=i: e.tensor_mul(
                QRA[0:32, h * TB:(h + 1) * TB],
                pb[i % 4][32 * j:32 * (j + 1), :],
                C["SCQ"][32 * j:32 * (j + 1), tb * TB:(tb + 1) * TB]))
            last = blk.add("vector", lambda e, h=h, j=j, i=i: e.tensor_mul(
                QRA[64:96, h * TB:(h + 1) * TB],
                pb[i % 4][32 * j:32 * (j + 1), :],
                C["SCQ"][32 * j:32 * (j + 1), tb * TB:(tb + 1) * TB]),
                inc=("ev" if j == 3 else None))
        return ("ev", last)

    stream_mm(ctx, blk, q_specs, S["WS"], NCH,
              moving=lambda i, kc: XLN[:, kc * TB:(kc + 1) * TB],
              psum_of=lambda i: pb[i % 4][:, :], evac=q_evac)
    blk.emit()

    blk = Blk(ctx)
    sqv = []
    for oc in range(NCH):
        v = blk.add("scalar", lambda e, oc=oc: e.activation(
            SQ[:, oc * TB:(oc + 1) * TB], QC[:, oc * TB:(oc + 1) * TB],
            AF.Square), inc="ea")
        sqv.append(v)
    for oc in range(NCH):
        blk.wait("tensor", "ea", sqv[oc])
        blk.mm(pb[4][0:16, :], C["EHALL"][:, oc * 16:(oc + 1) * 16],
               SQ[:, oc * TB:(oc + 1) * TB], start=(oc == 0), stop=False)
    v = None
    qr_mm = {}
    for h in range(H):
        if h >= 2:
            blk.wait("scalar", "mm", qr_mm[h - 2])
        vh = blk.add("scalar", lambda e, h=h: e.activation(
            SQR[0:32, (h % 2) * TB:(h % 2 + 1) * TB],
            QRA[0:32, h * TB:(h + 1) * TB], AF.Square), inc="ea")
        blk.wait("tensor", "ea", vh)
        v = blk.mm(pb[4][0:16, :], C["SELH"][:, h * 16:(h + 1) * 16],
                   SQR[0:32, (h % 2) * TB:(h % 2 + 1) * TB],
                   start=False, stop=(h == H - 1), inc="mm")
        qr_mm[h] = v
    blk.wait("scalar", "mm", v)
    v = blk.add("scalar", lambda e: e.activation(
        S["QN16"][:, :], pb[4][0:16, :], AF.Sqrt), inc="ea")
    blk.wait("vector", "ea", v)
    v = blk.add("vector", lambda e: e.tensor_scalar(
        S["B16"][:, :], S["QN16"][:, :], S["KMS16"][:, 0:1], -SQRT96,
        ALU.mult, ALU.mult), inc="ev")
    blk.wait("gpsimd", "ev", v)
    for h in range(H):
        blk.dma("gpsimd", QRA[32:33, h * TB:(h + 1) * TB],
                S["B16"][h:h + 1, :], "dg")
        blk.dma("gpsimd", QRA[96:97, h * TB:(h + 1) * TB],
                S["B16"][h:h + 1, :], "dg")
    blk.wait("tensor", "dg", ctx.cnt["dg"])
    blk.emit()

    nkt = 4 * (tb + 1)
    blk = Blk(ctx)
    AO = S["AO"]
    ea_t, prod_t, mm_t, pv_t, pv_in = {}, {}, {}, {}, {}
    ao_t, cp_t = {}, {}
    diag_gs = []

    _NOPV = os.environ.get("K_ATT_NOPV", "0") == "1"
    _NOMM2 = os.environ.get("K_ATT_NOMM2", "0") == "1"

    def emit_pv(h, kt, g):
        if _NOPV:
            pv_t[g] = ctx.cnt["mm"]
            return
        sem, tgt = prod_t[g]
        blk.wait("tensor", sem, tgt)
        if kt == 0 and h >= 2:
            blk.wait("tensor", "ev", ao_t[h - 2])
        pv_t[g] = blk.mm(pb[4 + h % 2][0:65, :],
                         VT[:, (kt * 16 + h) * 80:(kt * 16 + h) * 80 + 65],
                         pv_in[g][:, :], start=(kt == 0),
                         stop=(kt == nkt - 1), inc="mm")

    g = -1
    _H = int(os.environ.get("K_ATT_H", "0")) or H
    _EPI = os.environ.get("K_ATT_EPI", "1") == "1"
    for h in range(_H):
        hc, hr = h // 2, (h % 2) * 64
        for kt in range(nkt):
            g += 1
            sb_ = pb[g % 2]
            if g >= 2:
                blk.wait("tensor", "ea", ea_t[g - 2])
            blk.mm(sb_[:, :],
                   KC[hr:hr + 64, hc * T + kt * 128:hc * T + (kt + 1) * 128],
                   QC[hr:hr + 64, hc * TB:(hc + 1) * TB],
                   start=True, stop=_NOMM2)
            if _NOMM2:
                mm_t[g] = blk.add(
                    "tensor", lambda e: e.ldweights(C["IDB"][:, :]), inc="mm")
            else:
                qb_ = (h % 2) * 64
                mm_t[g] = blk.mm(sb_[:, :],
                                 KRA[qb_:qb_ + 33, kt * 128:(kt + 1) * 128],
                                 QRA[qb_:qb_ + 33, h * TB:(h + 1) * TB],
                                 start=False, stop=True, inc="mm")
            diag = kt >= 4 * tb
            pt = S["PT"][g % 2]
            blk.wait("scalar", "mm", mm_t[g])
            if g >= 2:
                blk.wait("scalar", "mm", pv_t[g - 2])
            ea_t[g] = blk.add("scalar", lambda e, sb_=sb_, pt=pt: e.activation(
                pt[:, :], sb_[:, :], AF.Exp, scale=INV_SQRT96), inc="ea")
            if diag:
                nd = len(diag_gs)
                ptm = S["PTM"][nd % 2]
                blk.wait("vector", "ea", ea_t[g])
                if nd >= 2:
                    blk.wait("vector", "mm", pv_t[diag_gs[nd - 2]])
                r = kt - 4 * tb
                vv = blk.add("vector", lambda e, pt=pt, ptm=ptm, r=r:
                             e.tensor_mul(ptm[:, :], pt[:, :],
                                          C["DMASK"][:, r * TB:(r + 1) * TB]),
                             inc="ev")
                prod_t[g] = ("ev", vv)
                pv_in[g] = ptm
                diag_gs.append(g)
            else:
                prod_t[g] = ("ea", ea_t[g])
                pv_in[g] = pt
            if kt >= 1:
                emit_pv(h, kt - 1, g - 1)
        emit_pv(h, nkt - 1, g)
        if not _EPI:
            continue
        ob = pb[4 + h % 2]
        rb = pb[6 + h % 2]
        blk.wait("vector", "mm", pv_t[g])
        rdv = blk.add("vector", lambda e, ob=ob, h=h: e.reciprocal(
            S["RD"][h % 2][:, :], ob[64:65, :]), inc="ev")
        blk.wait("tensor", "ev", rdv)
        if h >= 2:
            blk.wait("tensor", "ev", cp_t[h - 2])
        rbv = blk.mm(rb[0:64, :], C["ONESR"][0:1, 0:64],
                     S["RD"][h % 2][0:1, :], start=True, stop=True, inc="mm")
        blk.wait("vector", "mm", rbv)
        cp_t[h] = blk.add("vector", lambda e, rb=rb, h=h: e.tensor_copy(
            S["RDB"][h % 2][:, :], rb[0:64, :]), inc="ev")
        ao_t[h] = blk.add("vector", lambda e, ob=ob, hc=hc, hr=hr, h=h:
                          e.tensor_mul(AO[hr:hr + 64, hc * TB:(hc + 1) * TB],
                                       ob[0:64, :], S["RDB"][h % 2][:, :]),
                          inc="ev")
    blk.emit()

    blk = Blk(ctx)
    wo_specs = [wo[:, oc * 128:(oc + 1) * 128].rearrange("(c p) n -> p c n",
                                                         c=NCH)
                for oc in range(NCH)]
    st_t = {}

    def wo_evac(blk, i, mv):
        blk.wait("vector", "mm", mv)
        if i >= 2:
            blk.wait("vector", "ds", st_t[i - 2])
        tfo = S["TF1"] if i % 2 == 0 else S["TF3"]
        v = blk.add("vector", lambda e, i=i, tfo=tfo: e.tensor_add(
            tfo[:, :], pb[i % 4][:, :], XF[:, i * TB:(i + 1) * TB]), inc="ev")
        blk.wait("sync", "ev", v)
        st_t[i] = blk.dma(
            "sync", x1T[i * 128:(i + 1) * 128, tb * TB:(tb + 1) * TB],
            tfo[:, :], "ds")
        return ("ev", v)

    stream_mm(ctx, blk, wo_specs, S["WS"], NCH,
              moving=lambda i, kc: AO[:, kc * TB:(kc + 1) * TB],
              psum_of=lambda i: pb[i % 4][:, :], evac=wo_evac)
    blk.emit()

    blk = Blk(ctx)
    if tb + 1 < NTB:
        blk.dma("gpsimd", S["XF"][:, :].rearrange("p (c n) -> p c n", c=NCH),
                xT[:, (tb + 1) * TB:(tb + 2) * TB].rearrange(
                    "(c p) n -> p c n", c=NCH), "dg")
        blk.wait("gpsimd", "dg", ctx.cnt["dg"])
    else:
        blk.wait("sync", "ds", ctx.cnt["ds"])
        blk.wait("gpsimd", "dg", ctx.cnt["dg"])
    blk.emit()



CAP = 2560
BIG = float(2 ** 26)
RG8 = [[0, 1, 2, 3, 4, 5, 6, 7]]


def alloc_stage_B(nc):
    sb = nc.alloc_sbuf_tensor
    S = {}
    A2 = sb("B_A2", [128, 2 * NCH * TB], F32)
    S["XF"] = A2.ap()[:, 0:NCH * TB]
    S["XB"] = A2.ap().bitcast(BF16)[:, 2 * NCH * TB:3 * NCH * TB]
    S["SQ"] = A2.ap().bitcast(BF16)[:, 3 * NCH * TB:4 * NCH * TB]
    S["HT"] = A2.ap().bitcast(BF16)[:, 0:32 * TB]
    S["MU"] = sb("B_MU", [128, TB], F32)
    S["RSTD"] = sb("B_RSTD", [128, TB], F32)
    S["TF1"] = sb("B_TF1", [128, TB], F32)
    S["TF2"] = sb("B_TF2", [128, TB], F32)
    A1 = sb("B_A1", [128, NCH * CAP], BF16)
    S["XLNB"] = A1.ap()[:, 0:NCH * T]
    S["BUF"] = A1.ap()
    S["TOK"] = sb("B_TOK", [128, D], BF16)
    S["RWF"] = sb("B_RWF", [128, NCH * 8], F32)
    S["RBROW"] = sb("B_RBROW", [1, 8], BF16)
    S["LGT"] = sb("B_LGT", [128, 16 * 8], F32)
    S["GT"] = sb("B_GT", [128, 16 * 8], F32)
    S["MK1"] = sb("B_MK1", [128, 16 * 8], F32)
    S["MK2"] = sb("B_MK2", [128, 16 * 8], F32)
    S["MKB"] = sb("B_MKB", [128, 16 * 8], BF16)
    S["T8A"] = sb("B_T8A", [128, 8], F32)
    S["T8B"] = sb("B_T8B", [128, 8], F32)
    S["T8C"] = sb("B_T8C", [128, 8], F32)
    S["T8D"] = sb("B_T8D", [128, 8], F32)
    S["TC1"] = sb("B_TC1", [128, 1], F32)
    S["TC2"] = sb("B_TC2", [128, 1], F32)
    S["MKE"] = sb("B_MKE", [8, T], F32)
    S["CUM"] = sb("B_CUM", [8, T], F32)
    S["GCUM"] = sb("B_GCUM", [8, T], F32)
    S["DSTF"] = sb("B_DSTF", [8, T], F32)
    S["CNT8"] = sb("B_CNT8", [8, 1], F32)
    S["CNT64"] = sb("B_CNT64", [64, 1], F32)
    S["OFFP"] = sb("B_OFFP", [8, 1], F32)
    S["V8A"] = sb("B_V8A", [8, 4], F32)
    S["ROW8"] = sb("B_ROW8", [1, 16], F32)
    S["GOB"] = sb("B_GOB", [128, 8], F32)
    S["DELB"] = sb("B_DELB", [128, 8], F32)
    S["IDF1"] = sb("B_IDF1", [128, 20], F32)
    S["IDF2"] = sb("B_IDF2", [128, 20], F32)
    S["SIOTA"] = sb("B_SIOTA", [128, 20], F32)
    S["IDXG"] = sb("B_IDXG", [128, 20], I32)
    S["IDXL1"] = sb("B_IDXL1", [128, 16], I32)
    S["IDXL2"] = sb("B_IDXL2", [128, 16], I32)
    S["IDX1"] = sb("B_IDX1", [128, 16], I32)
    S["IDX2"] = sb("B_IDX2", [128, 16], I32)
    S["W1TOK"] = sb("B_W1TOK", [128, 16], F32)
    S["W2TOK"] = sb("B_W2TOK", [128, 16], F32)
    S["YF"] = sb("B_YF", [128, NCH * TB], BF16)
    S["B1S"] = sb("B_B1S", [128, 32], F32)
    S["B2S"] = sb("B_B2S", [128, 8], F32)
    S["BW"] = [sb(f"B_BW{i}", [128, 32 * 128], BF16) for i in range(2)]
    S["G1"] = sb("B_G1", [128, D], BF16)
    S["G2"] = sb("B_G2", [128, D], BF16)
    S["GS"] = sb("B_GS", [128, D], F32)
    S["XR"] = sb("B_XR", [128, NCH * 128], F32)
    S["OT"] = [sb(f"B_OT{i}", [128, 128], F32) for i in range(2)]
    S["SEL8F"] = sb("B_SEL8F", [64, 8], F32)
    S["SELLO"] = sb("B_SELLO", [64, 8], F32)
    S["SELPF"] = sb("B_SELPF", [64, 8], F32)
    S["TRI8S"] = sb("B_TRI8S", [8, 8], F32)
    S["SH8"] = sb("B_SH8", [8, 8], F32)
    S["R4096"] = sb("B_R4096", [8, 1], F32)
    S["RBM1"] = sb("B_RBM1", [128, 8], F32)
    return S


def emit_stage_B(ctx, S, C, pb, io):
    nc = ctx.nc
    x1T, x2T = io["x1T"], io["x2T"]
    rw, rb = io["L0_ffn_router_w"], io["L0_ffn_router_b"]
    w1e, w2e, b1e, b2e = io["w1_e"], io["w2_e"], io["b1_e"], io["b2_e"]
    myxln, mycnt = io["scr_myxln"], io["scr_mycnt"]
    cnt64d = io["scr_cnt64"]
    sendb, sball = io["scr_sendb"], io["scr_sball"]
    yd, yall = io["scr_yd"], io["scr_yall"]

    blk = Blk(ctx)
    blk.dma("sync", S["RWF"][:, :].rearrange("p (c n) -> p c n", c=NCH),
            rw[:, :].rearrange("(c p) n -> p c n", c=NCH), "ds")
    blk.dma("sync", S["SEL8F"][:, :], io["sel8f"][:, :], "ds")
    blk.dma("sync", S["SELLO"][:, :], io["sello"][:, :], "ds")
    blk.dma("sync", S["SELPF"][:, :], io["selp"][:, :], "ds")
    blk.dma("sync", S["TRI8S"][:, :], io["tri8s"][:, :], "ds")
    blk.dma("sync", S["SH8"][:, :], io["sh8"][:, :], "ds")
    blk.dma("sync", S["R4096"][:, :], io["r4096"][:, :], "ds")
    blk.dma("sync", S["RBM1"][:, :], io["rbasem1"][:, :], "ds")
    blk.dma("sync", S["SIOTA"][:, :], io["siota"][:, :], "ds")
    blk.dma("gpsimd", S["RBROW"][:, :], rb[None, :], "dg")
    blk.dma("gpsimd", S["XR"][0:32, 0:128],
            b1e[:].rearrange("(a b) -> a b", b=128), "dg")
    blk.dma("gpsimd", S["XR"][0:8, 128:256],
            b2e[:].rearrange("(a b) -> a b", b=128), "dg")
    blk.wait("tensor", "dg", ctx.cnt["dg"])
    blk.wait("tensor", "ds", ctx.cnt["ds"])
    blk.tr(pb[6][:, 0:32], S["XR"][0:32, 0:128], C["IDF"][0:32, 0:32])
    v2 = blk.tr(pb[7][:, 0:8], S["XR"][0:8, 128:256], C["IDF"][0:8, 0:8],
                inc="mm")
    blk.wait("vector", "mm", v2)
    blk.add("vector", lambda e: e.tensor_copy(S["B1S"][:, :], pb[6][:, 0:32]))
    blk.add("vector", lambda e: e.tensor_copy(S["B2S"][:, :], pb[7][:, 0:8]))
    blk.emit()

    for tb in range(NTB):
        blk0 = Blk(ctx)
        blk0.dma("gpsimd", S["XF"][:, :].rearrange("p (c n) -> p c n", c=NCH),
                 x1T[:, tb * TB:(tb + 1) * TB].rearrange("(c p) n -> p c n",
                                                         c=NCH), "dg")
        blk0.wait("gpsimd", "dg", ctx.cnt["dg"])
        blk0.emit()

        def logits_cb(blk, c, tf1, fv, tb=tb):
            blk.wait("tensor", "ev", fv)
            v = None
            for tt in range(4):
                v = blk.mm(pb[2 + tt][:, 0:8],
                           tf1[:, tt * 128:(tt + 1) * 128],
                           S["RWF"][:, c * 8:(c + 1) * 8],
                           start=(c == 0), stop=False,
                           inc=("mm" if tt == 3 else None))
            return v

        emit_ln(ctx, S, C, pb, S["XF"], None,
                dst_fn=lambda c, tb=tb: S["XLNB"][:, c * T + tb * TB:
                                                  c * T + (tb + 1) * TB],
                fp32_cb=logits_cb)

        blk = Blk(ctx)
        for tt in range(4):
            v = blk.mm(pb[2 + tt][:, 0:8], C["ONES128"][0:1, :],
                       S["RBROW"][0:1, :], start=False, stop=True, inc="mm")
            blk.wait("vector", "mm", v)
            blk.add("vector", lambda e, tt=tt, tb=tb: e.tensor_copy(
                S["LGT"][:, (tb * 4 + tt) * 8:(tb * 4 + tt + 1) * 8],
                pb[2 + tt][:, 0:8]), inc="ev")
        blk.emit()

        blk = Blk(ctx)
        pr_ev = {}
        for tt in range(4):
            for c in range(NCH):
                bk = c % 2
                if pr_ev.get(bk):
                    blk.wait("tensor", "ev", pr_ev[bk])
                v = blk.tr(pb[bk].bitcast(BF16)[:, 0:128],
                           S["XLNB"][:, c * T + tb * TB + tt * 128:
                                     c * T + tb * TB + (tt + 1) * 128],
                           C["IDB"][:, :], inc="mm")
                blk.wait("vector", "mm", v)
                pr_ev[bk] = blk.add(
                    "vector", lambda e, c=c, bk=bk: e.tensor_copy(
                        S["TOK"][:, c * 128:(c + 1) * 128],
                        pb[bk].bitcast(BF16)[:, 0:128]), inc="ev")
            blk.wait("sync", "ev", ctx.cnt["ev"])
            sv = blk.dma("sync",
                         myxln[tb * TB + tt * 128:tb * TB + (tt + 1) * 128, :],
                         S["TOK"][:, :], "ds")
            blk.wait("vector", "ds", sv)
        blk.emit()

        blk = Blk(ctx)
        for tt in range(4):
            j = tb * 4 + tt
            LG = S["LGT"][:, j * 8:(j + 1) * 8]
            blk.vf(lambda e, LG=LG: e.reduce_max(
                S["TC1"][:, :], LG, axis=AX.X))
            blk.vf(lambda e, LG=LG, j=j: e.tensor_scalar(
                S["MK1"][:, j * 8:(j + 1) * 8], LG, S["TC1"][:, 0:1], None,
                ALU.is_equal))
            blk.vf(lambda e, LG=LG, j=j: e.scalar_tensor_tensor(
                S["T8A"][:, :], S["MK1"][:, j * 8:(j + 1) * 8], -1e30, LG,
                ALU.mult, ALU.add))
            blk.vf(lambda e: e.reduce_max(
                S["TC2"][:, :], S["T8A"][:, :], axis=AX.X))
            blk.vf(lambda e, j=j: e.tensor_scalar(
                S["MK2"][:, j * 8:(j + 1) * 8], S["T8A"][:, :],
                S["TC2"][:, 0:1], None, ALU.is_equal))
            blk.vf(lambda e, j=j: e.tensor_add(
                S["MKB"][:, j * 8:(j + 1) * 8],
                S["MK1"][:, j * 8:(j + 1) * 8],
                S["MK2"][:, j * 8:(j + 1) * 8]))
            vv = blk.vf(lambda e: e.tensor_scalar(
                S["TC2"][:, :], S["TC1"][:, :], -1.0, None, ALU.mult))
            blk.wait("scalar", "ev", vv)
            av = blk.add("scalar", lambda e, LG=LG: e.activation(
                S["T8B"][:, :], LG, AF.Exp, bias=S["TC2"][:, 0:1]), inc="ea")
            blk.wait("vector", "ea", av)
            blk.vf(lambda e: e.reduce_sum(
                S["TC1"][:, :], S["T8B"][:, :], axis=AX.X))
            blk.vf(lambda e: e.reciprocal(
                S["TC2"][:, 0:1], S["TC1"][:, 0:1]))
            blk.vf(lambda e, j=j: e.tensor_scalar(
                S["GT"][:, j * 8:(j + 1) * 8], S["T8B"][:, :],
                S["TC2"][:, 0:1], None, ALU.mult))
        blk.emit()

    blk = Blk(ctx)
    pr = {}
    for j in range(16):
        bk = j % 2
        if pr.get(bk):
            blk.wait("tensor", "ev", pr[bk])
        v = blk.tr(pb[bk].bitcast(BF16)[0:8, 0:128],
                   S["MKB"][:, j * 8:(j + 1) * 8], C["IDB"][:, :], inc="mm")
        blk.wait("vector", "mm", v)
        pr[bk] = blk.add("vector", lambda e, j=j, bk=bk: e.tensor_copy(
            S["MKE"][0:8, j * 128:(j + 1) * 128],
            pb[bk].bitcast(BF16)[0:8, 0:128]), inc="ev")
    blk.wait("vector", "ev", ctx.cnt["ev"])
    v = blk.add("vector", lambda e: e.tensor_tensor_scan(
        S["CUM"][:, :], S["MKE"][:, :], S["MKE"][:, :], 0.0,
        ALU.add, ALU.bypass), inc="ev")
    blk.wait("vector", "ev", v)
    v2 = blk.add("vector", lambda e: e.tensor_copy(
        S["CNT8"][:, :], S["CUM"][:, T - 1:T]), inc="ev")
    blk.wait("sync", "ev", ctx.cnt["ev"])
    s2 = blk.dma("sync", mycnt[:, :], S["CNT8"][:, :], "ds")
    blk.wait("gpsimd", "ds", s2)
    blk.add("gpsimd", lambda e: e.collective_compute(
        "AllGather", ALU.bypass, replica_groups=RG8,
        ins=[mycnt.ap().opt()], outs=[cnt64d.ap().opt()]), inc="cc")
    blk.wait("gpsimd", "cc", ctx.cnt["cc"])
    l2 = blk.dma("gpsimd", S["CNT64"][:, :], cnt64d[:, :], "dg")
    blk.wait("gpsimd", "dg", l2)
    blk.emit()

    blk = Blk(ctx)
    vv = blk.mm(pb[5][0:8, 0:1], S["TRI8S"][:, :], S["CNT8"][:, :],
                start=True, stop=True, inc="mm")
    blk.wait("vector", "mm", vv)
    blk.vf(lambda e: e.tensor_copy(S["V8A"][0:8, 0:1], pb[5][0:8, 0:1]))
    dst_v = blk.vf(lambda e: e.tensor_scalar(
        S["DSTF"][:, :], S["CUM"][:, :], S["V8A"][0:8, 0:1], -1.0,
        ALU.add, ALU.add))
    blk.wait("tensor", "ev", dst_v)
    sc_prev = None
    cp_ev = {}
    for j in range(16):
        bk = j % 2
        if cp_ev.get(bk):
            blk.wait("tensor", "ev", cp_ev[bk])
        v = blk.tr(pb[bk][:, 0:8], S["DSTF"][0:8, j * 128:(j + 1) * 128],
                   C["IDF"][0:8, 0:8], inc="mm")
        blk.wait("vector", "mm", v)
        cp_ev[bk] = blk.vf(lambda e, bk=bk: e.tensor_copy(
            S["T8D"][:, :], pb[bk][:, 0:8]))
        blk.vf(lambda e, j=j: e.tensor_mul(
            S["T8C"][:, :], S["MK1"][:, j * 8:(j + 1) * 8], S["T8D"][:, :]))
        blk.vf(lambda e: e.reduce_sum(
            S["TC1"][:, :], S["T8C"][:, :], axis=AX.X))
        blk.vf(lambda e, j=j: e.tensor_copy(
            S["IDXL1"][:, j:j + 1], S["TC1"][:, :]))
        blk.vf(lambda e, j=j: e.tensor_mul(
            S["T8C"][:, :], S["MK2"][:, j * 8:(j + 1) * 8], S["T8D"][:, :]))
        blk.vf(lambda e: e.reduce_sum(
            S["TC1"][:, :], S["T8C"][:, :], axis=AX.X))
        v = blk.vf(lambda e, j=j: e.tensor_copy(
            S["IDXL2"][:, j:j + 1], S["TC1"][:, :]))
        blk.wait("gpsimd", "ev", v)
        if sc_prev is not None:
            blk.wait("gpsimd", "dg", sc_prev)
        lv = blk.dma("gpsimd", S["TOK"][:, :],
                     myxln[j * 128:(j + 1) * 128, :], "dg")
        blk.wait("gpsimd", "dg", lv)
        blk.add("gpsimd", lambda e, j=j: e.indirect_dma_start(
            sendb[0:128, :],
            IndirectOffsetOnAxis(S["IDXL1"][:, j:j + 1], 0),
            S["TOK"][:, :], None), inc="dg", n=16)
        sc_prev = blk.add("gpsimd", lambda e, j=j: e.indirect_dma_start(
            sendb[0:128, :],
            IndirectOffsetOnAxis(S["IDXL2"][:, j:j + 1], 0),
            S["TOK"][:, :], None), inc="dg", n=16)
    blk.wait("gpsimd", "dg", ctx.cnt["dg"])
    blk.add("gpsimd", lambda e: e.collective_compute(
        "AllGather", ALU.bypass, replica_groups=RG8,
        ins=[sendb.ap().opt()], outs=[sball.ap().opt()]), inc="cc")
    blk.wait("gpsimd", "cc", ctx.cnt["cc"])
    v = blk.mm(pb[4][0:8, 0:1], S["SELPF"][:, :], S["CNT64"][:, :],
               start=True, stop=True, inc="mm")
    blk.wait("vector", "mm", v)
    blk.vf(lambda e: e.tensor_copy(S["OFFP"][:, :], pb[4][0:8, 0:1]))
    blk.vf(lambda e: e.tensor_scalar(
        S["GCUM"][:, :], S["CUM"][:, :], S["OFFP"][0:8, 0:1], None, ALU.add))
    blk.emit()

    blk = Blk(ctx)
    v = blk.mm(pb[0][0:8, 0:1], S["SEL8F"][:, :], S["CNT64"][:, :],
               start=True, stop=True)
    v = blk.mm(pb[1][0:8, 0:1], S["SELLO"][:, :], S["CNT64"][:, :],
               start=True, stop=True, inc="mm")
    blk.wait("vector", "mm", v)
    blk.vf(lambda e: e.tensor_copy(S["V8A"][0:8, 0:1],
                                   pb[0][0:8, 0:1]))
    vv = blk.vf(lambda e: e.tensor_copy(
        S["V8A"][0:8, 1:2], pb[1][0:8, 0:1]))
    blk.wait("tensor", "ev", vv)
    v = blk.mm(pb[2][0:8, 0:1], S["TRI8S"][:, :], S["V8A"][0:8, 0:1],
               start=True, stop=True, inc="mm")
    blk.wait("vector", "mm", v)
    blk.vf(lambda e: e.tensor_copy(S["V8A"][0:8, 2:3], pb[2][0:8, 0:1]))
    blk.vf(lambda e: e.tensor_sub(
        S["V8A"][0:8, 3:4], S["V8A"][0:8, 1:2], S["V8A"][0:8, 2:3]))
    vv = blk.vf(lambda e: e.tensor_add(
        S["V8A"][0:8, 3:4], S["V8A"][0:8, 3:4], S["R4096"][0:8, 0:1]))
    blk.wait("tensor", "ev", vv)
    v = blk.mm(pb[3][0:8, 0:1], S["SH8"][:, :], S["V8A"][0:8, 3:4],
               start=True, stop=True, inc="mm")
    blk.wait("vector", "mm", v)
    vv = blk.vf(lambda e: e.tensor_sub(
        S["V8A"][0:8, 1:2], S["V8A"][0:8, 3:4], pb[3][0:8, 0:1]))
    blk.wait("tensor", "ev", vv)
    v = blk.tr(pb[4][0:1, 0:8], S["V8A"][0:8, 2:3], C["IDF"][0:8, 0:8])
    v = blk.tr(pb[5][0:1, 0:8], S["V8A"][0:8, 1:2], C["IDF"][0:8, 0:8],
               inc="mm")
    blk.wait("vector", "mm", v)
    blk.vf(lambda e: e.tensor_copy(S["ROW8"][0:1, 0:8], pb[4][0:1, 0:8]))
    vv = blk.vf(lambda e: e.tensor_copy(
        S["ROW8"][0:1, 8:16], pb[5][0:1, 0:8]))
    blk.wait("tensor", "ev", vv)
    v = blk.mm(pb[6][:, 0:8], C["ONESR"][0:1, :], S["ROW8"][0:1, 0:8],
               start=True, stop=True)
    v = blk.mm(pb[7][:, 0:8], C["ONESR"][0:1, :], S["ROW8"][0:1, 8:16],
               start=True, stop=True, inc="mm")
    blk.wait("vector", "mm", v)
    blk.vf(lambda e: e.tensor_copy(S["GOB"][:, :], pb[6][:, 0:8]))
    blk.vf(lambda e: e.tensor_copy(S["DELB"][:, :], pb[7][:, 0:8]))
    blk.vf(lambda e: e.tensor_copy(S["IDF1"][:, :], S["SIOTA"][:, :]))
    cur, nxt = "IDF1", "IDF2"
    for r in range(8):
        blk.vf(lambda e, r=r: e.tensor_scalar(
            S["GS"][:, 0:20], S["SIOTA"][:, :], S["GOB"][:, r:r + 1], None,
            ALU.is_ge))
        blk.vf(lambda e, r=r, cur=cur, nxt=nxt:
               e.scalar_tensor_tensor(
                   S[nxt][:, :], S["GS"][:, 0:20], S["DELB"][:, r:r + 1],
                   S[cur][:, :], ALU.mult, ALU.add))
        cur, nxt = nxt, cur
    blk.vf(lambda e, cur=cur, nxt=nxt: e.tensor_scalar_min(
        S[nxt][:, :], S[cur][:, :], 32767.0))
    v = blk.vf(lambda e, nxt=nxt: e.tensor_copy(
        S["IDXG"][:, :], S[nxt][:, :]))
    blk.wait("gpsimd", "ev", v)
    pr = {}
    g_prev = None
    for tt in range(20):
        if g_prev is not None:
            blk.wait("gpsimd", "mm", g_prev)
        lv = blk.add("gpsimd", lambda e, tt=tt: e.indirect_dma_start(
            S["TOK"][:, :], None, sball[0:128, :],
            IndirectOffsetOnAxis(S["IDXG"][:, tt:tt + 1], 0)),
            inc="dg", n=16)
        blk.wait("tensor", "dg", lv)
        for c in range(NCH):
            bk = c % 2
            if pr.get(bk):
                blk.wait("tensor", "ev", pr[bk])
            g_prev = blk.tr(pb[bk].bitcast(BF16)[:, 0:128],
                            S["TOK"][:, c * 128:(c + 1) * 128],
                            C["IDB"][:, :], inc="mm")
            blk.wait("vector", "mm", g_prev)
            pr[bk] = blk.add("vector", lambda e, c=c, tt=tt, bk=bk:
                             e.tensor_copy(
                                 S["BUF"][:, c * CAP + tt * 128:
                                          c * CAP + (tt + 1) * 128],
                                 pb[bk].bitcast(BF16)[:, 0:128]), inc="ev")
    blk.emit()

    for tb in range(5):
        blk = Blk(ctx)
        w1_specs = [w1e[:, hc * 128:(hc + 1) * 128].rearrange(
            "(c p) n -> p c n", c=NCH) for hc in range(32)]

        def h_evac(blk, i, mv, tb=tb):
            blk.wait("scalar", "mm", mv)
            return ("ea", blk.add("scalar", lambda e, i=i: e.activation(
                S["HT"][:, i * TB:(i + 1) * TB], pb[i % 4][:, :], AF.Gelu,
                bias=S["B1S"][:, i:i + 1]), inc="ea"))

        stream_mm(ctx, blk, w1_specs, S["BW"], NCH,
                  moving=lambda i, kc, tb=tb:
                  S["BUF"][:, kc * CAP + tb * TB:kc * CAP + (tb + 1) * TB],
                  psum_of=lambda i: pb[i % 4][:, :], evac=h_evac)
        blk.emit()
        blk = Blk(ctx)
        w2_specs = [w2e[:, oc * 128:(oc + 1) * 128].rearrange(
            "(c p) n -> p c n", c=32) for oc in range(NCH)]

        def y_evac(blk, i, mv, tb=tb):
            blk.wait("scalar", "mm", mv)
            return ("ea", blk.add("scalar", lambda e, i=i: e.activation(
                S["YF"][:, i * TB:(i + 1) * TB], pb[i % 4][:, :], AF.Identity,
                bias=S["B2S"][:, i:i + 1]), inc="ea"))

        stream_mm(ctx, blk, w2_specs, S["BW"], 32,
                  moving=lambda i, kc: S["HT"][:, kc * TB:(kc + 1) * TB],
                  psum_of=lambda i: pb[i % 4][:, :], evac=y_evac)
        blk.emit()
        blk = Blk(ctx)
        blk.wait("tensor", "ea", ctx.cnt["ea"])
        pr = {}
        for tt in range(4):
            for c in range(NCH):
                bk = c % 2
                if pr.get(bk):
                    blk.wait("tensor", "ev", pr[bk])
                v = blk.tr(pb[bk].bitcast(BF16)[:, 0:128],
                           S["YF"][:, c * TB + tt * 128:
                                   c * TB + (tt + 1) * 128],
                           C["IDB"][:, :], inc="mm")
                blk.wait("vector", "mm", v)
                pr[bk] = blk.add("vector", lambda e, c=c, bk=bk:
                                 e.tensor_copy(
                                     S["TOK"][:, c * 128:(c + 1) * 128],
                                     pb[bk].bitcast(BF16)[:, 0:128]),
                                 inc="ev")
            blk.wait("sync", "ev", ctx.cnt["ev"])
            sv = blk.dma("sync",
                         yd[tb * TB + tt * 128:tb * TB + (tt + 1) * 128, :],
                         S["TOK"][:, :], "ds")
            blk.wait("vector", "ds", sv)
        blk.emit()

    blk = Blk(ctx)
    blk.wait("gpsimd", "ds", ctx.cnt["ds"])
    blk.add("gpsimd", lambda e: e.collective_compute(
        "AllGather", ALU.bypass, replica_groups=RG8,
        ins=[yd.ap().opt()], outs=[yall.ap().opt()]), inc="cc")
    blk.wait("gpsimd", "cc", ctx.cnt["cc"])
    blk.emit()

    for j in range(16):
        blk = Blk(ctx)
        v = blk.tr(pb[0][:, 0:8], S["GCUM"][0:8, j * 128:(j + 1) * 128],
                   C["IDF"][0:8, 0:8], inc="mm")
        blk.wait("vector", "mm", v)
        blk.vf(lambda e: e.tensor_copy(S["T8A"][:, :], pb[0][:, 0:8]))
        blk.vf(lambda e: e.tensor_scalar(
            S["T8B"][:, :], S["T8A"][:, :], float(CAP), None, ALU.is_le))
        blk.vf(lambda e: e.tensor_add(
            S["T8D"][:, :], S["T8A"][:, :], S["RBM1"][:, :]))
        blk.vf(lambda e: e.tensor_copy(S["T8A"][:, :], S["T8D"][:, :]))
        for k, (MKk, IDXk, WTOKk) in enumerate(
                [(S["MK1"], S["IDX1"], S["W1TOK"]),
                 (S["MK2"], S["IDX2"], S["W2TOK"])]):
            blk.vf(lambda e, MKk=MKk, j=j: e.tensor_mul(
                S["T8C"][:, :], MKk[:, j * 8:(j + 1) * 8], S["T8B"][:, :]))
            blk.vf(lambda e: e.tensor_mul(
                S["T8D"][:, :], S["T8C"][:, :], S["T8A"][:, :]))
            blk.vf(lambda e: e.reduce_sum(
                S["TC1"][:, :], S["T8D"][:, :], axis=AX.X))
            blk.vf(lambda e, IDXk=IDXk, j=j: e.tensor_copy(
                IDXk[:, j:j + 1], S["TC1"][:, :]))
            blk.vf(lambda e, j=j: e.tensor_mul(
                S["T8D"][:, :], S["T8C"][:, :],
                S["GT"][:, j * 8:(j + 1) * 8]))
            blk.vf(lambda e, WTOKk=WTOKk, j=j: e.reduce_sum(
                WTOKk[:, j:j + 1], S["T8D"][:, :], axis=AX.X))
        blk.wait("gpsimd", "ev", ctx.cnt["ev"])
        blk.add("gpsimd", lambda e, j=j: e.indirect_dma_start(
            S["G1"][:, :], None, yall[0:128, :],
            IndirectOffsetOnAxis(S["IDX1"][:, j:j + 1], 0)), inc="dg", n=16)
        blk.add("gpsimd", lambda e, j=j: e.indirect_dma_start(
            S["G2"][:, :], None, yall[0:128, :],
            IndirectOffsetOnAxis(S["IDX2"][:, j:j + 1], 0)), inc="dg", n=16)
        lv = blk.dma("gpsimd", S["XR"][:, :].rearrange("p (c n) -> p c n",
                                                       c=NCH),
                     x1T[:, j * 128:(j + 1) * 128].rearrange(
                         "(c p) n -> p c n", c=NCH), "dg")
        blk.wait("vector", "dg", ctx.cnt["dg"])
        blk.add("vector", lambda e, j=j: e.tensor_scalar(
            S["GS"][:, :], S["G1"][:, :], S["W1TOK"][:, j:j + 1], None,
            ALU.mult))
        v = blk.add("vector", lambda e, j=j: e.scalar_tensor_tensor(
            S["GS"][:, :], S["G2"][:, :], S["W2TOK"][:, j:j + 1],
            S["GS"][:, :], ALU.mult, ALU.add), inc="ev")
        blk.wait("tensor", "ev", v)
        st_prev = {}
        ev_prev = {}
        for c in range(NCH):
            bk = 2 + c % 2
            if ev_prev.get(bk):
                blk.wait("tensor", "ev", ev_prev[bk])
            v = blk.tr(pb[bk][:, 0:128],
                       S["GS"][:, c * 128:(c + 1) * 128], C["IDF"][:, :],
                       inc="mm")
            blk.wait("vector", "mm", v)
            ot = S["OT"][c % 2]
            if st_prev.get(c % 2):
                blk.wait("vector", "ds", st_prev[c % 2])
            vv = blk.add("vector", lambda e, c=c, bk=bk, ot=ot: e.tensor_add(
                ot[:, :], pb[bk][:, 0:128],
                S["XR"][:, c * 128:(c + 1) * 128]), inc="ev")
            ev_prev[bk] = vv
            blk.wait("sync", "ev", vv)
            st_prev[c % 2] = blk.dma(
                "sync", x2T[c * 128:(c + 1) * 128, j * 128:(j + 1) * 128],
                ot[:, :], "ds")
        blk.emit()

    if "dbg" in io:
        blk = Blk(ctx)
        for nm, hd in io["dbg"].items():
            if nm == "BUFS":
                blk.dma("sync", hd[:, :], S["BUF"][:, 0:NCH * CAP], "ds")
            else:
                blk.dma("sync", hd[:, :], S[nm][:, :] if not isinstance(
                    S[nm], list) else S[nm][0][:, :], "ds")
        blk.wait("sync", "ds", ctx.cnt["ds"])
        blk.emit()

    blk = Blk(ctx)
    blk.wait("sync", "ds", ctx.cnt["ds"])
    blk.wait("gpsimd", "dg", ctx.cnt["dg"])
    blk.emit()



CW_C1, CW_C2, CW_C3 = 6.28125, 0.0019378662, -2.5590314e-06
INV2PI = 0.15915494
MAGIC = 12582912.0
TAY_A = float(0.125 / np.sqrt(2.0))
TAY_B = float(1.0 / np.sqrt(2.0))
PIO2 = float(np.float32(np.pi / 2))
TWO_PI = float(np.float32(2 * np.pi))
PI_F = float(np.float32(np.pi))


def alloc_stage_C(nc):
    sb = nc.alloc_sbuf_tensor
    S = {}
    A2 = sb("C_A2", [128, 2 * NCH * TB], F32)
    S["XF"] = A2.ap()[:, 0:NCH * TB]
    S["LN1F"] = A2.ap()[:, NCH * TB:2 * NCH * TB]
    S["XB"] = sb("C_XB", [128, NCH * TB], BF16)
    S["SQ"] = sb("C_SQ", [128, NCH * TB], BF16)
    S["MU"] = sb("C_MU", [128, TB], F32)
    S["RSTD"] = sb("C_RSTD", [128, TB], F32)
    S["TF1"] = sb("C_TF1", [128, TB], F32)
    S["TF2"] = sb("C_TF2", [128, TB], F32)
    S["TF3"] = None
    S["XLN2"] = sb("C_XLN2", [128, NCH * TB], BF16)
    S["TF3"] = S["MU"]
    QB = sb("C_QB", [128, NCH * T], BF16)
    KB = sb("C_KB", [128, NCH * T], BF16)
    S["QB"] = QB.ap()
    S["KB"] = KB.ap()
    S["VT"] = sb("C_VT", [128, 16 * 16 * 80], BF16)
    S["AO"] = S["XB"]
    S["CT"] = sb("C_CT", [128, T], BF16)
    S["ST"] = sb("C_ST", [128, T], BF16)
    qf = QB.ap().bitcast(F32)
    kf = KB.ap().bitcast(F32)
    S["A32A"] = qf[0:32, 0:T]
    S["A32B"] = qf[0:32, T:2 * T]
    S["A32C"] = qf[0:32, 2 * T:3 * T]
    S["SINT"] = qf[0:32, 3 * T:4 * T]
    S["COST"] = kf[0:32, 0:T]
    S["IOTA32"] = kf[0:32, T:2 * T]
    S["TH"] = sb("C_TH", [32, 4], F32)
    S["TAYB"] = sb("C_TAYB", [128, 1], F32)
    S["LNW"] = sb("C_LNW", [128, NCH], F32)
    S["LNB"] = sb("C_LNB", [128, NCH], F32)
    S["VSH"] = sb("C_VSH", [65, 16], F32)
    S["RD"] = [sb(f"C_RD{i}", [1, TB], F32) for i in range(2)]
    S["RDB"] = [sb(f"C_RDB{i}", [64, TB], F32) for i in range(2)]
    S["NUM"] = [sb(f"C_NUM{i}", [64, TB], BF16) for i in range(2)]
    S["PT"] = [sb(f"C_PT{i}", [128, TB], BF16) for i in range(2)]
    S["WS"] = [sb(f"C_WS{i}", [128, NCH * 128], BF16) for i in range(2)]
    S["WM"] = S["WS"]
    S["BASE32"] = sb("C_BASE32", [32, 1], F32)
    S["ECX"] = sb("C_ECX", [32, 128], F32)
    S["ESX"] = sb("C_ESX", [32, 128], F32)
    S["PERM"] = sb("C_PERM", [128, 128], BF16)
    return S


def emit_stage_C(ctx, S, C, pb, io):
    nc = ctx.nc
    x2T, x3T = io["x2T"], io["x3T"]
    wq, wkv, wo = io["L1_attn_w_q"], io["L1_attn_w_kv"], io["L1_attn_w_o"]
    lnw, lnb, theta = io["L1_attn_ln_w"], io["L1_attn_ln_b"], \
        io["L1_attn_theta"]

    blk = Blk(ctx)
    blk.dma("sync", S["IOTA32"][:, :], io["c_iota32"][:, :], "ds")
    blk.dma("sync", S["BASE32"][:, :], io["c_base32"][:, :], "ds")
    blk.dma("sync", S["ECX"][:, :], io["c_ec"][:, :], "ds")
    blk.dma("sync", S["ESX"][:, :], io["c_es"][:, :], "ds")
    blk.dma("sync", S["PERM"][:, :], io["c_perm"][:, :], "ds")
    blk.dma("gpsimd", S["TH"][0:1, 0:1], theta[None, None], "dg")
    blk.add("gpsimd", lambda e: e.memset(S["TAYB"][:, :], TAY_B))
    blk.dma("gpsimd", S["XF"][0:8, 0:128],
            lnw[:].rearrange("(a b) -> a b", b=128), "dg")
    blk.dma("gpsimd", S["XF"][0:8, 128:256],
            lnb[:].rearrange("(a b) -> a b", b=128), "dg")
    blk.wait("tensor", "dg", ctx.cnt["dg"])
    blk.wait("tensor", "ds", ctx.cnt["ds"])
    blk.tr(pb[0][:, 0:8], S["XF"][0:8, 0:128], C["IDF"][0:8, 0:8])
    v = blk.tr(pb[1][:, 0:8], S["XF"][0:8, 128:256], C["IDF"][0:8, 0:8],
               inc="mm")
    blk.wait("vector", "mm", v)
    blk.vf(lambda e: e.tensor_copy(S["LNW"][:, :], pb[0][:, 0:8]))
    blk.vf(lambda e: e.tensor_copy(S["LNB"][:, :], pb[1][:, 0:8]))
    v = blk.mm(pb[2][0:32, 0:1], C["ONESR"][0:1, 0:32], S["TH"][0:1, 0:1],
               start=True, stop=True, inc="mm")
    blk.wait("vector", "mm", v)
    blk.vf(lambda e: e.tensor_scalar(
        S["TH"][0:32, 1:2], pb[2][0:32, 0:1], 1.0 / 220.0, None, ALU.mult))
    blk.vf(lambda e: e.scalar_tensor_tensor(
        S["TH"][0:32, 3:4], S["TH"][0:32, 1:2], -220.0, pb[2][0:32, 0:1],
        ALU.mult, ALU.add))
    blk.vf(lambda e: e.scalar_tensor_tensor(
        S["TH"][0:32, 1:2], S["TH"][0:32, 3:4], 1.0 / 220.0,
        S["TH"][0:32, 1:2], ALU.mult, ALU.add))
    blk.vf(lambda e: e.tensor_mul(
        S["TH"][0:32, 2:3], S["BASE32"][:, :], S["TH"][0:32, 1:2]))
    blk.vf(lambda e: e.tensor_scalar(
        S["A32A"][:, :], S["IOTA32"][:, :], S["TH"][0:32, 2:3], None,
        ALU.mult))
    for (dst, shift) in ((S["SINT"], 0.0), (S["COST"], 0.25)):
        blk.vf(lambda e, shift=shift: e.tensor_scalar(
            S["A32B"][:, :], S["A32A"][:, :], INV2PI, shift + MAGIC,
            ALU.mult, ALU.add))
        blk.vf(lambda e: e.tensor_scalar(
            S["A32B"][:, :], S["A32B"][:, :], -MAGIC, None, ALU.add))
        blk.vf(lambda e: e.scalar_tensor_tensor(
            S["A32C"][:, :], S["A32B"][:, :], -CW_C1, S["A32A"][:, :],
            ALU.mult, ALU.add))
        blk.vf(lambda e: e.scalar_tensor_tensor(
            S["A32C"][:, :], S["A32B"][:, :], -CW_C2, S["A32C"][:, :],
            ALU.mult, ALU.add))
        blk.vf(lambda e: e.scalar_tensor_tensor(
            S["A32C"][:, :], S["A32B"][:, :], -CW_C3, S["A32C"][:, :],
            ALU.mult, ALU.add))
        if shift:
            blk.vf(lambda e: e.tensor_scalar(
                S["A32C"][:, :], S["A32C"][:, :], PIO2, None, ALU.add))
        blk.vf(lambda e: e.tensor_scalar(
            S["A32B"][:, :], S["A32C"][:, :], PI_F, None, ALU.is_gt))
        blk.vf(lambda e: e.scalar_tensor_tensor(
            S["A32C"][:, :], S["A32B"][:, :], -TWO_PI, S["A32C"][:, :],
            ALU.mult, ALU.add))
        blk.vf(lambda e: e.tensor_scalar(
            S["A32B"][:, :], S["A32C"][:, :], -PI_F, None, ALU.is_lt))
        v = blk.vf(lambda e: e.scalar_tensor_tensor(
            S["A32C"][:, :], S["A32B"][:, :], TWO_PI, S["A32C"][:, :],
            ALU.mult, ALU.add))
        blk.wait("scalar", "ev", v)
        blk.add("scalar", lambda e, dst=dst: e.activation(
            dst[:, :], S["A32C"][:, :], AF.Sin), inc="ea")
    blk.wait("tensor", "ea", ctx.cnt["ea"])
    ex_ev = {}
    for half, (tab, exp_, dst) in enumerate(
            [(S["COST"], S["ECX"], S["CT"]), (S["SINT"], S["ESX"], S["ST"])]):
        for sl in range(4):
            b = (half * 4 + sl) % 4
            if ex_ev.get(b):
                blk.wait("tensor", "ev", ex_ev[b])
            v = blk.mm(pb[b][:, :], exp_[:, :],
                       tab[:, sl * TB:(sl + 1) * TB], start=True, stop=True,
                       inc="mm")
            blk.wait("vector", "mm", v)
            ex_ev[b] = blk.add("vector", lambda e, dst=dst, sl=sl, b=b:
                               e.tensor_copy(dst[:, sl * TB:(sl + 1) * TB],
                                             pb[b][:, :]), inc="ev")
    blk.emit()

    for tb in range(NTB):
        blk0 = Blk(ctx)
        blk0.dma("gpsimd", S["XF"][:, :].rearrange("p (c n) -> p c n", c=NCH),
                 x2T[:, tb * TB:(tb + 1) * TB].rearrange(
                     "(c p) n -> p c n", c=NCH), "dg")
        blk0.wait("gpsimd", "dg", ctx.cnt["dg"])
        blk0.emit()
        emit_ln(ctx, S, C, pb, S["XF"], None,
                dst_fn=lambda c: S["LN1F"][:, c * TB:(c + 1) * TB])
        emit_ln(ctx, S, C, pb, S["LN1F"], S["XLN2"],
                ln_w_col=S["LNW"], ln_b_col=S["LNB"])

        blk = Blk(ctx)
        q_specs = [wq[:, oc * 128:(oc + 1) * 128].rearrange(
            "(c p) n -> p c n", c=NCH) for oc in range(NCH)] + \
            [wkv[:, oc * 128:(oc + 1) * 128].rearrange(
                "(c p) n -> p c n", c=NCH) for oc in range(NCH)]

        def qk_evac(blk, i, mv, tb=tb):
            blk.wait("vector", "mm", mv)
            dst = S["QB"] if i < NCH else S["KB"]
            oc = i % NCH
            return ("ev", blk.add("vector", lambda e, dst=dst, oc=oc:
                                  e.tensor_copy(
                                      dst[:, oc * T + tb * TB:
                                          oc * T + tb * TB + TB],
                                      pb[i % 4][:, :]), inc="ev"))

        stream_mm(ctx, blk, q_specs, S["WS"], NCH,
                  moving=lambda i, kc: S["XLN2"][:, kc * TB:(kc + 1) * TB],
                  psum_of=lambda i: pb[i % 4][:, :], evac=qk_evac)
        blk.emit()

        blk = Blk(ctx)
        m4ld = {}
        for kc in range(NCH):
            mb = S["WM"][kc % 2]
            if kc >= 2:
                blk.wait("gpsimd", "mm", m4ld[kc - 2])
            ld = blk.dma("gpsimd", mb[:, 0:D],
                         wkv[kc * 128:(kc + 1) * 128, D:2 * D], "dg")
            blk.wait("tensor", "dg", ld)
            lastmm = None
            for tt in range(4):
                for nv in range(2):
                    i = tt * 2 + nv
                    last_of_kc = (tt == 3 and nv == 1)
                    lastmm = blk.mm(
                        pb[i][:, :],
                        S["XLN2"][:, kc * TB + tt * 128:
                                  kc * TB + tt * 128 + 128],
                        mb[:, nv * 512:(nv + 1) * 512],
                        start=(kc == 0), stop=(kc == NCH - 1),
                        inc=("mm" if (kc == NCH - 1 or last_of_kc)
                             else None))
                    if kc == NCH - 1:
                        kt = tb * 4 + tt
                        blk.wait("vector", "mm", lastmm)
                        dst = S["VT"][:, kt * 1280 + nv * 640:
                                      kt * 1280 + (nv + 1) * 640] \
                            .rearrange("p (h n) -> p h n", h=8)[:, :, 0:64]
                        srcp = pb[i][:, :].rearrange("p (h n) -> p h n", h=8)
                        blk.add("vector", lambda e, dst=dst, srcp=srcp:
                                e.tensor_copy(dst, srcp), inc="ev")
            m4ld[kc] = lastmm
        blk.emit()

    blk = Blk(ctx)
    blk.add("gpsimd", lambda e: e.memset(
        S["VT"][:, :].rearrange("p (a b) -> p a b", b=80)[:, :, 64:65], 1.0))
    gv = blk.add("gpsimd", lambda e: e.engine_nop(), inc="eg")
    blk.wait("tensor", "eg", gv)
    for h in range(H):
        for kt in range(16):
            blk.mm(pb[h % 2][0:65, h // 2:h // 2 + 1],
                   S["VT"][:, (kt * 16 + h) * 80:(kt * 16 + h) * 80 + 65],
                   C["ONESC"][:, 0:1],
                   start=(kt == 0), stop=(kt == 15),
                   inc=("mm" if kt == 15 else None))
    blk.wait("vector", "mm", ctx.cnt["mm"])
    blk.vf(lambda e: e.tensor_scalar(
        S["VSH"][:, 0:8], pb[0][0:65, 0:8], 0.5, None, ALU.mult))
    blk.vf(lambda e: e.tensor_scalar(
        S["VSH"][:, 8:16], pb[1][0:65, 0:8], 0.5, None, ALU.mult))
    blk.emit()

    for X in (S["QB"], S["KB"]):
        blk = Blk(ctx)
        pr = {}
        for c in range(NCH):
            for sl in range(4):
                i = c * 4 + sl
                bk = i % 2
                if pr.get(bk):
                    blk.wait("tensor", "ev", pr[bk])
                v = blk.mm(pb[bk][:, :], S["PERM"][:, :],
                           X[:, c * T + sl * TB:c * T + (sl + 1) * TB],
                           start=True, stop=True, inc="mm")
                blk.wait("vector", "mm", v)
                blk.add("vector", lambda e, X=X, c=c, sl=sl: e.tensor_mul(
                    S["TF1"][:, :],
                    X[:, c * T + sl * TB:c * T + (sl + 1) * TB],
                    S["CT"][:, sl * TB:(sl + 1) * TB]))
                blk.add("vector", lambda e, bk=bk, sl=sl: e.tensor_mul(
                    S["TF2"][:, :], pb[bk][:, :],
                    S["ST"][:, sl * TB:(sl + 1) * TB]))
                pr[bk] = blk.add("vector", lambda e, X=X, c=c, sl=sl:
                                 e.tensor_add(
                                     X[:, c * T + sl * TB:
                                       c * T + (sl + 1) * TB],
                                     S["TF1"][:, :], S["TF2"][:, :]),
                                 inc="ev")
        blk.emit()

    for tb in range(NTB):
        blk0 = Blk(ctx)
        blk0.dma("gpsimd", S["XF"][:, :].rearrange("p (c n) -> p c n", c=NCH),
                 x2T[:, tb * TB:(tb + 1) * TB].rearrange(
                     "(c p) n -> p c n", c=NCH), "dg")
        blk0.wait("gpsimd", "dg", ctx.cnt["dg"])
        blk0.emit()

        blk = Blk(ctx)
        AO = S["AO"]
        ea_t, mm_t, pv_t = {}, {}, {}
        ao_t, cp_t = {}, {}
        g = -1
        nkt = 16

        def emit_pv(h, kt, g):
            blk.wait("tensor", "ea", ea_t[g])
            if kt == 0 and h >= 2:
                blk.wait("tensor", "ev", ao_t[h - 2])
            pv_t[g] = blk.mm(pb[4 + h % 2][0:65, :],
                             S["VT"][:, (kt * 16 + h) * 80:
                                     (kt * 16 + h) * 80 + 65],
                             S["PT"][g % 2][:, :], start=(kt == 0),
                             stop=(kt == nkt - 1), inc="mm")

        for h in range(H):
            hc, hr = h // 2, (h % 2) * 64
            for kt in range(nkt):
                g += 1
                sb_ = pb[g % 2]
                if g >= 2:
                    blk.wait("tensor", "ea", ea_t[g - 2])
                mm_t[g] = blk.mm(
                    sb_[:, :],
                    S["KB"][hr:hr + 64, hc * T + kt * 128:
                            hc * T + (kt + 1) * 128],
                    S["QB"][hr:hr + 64, hc * T + tb * TB:
                            hc * T + (tb + 1) * TB],
                    start=True, stop=True, inc="mm")
                pt = S["PT"][g % 2]
                blk.wait("scalar", "mm", mm_t[g])
                if g >= 2:
                    blk.wait("scalar", "mm", pv_t[g - 2])
                ea_t[g] = blk.add("scalar", lambda e, sb_=sb_, pt=pt:
                                  e.activation(pt[:, :], sb_[:, :], AF.Square,
                                               bias=S["TAYB"][:, 0:1],
                                               scale=TAY_A), inc="ea")
                if kt >= 1:
                    emit_pv(h, kt - 1, g - 1)
            emit_pv(h, nkt - 1, g)
            ob = pb[4 + h % 2]
            rb = pb[6 + h % 2]
            blk.wait("scalar", "mm", pv_t[g])
            if h >= 2:
                blk.wait("scalar", "ev", ao_t[h - 2])
            vcol = (h % 2) * 8 + h // 2
            nv = blk.add("scalar", lambda e, ob=ob, h=h, vcol=vcol:
                         e.activation(S["NUM"][h % 2][:, :], ob[0:64, :],
                                      AF.Identity,
                                      bias=S["VSH"][0:64, vcol:vcol + 1]),
                         inc="ea")
            blk.wait("vector", "mm", pv_t[g])
            rdv = blk.vf(lambda e, ob=ob, h=h: e.tensor_scalar(
                S["RD"][h % 2][:, :], ob[64:65, :], 1024.0, None, ALU.add))
            rdv = blk.vf(lambda e, h=h: e.reciprocal(
                S["RD"][h % 2][:, :], S["RD"][h % 2][:, :]))
            blk.wait("tensor", "ev", rdv)
            if h >= 2:
                blk.wait("tensor", "ev", cp_t[h - 2])
            rbv = blk.mm(rb[0:64, :], C["ONESR"][0:1, 0:64],
                         S["RD"][h % 2][0:1, :], start=True, stop=True,
                         inc="mm")
            blk.wait("vector", "mm", rbv)
            cp_t[h] = blk.add("vector", lambda e, rb=rb, h=h: e.tensor_copy(
                S["RDB"][h % 2][:, :], rb[0:64, :]), inc="ev")
            blk.wait("vector", "ea", nv)
            ao_t[h] = blk.add("vector", lambda e, h=h, hc=hc, hr=hr:
                              e.tensor_mul(
                                  AO[hr:hr + 64, hc * TB:(hc + 1) * TB],
                                  S["NUM"][h % 2][:, :],
                                  S["RDB"][h % 2][:, :]), inc="ev")
        blk.emit()

        blk = Blk(ctx)
        wo_specs = [wo[:, oc * 128:(oc + 1) * 128].rearrange(
            "(c p) n -> p c n", c=NCH) for oc in range(NCH)]
        st_t = {}

        def wo_evac(blk, i, mv, tb=tb):
            blk.wait("vector", "mm", mv)
            if i >= 2:
                blk.wait("vector", "ds", st_t[i - 2])
            tfo = S["TF1"] if i % 2 == 0 else S["TF3"]
            v = blk.add("vector", lambda e, i=i, tfo=tfo: e.tensor_add(
                tfo[:, :], pb[i % 4][:, :], S["XF"][:, i * TB:(i + 1) * TB]),
                inc="ev")
            blk.wait("sync", "ev", v)
            st_t[i] = blk.dma(
                "sync", x3T[i * 128:(i + 1) * 128, tb * TB:(tb + 1) * TB],
                tfo[:, :], "ds")
            return ("ev", v)

        stream_mm(ctx, blk, wo_specs, S["WS"], NCH,
                  moving=lambda i, kc: AO[:, kc * TB:(kc + 1) * TB],
                  psum_of=lambda i: pb[i % 4][:, :], evac=wo_evac)
        blk.emit()

    if "dbgC" in io:
        blk = Blk(ctx)
        for nm, hd in io["dbgC"].items():
            blk.dma("sync", hd[:, :], S[nm][:, :] if nm != "AO"
                    else S["AO"][:, :], "ds")
        blk.emit()

    blk = Blk(ctx)
    blk.wait("sync", "ds", ctx.cnt["ds"])
    blk.emit()


C_SHAPES = {"L1_attn_ln_w": [1024], "L1_attn_ln_b": [1024],
            "L1_attn_w_q": [1024, 1024], "L1_attn_w_kv": [1024, 2048],
            "L1_attn_w_o": [1024, 1024], "L1_attn_theta": [1]}


def declare_stage_C_io(nc, io):
    for name, shape in C_SHAPES.items():
        io[name] = nc.dram_tensor(name, shape, F32, kind="ExternalInput")
    if os.environ.get("K_DEBUG_OUT", "0") == "1":
        io["dbgC"] = {}
        for nm, shape, dt in [("CT", [128, T], BF16), ("ST", [128, T], BF16),
                              ("QB", [128, NCH * T], BF16),
                              ("KB", [128, NCH * T], BF16),
                              ("VSH", [65, 16], F32),
                              ("VT", [128, 16 * 16 * 80], BF16),
                              ("XLN2", [128, NCH * TB], BF16),
                              ("AO", [128, NCH * TB], BF16)]:
            io["dbgC"][nm] = nc.dram_tensor("dbgC_" + nm, shape, dt,
                                            kind="ExternalOutput")


def stage_C_in_map(x2_batch, inputs):
    m = {}
    if x2_batch is not None:
        m["x2T"] = np.ascontiguousarray(x2_batch.T.astype(np.float32))
    for k in C_SHAPES:
        if k == "L1_attn_theta":
            m[k] = np.asarray(inputs[k], np.float32).reshape(1)
        else:
            m[k] = np.asarray(inputs[k], np.float32)
    for k, v in CONSTS.items():
        m["c_" + k] = v
    return m


def build_program_C():
    nc = bass.Bass("TRN2", target_bir_lowering=False)
    io = {}
    io["x2T"] = nc.dram_tensor("x2T", [D, T], F32, kind="ExternalInput")
    io["x3T"] = nc.dram_tensor("x3T", [D, T], F32, kind="ExternalOutput")
    declare_stage_C_io(nc, io)
    for k, v in CONSTS.items():
        io["c_" + k] = nc.dram_tensor("c_" + k, list(v.shape),
                                      BF16 if v.dtype == BF else F32,
                                      kind="ExternalInput")
    ctx = Ctx(nc)
    for s in ("dg", "ds", "mm", "ev", "ea", "eg"):
        ctx.add_sem(s)
    C = alloc_consts(nc, CORE_CONSTS)
    S = alloc_stage_C(nc)
    pb = [nc.alloc_psum_tensor(f"pb{i}", [128, TB], F32) for i in range(8)]
    blk = Blk(ctx)
    load_consts(blk, C, {k: io["c_" + k] for k in CONSTS})
    blk.wait("sync", "ds", ctx.cnt["ds"])
    blk.emit()
    emit_stage_C(ctx, S, C, pb, io)
    return nc



def alloc_stage_D(nc):
    sb = nc.alloc_sbuf_tensor
    S = {}
    A2 = sb("D_A2", [128, 2 * NCH * TB], F32)
    S["XF"] = A2.ap()[:, 0:NCH * TB]
    S["XB"] = A2.ap().bitcast(BF16)[:, 2 * NCH * TB:3 * NCH * TB]
    S["SQ"] = A2.ap().bitcast(BF16)[:, 3 * NCH * TB:4 * NCH * TB]
    S["HALLG"] = A2.ap().bitcast(BF16)[:, 0:8 * 2 * TB]
    YA = sb("D_YA", [128, NCH * 2 * TB], F32)
    S["YACC"] = YA.ap()
    S["MU"] = YA.ap()[:, 0:TB]
    S["RSTD"] = YA.ap()[:, TB:2 * TB]
    S["TF1"] = YA.ap()[:, 2 * TB:3 * TB]
    S["TF2"] = YA.ap()[:, 3 * TB:4 * TB]
    S["TF3"] = S["MU"]
    S["GTW"] = YA.ap()[:, 4 * TB:4 * TB + 256]
    S["LGT"] = YA.ap()[:, 4 * TB + 256:4 * TB + 512]
    S["GTE"] = YA.ap()[0:16, 5 * TB:5 * TB + T]
    S["XLND"] = sb("D_XLND", [128, NCH * T], BF16)
    S["WPH"] = sb("D_WPH", [128, 4 * 4096], BF16)
    S["W2B"] = sb("D_W2B", [128, 2 * 4096], BF16)
    S["GBCS"] = sb("D_GBCS", [128, 4 * TB], F32)
    S["GROW"] = sb("D_GROW", [1, 4 * TB], F32)
    S["TM1"] = [sb(f"D_TM1{i}", [128, TB], BF16) for i in range(2)]
    S["TM2"] = [sb(f"D_TM2{i}", [128, TB], BF16) for i in range(2)]
    S["TM3"] = [sb(f"D_TM3{i}", [128, TB], BF16) for i in range(2)]
    S["GWTF"] = sb("D_GWTF", [128, NCH * 16], F32)
    S["GTEB"] = sb("D_GTEB", [17, T], BF16)
    S["EBB"] = sb("D_EBB", [128, 16], F32)
    S["T16A"] = sb("D_T16A", [128, 16], F32)
    S["T16B"] = sb("D_T16B", [128, 16], F32)
    S["TC1"] = sb("D_TC1", [128, 1], F32)
    S["TC2"] = sb("D_TC2", [128, 1], F32)
    S["B1SD"] = sb("D_B1SD", [128, 64], F32)
    S["B3SD"] = sb("D_B3SD", [128, 64], F32)
    S["SB1D"] = sb("D_SB1D", [128, 8], F32)
    S["SB3D"] = sb("D_SB3D", [128, 8], F32)
    S["B2SB"] = sb("D_B2SB", [17, D], BF16)
    S["WS"] = [sb(f"D_WS{i}", [128, NCH * 128], BF16) for i in range(2)]
    S["STG"] = sb("D_STG", [128, NCH * 128], BF16)
    S["STGF"] = sb("D_STGF", [128, 5 * 128], F32)
    return S


def emit_prep_D(ctx, S, C, pb, io):
    nc = ctx.nc
    w1p, w3p, w2p = io["w1_pair"], io["w3_pair"], io["w2_pair"]
    shm = io["sh_one"]
    myw1t, myw3t, myw2t, mysht = (io["scr_myw1t"], io["scr_myw3t"],
                                  io["scr_myw2t"], io["scr_mysht"])
    allw1t, allw3t, allw2t, allsh = (io["scr_allw1t"], io["scr_allw3t"],
                                     io["scr_allw2t"], io["scr_allsh"])
    blk = Blk(ctx)
    jobs = []
    for le in range(2):
        jobs.append((lambda rb, le=le: w1p[le, :, rb * 128:(rb + 1) * 128]
                     .rearrange("(c p) n -> p c n", c=4),
                     myw1t, le * 1024, 8, 4))
        jobs.append((lambda rb, le=le: w3p[le, :, rb * 128:(rb + 1) * 128]
                     .rearrange("(c p) n -> p c n", c=4),
                     myw3t, le * 1024, 8, 4))
        jobs.append((lambda rb, le=le: w2p[le, :, rb * 128:(rb + 1) * 128]
                     .rearrange("(c p) n -> p c n", c=8),
                     myw2t, le * 512, 4, 8))
    jobs.append((lambda rb: shm[:, rb * 128:(rb + 1) * 128]
                 .rearrange("(c p) n -> p c n", c=8),
                 mysht, 0, 8, 8))
    st_t = {}
    prev_ev = {}
    i = 0
    for (srcf, dstd, dro, nrb, nfc) in jobs:
        for rb in range(nrb):
            buf = S["WS"][i % 2]
            obuf = S["STG"]
            if i >= 2:
                blk.wait("gpsimd", "mm", st_t[i - 2][0])
            ld = blk.dma("gpsimd", buf[:, 0:nfc * 128].rearrange(
                "p (c n) -> p c n", c=nfc), srcf(rb), "dg")
            blk.wait("tensor", "dg", ld)
            if i >= 1:
                blk.wait("vector", "ds", st_t[i - 1][1])
            lastmm = None
            vv = None
            for c in range(nfc):
                pbb = pb[4 + c % 2]
                if prev_ev.get(c % 2):
                    blk.wait("tensor", "ev", prev_ev[c % 2])
                lastmm = blk.tr(pbb.bitcast(BF16)[:, 0:128],
                                buf[:, c * 128:(c + 1) * 128],
                                C["IDB"][:, :], inc="mm")
                blk.wait("vector", "mm", lastmm)
                vv = blk.add("vector", lambda e, obuf=obuf, c=c, pbb=pbb:
                             e.tensor_copy(obuf[:, c * 128:(c + 1) * 128],
                                           pbb.bitcast(BF16)[:, 0:128]),
                             inc="ev")
                prev_ev[c % 2] = vv
            blk.wait("sync", "ev", vv)
            sv = blk.dma("sync",
                         dstd[dro + rb * 128:dro + (rb + 1) * 128,
                              0:nfc * 128],
                         obuf[:, 0:nfc * 128], "ds")
            st_t[i] = (lastmm, sv)
            i += 1
    blk.wait("gpsimd", "ds", ctx.cnt["ds"])
    for (src, dst) in [(myw1t, allw1t), (myw3t, allw3t), (myw2t, allw2t),
                       (mysht, allsh)]:
        blk.add("gpsimd", lambda e, src=src, dst=dst: e.collective_compute(
            "AllGather", ALU.bypass, replica_groups=RG8,
            ins=[src.ap().opt()], outs=[dst.ap().opt()]), inc="cc")
    blk.wait("gpsimd", "cc", ctx.cnt["cc"])
    blk.emit()


def emit_stage_D(ctx, S, C, pb, io):
    nc = ctx.nc
    x3T, outT = io["x3T"], io["outT"]
    allw1t, allw3t, allw2t, allsh = (io["scr_allw1t"], io["scr_allw3t"],
                                     io["scr_allw2t"], io["scr_allsh"])
    gted = io["scr_gted"]

    blk = Blk(ctx)
    gw_ev = {}
    for cb in range(NCH):
        ld = blk.dma("gpsimd", S["STGF"][0:16, 0:128],
                     io["L1_ffn_gate_w"][:, cb * 128:(cb + 1) * 128], "dg")
        blk.wait("tensor", "dg", ld)
        if cb >= 1:
            blk.wait("tensor", "ev", gw_ev[cb - 1])
        v = blk.tr(pb[0][:, 0:16], S["STGF"][0:16, 0:128],
                   C["IDF"][0:16, 0:16], inc="mm")
        blk.wait("vector", "mm", v)
        gw_ev[cb] = blk.add("vector", lambda e, cb=cb: e.tensor_copy(
            S["GWTF"][:, cb * 16:(cb + 1) * 16], pb[0][:, 0:16]), inc="ev")
        blk.wait("gpsimd", "ev", gw_ev[cb])
    ld = blk.dma("gpsimd", S["STGF"][0:64, 0:128],
                 io["L1_ffn_b1"][:, :].rearrange("a (b n) -> (a b) n", b=4),
                 "dg")
    ld = blk.dma("gpsimd", S["STGF"][0:64, 128:256],
                 io["L1_ffn_b3"][:, :].rearrange("a (b n) -> (a b) n", b=4),
                 "dg")
    ld = blk.dma("gpsimd", S["STGF"][0:8, 256:384],
                 io["L1_ffn_sb1"][:].rearrange("(a n) -> a n", n=128), "dg")
    ld = blk.dma("gpsimd", S["STGF"][0:8, 384:512],
                 io["L1_ffn_sb3"][:].rearrange("(a n) -> a n", n=128), "dg")
    ld = blk.dma("gpsimd", S["B2SB"][0:16, :], io["L1_ffn_b2"][:, :], "dg")
    ld = blk.dma("gpsimd", S["B2SB"][16:17, :], io["L1_ffn_sb2"][None, :],
                 "dg")
    ld = blk.dma("gpsimd", S["STGF"][0:1, 512:528],
                 io["L1_ffn_expert_bias"][None, :], "dg")
    blk.wait("tensor", "dg", ctx.cnt["dg"])
    v = blk.tr(pb[1][:, 0:64], S["STGF"][0:64, 0:128], C["IDF"][0:64, 0:64])
    v = blk.tr(pb[2][:, 0:64], S["STGF"][0:64, 128:256],
               C["IDF"][0:64, 0:64])
    v = blk.tr(pb[3][:, 0:8], S["STGF"][0:8, 256:384], C["IDF"][0:8, 0:8])
    v = blk.tr(pb[6][:, 0:8], S["STGF"][0:8, 384:512],
               C["IDF"][0:8, 0:8], inc="mm")
    vb = blk.mm(pb[7][:, 0:16], C["ONESR"][0:1, :], S["STGF"][0:1, 512:528],
                start=True, stop=True, inc="mm")
    blk.wait("vector", "mm", vb)
    blk.vf(lambda e: e.tensor_copy(S["B1SD"][:, :], pb[1][:, 0:64]))
    blk.vf(lambda e: e.tensor_copy(S["B3SD"][:, :], pb[2][:, 0:64]))
    blk.vf(lambda e: e.tensor_copy(S["SB1D"][:, :], pb[3][:, 0:8]))
    blk.vf(lambda e: e.tensor_copy(S["SB3D"][:, :], pb[6][:, 0:8]))
    blk.vf(lambda e: e.tensor_copy(S["EBB"][:, :], pb[7][:, 0:16]))
    blk.emit()

    for tb in range(NTB):
        blk0 = Blk(ctx)
        blk0.dma("gpsimd", S["XF"][:, :].rearrange("p (c n) -> p c n", c=NCH),
                 x3T[:, tb * TB:(tb + 1) * TB].rearrange("(c p) n -> p c n",
                                                         c=NCH), "dg")
        blk0.wait("gpsimd", "dg", ctx.cnt["dg"])
        blk0.emit()

        def logits_cb(blk, c, tf1, fv, tb=tb):
            blk.wait("tensor", "ev", fv)
            v = None
            for tt in range(4):
                last = (c == NCH - 1)
                v = blk.mm(pb[2 + tt][:, 0:16],
                           tf1[:, tt * 128:(tt + 1) * 128],
                           S["GWTF"][:, c * 16:(c + 1) * 16],
                           start=(c == 0), stop=last,
                           inc=("mm" if tt == 3 else None))
            return v

        emit_ln(ctx, S, C, pb, S["XF"], None,
                dst_fn=lambda c, tb=tb: S["XLND"][:, c * T + tb * TB:
                                                  c * T + (tb + 1) * TB],
                fp32_cb=logits_cb)

        blk = Blk(ctx)
        blk.wait("vector", "mm", ctx.cnt["mm"])
        for tt in range(4):
            blk.add("vector", lambda e, tt=tt, tb=tb: e.tensor_copy(
                S["LGT"][:, (tb * 4 + tt) * 16:(tb * 4 + tt + 1) * 16],
                pb[2 + tt][:, 0:16]), inc="ev")
        for tt in range(4):
            j = tb * 4 + tt
            LG = S["LGT"][:, j * 16:(j + 1) * 16]
            blk.vf(lambda e, LG=LG: e.tensor_add(
                S["T16A"][:, :], LG, S["EBB"][:, :]))
            blk.vf(lambda e: e.reduce_max(S["TC1"][:, :], S["T16A"][:, :],
                                          axis=AX.X))
            blk.vf(lambda e: e.tensor_scalar(
                S["T16B"][:, :], S["T16A"][:, :], S["TC1"][:, 0:1], None,
                ALU.is_equal))
            blk.vf(lambda e: e.scalar_tensor_tensor(
                S["T16A"][:, :], S["T16B"][:, :], -1e30, S["T16A"][:, :],
                ALU.mult, ALU.add))
            blk.vf(lambda e: e.reduce_max(S["TC1"][:, :], S["T16A"][:, :],
                                          axis=AX.X))
            blk.vf(lambda e: e.tensor_scalar(
                S["T16A"][:, :], S["T16A"][:, :], S["TC1"][:, 0:1], None,
                ALU.is_equal))
            mv = blk.vf(lambda e: e.tensor_add(
                S["T16B"][:, :], S["T16B"][:, :], S["T16A"][:, :]))
            blk.wait("scalar", "ev", mv)
            av = blk.add("scalar", lambda e, LG=LG: e.activation(
                S["T16A"][:, :], LG, AF.Sigmoid), inc="ea")
            blk.wait("vector", "ea", av)
            blk.vf(lambda e: e.tensor_mul(
                S["T16A"][:, :], S["T16A"][:, :], S["T16B"][:, :]))
            blk.vf(lambda e: e.reduce_sum(S["TC1"][:, :], S["T16A"][:, :],
                                          axis=AX.X))
            blk.vf(lambda e: e.tensor_scalar(
                S["TC1"][:, :], S["TC1"][:, :], 1e-10, None, ALU.add))
            blk.vf(lambda e: e.reciprocal(S["TC2"][:, 0:1],
                                          S["TC1"][:, 0:1]))
            blk.vf(lambda e, j=j: e.tensor_scalar(
                S["GTW"][:, j * 16:(j + 1) * 16], S["T16A"][:, :],
                S["TC2"][:, 0:1], None, ALU.mult))
        blk.emit()

    blk = Blk(ctx)
    pr = {}
    for j in range(16):
        bk = j % 2
        if pr.get(bk):
            blk.wait("tensor", "ev", pr[bk])
        v = blk.tr(pb[bk][0:16, 0:128], S["GTW"][:, j * 16:(j + 1) * 16],
                   C["IDF"][:, :], inc="mm")
        blk.wait("vector", "mm", v)
        pr[bk] = blk.add("vector", lambda e, j=j, bk=bk: e.tensor_copy(
            S["GTE"][0:16, j * 128:(j + 1) * 128], pb[bk][0:16, 0:128]),
            inc="ev")
    blk.wait("vector", "ev", ctx.cnt["ev"])
    growb = S["GROW"].ap().bitcast(BF16)
    blk.add("gpsimd", lambda e: e.memset(growb[0:1, 0:T], 1.0))
    blk.dma("gpsimd", S["GTEB"][16:17, :], growb[0:1, 0:T], "dg")
    v = blk.add("vector", lambda e: e.tensor_copy(S["GTEB"][0:16, :],
                                                  S["GTE"][:, :]), inc="ev")
    blk.wait("sync", "ev", v)
    blk.dma("sync", gted[:, :], S["GTE"][:, :], "ds")
    blk.wait("sync", "ds", ctx.cnt["ds"])
    blk.emit()

    blk = Blk(ctx)
    HALLG, YACC = S["HALLG"], S["YACC"]
    WPH, W2B, GBCS, GROW = S["WPH"], S["W2B"], S["GBCS"], S["GROW"]
    HTB = 2 * TB
    NG = 9
    bank_free = {}
    tm_ev = {}
    phA_mm = {}
    phB_mm = {}
    store_ds = {}

    def cut():
        nonlocal blk
        blk.emit()
        blk = Blk(ctx)

    dx_nogate = os.environ.get("K_DX_NOGATE", "0") == "1"
    dx_noa = os.environ.get("K_DX_NOA", "0") == "1"
    dx_nob = os.environ.get("K_DX_NOB", "0") == "1"

    def use_banks(banks):
        for b in banks:
            if b in bank_free:
                sem, tgt = bank_free[b]
                blk.wait("tensor", sem, tgt)

    for pp in range(2):
        tb0 = 2 * pp
        tile_ev = {}
        if pp == 1:
            blk.wait("gpsimd", "ds", store_ds[0])
        init_dg = {}
        for dch in range(NCH):
            init_dg[dch] = blk.dma(
                "gpsimd", YACC[:, dch * HTB:(dch + 1) * HTB],
                x3T[dch * 128:(dch + 1) * 128,
                    pp * HTB:(pp + 1) * HTB], "dg")
        blk.wait("tensor", "dg", ctx.cnt["dg"])
        for ti in ([] if os.environ.get("K_DX_NOBIAS", "0") == "1"
                   else range(16)):
            tbi, dch = ti // 8, ti % 8
            b = ti % 4 + ((ti // 4) % 2) * 4
            use_banks([b])
            mv = blk.mm(pb[b][:, :],
                        S["B2SB"][0:17, dch * 128:(dch + 1) * 128],
                        S["GTEB"][0:17, (tb0 + tbi) * TB:
                                  (tb0 + tbi + 1) * TB],
                        start=True, stop=True, inc="mm")
            blk.wait("vector", "mm", mv)
            blk.wait("vector", "dg", init_dg[dch])
            dst = YACC[:, dch * HTB + tbi * TB:dch * HTB + (tbi + 1) * TB]
            v = blk.add("vector", lambda e_, dst=dst, b=b:
                        e_.tensor_add(dst, dst, pb[b][:, :]), inc="ev")
            tile_ev[(tbi, dch)] = v
            bank_free[b] = ("ev", v)

        for g in range(NG):
            shared = (g == 8)
            prev = (pp, g - 1) if g > 0 else ((0, NG - 1) if pp == 1
                                              else None)
            if prev is not None:
                blk.wait("gpsimd", "mm", phA_mm[prev])
            if not shared:
                for kind in range(2):
                    srcd = allw1t if kind == 0 else allw3t
                    for el in range(2):
                        e = 2 * g + el
                        blk.dma("gpsimd",
                                WPH[:, (kind * 2 + el) * 4096:
                                    (kind * 2 + el + 1) * 4096]
                                .rearrange("p (c n) -> p c n", c=NCH),
                                srcd[e * 1024:(e + 1) * 1024, :]
                                .rearrange("(c p) n -> p c n", c=NCH),
                                "dg")
                for el in range(2):
                    for tbi in range(2):
                        blk.dma("gpsimd",
                                GROW[0:1, (el * 2 + tbi) * TB:
                                     (el * 2 + tbi + 1) * TB],
                                gted[2 * g + el:2 * g + el + 1,
                                     (tb0 + tbi) * TB:
                                     (tb0 + tbi + 1) * TB], "dg")
            else:
                blk.dma("gpsimd",
                        WPH[:, 0:8192].rearrange("p (c n) -> p c n", c=NCH),
                        allsh[0:1024, :].rearrange("(c p) n -> p c n",
                                                   c=NCH), "dg")
                blk.dma("gpsimd",
                        WPH[:, 8192:16384].rearrange("p (c n) -> p c n",
                                                     c=NCH),
                        allsh[1024:2048, :].rearrange("(c p) n -> p c n",
                                                      c=NCH), "dg")
            w13_ld = ctx.cnt["dg"]
            if prev is not None:
                blk.wait("gpsimd", "mm", phB_mm[prev])
            if not shared:
                for el in range(2):
                    e = 2 * g + el
                    blk.dma("gpsimd",
                            W2B[:, el * 4096:(el + 1) * 4096]
                            .rearrange("p (c n) -> p c n", c=4),
                            allw2t[e * 512:(e + 1) * 512, :]
                            .rearrange("(c p) n -> p c n", c=4), "dg")
            else:
                blk.dma("gpsimd",
                        W2B[:, 0:8192].rearrange("p (c n) -> p c n", c=NCH),
                        allsh[2048:3072, :].rearrange("(c p) n -> p c n",
                                                      c=NCH), "dg")
            w2_ld = ctx.cnt["dg"]

            blk.wait("tensor", "dg", w13_ld)
            if dx_nogate:
                pass
            elif not shared:
                for el in range(2):
                    for tbi in range(2):
                        b = el * 2 + tbi
                        use_banks([b])
                        mv = blk.mm(pb[b][:, :], C["ONESR"][0:1, :],
                                    GROW[0:1, (el * 2 + tbi) * TB:
                                         (el * 2 + tbi + 1) * TB],
                                    start=True, stop=True, inc="mm")
                        blk.wait("vector", "mm", mv)
                        gv = blk.add(
                            "vector", lambda e_, el=el, tbi=tbi, b=b:
                            e_.tensor_copy(
                                GBCS[:, (el * 2 + tbi) * TB:
                                     (el * 2 + tbi + 1) * TB],
                                pb[b][:, :]), inc="ev")
                        bank_free[b] = ("ev", gv)

            jobs = ([(el, oc) for el in range(2) for oc in range(4)]
                    if not shared else [(0, oc) for oc in range(8)])
            if dx_noa:
                jobs = []
            evA = None
            for j, (el, oc) in enumerate(jobs):
                bb = (j % 2) * 4
                use_banks([bb + k for k in range(4)])
                mvt = {}
                for kind in range(2):
                    for tbi in range(2):
                        bk = bb + kind * 2 + tbi
                        for kc in range(NCH):
                            if not shared:
                                o0 = ((kind * 2 + el) * 4096 + kc * 512
                                      + oc * 128)
                            else:
                                o0 = kind * 8192 + kc * 1024 + oc * 128
                            mvt[(kind, tbi)] = blk.mm(
                                pb[bk][:, :], WPH[:, o0:o0 + 128],
                                S["XLND"][:, kc * T + (tb0 + tbi) * TB:
                                          kc * T + (tb0 + tbi + 1) * TB],
                                start=(kc == 0), stop=(kc == NCH - 1),
                                inc=("mm" if kc == NCH - 1 else None))
                if not shared:
                    b1c = S["B1SD"][:, (2 * g + el) * 4 + oc:
                                    (2 * g + el) * 4 + oc + 1]
                    b3c = S["B3SD"][:, (2 * g + el) * 4 + oc:
                                    (2 * g + el) * 4 + oc + 1]
                    s = el * 4 + oc
                else:
                    b1c = S["SB1D"][:, oc:oc + 1]
                    b3c = S["SB3D"][:, oc:oc + 1]
                    s = oc
                for tbi in range(2):
                    q = tbi
                    bw1, bw3 = bb + tbi, bb + 2 + tbi
                    blk.wait("scalar", "mm", mvt[(1, tbi)])
                    if q in tm_ev:
                        blk.wait("scalar", "ev", tm_ev[q])
                    t1 = blk.add("scalar", lambda e_, q=q, bw1=bw1, b1c=b1c:
                                 e_.activation(S["TM1"][q][:, :],
                                               pb[bw1][:, :], AF.Silu,
                                               bias=b1c), inc="ea")
                    t2 = blk.add("scalar", lambda e_, q=q, bw3=bw3, b3c=b3c:
                                 e_.activation(S["TM2"][q][:, :],
                                               pb[bw3][:, :], AF.Identity,
                                               bias=b3c), inc="ea")
                    blk.wait("vector", "ea", t2)
                    dsth = HALLG[:, s * HTB + tbi * TB:
                                 s * HTB + (tbi + 1) * TB]
                    if not shared and not dx_nogate:
                        blk.add("vector", lambda e_, q=q:
                                e_.tensor_mul(S["TM3"][q][:, :],
                                              S["TM1"][q][:, :],
                                              S["TM2"][q][:, :]))
                        hv = blk.add(
                            "vector", lambda e_, q=q, dsth=dsth, el=el,
                            tbi=tbi: e_.tensor_mul(
                                dsth, S["TM3"][q][:, :],
                                GBCS[:, (el * 2 + tbi) * TB:
                                     (el * 2 + tbi + 1) * TB]), inc="ev")
                    else:
                        hv = blk.add(
                            "vector", lambda e_, q=q, dsth=dsth:
                            e_.tensor_mul(dsth, S["TM1"][q][:, :],
                                          S["TM2"][q][:, :]), inc="ev")
                    tm_ev[q] = hv
                    bank_free[bw1] = ("ea", t1)
                    bank_free[bw3] = ("ea", t2)
                    evA = hv
            phA_mm[(pp, g)] = ctx.cnt["mm"]
            cut()

            blk.wait("tensor", "dg", w2_ld)
            blk.wait("tensor", "ev", evA)
            for ti in ([] if dx_nob else range(16)):
                tbi, dch = ti // 8, ti % 8
                b = ti % 4 + ((ti // 4) % 2) * 4
                use_banks([b])
                nfc = 4 if not shared else 8
                nel = 2 if not shared else 1
                first = True
                mv = None
                for el in range(nel):
                    for fc in range(nfc):
                        if not shared:
                            o0 = el * 4096 + fc * 1024 + dch * 128
                            s = el * 4 + fc
                        else:
                            o0 = fc * 1024 + dch * 128
                            s = fc
                        last = (el == nel - 1 and fc == nfc - 1)
                        mv = blk.mm(pb[b][:, :], W2B[:, o0:o0 + 128],
                                    HALLG[:, s * HTB + tbi * TB:
                                          s * HTB + (tbi + 1) * TB],
                                    start=first, stop=last,
                                    inc=("mm" if last else None))
                        first = False
                blk.wait("vector", "mm", mv)
                if (tbi, dch) in tile_ev:
                    blk.wait("vector", "ev", tile_ev[(tbi, dch)])
                dst = YACC[:, dch * HTB + tbi * TB:dch * HTB
                           + (tbi + 1) * TB]
                if os.environ.get("K_DX_B_COPY", "0") == "1":
                    v = blk.add("vector", lambda e_, dst=dst, b=b:
                                e_.tensor_copy(dst, pb[b][:, :]), inc="ev")
                else:
                    v = blk.add("vector", lambda e_, dst=dst, b=b:
                                e_.tensor_add(dst, dst, pb[b][:, :]),
                                inc="ev")
                tile_ev[(tbi, dch)] = v
                bank_free[b] = ("ev", v)
            phB_mm[(pp, g)] = ctx.cnt["mm"]
            cut()

        blk.wait("sync", "ev", ctx.cnt["ev"])
        for dch in range(NCH):
            blk.dma("sync", outT[dch * 128:(dch + 1) * 128,
                                 pp * HTB:(pp + 1) * HTB],
                    YACC[:, dch * HTB:(dch + 1) * HTB], "ds")
        store_ds[pp] = ctx.cnt["ds"]
    blk.emit()

    if "dbgD" in io:
        blk = Blk(ctx)
        for nm, hd in io["dbgD"].items():
            blk.dma("sync", hd[:, :], S[nm][:, :], "ds")
        blk.emit()

    blk = Blk(ctx)
    blk.wait("sync", "ds", ctx.cnt["ds"])
    blk.wait("gpsimd", "dg", ctx.cnt["dg"])
    blk.emit()


D_SHAPES = {"L1_ffn_gate_w": [16, 1024], "L1_ffn_expert_bias": [16],
            "L1_ffn_b1": [16, 512], "L1_ffn_b3": [16, 512],
            "L1_ffn_b2": [16, 1024], "L1_ffn_sb1": [1024],
            "L1_ffn_sb3": [1024], "L1_ffn_sb2": [1024],
            "w1_pair": [2, 512, 1024], "w3_pair": [2, 512, 1024],
            "w2_pair": [2, 1024, 512], "sh_one": [1024, 1024]}


def declare_stage_D_io(nc, io):
    for name, shape in D_SHAPES.items():
        io[name] = nc.dram_tensor(name, shape, F32, kind="ExternalInput")
    if os.environ.get("K_DEBUG_OUT", "0") == "1":
        io["dbgD"] = {}
        for nm, shape, dt in [("GWTF", [128, 128], F32),
                              ("EBB", [128, 16], F32),
                              ("B1SD", [128, 64], F32),
                              ("SB1D", [128, 8], F32),
                              ("XLND", [128, NCH * T], BF16)]:
            io["dbgD"][nm] = nc.dram_tensor("dbgD_" + nm, shape, dt,
                                            kind="ExternalOutput")
    io["scr_myw1t"] = nc.dram_tensor("scr_myw1t", [2048, 512], BF16)
    io["scr_myw3t"] = nc.dram_tensor("scr_myw3t", [2048, 512], BF16)
    io["scr_myw2t"] = nc.dram_tensor("scr_myw2t", [1024, 1024], BF16)
    io["scr_mysht"] = nc.dram_tensor("scr_mysht", [1024, 1024], BF16)
    io["scr_allw1t"] = nc.dram_tensor("scr_allw1t", [16384, 512], BF16)
    io["scr_allw3t"] = nc.dram_tensor("scr_allw3t", [16384, 512], BF16)
    io["scr_allw2t"] = nc.dram_tensor("scr_allw2t", [8192, 1024], BF16)
    io["scr_allsh"] = nc.dram_tensor("scr_allsh", [8192, 1024], BF16)
    io["scr_gted"] = nc.dram_tensor("scr_gted", [16, T], F32)


def stage_D_in_map(core, x3_batch, inputs):
    m = {}
    if x3_batch is not None:
        m["x3T"] = np.ascontiguousarray(x3_batch.T.astype(np.float32))
    for k in ("L1_ffn_gate_w", "L1_ffn_expert_bias", "L1_ffn_b1",
              "L1_ffn_b3", "L1_ffn_b2", "L1_ffn_sb1", "L1_ffn_sb3",
              "L1_ffn_sb2"):
        m[k] = np.asarray(inputs[k], np.float32)
    m["w1_pair"] = np.ascontiguousarray(
        np.asarray(inputs["L1_ffn_w1"], np.float32)[2 * core:2 * core + 2])
    m["w3_pair"] = np.ascontiguousarray(
        np.asarray(inputs["L1_ffn_w3"], np.float32)[2 * core:2 * core + 2])
    m["w2_pair"] = np.ascontiguousarray(
        np.asarray(inputs["L1_ffn_w2"], np.float32)[2 * core:2 * core + 2])
    if core == 0:
        sh = np.asarray(inputs["L1_ffn_sw1"], np.float32)
    elif core == 1:
        sh = np.asarray(inputs["L1_ffn_sw3"], np.float32)
    elif core == 2:
        sh = np.asarray(inputs["L1_ffn_sw2"], np.float32)
    else:
        sh = np.zeros((1024, 1024), np.float32)
    m["sh_one"] = np.ascontiguousarray(sh)
    for k, v in CONSTS.items():
        m["c_" + k] = v
    return m


def build_program_D():
    nc = bass.Bass("TRN2", target_bir_lowering=False)
    io = {}
    io["x3T"] = nc.dram_tensor("x3T", [D, T], F32, kind="ExternalInput")
    io["outT"] = nc.dram_tensor("outT", [D, T], F32, kind="ExternalOutput")
    declare_stage_D_io(nc, io)
    for k, v in CONSTS.items():
        io["c_" + k] = nc.dram_tensor("c_" + k, list(v.shape),
                                      BF16 if v.dtype == BF else F32,
                                      kind="ExternalInput")
    ctx = Ctx(nc)
    for s in ("dg", "ds", "mm", "ev", "ea", "cc"):
        ctx.add_sem(s)
    C = alloc_consts(nc, CORE_CONSTS)
    S = alloc_stage_D(nc)
    pb = [nc.alloc_psum_tensor(f"pb{i}", [128, TB], F32) for i in range(8)]
    blk = Blk(ctx)
    load_consts(blk, C, {k: io["c_" + k] for k in CONSTS})
    blk.wait("sync", "ds", ctx.cnt["ds"])
    blk.emit()
    emit_prep_D(ctx, S, C, pb, io)
    emit_stage_D(ctx, S, C, pb, io)
    return nc



A_WEIGHTS = ["L0_attn_W_DQ", "L0_attn_W_UQ", "L0_attn_W_DKV", "L0_attn_W_UK",
             "L0_attn_W_UV", "L0_attn_W_QR", "L0_attn_W_KR", "L0_attn_W_O"]
A_SHAPES = {"L0_attn_W_DQ": [1024, 4096], "L0_attn_W_UQ": [4096, 1024],
            "L0_attn_W_DKV": [1024, 256], "L0_attn_W_UK": [256, 1024],
            "L0_attn_W_UV": [256, 1024], "L0_attn_W_QR": [4096, 512],
            "L0_attn_W_KR": [1024, 32], "L0_attn_W_O": [1024, 1024]}


def build_program_A():
    nc = bass.Bass("TRN2", target_bir_lowering=False)
    io = {}
    io["xT"] = nc.dram_tensor("xT", [D, T], F32, kind="ExternalInput")
    io["x1T"] = nc.dram_tensor("x1T", [D, T], F32, kind="ExternalOutput")
    for name in A_WEIGHTS:
        io[name] = nc.dram_tensor(name, A_SHAPES[name], F32,
                                  kind="ExternalInput")
    for k, v in CONSTS.items():
        io["c_" + k] = nc.dram_tensor("c_" + k, list(v.shape),
                                      BF16 if v.dtype == BF else F32,
                                      kind="ExternalInput")
    io["scr_WDQT"] = nc.dram_tensor("scr_WDQT", [4096, 1024], BF16)
    io["scr_WDKVT"] = nc.dram_tensor("scr_WDKVT", [256, 1024], BF16)
    io["scr_M1"] = nc.dram_tensor("scr_M1", [1024, 1024], BF16)
    io["scr_M2"] = nc.dram_tensor("scr_M2", [1024, 512], BF16)
    io["scr_M3"] = nc.dram_tensor("scr_M3", [1024, 1024], BF16)
    io["scr_M4"] = nc.dram_tensor("scr_M4", [1024, 1024], BF16)

    ctx = Ctx(nc)
    for s in ("dg", "ds", "mm", "ev", "ea"):
        ctx.add_sem(s)
    C = alloc_consts(nc, A_CONSTS)
    S = alloc_stage_A(nc)
    pb = [nc.alloc_psum_tensor(f"pb{i}", [128, TB], F32) for i in range(8)]

    blk = Blk(ctx)
    load_consts(blk, C, {k: io["c_" + k] for k in CONSTS})
    blk.wait("sync", "ds", ctx.cnt["ds"])
    blk.emit()

    emit_stage_A(ctx, S, C, pb, io)
    return nc


def stage_A_in_map(x_batch, inputs):
    m = {"xT": np.ascontiguousarray(x_batch.T)}
    for k in A_WEIGHTS:
        m[k] = np.asarray(inputs[k], dtype=np.float32)
    for k, v in CONSTS.items():
        m["c_" + k] = v
    return m


B_SHAPES = {"L0_ffn_router_w": [1024, 8], "L0_ffn_router_b": [8],
            "w1_e": [1024, 4096], "b1_e": [4096],
            "w2_e": [4096, 1024], "b2_e": [1024],
            "sel8f": [64, 8], "sello": [64, 8], "selp": [64, 8],
            "tri8s": [8, 8], "sh8": [8, 8], "r4096": [8, 1],
            "rbasem1": [128, 8], "siota": [128, 20]}


def declare_stage_B_io(nc, io):
    dbg = os.environ.get("K_DEBUG_OUT", "0") == "1"
    kind = "ExternalOutput" if dbg else "Internal"
    for name, shape in B_SHAPES.items():
        io[name] = nc.dram_tensor(name, shape, F32, kind="ExternalInput")
    io["scr_myxln"] = nc.dram_tensor("scr_myxln", [T, D], BF16, kind=kind)
    io["scr_mycnt"] = nc.dram_tensor("scr_mycnt", [8, 1], F32)
    io["scr_cnt64"] = nc.dram_tensor("scr_cnt64", [64, 1], F32)
    io["scr_sendb"] = nc.dram_tensor("scr_sendb", [2 * T, D], BF16)
    io["scr_sball"] = nc.dram_tensor("scr_sball", [16 * T, D], BF16)
    io["scr_yd"] = nc.dram_tensor("scr_yd", [CAP, D], BF16)
    io["scr_yall"] = nc.dram_tensor("scr_yall", [8 * CAP, D], BF16)
    if dbg:
        io["dbg"] = {}
        for nm, shape, dt in [("CNT64", [64, 1], F32), ("GCUM", [8, T], F32),
                              ("IDXL1", [128, 16], I32),
                              ("IDXL2", [128, 16], I32),
                              ("IDXG", [128, 20], I32),
                              ("IDX1", [128, 16], I32),
                              ("IDX2", [128, 16], I32),
                              ("W1TOK", [128, 16], F32),
                              ("W2TOK", [128, 16], F32),
                              ("GT", [128, 128], F32),
                              ("LGT", [128, 128], F32),
                              ("MK1", [128, 128], F32),
                              ("MK2", [128, 128], F32),
                              ("G1", [128, D], BF16),
                              ("G2", [128, D], BF16),
                              ("BUFS", [128, NCH * CAP], BF16)]:
            io["dbg"][nm] = nc.dram_tensor("dbg_" + nm, shape, dt,
                                           kind="ExternalOutput")


def stage_B_in_map(core, x1_batch, inputs):
    m = {}
    if x1_batch is not None:
        m["x1T"] = np.ascontiguousarray(x1_batch.T.astype(np.float32))
    m["L0_ffn_router_w"] = np.asarray(inputs["L0_ffn_router_w"], np.float32)
    m["L0_ffn_router_b"] = np.asarray(inputs["L0_ffn_router_b"], np.float32)
    m["w1_e"] = np.ascontiguousarray(np.asarray(inputs["L0_ffn_w1"],
                                                np.float32)[core])
    m["b1_e"] = np.ascontiguousarray(np.asarray(inputs["L0_ffn_b1"],
                                                np.float32)[core])
    m["w2_e"] = np.ascontiguousarray(np.asarray(inputs["L0_ffn_w2"],
                                                np.float32)[core])
    m["b2_e"] = np.ascontiguousarray(np.asarray(inputs["L0_ffn_b2"],
                                                np.float32)[core])
    sel8f = np.zeros((64, 8), dtype=np.float32)
    sello = np.zeros((64, 8), dtype=np.float32)
    selp = np.zeros((64, 8), dtype=np.float32)
    for r in range(8):
        sel8f[8 * r + core, r] = 1.0
        for e in range(8):
            if e < core:
                sello[8 * r + e, r] = 1.0
            if r < core:
                selp[8 * r + e, e] = 1.0
    m["sel8f"] = sel8f
    m["sello"] = sello
    m["selp"] = selp
    m["tri8s"] = np.triu(np.ones((8, 8), np.float32), 1).copy()
    sh8 = np.zeros((8, 8), np.float32)
    for mm_ in range(1, 8):
        sh8[mm_ - 1, mm_] = 1.0
    m["sh8"] = sh8
    m["r4096"] = (np.arange(8, dtype=np.float32) * 4096).reshape(8, 1)
    rb = np.zeros((128, 8), np.float32)
    rb[:, :] = (np.arange(8) * CAP - 1.0)[None, :]
    m["rbasem1"] = rb
    si = np.zeros((128, 20), np.float32)
    si[:, :] = np.arange(128)[:, None] + 128 * np.arange(20)[None, :]
    m["siota"] = si
    for k, v in CONSTS.items():
        m["c_" + k] = v
    return m


def build_program_B():
    nc = bass.Bass("TRN2", target_bir_lowering=False)
    io = {}
    io["x1T"] = nc.dram_tensor("x1T", [D, T], F32, kind="ExternalInput")
    io["x2T"] = nc.dram_tensor("x2T", [D, T], F32, kind="ExternalOutput")
    declare_stage_B_io(nc, io)
    for k, v in CONSTS.items():
        io["c_" + k] = nc.dram_tensor("c_" + k, list(v.shape),
                                      BF16 if v.dtype == BF else F32,
                                      kind="ExternalInput")
    ctx = Ctx(nc)
    for s in ("dg", "ds", "mm", "ev", "ea", "cc"):
        ctx.add_sem(s)
    C = alloc_consts(nc, CORE_CONSTS)
    S = alloc_stage_B(nc)
    pb = [nc.alloc_psum_tensor(f"pb{i}", [128, TB], F32) for i in range(8)]
    blk = Blk(ctx)
    load_consts(blk, C, {k: io["c_" + k] for k in CONSTS})
    blk.wait("sync", "ds", ctx.cnt["ds"])
    blk.emit()
    emit_stage_B(ctx, S, C, pb, io)
    return nc



class _StageAlloc:

    def __init__(self, nc, stack):
        self.nc = nc
        self.stack = stack

    def alloc_sbuf_tensor(self, *a, **kw):
        return self.stack.enter_context(self.nc.sbuf_tensor(*a, **kw))


def _drain(ctx, bar=None):
    blk = Blk(ctx)
    for sem in ("dg", "ds", "cc"):
        if sem in ctx.cnt:
            blk.wait("gpsimd", sem, ctx.cnt[sem])
    if bar is not None:
        src, dst = bar
        blk.add("gpsimd", lambda e: e.collective_compute(
            "AllGather", ALU.bypass, replica_groups=RG8,
            ins=[src.ap().opt()], outs=[dst.ap().opt()]), inc="cc")
        blk.wait("gpsimd", "cc", ctx.cnt["cc"])
    blk.emit()


def build_program_fused():
    nc = bass.Bass("TRN2", target_bir_lowering=False)
    io = {}
    io["xT"] = nc.dram_tensor("xT", [D, T], F32, kind="ExternalInput")
    xkind = ("ExternalOutput" if os.environ.get("K_XOUT", "0") == "1"
             else "Internal")
    io["x1T"] = nc.dram_tensor("x1T", [D, T], F32, kind=xkind)
    io["x2T"] = nc.dram_tensor("x2T", [D, T], F32, kind=xkind)
    io["x3T"] = nc.dram_tensor("x3T", [D, T], F32, kind=xkind)
    io["outT"] = nc.dram_tensor("outT", [D, T], F32, kind="ExternalOutput")
    for name in A_WEIGHTS:
        io[name] = nc.dram_tensor(name, A_SHAPES[name], F32,
                                  kind="ExternalInput")
    io["scr_WDQT"] = nc.dram_tensor("scr_WDQT", [4096, 1024], BF16)
    io["scr_WDKVT"] = nc.dram_tensor("scr_WDKVT", [256, 1024], BF16)
    io["scr_M1"] = nc.dram_tensor("scr_M1", [1024, 1024], BF16)
    io["scr_M2"] = nc.dram_tensor("scr_M2", [1024, 512], BF16)
    io["scr_M3"] = nc.dram_tensor("scr_M3", [1024, 1024], BF16)
    io["scr_M4"] = nc.dram_tensor("scr_M4", [1024, 1024], BF16)
    declare_stage_B_io(nc, io)
    declare_stage_C_io(nc, io)
    declare_stage_D_io(nc, io)
    for k, v in CONSTS.items():
        io["c_" + k] = nc.dram_tensor("c_" + k, list(v.shape),
                                      BF16 if v.dtype == BF else F32,
                                      kind="ExternalInput")
    cio = {k: io["c_" + k] for k in CONSTS}
    bars = {}
    for st in ("A", "B", "C"):
        bars[st] = (nc.dram_tensor(f"bar_{st}_src", [1, 32], F32,
                                   kind="Internal"),
                    nc.dram_tensor(f"bar_{st}_dst", [8, 32], F32,
                                   kind="Internal"))
    pb = [nc.alloc_psum_tensor(f"pb{i}", [128, TB], F32) for i in range(8)]
    C = alloc_consts(nc, CORE_CONSTS)

    with contextlib.ExitStack() as stk:
        shim = _StageAlloc(nc, stk)
        ctx = Ctx(nc, prefix="A_")
        for s in ("dg", "ds", "mm", "ev", "ea", "cc"):
            ctx.add_sem(s)
        CA = dict(C)
        CA.update(alloc_consts(shim,
                               [k for k in A_CONSTS if k not in CORE_CONSTS]))
        S = alloc_stage_A(shim)
        blk = Blk(ctx)
        load_consts(blk, CA, cio)
        blk.wait("sync", "ds", ctx.cnt["ds"])
        blk.emit()
        emit_stage_A(ctx, S, CA, pb, io)
        _drain(ctx, bar=bars["A"])

    with contextlib.ExitStack() as stk:
        shim = _StageAlloc(nc, stk)
        ctx = Ctx(nc, prefix="B_")
        for s in ("dg", "ds", "mm", "ev", "ea", "cc"):
            ctx.add_sem(s)
        S = alloc_stage_B(shim)
        emit_stage_B(ctx, S, C, pb, io)
        _drain(ctx, bar=bars["B"])

    with contextlib.ExitStack() as stk:
        shim = _StageAlloc(nc, stk)
        ctx = Ctx(nc, prefix="C_")
        for s in ("dg", "ds", "mm", "ev", "ea", "eg", "cc"):
            ctx.add_sem(s)
        S = alloc_stage_C(shim)
        emit_stage_C(ctx, S, C, pb, io)
        _drain(ctx, bar=bars["C"])

    with contextlib.ExitStack() as stk:
        shim = _StageAlloc(nc, stk)
        ctx = Ctx(nc, prefix="D_")
        for s in ("dg", "ds", "mm", "ev", "ea", "cc"):
            ctx.add_sem(s)
        S = alloc_stage_D(shim)
        emit_prep_D(ctx, S, C, pb, io)
        emit_stage_D(ctx, S, C, pb, io)
        _drain(ctx)
    return nc


def kernel(**inputs):
    x = np.asarray(inputs["L0_attn_x"], dtype=np.float32)
    n = 8
    ncF = build_program_fused()
    maps = []
    for c in range(n):
        m = {}
        m.update(stage_A_in_map(x[c], inputs))
        m.update(stage_B_in_map(c, None, inputs))
        m.update(stage_C_in_map(None, inputs))
        m.update(stage_D_in_map(c, None, inputs))
        maps.append(m)
    res = run_bass_kernel_spmd(ncF, maps, core_ids=list(range(n)))
    out = np.stack([np.asarray(res.results[c]["outT"]).T
                    for c in range(n)], axis=0)
    return out.astype(np.float32)
